# version 03
# baseline (speedup 1.0000x reference)
import numpy as np
import ml_dtypes as mld
from contextlib import ExitStack

import concourse.bass as bass
import concourse.mybir as mybir
from concourse.bass_utils import run_bass_kernel_spmd
from concourse.dve_spec import Spec, Src0, Src1, Zero, maxx, minn, select, lower
from concourse.dve_spec import C0, C1, C2
from concourse.dve_ops import DveOp
from concourse.dve_table_gen import dve_ver_for

F32 = mybir.dt.float32
BF16 = mybir.dt.bfloat16
I32 = mybir.dt.int32
AF = mybir.ActivationFunctionType
ALU = mybir.AluOpType
AX = mybir.AxisListType
bf16 = mld.bfloat16

T = 2048
D = 1024
DC = D // 128
NT = 512
TC4 = T // NT
NCORES = 8

def _register_custom_ops():
    import concourse.dve_ops as dvo
    ver = dve_ver_for("TRN2")

    def mk(name, spec, reference):
        for o in dvo.OPS:
            if o.name == name:
                return o
        sha = lower(spec, ver=ver, perf_en=False, name=name).sha(ver)
        op = DveOp(name, spec, subdim=False, uops_sha={ver: sha}, reference=reference)
        dvo.OPS.append(op)
        row = dvo._CUSTOM_DVE_ROW_BASE + len(dvo.OPS) - 1
        assert row <= 31, f"custom DVE row overflow: {row}"
        dvo._SUB_OPCODE_FOR_NAME[name] = row
        dvo.CUSTOM_DVE_REFS[name] = reference
        dvo.CUSTOM_DVE_SPECS[name] = spec
        return op

    _v = minn(maxx(Src0, C0), C1)
    sgn = mk(
        "FXTRUNC_SGN",
        Spec(body=select(_v > Zero, Src1 - (Src1 > _v), Src1 + (Src1 < _v))),
        lambda in0, in1, s0, s1, imm2: np.where(
            np.clip(in0, s0, s1) > 0,
            in1 - (in1 > np.clip(in0, s0, s1)),
            in1 + (in1 < np.clip(in0, s0, s1)),
        ),
    )
    pos = mk(
        "FXTRUNC_POS",
        Spec(body=(Src1 - (Src1 > Src0 * C0)) * C2),
        lambda in0, in1, s0, s1, imm2: (in1 - (in1 > in0 * s0)) * imm2,
    )
    return sgn, pos

FXTRUNC_SGN, FXTRUNC_POS = _register_custom_ops()

class Eng:

    _PASS = {"wait_ge", "then_inc", "sem_inc", "nop", "drain", "engine_nop"}

    def __init__(self, ctx, raw, key, serialize=True):
        self._ctx = ctx
        self._raw = raw
        self._key = key
        self._ser = serialize

    def wait(self, key, val):
        if val > 0:
            self._raw.wait_ge(self._ctx.sem(key), val)

    def raw(self):
        return self._raw

    def dma(self, dst, src, key="dml"):
        self._raw.dma_start(dst, src).then_inc(self._ctx.sem(key), 16)
        return self._ctx.bump(key, 16)

    def matmul(self, out, lhsT, rhs, start=True, stop=True, **kw):
        ins = self._raw.matmul(out, lhsT, rhs, start=start, stop=stop, **kw)
        ins.then_inc(self._ctx.sem(self._key), 1)
        self._ctx.bump(self._key, 1)
        return ins

    def custom(self, op, **kw):
        return self._wrap(self._raw._custom_dve)(op, **kw)

    def _wrap(self, raw_fn):
        ctx = self._ctx
        key = self._key

        def call(*a, **k):
            sem = ctx.sem(key)
            if self._ser and ctx.cnt[key] > 0:
                self._raw.wait_ge(sem, ctx.cnt[key])
            ins = raw_fn(*a, **k)
            ins.then_inc(sem, 1)
            ctx.bump(key, 1)
            return ins

        return call

    def __getattr__(self, name):
        raw_fn = getattr(self._raw, name)
        if name in Eng._PASS or name.startswith("dma") or name.startswith("reg"):
            return raw_fn
        return self._wrap(raw_fn)


class Ctx:
    def __init__(self, nc):
        self.nc = nc
        self.scopes = [ExitStack()]
        self._sems = {}
        self.cnt = {}
        self._uid = 0

    def uid(self):
        self._uid += 1
        return self._uid

    def sem(self, name):
        if name not in self._sems:
            self._sems[name] = self.scopes[0].enter_context(self.nc.semaphore(name))
            self.cnt[name] = 0
        return self._sems[name]

    def bump(self, name, k=1):
        self.sem(name)
        self.cnt[name] += k
        return self.cnt[name]

    def val(self, name):
        self.sem(name)
        return self.cnt[name]

    def v(self, raw):
        return Eng(self, raw, "SV")

    def s(self, raw):
        return Eng(self, raw, "SS")

    def q(self, raw):
        return Eng(self, raw, "SQ")

    def t(self, raw):
        return Eng(self, raw, "SP", serialize=False)

    def sbuf(self, name, shape, dt=F32):
        return self.scopes[-1].enter_context(
            self.nc.sbuf_tensor(f"{name}_{self.uid()}", shape, dt))

    def psum(self, name, shape, dt=F32):
        return self.scopes[-1].enter_context(
            self.nc.psum_tensor(f"{name}_{self.uid()}", shape, dt))

    def push(self):
        self.scopes.append(ExitStack())

    def pop(self):
        self.scopes.pop().close()

    def close(self):
        while self.scopes:
            self.scopes.pop().close()


def load_weight_bf16(ctx, w_hbm, K, M, name="w", nstg=2):
    nc = ctx.nc
    KC = K // 128
    wb = ctx.sbuf(name, [128, KC * M], BF16)
    ctx.push()
    nstg = min(nstg, KC)
    stg = [ctx.sbuf(name + f"_stg{i}", [128, M], F32) for i in range(nstg)]
    with nc.Block(f"ldw_{name}_{ctx.uid()}") as blk:
        d0 = [ctx.val("dmlA"), ctx.val("dmlB")]
        sv0 = ctx.val("SV")

        @blk.vector
        def _(vr):
            v = ctx.v(vr)
            for kc in range(KC):
                par = kc % 2
                v.raw().wait_ge(ctx.sem("dmlA" if par == 0 else "dmlB"),
                                d0[par] + 16 * (kc // 2 + 1))
                v.tensor_copy(wb[:, kc * M:(kc + 1) * M], stg[kc % nstg][:, 0:M])

        @blk.sync
        def _(s):
            for kc in range(KC):
                par = kc % 2
                if kc >= nstg:
                    s.wait_ge(ctx.sem("SV"), sv0 + kc - nstg + 1)
                psem = ctx.sem("dmlA" if par == 0 else "dmlB")
                s.dma_start(stg[kc % nstg][:, 0:M],
                            w_hbm[kc * 128:(kc + 1) * 128, :]).then_inc(psem, 16)
                ctx.bump("dmlA" if par == 0 else "dmlB", 16)
    ctx.pop()
    return wb


def load_raw(ctx, hbm_ap, P, Fr, name="c", dt=F32):
    nc = ctx.nc
    tile = ctx.sbuf(name, [P, Fr], dt)
    with nc.Block(f"ldc_{name}_{ctx.uid()}") as blk:
        @blk.sync
        def _(s):
            s.dma_start(tile[:, :], hbm_ap).then_inc(ctx.sem("dml"), 16)
            s.wait_ge(ctx.sem("dml"), ctx.bump("dml", 16))
    return tile


def emit_spill(ctx, sb, dram_ap, nch, width, name="sp"):
    nc = ctx.nc
    with nc.Block(f"spill_{name}_{ctx.uid()}") as blk:
        @blk.sync
        def _(s):
            for c in range(nch):
                s.dma_start(dram_ap[c * 128:(c + 1) * 128, :],
                            sb[:, c * width:(c + 1) * width]).then_inc(ctx.sem("dml"), 16)
                ctx.bump("dml", 16)
            s.wait_ge(ctx.sem("dml"), ctx.val("dml"))


def emit_restore(ctx, sb, dram_ap, nch, width, name="rs"):
    nc = ctx.nc
    with nc.Block(f"rest_{name}_{ctx.uid()}") as blk:
        @blk.sync
        def _(s):
            for c in range(nch):
                s.dma_start(sb[:, c * width:(c + 1) * width],
                            dram_ap[c * 128:(c + 1) * 128, :]).then_inc(ctx.sem("dml"), 16)
                ctx.bump("dml", 16)
            s.wait_ge(ctx.sem("dml"), ctx.val("dml"))


def emit_mm(ctx, lhsT_fn, rhs_fn, KC, MC, TCn, epilogue, Nw=NT, name="mm",
            psum_banks=4, epi_engine="vector", Mp=128):
    nc = ctx.nc
    ekey = "SV" if epi_engine == "vector" else "SS"
    ctx.push()
    ps = [ctx.psum(f"ps_{name}", [Mp, Nw]) for _ in range(psum_banks)]

    with nc.Block(f"{name}_{ctx.uid()}") as blk:
        p0 = ctx.val("SP")
        epi_done = []

        sec = blk.vector if epi_engine == "vector" else blk.scalar

        @sec
        def _(er):
            eng = ctx.v(er) if epi_engine == "vector" else ctx.s(er)
            for i in range(MC * TCn):
                mc, tc = divmod(i, TCn)
                b = i % psum_banks
                eng.wait("SP", p0 + (i + 1) * KC)
                epilogue(eng, ps[b][:, :], mc, tc)
                epi_done.append(ctx.val(ekey))

        @blk.tensor
        def _(tr):
            te = ctx.t(tr)
            for i in range(MC * TCn):
                mc, tc = divmod(i, TCn)
                b = i % psum_banks
                if i >= psum_banks:
                    te.wait(ekey, epi_done[i - psum_banks])
                for kc in range(KC):
                    te.matmul(ps[b][:, :], lhsT_fn(kc, mc), rhs_fn(kc, tc),
                              start=(kc == 0), stop=(kc == KC - 1))
    ctx.pop()


def emit_ln_tc(ctx, get_x, xh_put, consts, tc, name="ln"):
    nc = ctx.nc
    onescol = consts["onescol"]
    onesrow = consts["onesrow"]
    ctx.push()
    mean_r = ctx.sbuf(f"{name}_mean", [1, NT])
    var_r = ctx.sbuf(f"{name}_var", [1, NT])
    xsq = ctx.sbuf(f"{name}_xsq", [128, NT])
    tmp = ctx.sbuf(f"{name}_tmp", [128, NT])
    tmp2 = ctx.sbuf(f"{name}_tmp2", [128, NT])
    rstd_b = ctx.sbuf(f"{name}_rstdb", [128, NT])
    ps_s = ctx.psum(f"{name}_ps_s", [1, NT])
    ps_q = ctx.psum(f"{name}_ps_q", [1, NT])
    ps_mb = ctx.psum(f"{name}_ps_mb", [128, NT])
    ps_vb = ctx.psum(f"{name}_ps_vb", [128, NT])

    with nc.Block(f"{name}_t{tc}_{ctx.uid()}") as blk:
        p0 = ctx.val("SP")
        ss0 = ctx.val("SS")
        sq_done = []
        stats_done = []

        @blk.vector
        def _(vr):
            v = ctx.v(vr)
            for c in range(DC):
                if c >= 1:
                    v.wait("SP", p0 + 2 * c)
                v.tensor_tensor(xsq[:, :], get_x(c), get_x(c), ALU.mult)
                sq_done.append(ctx.val("SV"))
            v.wait("SP", p0 + 2 * DC)
            v.tensor_scalar(mean_r[:, :], ps_s[:, :], 1.0 / D, None, ALU.mult)
            v.tensor_scalar(var_r[:, :], ps_q[:, :], 1.0 / D, 1e-5, ALU.mult, ALU.add)
            v.tensor_tensor(tmp[0:1, :], mean_r[:, :], mean_r[:, :], ALU.mult)
            v.tensor_tensor(var_r[:, :], var_r[:, :], tmp[0:1, :], ALU.subtract)
            stats_done.append(ctx.val("SV"))
            v.wait("SP", p0 + 2 * DC + 1)
            v.wait("SS", ss0 + 2)
            for c in range(DC):
                bf_ap, f32_ap = xh_put(c)
                v.tensor_tensor(tmp2[:, :], get_x(c), ps_mb[:, :], ALU.subtract)
                if f32_ap is not None:
                    v.tensor_tensor(f32_ap, tmp2[:, :], rstd_b[:, :], ALU.mult)
                    v.tensor_copy(bf_ap, f32_ap)
                else:
                    v.tensor_tensor(bf_ap, tmp2[:, :], rstd_b[:, :], ALU.mult)

        @blk.tensor
        def _(tr):
            te = ctx.t(tr)
            for c in range(DC):
                te.matmul(ps_s[:, :], onescol[:, :], get_x(c),
                          start=(c == 0), stop=(c == DC - 1))
                te.wait("SV", sq_done[c])
                te.matmul(ps_q[:, :], onescol[:, :], xsq[:, :],
                          start=(c == 0), stop=(c == DC - 1))
            te.wait("SV", stats_done[0])
            te.matmul(ps_mb[:, :], onesrow[:, :], mean_r[:, :], start=True, stop=True)
            te.matmul(ps_vb[:, :], onesrow[:, :], var_r[:, :], start=True, stop=True)

        @blk.scalar
        def _(sr):
            sc = ctx.s(sr)
            sc.wait("SP", p0 + 2 * DC + 2)
            sc.activation(tmp[:, :], ps_vb[:, :], AF.Ln)
            sc.activation(rstd_b[:, :], tmp[:, :], AF.Exp, scale=-0.5)
    ctx.pop()


def emit_ln(ctx, x, xh_bf, consts, xh_f32=None, name="ln"):
    for tc in range(TC4):
        emit_ln_tc(
            ctx,
            lambda c, tc=tc: x[:, c * T + tc * NT: c * T + tc * NT + NT],
            lambda c, tc=tc: (
                xh_bf[:, c * T + tc * NT: c * T + tc * NT + NT],
                None if xh_f32 is None else xh_f32[:, c * T + tc * NT: c * T + tc * NT + NT],
            ),
            consts, tc, name=name)


def emit_l2norm(ctx, zin, zout, consts, name="l2n", gelu_res_x=None):
    nc = ctx.nc
    onescol_b = consts["onescolb"]
    onesrow = consts["onesrow"]
    ctx.push()
    zsq = ctx.sbuf(f"{name}_zsq", [128, NT], BF16)
    rr = ctx.sbuf(f"{name}_rr", [1, NT])
    tmp = ctx.sbuf(f"{name}_tmp", [128, NT])
    rb_sb = ctx.sbuf(f"{name}_rb", [128, NT])
    tmpg = [ctx.sbuf(f"{name}_tg", [128, NT]) for _ in range(2)]
    tmpg2 = [ctx.sbuf(f"{name}_tg2", [128, NT]) for _ in range(2)]
    ps_q = ctx.psum(f"{name}_psq", [1, NT])
    ps_rb = ctx.psum(f"{name}_psrb", [128, NT])

    for tc in range(TC4):
        with nc.Block(f"{name}_t{tc}_{ctx.uid()}") as blk:
            p0 = ctx.val("SP")
            sq_done = []
            rr_done = []
            rb_done = []
            mul_done = []

            @blk.vector
            def _(vr):
                v = ctx.v(vr)
                for c in range(DC):
                    if c >= 1:
                        v.wait("SP", p0 + c)
                    zsl = zin[:, c * T + tc * NT: c * T + tc * NT + NT]
                    v.tensor_tensor(zsq[:, :], zsl, zsl, ALU.mult)
                    sq_done.append(ctx.val("SV"))
                v.wait("SP", p0 + DC)
                v.tensor_scalar(rr[:, :], ps_q[:, :], 1e-24, None, ALU.max)
                rr_done.append(ctx.val("SV"))

            @blk.tensor
            def _(tr):
                te = ctx.t(tr)
                for c in range(DC):
                    te.wait("SV", sq_done[c])
                    te.matmul(ps_q[:, :], onescol_b[:, :], zsq[:, :],
                              start=(c == 0), stop=(c == DC - 1))
                te.wait("SV", rr_done[0])
                te.matmul(ps_rb[:, :], onesrow[:, :], rr[:, :], start=True, stop=True)

            @blk.scalar
            def _(sr):
                sc = ctx.s(sr)
                sc.wait("SP", p0 + DC + 1)
                sc.activation(tmp[:, :], ps_rb[:, :], AF.Ln)
                sc.activation(rb_sb[:, :], tmp[:, :], AF.Exp, scale=-0.5)
                rb_done.append(ctx.val("SS"))

            if gelu_res_x is None:
                @blk.vector
                def _(vr):
                    v = ctx.v(vr)
                    v.wait("SS", rb_done[0])
                    for c in range(DC):
                        zsl = zin[:, c * T + tc * NT: c * T + tc * NT + NT]
                        v.tensor_tensor(zout[:, c * T + tc * NT: c * T + tc * NT + NT],
                                        zsl, rb_sb[:, :], ALU.mult)
            else:
                @blk.vector
                def _(vr):
                    v = ctx.v(vr)
                    v.wait("SS", rb_done[0])
                    for c in range(DC):
                        zsl = zin[:, c * T + tc * NT: c * T + tc * NT + NT]
                        if c >= 2:
                            v.wait("SS", rb_done[0] + c - 1)
                        v.tensor_tensor(tmpg[c % 2][:, :], zsl, rb_sb[:, :], ALU.mult)
                        mul_done.append(ctx.val("SV"))
                        v.wait("SS", rb_done[0] + c + 1)
                        xs = gelu_res_x[:, c * T + tc * NT: c * T + tc * NT + NT]
                        v.tensor_tensor(xs, xs, tmpg2[c % 2][:, :], ALU.add)

                @blk.scalar
                def _(sr):
                    sc = ctx.s(sr)
                    for c in range(DC):
                        sc.wait("SV", mul_done[c])
                        sc.activation(tmpg2[c % 2][:, :], tmpg[c % 2][:, :], AF.Gelu)
    ctx.pop()


def emit_l1_ffn(ctx, x, weights, consts):
    nc = ctx.nc
    ctx.push()
    w1 = load_weight_bf16(ctx, weights["L1_ffn_w1"].ap(), 1024, 4096, "l1f_w1")
    b1 = load_raw(ctx, weights["L1_ffn_b1c"].ap(), 128, 32, "l1f_b1")
    b2 = load_raw(ctx, weights["L1_ffn_b2c"].ap(), 128, 8, "l1f_b2")
    xh = ctx.sbuf("l1f_xh", [128, DC * NT], BF16)
    h = ctx.sbuf("l1f_h", [128, 32 * NT], BF16)

    for tc in range(TC4):
        emit_ln_tc(ctx,
                   lambda c: x[:, c * T + tc * NT: c * T + tc * NT + NT],
                   lambda c: (xh[:, c * NT:(c + 1) * NT], None),
                   consts, tc, name="l1f_ln")

        def epi1(eng, ps_ap, mc, _tc):
            eng.activation(h[:, mc * NT:(mc + 1) * NT], ps_ap,
                           AF.Gelu, bias=b1[:, mc:mc + 1])

        emit_mm(ctx,
                lambda kc, mc: w1[:, kc * 4096 + mc * 128: kc * 4096 + mc * 128 + 128],
                lambda kc, _tc: xh[:, kc * NT:(kc + 1) * NT],
                KC=8, MC=32, TCn=1, epilogue=epi1, name=f"l1f_m1_{tc}",
                epi_engine="scalar")

        emit_mm_kstream(ctx, weights["L1_ffn_w2"].ap(), 4096, 1024,
                        lambda kc: h[:, kc * NT:(kc + 1) * NT],
                        lambda eng, ps_ap, mc, tc=tc: eng.scalar_tensor_tensor(
                            x[:, mc * T + tc * NT: mc * T + tc * NT + NT],
                            ps_ap, b2[:, mc:mc + 1],
                            x[:, mc * T + tc * NT: mc * T + tc * NT + NT],
                            ALU.add, ALU.add),
                        name=f"l1f_m2_{tc}")
    ctx.pop()


def emit_mm_kstream(ctx, w_hbm, K, M, rhs_fn, epilogue, Nw=NT, name="mmk"):
    nc = ctx.nc
    KC = K // 128
    MC = M // 128
    assert MC <= 8
    ctx.push()
    ps = [ctx.psum(f"ps_{name}", [128, Nw]) for _ in range(MC)]
    stg = [ctx.sbuf(f"{name}_stg{i}", [128, M], F32) for i in range(2)]
    wt = [ctx.sbuf(f"{name}_wt{i}", [128, M], BF16) for i in range(2)]

    with nc.Block(f"{name}_{ctx.uid()}") as blk:
        d0 = [ctx.val("dmlA"), ctx.val("dmlB")]
        p0 = ctx.val("SP")
        cv_done = []

        @blk.vector
        def _(vr):
            v = ctx.v(vr)
            for kc in range(KC):
                par = kc % 2
                v.raw().wait_ge(ctx.sem("dmlA" if par == 0 else "dmlB"),
                                d0[par] + 16 * (kc // 2 + 1))
                if kc >= 2:
                    v.wait("SP", p0 + (kc - 1) * MC)
                v.tensor_copy(wt[kc % 2][:, :], stg[kc % 2][:, :])
                cv_done.append(ctx.val("SV"))
            for mc in range(MC):
                v.wait("SP", p0 + KC * MC)
                epilogue(v, ps[mc][:, :], mc)

        @blk.sync
        def _(s):
            for kc in range(KC):
                par = kc % 2
                if kc >= 2:
                    s.wait_ge(ctx.sem("SV"), cv_done[kc - 1])
                psem = ctx.sem("dmlA" if par == 0 else "dmlB")
                s.dma_start(stg[kc % 2][:, :],
                            w_hbm[kc * 128:(kc + 1) * 128, :]).then_inc(psem, 16)
                ctx.bump("dmlA" if par == 0 else "dmlB", 16)

        @blk.tensor
        def _(tr):
            te = ctx.t(tr)
            for kc in range(KC):
                te.wait("SV", cv_done[kc])
                for mc in range(MC):
                    te.matmul(ps[mc][:, :], wt[kc % 2][:, mc * 128:(mc + 1) * 128],
                              rhs_fn(kc), start=(kc == 0), stop=(kc == KC - 1))
    ctx.pop()


def emit_l3_ffn(ctx, x, weights, consts):
    nc = ctx.nc
    ctx.push()
    xh = ctx.sbuf("l3f_xh", [128, DC * T], BF16)
    emit_ln(ctx, x, xh, consts, name="l3f_ln")

    ctx.push()
    rt_fm = ctx.sbuf("l3f_rt", [8, T])
    coef_fm = ctx.sbuf("l3f_coef", [8, T])
    ctx.push()
    router = load_weight_bf16(ctx, weights["L3_ffn_router"].ap(), 1024, 8, "l3f_rtw")

    emit_mm(ctx, lambda kc, mc: router[:, kc * 8:(kc + 1) * 8],
            lambda kc, tc: xh[:, kc * T + tc * NT: kc * T + tc * NT + NT],
            KC=8, MC=1, TCn=TC4,
            epilogue=lambda eng, ps, mc, tc: eng.tensor_copy(
                rt_fm[:, tc * NT:(tc + 1) * NT], ps),
            name="l3f_rtmm", psum_banks=2, Mp=8)

    ident = consts["identf"]
    rt_tm = ctx.sbuf("l3f_rttm", [128, 16 * 8])
    co_tm = ctx.sbuf("l3f_cotm", [128, 16 * 8])
    tmp_a = ctx.sbuf("l3f_ta", [128, 16 * 8])
    tmp_b = ctx.sbuf("l3f_tb", [128, 16 * 8])
    s1 = ctx.sbuf("l3f_s1", [128, 16])
    s1b = ctx.sbuf("l3f_s1b", [128, 16])
    pst2 = [ctx.psum("l3f_pst", [128, 128]) for _ in range(2)]

    def b3(t):
        return bass.AP(t, 0, [[16, 128], [1, 16], [0, 8]])

    def r3(t):
        return t[:, :].rearrange("p (a b) -> p a b", b=8)

    with nc.Block(f"l3f_rtt_{ctx.uid()}") as blk:
        p0 = ctx.val("SP")
        cp_done = []

        @blk.vector
        def _(vr):
            v = ctx.v(vr)
            for i in range(16):
                v.wait("SP", p0 + i + 1)
                v.tensor_copy(rt_tm[:, i * 8:(i + 1) * 8], pst2[i % 2][:, 0:8])
                cp_done.append(ctx.val("SV"))

        @blk.tensor
        def _(tr):
            te = ctx.t(tr)
            for i in range(16):
                if i >= 2:
                    te.wait("SV", cp_done[i - 2])
                te.transpose(pst2[i % 2][:, 0:8], rt_fm[:, i * 128:(i + 1) * 128],
                             ident[0:8, 0:8])

    with nc.Block(f"l3f_top2_{ctx.uid()}") as blk:
        ss0 = ctx.val("SS")
        pre_done = []
        sum_done = []

        @blk.vector
        def _(vr):
            v = ctx.v(vr)
            v.reduce_max(s1[:, :], r3(rt_tm), AX.X, negate=True)
            v.tensor_tensor(r3(tmp_a), r3(rt_tm), b3(s1), ALU.add)
            pre_done.append(ctx.val("SV"))
            v.wait("SS", ss0 + 1)
            v.reduce_sum(s1[:, :], r3(tmp_b), AX.X)
            sum_done.append(ctx.val("SV"))
            v.wait("SS", ss0 + 3)
            v.tensor_tensor(r3(rt_tm), r3(tmp_b), b3(s1b), ALU.mult)
            v.reduce_max(s1[:, :], r3(rt_tm), AX.X)
            v.tensor_tensor(r3(tmp_a), r3(rt_tm), b3(s1), ALU.is_lt)
            v.tensor_tensor(tmp_b[:, :], rt_tm[:, :], tmp_a[:, :], ALU.mult)
            v.reduce_max(s1[:, :], r3(tmp_b), AX.X)
            v.tensor_tensor(r3(tmp_a), r3(rt_tm), b3(s1), ALU.is_ge)
            v.tensor_tensor(co_tm[:, :], rt_tm[:, :], tmp_a[:, :], ALU.mult)

        @blk.scalar
        def _(sr):
            sc = ctx.s(sr)
            sc.wait("SV", pre_done[0])
            sc.activation(tmp_b[:, :], tmp_a[:, :], AF.Exp)
            sc.wait("SV", sum_done[0])
            sc.activation(tmp_a[:, 0:16], s1[:, :], AF.Ln)
            sc.activation(s1b[:, :], tmp_a[:, 0:16], AF.Exp, scale=-1.0)

    with nc.Block(f"l3f_cot_{ctx.uid()}") as blk:
        p0 = ctx.val("SP")
        cp2 = []

        @blk.vector
        def _(vr):
            v = ctx.v(vr)
            for i in range(16):
                v.wait("SP", p0 + i + 1)
                v.tensor_copy(coef_fm[:, i * 128:(i + 1) * 128], pst2[0][0:8, 0:128])
                cp2.append(ctx.val("SV"))

        @blk.tensor
        def _(tr):
            te = ctx.t(tr)
            for i in range(16):
                if i >= 1:
                    te.wait("SV", cp2[i - 1])
                te.transpose(pst2[0][0:8, 0:128], co_tm[:, i * 8:(i + 1) * 8],
                             ident[:, :])
    ctx.pop()

    inner_dram4 = [nc.dram_tensor(f"l3f_innerd{tc}_{ctx.uid()}", [32 * 128, NT], BF16,
                                  kind="Internal") for tc in range(TC4)]
    for es in range(2):
        ctx.push()
        wg = load_weight_bf16(ctx, weights["L3_ffn_wg"].ap()[:, es * 2048:(es + 1) * 2048],
                              1024, 2048, f"l3f_wg{es}")
        wu = load_weight_bf16(ctx, weights["L3_ffn_wu"].ap()[:, es * 2048:(es + 1) * 2048],
                              1024, 2048, f"l3f_wu{es}")
        inner = ctx.sbuf("l3f_inner", [128, 16 * NT], BF16)
        gtile = [ctx.sbuf("l3f_gt", [128, NT], BF16) for _ in range(2)]

        for tc in range(TC4):
            ctx.push()
            psg = [ctx.psum("l3f_psg", [128, NT]) for _ in range(2)]
            psu = [ctx.psum("l3f_psu", [128, NT]) for _ in range(2)]
            with nc.Block(f"l3f_gu_{es}_{tc}_{ctx.uid()}") as blk:
                p0 = ctx.val("SP")
                ss0 = ctx.val("SS")
                sv0 = ctx.val("SV")
                silu_at = lambda m: ss0 + m + 1
                mul_at = lambda m: sv0 + m + 1

                @blk.scalar
                def _(sr):
                    sc = ctx.s(sr)
                    for mc in range(16):
                        sc.wait("SP", p0 + (2 * mc + 1) * 8)
                        if mc >= 2:
                            sc.wait("SV", mul_at(mc - 2))
                        sc.activation(gtile[mc % 2][:, :], psg[mc % 2][:, :], AF.Silu)

                @blk.vector
                def _(vr):
                    v = ctx.v(vr)
                    for mc in range(16):
                        v.wait("SP", p0 + (2 * mc + 2) * 8)
                        v.wait("SS", silu_at(mc))
                        v.tensor_tensor(inner[:, mc * NT:(mc + 1) * NT],
                                        psu[mc % 2][:, :], gtile[mc % 2][:, :],
                                        ALU.mult)

                @blk.tensor
                def _(tr):
                    te = ctx.t(tr)
                    for mc in range(16):
                        if mc >= 2:
                            te.wait("SS", silu_at(mc - 2))
                            te.wait("SV", mul_at(mc - 2))
                        for kc in range(8):
                            te.matmul(psg[mc % 2][:, :],
                                      wg[:, kc * 2048 + mc * 128: kc * 2048 + mc * 128 + 128],
                                      xh[:, kc * T + tc * NT: kc * T + tc * NT + NT],
                                      start=(kc == 0), stop=(kc == 7))
                        for kc in range(8):
                            te.matmul(psu[mc % 2][:, :],
                                      wu[:, kc * 2048 + mc * 128: kc * 2048 + mc * 128 + 128],
                                      xh[:, kc * T + tc * NT: kc * T + tc * NT + NT],
                                      start=(kc == 0), stop=(kc == 7))
            ctx.pop()

            ctx.push()
            psc = ctx.psum("l3f_psc", [128, NT])
            with nc.Block(f"l3f_coef_{es}_{tc}_{ctx.uid()}") as blk:
                p0 = ctx.val("SP")
                cmul = []

                @blk.vector
                def _(vr):
                    v = ctx.v(vr)
                    for el in range(4):
                        v.wait("SP", p0 + el + 1)
                        iv = inner[:, el * 4 * NT:(el + 1) * 4 * NT].rearrange(
                            "p (c n) -> p c n", n=NT)
                        pb = bass.AP(psc, 0, [[NT, 128], [0, 4], [1, NT]])
                        v.tensor_tensor(iv, iv, pb, ALU.mult)
                        cmul.append(ctx.val("SV"))

                @blk.tensor
                def _(tr):
                    te = ctx.t(tr)
                    for el in range(4):
                        e = es * 4 + el
                        if el >= 1:
                            te.wait("SV", cmul[el - 1])
                        te.matmul(psc[:, :], consts["esel"][:, e * 128:(e + 1) * 128],
                                  coef_fm[:, tc * NT:(tc + 1) * NT],
                                  start=True, stop=True)
            ctx.pop()
            emit_spill(ctx, inner, inner_dram4[tc].ap()[es * 2048:(es + 1) * 2048, :],
                       16, NT, f"l3f_in{es}_{tc}")
        ctx.pop()
    ctx.pop()

    ctx.push()
    wd = load_weight_bf16(ctx, weights["L3_ffn_wd"].ap(), 4096, 1024, "l3f_wd")
    inner = ctx.sbuf("l3f_inner2", [128, 32 * NT], BF16)
    for tc in range(TC4):
        emit_restore(ctx, inner, inner_dram4[tc].ap(), 32, NT, f"l3f_in{tc}")
        emit_mm(ctx,
                lambda kc, mc: wd[:, kc * 1024 + mc * 128: kc * 1024 + mc * 128 + 128],
                lambda kc, _tc: inner[:, kc * NT:(kc + 1) * NT],
                KC=32, MC=8, TCn=1,
                epilogue=lambda eng, ps, mc, _tc, tc=tc: eng.tensor_tensor(
                    x[:, mc * T + tc * NT: mc * T + tc * NT + NT],
                    x[:, mc * T + tc * NT: mc * T + tc * NT + NT], ps, ALU.add),
                name=f"l3f_down_{tc}")
    ctx.pop()

    ctx.push()
    wsg = load_weight_bf16(ctx, weights["L3_ffn_shared_gate"].ap(), 1024, 1024, "l3f_wsg")
    wsu = load_weight_bf16(ctx, weights["L3_ffn_shared_up"].ap(), 1024, 1024, "l3f_wsu")
    sboth = ctx.sbuf("l3f_sb", [128, DC * T], BF16)
    stile = [ctx.sbuf("l3f_st", [128, NT], BF16) for _ in range(2)]
    ctx.push()
    psg = [ctx.psum("l3f_psg2", [128, NT]) for _ in range(2)]
    psu = [ctx.psum("l3f_psu2", [128, NT]) for _ in range(2)]
    with nc.Block(f"l3f_sgu_{ctx.uid()}") as blk:
        p0 = ctx.val("SP")
        ss0 = ctx.val("SS")
        sv0 = ctx.val("SV")
        silu_at = lambda m: ss0 + m + 1
        mul_at = lambda m: sv0 + m + 1

        @blk.scalar
        def _(sr):
            sc = ctx.s(sr)
            for i in range(32):
                sc.wait("SP", p0 + (2 * i + 1) * 8)
                if i >= 2:
                    sc.wait("SV", mul_at(i - 2))
                sc.activation(stile[i % 2][:, :], psg[i % 2][:, :], AF.Silu)

        @blk.vector
        def _(vr):
            v = ctx.v(vr)
            for i in range(32):
                mc, tc = divmod(i, TC4)
                v.wait("SP", p0 + (2 * i + 2) * 8)
                v.wait("SS", silu_at(i))
                v.tensor_tensor(sboth[:, mc * T + tc * NT: mc * T + tc * NT + NT],
                                psu[i % 2][:, :], stile[i % 2][:, :], ALU.mult)

        @blk.tensor
        def _(tr):
            te = ctx.t(tr)
            for i in range(32):
                mc, tc = divmod(i, TC4)
                if i >= 2:
                    te.wait("SS", silu_at(i - 2))
                    te.wait("SV", mul_at(i - 2))
                for kc in range(8):
                    te.matmul(psg[i % 2][:, :],
                              wsg[:, kc * 1024 + mc * 128: kc * 1024 + mc * 128 + 128],
                              xh[:, kc * T + tc * NT: kc * T + tc * NT + NT],
                              start=(kc == 0), stop=(kc == 7))
                for kc in range(8):
                    te.matmul(psu[i % 2][:, :],
                              wsu[:, kc * 1024 + mc * 128: kc * 1024 + mc * 128 + 128],
                              xh[:, kc * T + tc * NT: kc * T + tc * NT + NT],
                              start=(kc == 0), stop=(kc == 7))
    ctx.pop()

    wsd = load_weight_bf16(ctx, weights["L3_ffn_shared_down"].ap(), 1024, 1024, "l3f_wsd")
    emit_mm(ctx,
            lambda kc, mc: wsd[:, kc * 1024 + mc * 128: kc * 1024 + mc * 128 + 128],
            lambda kc, tc: sboth[:, kc * T + tc * NT: kc * T + tc * NT + NT],
            KC=8, MC=8, TCn=TC4,
            epilogue=lambda eng, ps, mc, tc: eng.tensor_tensor(
                x[:, mc * T + tc * NT: mc * T + tc * NT + NT],
                x[:, mc * T + tc * NT: mc * T + tc * NT + NT], ps, ALU.add),
            name="l3f_sdown")
    ctx.pop()
    ctx.pop()


def emit_l2_ffn(ctx, x, weights, consts):
    nc = ctx.nc
    ctx.push()
    zA_dram = nc.dram_tensor(f"l2f_zad_{ctx.uid()}", [D, T], BF16, kind="Internal")

    ctx.push()
    q_fm = ctx.sbuf("l2f_q", [128, DC * T], BF16)
    k_fm = ctx.sbuf("l2f_k", [128, DC * T], BF16)
    v_tm = ctx.sbuf("l2f_v", [128, 16 * 1024], BF16)
    TH = T // 2
    for th in range(2):
        ctx.push()
        xh = ctx.sbuf("l2f_xh", [128, DC * TH], BF16)
        for tci in range(2):
            tc = th * 2 + tci
            emit_ln_tc(ctx,
                       lambda c: x[:, c * T + tc * NT: c * T + tc * NT + NT],
                       lambda c: (xh[:, c * TH + tci * NT: c * TH + tci * NT + NT], None),
                       consts, tc, name="l2f_ln")

        ctx.push()
        wq = load_weight_bf16(ctx, weights["L2_ffn_Wq"].ap(), 1024, 1024, "l2f_wq")
        emit_mm(ctx,
                lambda kc, mc: wq[:, kc * 1024 + mc * 128: kc * 1024 + mc * 128 + 128],
                lambda kc, tci: xh[:, kc * TH + tci * NT: kc * TH + tci * NT + NT],
                KC=8, MC=8, TCn=2,
                epilogue=lambda eng, ps, mc, tci, th=th: eng.tensor_copy(
                    q_fm[:, mc * T + (th * 2 + tci) * NT: mc * T + (th * 2 + tci) * NT + NT], ps),
                name=f"l2f_qm{th}")
        ctx.pop()

        ctx.push()
        wk = load_weight_bf16(ctx, weights["L2_ffn_Wk"].ap(), 1024, 1024, "l2f_wk")
        emit_mm(ctx,
                lambda kc, mc: wk[:, kc * 1024 + mc * 128: kc * 1024 + mc * 128 + 128],
                lambda kc, tci: xh[:, kc * TH + tci * NT: kc * TH + tci * NT + NT],
                KC=8, MC=8, TCn=2,
                epilogue=lambda eng, ps, mc, tci, th=th: eng.tensor_scalar(
                    k_fm[:, mc * T + (th * 2 + tci) * NT: mc * T + (th * 2 + tci) * NT + NT],
                    ps, 1.0 / 32.0, None, ALU.mult),
                name=f"l2f_km{th}")
        ctx.pop()

        ctx.push()
        wv = load_weight_bf16(ctx, weights["L2_ffn_Wv"].ap(), 1024, 1024, "l2f_wv")
        emit_mm(ctx,
                lambda kc, mc: xh[:, kc * TH + mc * 128: kc * TH + mc * 128 + 128],
                lambda kc, nh: wv[:, kc * 1024 + nh * NT: kc * 1024 + nh * NT + NT],
                KC=8, MC=8, TCn=2,
                epilogue=lambda eng, ps, mc, nh, th=th: eng.tensor_copy(
                    v_tm[:, (th * 8 + mc) * 1024 + nh * NT: (th * 8 + mc) * 1024 + nh * NT + NT],
                    ps),
                name=f"l2f_vm{th}")
        ctx.pop()
        ctx.pop()

    ctx.push()
    expS = ctx.sbuf("l2f_es", [128, T], BF16)
    negmx = ctx.sbuf("l2f_nm", [128, 1])
    sums4 = ctx.sbuf("l2f_s4", [128, 4])
    rcol = ctx.sbuf("l2f_rc", [128, 1])
    rcol2 = ctx.sbuf("l2f_rc2", [128, 1])
    vt_h = ctx.sbuf("l2f_vt", [128, 64], BF16)
    oev = ctx.sbuf("l2f_oev", [64, T], BF16)
    psS = ctx.psum("l2f_psS", [128, T])
    psO = ctx.psum("l2f_psO", [64, T])

    for h in range(16):
        hb = (h % 2) * 64
        hc = h // 2
        with nc.Block(f"l2f_att_{h}_{ctx.uid()}") as blk:
            p0 = ctx.val("SP")
            v0 = ctx.val("SV")
            s0 = ctx.val("SS")

            @blk.tensor
            def _(tr):
                te = ctx.t(tr)
                for kt in range(16):
                    if kt >= 1:
                        te.wait("SS", s0 + 6 * (kt - 1) + 4)
                    for qc in range(TC4):
                        te.matmul(psS[:, qc * NT:(qc + 1) * NT],
                                  k_fm[hb:hb + 64, hc * T + kt * 128: hc * T + kt * 128 + 128],
                                  q_fm[hb:hb + 64, hc * T + qc * NT: hc * T + qc * NT + NT],
                                  start=True, stop=True)
                    te.wait("SV", v0 + 3 * kt + 3)
                    for qc in range(TC4):
                        te.matmul(psO[:, qc * NT:(qc + 1) * NT], vt_h[:, :],
                                  expS[:, qc * NT:(qc + 1) * NT],
                                  start=(kt == 0), stop=(kt == 15))

            @blk.vector
            def _(vr):
                v = ctx.v(vr)
                for kt in range(16):
                    v.wait("SP", p0 + 8 * kt + 4)
                    v.reduce_max(negmx[:, :], psS[:, :], AX.X, negate=True)
                    v.wait("SS", s0 + 6 * kt + 4)
                    v.reduce_sum(rcol[:, :], sums4[:, :], AX.X)
                    v.wait("SS", s0 + 6 * kt + 6)
                    v.tensor_scalar(vt_h[:, :],
                                    v_tm[:, kt * 1024 + h * 64: kt * 1024 + h * 64 + 64],
                                    rcol[:, :], None, ALU.mult)

            @blk.scalar
            def _(sr):
                sc = ctx.s(sr)
                for kt in range(16):
                    sc.wait("SV", v0 + 3 * kt + 1)
                    for qc in range(TC4):
                        sc.activation(expS[:, qc * NT:(qc + 1) * NT],
                                      psS[:, qc * NT:(qc + 1) * NT], AF.Exp,
                                      bias=negmx[:, :],
                                      accum_out=sums4[:, qc:qc + 1])
                    sc.wait("SV", v0 + 3 * kt + 2)
                    sc.activation(rcol2[:, :], rcol[:, :], AF.Ln)
                    sc.activation(rcol[:, :], rcol2[:, :], AF.Exp, scale=-1.0)

        with nc.Block(f"l2f_oev_{h}_{ctx.uid()}") as blk:
            ev_done = []

            @blk.vector
            def _(vr):
                v = ctx.v(vr)
                v.tensor_copy(oev[:, :], psO[:, :])
                ev_done.append(ctx.val("SV"))

            @blk.sync
            def _(s):
                s.wait_ge(ctx.sem("SV"), ev_done[0])
                s.dma_start(zA_dram.ap()[h * 64:(h + 1) * 64, :],
                            oev[:, :]).then_inc(ctx.sem("dml"), 16)
                s.wait_ge(ctx.sem("dml"), ctx.bump("dml", 16))
    ctx.pop()
    ctx.pop()

    ctx.push()
    zA = ctx.sbuf("l2f_zA", [128, DC * T], BF16)
    zB = ctx.sbuf("l2f_zB", [128, DC * T], BF16)
    emit_restore(ctx, zA, zA_dram.ap(), DC, T, "l2f_za")
    ctx.push()
    wo = load_weight_bf16(ctx, weights["L2_ffn_Wo"].ap(), 1024, 1024, "l2f_wo")
    emit_mm(ctx,
            lambda kc, mc: wo[:, kc * 1024 + mc * 128: kc * 1024 + mc * 128 + 128],
            lambda kc, tc: zA[:, kc * T + tc * NT: kc * T + tc * NT + NT],
            KC=8, MC=8, TCn=TC4,
            epilogue=lambda eng, ps, mc, tc: eng.tensor_copy(
                zB[:, mc * T + tc * NT: mc * T + tc * NT + NT], ps),
            name="l2f_wo")
    ctx.pop()

    emit_l2norm(ctx, zB, zA, consts, name="l2f_n1")

    ctx.push()
    wf = load_weight_bf16(ctx, weights["L2_ffn_Wff"].ap(), 1024, 1024, "l2f_wf")
    emit_mm(ctx,
            lambda kc, mc: wf[:, kc * 1024 + mc * 128: kc * 1024 + mc * 128 + 128],
            lambda kc, tc: zA[:, kc * T + tc * NT: kc * T + tc * NT + NT],
            KC=8, MC=8, TCn=TC4,
            epilogue=lambda eng, ps, mc, tc: eng.tensor_copy(
                zB[:, mc * T + tc * NT: mc * T + tc * NT + NT], ps),
            name="l2f_wf")
    ctx.pop()

    emit_l2norm(ctx, zB, zA, consts, name="l2f_n2", gelu_res_x=x)
    ctx.pop()
    ctx.pop()


def emit_l3_attn(ctx, x, weights, consts):
    nc = ctx.nc
    ctx.push()
    ao_dram = nc.dram_tensor(f"l3a_aod_{ctx.uid()}", [D, T], BF16, kind="Internal")
    ctx.push()
    q_fm = ctx.sbuf("l3a_q", [128, DC * T], BF16)
    k_fm = ctx.sbuf("l3a_k", [128, DC * T], BF16)
    v_tm = ctx.sbuf("l3a_v", [128, 16 * 1024], BF16)
    bq = load_raw(ctx, weights["L3_attn_bqc"].ap(), 128, 16, "l3a_bq")
    vbias = ctx.sbuf("l3a_vb", [128, 1024])

    vbr = load_raw(ctx, weights["L3_attn_bvrow"].ap(), 1, 1024, "l3a_vbr")
    ctx.push()
    psb = ctx.psum("l3a_psb", [128, NT])
    with nc.Block(f"l3a_vb_{ctx.uid()}") as blk:
        p0 = ctx.val("SP")
        sv0 = ctx.val("SV")

        @blk.vector
        def _(vr):
            v = ctx.v(vr)
            for i in range(2):
                v.wait("SP", p0 + i + 1)
                v.tensor_copy(vbias[:, i * NT:(i + 1) * NT], psb[:, :])

        @blk.tensor
        def _(tr):
            te = ctx.t(tr)
            for i in range(2):
                if i >= 1:
                    te.wait("SV", sv0 + i)
                te.matmul(psb[:, :], consts["onesrow"][:, :],
                          vbr[0:1, i * NT:(i + 1) * NT], start=True, stop=True)
    ctx.pop()

    TH = T // 2
    for th in range(2):
        ctx.push()
        xh = ctx.sbuf("l3a_xh", [128, DC * TH], BF16)
        for tci in range(2):
            tc = th * 2 + tci
            emit_ln_tc(ctx,
                       lambda c: x[:, c * T + tc * NT: c * T + tc * NT + NT],
                       lambda c: (xh[:, c * TH + tci * NT: c * TH + tci * NT + NT], None),
                       consts, tc, name="l3a_ln")

        ctx.push()
        wq = load_weight_bf16(ctx, weights["L3_attn_wqkvT"].ap()[:, 0:1024],
                              1024, 1024, "l3a_wq", nstg=1)
        emit_mm(ctx,
                lambda kc, mc: wq[:, kc * 1024 + mc * 128: kc * 1024 + mc * 128 + 128],
                lambda kc, tci: xh[:, kc * TH + tci * NT: kc * TH + tci * NT + NT],
                KC=8, MC=8, TCn=2,
                epilogue=lambda eng, ps, mc, tci, th=th: eng.tensor_scalar(
                    q_fm[:, mc * T + (th * 2 + tci) * NT: mc * T + (th * 2 + tci) * NT + NT],
                    ps, bq[:, mc:mc + 1], 0.125, ALU.add, ALU.mult),
                name=f"l3a_qm{th}")
        ctx.pop()

        ctx.push()
        wk = load_weight_bf16(ctx, weights["L3_attn_wqkvT"].ap()[:, 1024:2048],
                              1024, 1024, "l3a_wk", nstg=1)
        emit_mm(ctx,
                lambda kc, mc: wk[:, kc * 1024 + mc * 128: kc * 1024 + mc * 128 + 128],
                lambda kc, tci: xh[:, kc * TH + tci * NT: kc * TH + tci * NT + NT],
                KC=8, MC=8, TCn=2,
                epilogue=lambda eng, ps, mc, tci, th=th: eng.tensor_scalar(
                    k_fm[:, mc * T + (th * 2 + tci) * NT: mc * T + (th * 2 + tci) * NT + NT],
                    ps, bq[:, 8 + mc:8 + mc + 1], None, ALU.add),
                name=f"l3a_km{th}")
        ctx.pop()

        ctx.push()
        wv = load_weight_bf16(ctx, weights["L3_attn_wqkvT"].ap()[:, 2048:3072],
                              1024, 1024, "l3a_wv", nstg=1)
        emit_mm(ctx,
                lambda kc, mc: xh[:, kc * TH + mc * 128: kc * TH + mc * 128 + 128],
                lambda kc, nh: wv[:, kc * 1024 + nh * NT: kc * 1024 + nh * NT + NT],
                KC=8, MC=8, TCn=2,
                epilogue=lambda eng, ps, mc, nh, th=th: eng.tensor_tensor(
                    v_tm[:, (th * 8 + mc) * 1024 + nh * NT: (th * 8 + mc) * 1024 + nh * NT + NT],
                    ps, vbias[:, nh * NT:(nh + 1) * NT], ALU.add),
                name=f"l3a_vm{th}")
        ctx.pop()
        ctx.pop()

    masks = load_raw(ctx, weights["L3_attn_masks"].ap(), 128, 3 * 384, "l3a_mk")
    ctx.push()
    s_sb = ctx.sbuf("l3a_ssb", [128, 384])
    p_bf = ctx.sbuf("l3a_pbf", [128, 384], BF16)
    pt_sb = ctx.sbuf("l3a_pt", [128, 384], BF16)
    negm = ctx.sbuf("l3a_ng", [128, 1])
    rsum = ctx.sbuf("l3a_rs", [128, 1])
    rrec = ctx.sbuf("l3a_rr", [128, 1])
    rrec2 = ctx.sbuf("l3a_rr2", [128, 1])
    oevt = ctx.sbuf("l3a_oevt", [64, 128], BF16)
    psS = [ctx.psum("l3a_psS", [128, 384]) for _ in range(2)]
    psT = [ctx.psum("l3a_psT", [128, 384], BF16) for _ in range(2)]
    psO = [ctx.psum("l3a_psO", [64, 128]) for _ in range(2)]

    for h in range(16):
        hb = (h % 2) * 64
        hc = h // 2
        for qt in range(16):
            kw = min(max(qt * 128 - 128, 0), T - 384)
            kc0 = kw // 128
            dlt = (qt * 128 - kw) // 128
            pb = qt % 2
            with nc.Block(f"l3a_att_{h}_{qt}_{ctx.uid()}") as blk:
                p0 = ctx.val("SP")
                sv0 = ctx.val("SV")
                ss0 = ctx.val("SS")

                @blk.vector
                def _(vr):
                    v = ctx.v(vr)
                    v.wait("SP", p0 + 1)
                    v.tensor_tensor(s_sb[:, :], psS[pb][:, :],
                                    masks[:, dlt * 384:(dlt + 1) * 384], ALU.add)
                    v.reduce_max(negm[:, :], s_sb[:, :], AX.X, negate=True)
                    v.wait("SS", ss0 + 3)
                    v.tensor_scalar(p_bf[:, :], p_bf[:, :], rrec[:, :], None, ALU.mult)
                    v.wait("SP", p0 + 4)
                    v.tensor_copy(pt_sb[:, :], psT[pb][:, :])
                    v.wait("SP", p0 + 7)
                    v.tensor_copy(oevt[:, :], psO[pb][:, :])

                @blk.scalar
                def _(sr):
                    sc = ctx.s(sr)
                    sc.wait("SV", sv0 + 2)
                    sc.activation(p_bf[:, :], s_sb[:, :], AF.Exp, bias=negm[:, :],
                                  accum_out=rsum[:, :])
                    sc.activation(rrec2[:, :], rsum[:, :], AF.Ln)
                    sc.activation(rrec[:, :], rrec2[:, :], AF.Exp, scale=-1.0)

                @blk.tensor
                def _(tr):
                    te = ctx.t(tr)
                    te.matmul(psS[pb][:, :],
                              q_fm[hb:hb + 64, hc * T + qt * 128: hc * T + qt * 128 + 128],
                              k_fm[hb:hb + 64, hc * T + kw: hc * T + kw + 384],
                              start=True, stop=True)
                    te.wait("SV", sv0 + 3)
                    for c in range(3):
                        te.transpose(psT[pb][:, c * 128:(c + 1) * 128],
                                     p_bf[:, c * 128:(c + 1) * 128],
                                     consts["identb"][:, :])
                    te.wait("SV", sv0 + 4)
                    for c in range(3):
                        te.matmul(psO[pb][:, :],
                                  v_tm[:, (kc0 + c) * 1024 + h * 64: (kc0 + c) * 1024 + h * 64 + 64],
                                  pt_sb[:, c * 128:(c + 1) * 128],
                                  start=(c == 0), stop=(c == 2))

                @blk.sync
                def _(s):
                    s.wait_ge(ctx.sem("SV"), sv0 + 5)
                    s.dma_start(
                        ao_dram.ap()[hc * 128 + hb: hc * 128 + hb + 64,
                                     qt * 128:(qt + 1) * 128],
                        oevt[:, :]).then_inc(ctx.sem("dml"), 16)
                    s.wait_ge(ctx.sem("dml"), ctx.bump("dml", 16))
    ctx.pop()
    ctx.pop()

    ctx.push()
    ao_fm = ctx.sbuf("l3a_ao", [128, DC * T], BF16)
    emit_restore(ctx, ao_fm, ao_dram.ap(), DC, T, "l3a_ao")
    wo = load_weight_bf16(ctx, weights["L3_attn_woutT"].ap(), 1024, 1024, "l3a_wo")
    bo = load_raw(ctx, weights["L3_attn_boc"].ap(), 128, 8, "l3a_bo")
    emit_mm(ctx,
            lambda kc, mc: wo[:, kc * 1024 + mc * 128: kc * 1024 + mc * 128 + 128],
            lambda kc, tc: ao_fm[:, kc * T + tc * NT: kc * T + tc * NT + NT],
            KC=8, MC=8, TCn=TC4,
            epilogue=lambda eng, ps, mc, tc: eng.scalar_tensor_tensor(
                x[:, mc * T + tc * NT: mc * T + tc * NT + NT],
                ps, bo[:, mc:mc + 1],
                x[:, mc * T + tc * NT: mc * T + tc * NT + NT], ALU.add, ALU.add),
            name="l3a_wo")
    ctx.pop()
    ctx.pop()


def _fxp_epi(eng, ps_ap, out_ap, final_scale, tmp, tmpi, tmp2):
    eng.tensor_scalar(tmp, ps_ap, 4096.0, None, ALU.mult)
    eng.tensor_scalar(tmpi, tmp, -32767.0, 32767.0, ALU.max, ALU.min)
    eng.custom(FXTRUNC_SGN, out=tmp2, in0=tmp, in1=tmpi, s0=-32767.0, s1=32767.0)
    eng.tensor_scalar(out_ap, tmp2, final_scale, None, ALU.mult)


def emit_l1_attn(ctx, x, weights, consts):
    nc = ctx.nc
    ctx.push()
    ao_dram = nc.dram_tensor(f"l1a_aod_{ctx.uid()}", [D, T], BF16, kind="Internal")
    ln4096 = load_raw(ctx, weights["L1_attn_ln4096"].ap(), 1, 1, "l1a_l4")
    ctx.push()
    q_fm = ctx.sbuf("l1a_q", [128, DC * T], BF16)
    k_fm = ctx.sbuf("l1a_k", [128, DC * T], BF16)
    v_tm = ctx.sbuf("l1a_v", [128, 16 * 1024], BF16)
    ftmp = ctx.sbuf("l1a_ft", [128, NT])
    ftmpi = ctx.sbuf("l1a_fti", [128, NT], I32)
    ftmp2 = ctx.sbuf("l1a_ft2", [128, NT])

    TH = T // 2
    for th in range(2):
        ctx.push()
        xh = ctx.sbuf("l1a_xh", [128, DC * TH], BF16)
        for tci in range(2):
            tc = th * 2 + tci
            emit_ln_tc(ctx,
                       lambda c: x[:, c * T + tc * NT: c * T + tc * NT + NT],
                       lambda c: (xh[:, c * TH + tci * NT: c * TH + tci * NT + NT], None),
                       consts, tc, name="l1a_ln")

        ctx.push()
        wq = load_weight_bf16(ctx, weights["L1_attn_wqkvT"].ap()[:, 0:1024],
                              1024, 1024, "l1a_wq", nstg=1)
        emit_mm(ctx,
                lambda kc, mc: wq[:, kc * 1024 + mc * 128: kc * 1024 + mc * 128 + 128],
                lambda kc, tci: xh[:, kc * TH + tci * NT: kc * TH + tci * NT + NT],
                KC=8, MC=8, TCn=2,
                epilogue=lambda eng, ps, mc, tci, th=th: _fxp_epi(
                    eng, ps,
                    q_fm[:, mc * T + (th * 2 + tci) * NT: mc * T + (th * 2 + tci) * NT + NT],
                    0.125, ftmp[:, :], ftmpi[:, :], ftmp2[:, :]),
                name=f"l1a_qm{th}", psum_banks=3)
        ctx.pop()

        ctx.push()
        wk = load_weight_bf16(ctx, weights["L1_attn_wqkvT"].ap()[:, 1024:2048],
                              1024, 1024, "l1a_wk", nstg=1)
        emit_mm(ctx,
                lambda kc, mc: wk[:, kc * 1024 + mc * 128: kc * 1024 + mc * 128 + 128],
                lambda kc, tci: xh[:, kc * TH + tci * NT: kc * TH + tci * NT + NT],
                KC=8, MC=8, TCn=2,
                epilogue=lambda eng, ps, mc, tci, th=th: _fxp_epi(
                    eng, ps,
                    k_fm[:, mc * T + (th * 2 + tci) * NT: mc * T + (th * 2 + tci) * NT + NT],
                    2.0 ** -12, ftmp[:, :], ftmpi[:, :], ftmp2[:, :]),
                name=f"l1a_km{th}", psum_banks=3)
        ctx.pop()

        ctx.push()
        wv = load_weight_bf16(ctx, weights["L1_attn_wqkvT"].ap()[:, 2048:3072],
                              1024, 1024, "l1a_wv", nstg=1)
        emit_mm(ctx,
                lambda kc, mc: xh[:, kc * TH + mc * 128: kc * TH + mc * 128 + 128],
                lambda kc, nh: wv[:, kc * 1024 + nh * NT: kc * 1024 + nh * NT + NT],
                KC=8, MC=8, TCn=2,
                epilogue=lambda eng, ps, mc, nh, th=th: _fxp_epi(
                    eng, ps,
                    v_tm[:, (th * 8 + mc) * 1024 + nh * NT: (th * 8 + mc) * 1024 + nh * NT + NT],
                    2.0 ** -12, ftmp[:, :], ftmpi[:, :], ftmp2[:, :]),
                name=f"l1a_vm{th}", psum_banks=3)
        ctx.pop()
        ctx.pop()

    ctx.push()
    expS = ctx.sbuf("l1a_es", [128, 16 * NT], BF16)
    fint = ctx.sbuf("l1a_fi", [128, NT])
    rint = ctx.sbuf("l1a_ri", [128, NT], I32)
    ntile = ctx.sbuf("l1a_nt", [128, NT])
    nint = ctx.sbuf("l1a_ni", [128, NT], I32)
    attn_bf = [ctx.sbuf("l1a_ab", [128, NT], BF16) for _ in range(2)]
    rb_sb = ctx.sbuf("l1a_rb", [128, NT])
    rrow = ctx.sbuf("l1a_rrow", [1, NT])
    rrow2 = ctx.sbuf("l1a_rrow2", [1, NT])
    oevt = ctx.sbuf("l1a_oevt", [64, NT], BF16)
    oft = ctx.sbuf("l1a_oft", [64, NT])
    ofti = ctx.sbuf("l1a_ofti", [64, NT], I32)
    oft2 = ctx.sbuf("l1a_oft2", [64, NT])
    psS = [ctx.psum("l1a_psS", [128, NT]) for _ in range(2)]
    psSum = ctx.psum("l1a_psSum", [1, NT])
    psRB = ctx.psum("l1a_psRB", [128, NT])
    psO = ctx.psum("l1a_psO", [64, NT])

    for h in range(16):
        hb = (h % 2) * 64
        hc = h // 2
        for qq in range(TC4):
            with nc.Block(f"l1a_att_{h}_{qq}_{ctx.uid()}") as blk:
                p0 = ctx.val("SP")
                sv0 = ctx.val("SV")
                ss0 = ctx.val("SS")

                @blk.vector
                def _(vr):
                    v = ctx.v(vr)
                    for kt in range(16):
                        v.wait("SP", p0 + 2 * kt + 1)
                        if kt >= 1:
                            v.wait("SS", ss0 + kt)
                        v.tensor_scalar(rint[:, :], psS[kt % 2][:, :],
                                        -32767.0, 32767.0, ALU.max, ALU.min)
                        v.custom(FXTRUNC_SGN, out=fint[:, :], in0=psS[kt % 2][:, :],
                                 in1=rint[:, :], s0=-32767.0, s1=32767.0)
                    v.wait("SP", p0 + 33)
                    v.tensor_copy(rb_sb[:, :], psRB[:, :])
                    for kt in range(16):
                        v.tensor_tensor(ntile[:, :], expS[:, kt * NT:(kt + 1) * NT],
                                        rb_sb[:, :], ALU.mult)
                        v.tensor_scalar(nint[:, :], ntile[:, :], 1.0, None, ALU.mult)
                        if kt >= 2:
                            v.wait("SP", p0 + 33 + kt - 1)
                        v.custom(FXTRUNC_POS, out=attn_bf[kt % 2][:, :], in0=ntile[:, :],
                                 in1=nint[:, :], s0=1.0, imm2=2.0 ** -12)

                @blk.scalar
                def _(sr):
                    sc = ctx.s(sr)
                    for kt in range(16):
                        sc.wait("SV", sv0 + 2 * kt + 2)
                        sc.activation(expS[:, kt * NT:(kt + 1) * NT], fint[:, :],
                                      AF.Exp, scale=2.0 ** -12)
                    sc.wait("SP", p0 + 32)
                    sc.activation(rrow2[:, :], psSum[:, :], AF.Ln)
                    sc.activation(rrow[:, :], rrow2[:, :], AF.Exp,
                                  scale=-1.0, bias=ln4096[0:1, 0:1])

                @blk.tensor
                def _(tr):
                    te = ctx.t(tr)
                    for kt in range(16):
                        if kt >= 2:
                            te.wait("SS", ss0 + kt - 1)
                        te.matmul(psS[kt % 2][:, :],
                                  k_fm[hb:hb + 64, hc * T + kt * 128: hc * T + kt * 128 + 128],
                                  q_fm[hb:hb + 64, hc * T + qq * NT: hc * T + qq * NT + NT],
                                  start=True, stop=True)
                        te.wait("SS", ss0 + kt + 1)
                        te.matmul(psSum[:, :], consts["onescolb"][:, :],
                                  expS[:, kt * NT:(kt + 1) * NT],
                                  start=(kt == 0), stop=(kt == 15))
                    te.wait("SS", ss0 + 18)
                    te.matmul(psRB[:, :], consts["onesrow"][:, :], rrow[:, :],
                              start=True, stop=True)
                    for kt in range(16):
                        te.wait("SV", sv0 + 33 + 3 * kt + 3)
                        te.matmul(psO[:, :],
                                  v_tm[:, kt * 1024 + h * 64: kt * 1024 + h * 64 + 64],
                                  attn_bf[kt % 2][:, :],
                                  start=(kt == 0), stop=(kt == 15))

            with nc.Block(f"l1a_oev_{h}_{qq}_{ctx.uid()}") as blk:
                done = []

                @blk.vector
                def _(vr):
                    v = ctx.v(vr)
                    _fxp_epi(v, psO[:, :], oevt[:, :], 2.0 ** -12,
                             oft[:, :], ofti[:, :], oft2[:, :])
                    done.append(ctx.val("SV"))

                @blk.sync
                def _(s):
                    s.wait_ge(ctx.sem("SV"), done[0])
                    s.dma_start(
                        ao_dram.ap()[hc * 128 + hb: hc * 128 + hb + 64,
                                     qq * NT:(qq + 1) * NT],
                        oevt[:, :]).then_inc(ctx.sem("dml"), 16)
                    s.wait_ge(ctx.sem("dml"), ctx.bump("dml", 16))
    ctx.pop()
    ctx.pop()

    ctx.push()
    ao_fm = ctx.sbuf("l1a_ao", [128, DC * T], BF16)
    emit_restore(ctx, ao_fm, ao_dram.ap(), DC, T, "l1a_ao")
    wp = load_weight_bf16(ctx, weights["L1_attn_wprojT"].ap(), 1024, 1024, "l1a_wp")
    bp = load_raw(ctx, weights["L1_attn_bpc"].ap(), 128, 8, "l1a_bp")
    ftmp = ctx.sbuf("l1a_pt", [128, NT])
    ftmpi = ctx.sbuf("l1a_pti", [128, NT], I32)
    ftmp2 = ctx.sbuf("l1a_pt2", [128, NT])

    def epi_p(eng, ps, mc, tc):
        eng.tensor_scalar(ftmp[:, :], ps, bp[:, mc:mc + 1], 4096.0, ALU.add, ALU.mult)
        eng.tensor_scalar(ftmpi[:, :], ftmp[:, :], -32767.0, 32767.0, ALU.max, ALU.min)
        eng.custom(FXTRUNC_SGN, out=ftmp2[:, :], in0=ftmp[:, :], in1=ftmpi[:, :],
                   s0=-32767.0, s1=32767.0)
        xs = x[:, mc * T + tc * NT: mc * T + tc * NT + NT]
        eng.scalar_tensor_tensor(xs, ftmp2[:, :], 2.0 ** -12, xs, ALU.mult, ALU.add)

    emit_mm(ctx,
            lambda kc, mc: wp[:, kc * 1024 + mc * 128: kc * 1024 + mc * 128 + 128],
            lambda kc, tc: ao_fm[:, kc * T + tc * NT: kc * T + tc * NT + NT],
            KC=8, MC=8, TCn=TC4, epilogue=epi_p, name="l1a_wp", psum_banks=3)
    ctx.pop()
    ctx.pop()


def emit_l0_ffn(ctx, x, weights, consts):
    nc = ctx.nc
    ctx.push()
    mask = ctx.sbuf("l0f_mk", [64, T], BF16)
    inner = ctx.sbuf("l0f_in", [128, 32 * 128], BF16)
    inner2 = ctx.sbuf("l0f_in2", [128, 32 * 128], BF16)
    rp = load_raw(ctx, weights["L0_ffn_rp"].ap(), 64, 32 * 128, "l0f_rp", BF16)
    besf = load_raw(ctx, weights["L0_ffn_besf"].ap(), 128, 32, "l0f_besf")

    ctx.push()
    xh = ctx.sbuf("l0f_xh", [128, DC * T], BF16)
    emit_ln(ctx, x, xh, consts, name="l0f_ln")

    ctx.push()
    logits = ctx.sbuf("l0f_lg", [64, T])
    gmax = ctx.sbuf("l0f_gm", [64, 128])
    ctx.push()
    ctl = load_weight_bf16(ctx, weights["L0_ffn_ctlR"].ap(), 1024, 64, "l0f_ctl")
    tieT = load_raw(ctx, weights["L0_ffn_tieT"].ap(), 64, T, "l0f_tie")
    emit_mm(ctx, lambda kc, mc: ctl[:, kc * 64:(kc + 1) * 64],
            lambda kc, tc: xh[:, kc * T + tc * NT: kc * T + tc * NT + NT],
            KC=8, MC=1, TCn=TC4,
            epilogue=lambda eng, ps, mc, tc: eng.tensor_tensor(
                logits[:, tc * NT:(tc + 1) * NT], ps,
                tieT[:, tc * NT:(tc + 1) * NT], ALU.add),
            name="l0f_lgmm", psum_banks=2, Mp=64)
    ctx.pop()

    with nc.Block(f"l0f_mask_{ctx.uid()}") as blk:
        @blk.vector
        def _(vr):
            v = ctx.v(vr)
            lg3 = logits[:, :].rearrange("p (a b) -> p a b", b=16)
            v.reduce_max(gmax[:, :], lg3, AX.X)
            gb = bass.AP(gmax, 0, [[128, 64], [1, 128], [0, 16]])
            v.tensor_tensor(mask[:, :].rearrange("p (a b) -> p a b", b=16),
                            lg3, gb, ALU.is_equal)
    ctx.pop()

    ctx.push()
    f1R = load_weight_bf16(ctx, weights["L0_ffn_f1R"].ap(), 1024, 4096, "l0f_f1",
                           nstg=1)
    mrep = ctx.sbuf("l0f_mr", [128, NT], BF16)
    ptile = ctx.sbuf("l0f_pt", [128, NT])
    psG = [ctx.psum("l0f_psG", [128, NT]) for _ in range(2)]
    psM = ctx.psum("l0f_psM", [128, NT])
    for tc in range(TC4):
        for ch in range(32):
            with nc.Block(f"l0f_g_{tc}_{ch}_{ctx.uid()}") as blk:
                p0 = ctx.val("SP")
                sv0 = ctx.val("SV")

                @blk.vector
                def _(vr):
                    v = ctx.v(vr)
                    v.wait("SP", p0 + 1)
                    v.tensor_copy(mrep[:, :], psM[:, :])
                    v.wait("SP", p0 + 9)
                    v.tensor_tensor(ptile[:, :], psG[ch % 2][:, :], mrep[:, :],
                                    ALU.mult)
                    with nc.allow_low_precision(reason="bf16 store of fp32 group-sum"):
                        v.reduce_sum(inner[:, ch * 128 + tc * 32: ch * 128 + tc * 32 + 32],
                                     ptile[:, :].rearrange("p (a b) -> p a b", b=16), AX.X)

                @blk.tensor
                def _(tr):
                    te = ctx.t(tr)
                    te.matmul(psM[:, :], rp[:, ch * 128:(ch + 1) * 128],
                              mask[:, tc * NT:(tc + 1) * NT], start=True, stop=True)
                    for kc in range(8):
                        te.matmul(psG[ch % 2][:, :],
                                  f1R[:, kc * 4096 + ch * 128: kc * 4096 + ch * 128 + 128],
                                  xh[:, kc * T + tc * NT: kc * T + tc * NT + NT],
                                  start=(kc == 0), stop=(kc == 7))

    with nc.Block(f"l0f_relu_{ctx.uid()}") as blk:
        @blk.scalar
        def _(sr):
            sc = ctx.s(sr)
            for ch in range(32):
                sc.activation(inner2[:, ch * 128:(ch + 1) * 128],
                              inner[:, ch * 128:(ch + 1) * 128], AF.Relu,
                              bias=besf[:, ch:ch + 1])
    ctx.pop()
    ctx.pop()

    ctx.push()
    f2R = load_weight_bf16(ctx, weights["L0_ffn_f2R"].ap(), 4096, 1024, "l0f_f2",
                           nstg=1)
    V = ctx.sbuf("l0f_V", [128, 32 * NT], BF16)
    mrep = ctx.sbuf("l0f_mr2", [128, NT], BF16)
    psM = ctx.psum("l0f_psM2", [128, NT])
    for tc in range(TC4):
        for ch in range(32):
            with nc.Block(f"l0f_v_{tc}_{ch}_{ctx.uid()}") as blk:
                p0 = ctx.val("SP")

                @blk.vector
                def _(vr):
                    v = ctx.v(vr)
                    v.wait("SP", p0 + 1)
                    v.tensor_copy(mrep[:, :], psM[:, :])
                    iexp = bass.AP(inner2, ch * 128 + tc * 32,
                                   [[32 * 128, 128], [1, 32], [0, 16]])
                    v.tensor_tensor(V[:, ch * NT:(ch + 1) * NT].rearrange(
                        "p (a b) -> p a b", b=16), iexp, mrep[:, :].rearrange(
                        "p (a b) -> p a b", b=16), ALU.mult)

                @blk.tensor
                def _(tr):
                    te = ctx.t(tr)
                    te.matmul(psM[:, :], rp[:, ch * 128:(ch + 1) * 128],
                              mask[:, tc * NT:(tc + 1) * NT], start=True, stop=True)

        emit_mm(ctx,
                lambda kc, mc: f2R[:, kc * 1024 + mc * 128: kc * 1024 + mc * 128 + 128],
                lambda kc, _tc: V[:, kc * NT:(kc + 1) * NT],
                KC=32, MC=8, TCn=1,
                epilogue=lambda eng, ps, mc, _tc, tc=tc: eng.tensor_tensor(
                    x[:, mc * T + tc * NT: mc * T + tc * NT + NT],
                    x[:, mc * T + tc * NT: mc * T + tc * NT + NT], ps, ALU.add),
                name=f"l0f_f2_{tc}")
    ctx.pop()
    ctx.pop()


def emit_l0_attn(ctx, x, weights, consts):
    nc = ctx.nc
    ctx.push()
    rg = load_raw(ctx, weights["L0_rg"].ap(), 128, 16 * 128, "l0a_rg", BF16)
    bd = load_raw(ctx, weights["L0_bd"].ap(), 128, 16 * 128, "l0a_bd", BF16)
    r2 = load_raw(ctx, weights["L0_r2"].ap(), 16, 128, "l0a_r2", BF16)
    alog = load_raw(ctx, weights["L0_attn_alogpp"].ap(), 128, 128, "l0a_al")
    app = ctx.sbuf("l0a_app", [128, 128])
    bdl = load_raw(ctx, weights["L0_attn_bdlc"].ap(), 64, 1, "l0a_bdl")
    bbm = load_raw(ctx, weights["L0_attn_bbmc"].ap(), 16, 1, "l0a_bbm")
    bcm = load_raw(ctx, weights["L0_attn_bcmc"].ap(), 16, 1, "l0a_bcm")
    bup = load_raw(ctx, weights["L0_attn_bupc"].ap(), 128, 8, "l0a_bup")

    with nc.Block(f"l0a_app_{ctx.uid()}") as blk:
        @blk.scalar
        def _(sr):
            sc = ctx.s(sr)
            sc.activation(app[:, :], alog[:, :], AF.Exp)
        @blk.vector
        def _(vr):
            v = ctx.v(vr)
            v.wait("SS", ctx.val("SS"))
            v.tensor_scalar(app[:, :], app[:, :], -1.0, None, ALU.mult)

    delta_bf = ctx.sbuf("l0a_delta", [128, DC * T], BF16)
    dx_bf = ctx.sbuf("l0a_dx", [128, DC * T], BF16)
    bm_bf = ctx.sbuf("l0a_bm", [16, T], BF16)
    cm_bf = ctx.sbuf("l0a_cm", [16, T], BF16)
    ctx.push()
    xh = ctx.sbuf("l0a_xh", [128, DC * T], BF16)
    emit_ln(ctx, x, xh, consts, name="l0a_ln")

    dlow_bf = ctx.sbuf("l0a_dlow", [64, T], BF16)
    praw = delta_bf
    ctx.push()
    wdbc = load_weight_bf16(ctx, weights["L0_attn_wdbcT"].ap(), 1024, 96, "l0a_wdbc")

    emit_mm(ctx, lambda kc, mc: wdbc[:, kc * 96: kc * 96 + 64],
            lambda kc, tc: xh[:, kc * T + tc * NT: kc * T + tc * NT + NT],
            KC=8, MC=1, TCn=TC4,
            epilogue=lambda eng, ps, mc, tc: eng.tensor_scalar(
                dlow_bf[:, tc * NT:(tc + 1) * NT], ps,
                bdl[:, 0:1], None, ALU.add),
            name="l0a_dbc1", Mp=64)

    emit_mm(ctx, lambda kc, mc: wdbc[:, kc * 96 + 64: kc * 96 + 80],
            lambda kc, tc: xh[:, kc * T + tc * NT: kc * T + tc * NT + NT],
            KC=8, MC=1, TCn=TC4,
            epilogue=lambda eng, ps, mc, tc: eng.tensor_scalar(
                bm_bf[:, tc * NT:(tc + 1) * NT], ps,
                bbm[:, 0:1], None, ALU.add),
            name="l0a_dbc2", Mp=16)

    emit_mm(ctx, lambda kc, mc: wdbc[:, kc * 96 + 80: kc * 96 + 96],
            lambda kc, tc: xh[:, kc * T + tc * NT: kc * T + tc * NT + NT],
            KC=8, MC=1, TCn=TC4,
            epilogue=lambda eng, ps, mc, tc: eng.tensor_scalar(
                cm_bf[:, tc * NT:(tc + 1) * NT], ps,
                bcm[:, 0:1], None, ALU.add),
            name="l0a_dbc3", Mp=16)
    ctx.pop()

    ctx.push()
    wup = ctx.sbuf("l0a_wup", [64, 1024], BF16)
    ctx.push()
    wup_f = load_raw(ctx, weights["L0_attn_wupT"].ap(), 64, 1024, "l0a_wupf")
    with nc.Block(f"l0a_wupc_{ctx.uid()}") as blk:
        @blk.vector
        def _(vr):
            ctx.v(vr).tensor_copy(wup[:, :], wup_f[:, :])
    ctx.pop()
    emit_mm(ctx, lambda kc, mc: wup[:, mc * 128:(mc + 1) * 128],
            lambda kc, tc: dlow_bf[:, tc * NT:(tc + 1) * NT],
            KC=1, MC=8, TCn=TC4,
            epilogue=lambda eng, ps, mc, tc: eng.tensor_scalar(
                praw[:, mc * T + tc * NT: mc * T + tc * NT + NT], ps,
                bup[:, mc:mc + 1], None, ALU.add),
            name="l0a_dup")
    ctx.pop()

    ctx.push()
    spt = ctx.sbuf("l0a_spt", [128, NT])
    spe = ctx.sbuf("l0a_spe", [128, NT])
    spl = ctx.sbuf("l0a_spl", [128, NT])
    spr = ctx.sbuf("l0a_spr", [128, NT])
    for i in range(DC * TC4):
        mc, tc = divmod(i, TC4)
        sl = slice(mc * T + tc * NT, mc * T + tc * NT + NT)
        with nc.Block(f"l0a_sp_{i}_{ctx.uid()}") as blk:
            sv0 = ctx.val("SV")
            ss0 = ctx.val("SS")

            @blk.vector
            def _(vr):
                v = ctx.v(vr)
                v.tensor_scalar(spr[:, :], praw[:, sl], 0.0, None, ALU.max)
                v.scalar_tensor_tensor(spt[:, :], spr[:, :], 2.0, praw[:, sl],
                                       ALU.mult, ALU.subtract)
                v.wait("SS", ss0 + 2)
                v.tensor_tensor(delta_bf[:, sl], spr[:, :], spl[:, :], ALU.add)
                v.tensor_tensor(dx_bf[:, sl], delta_bf[:, sl], xh[:, sl], ALU.mult)

            @blk.scalar
            def _(sr):
                sc = ctx.s(sr)
                sc.wait("SV", sv0 + 2)
                sc.activation(spe[:, :], spt[:, :], AF.Exp, scale=-1.0)
                sc.activation(spl[:, :], spe[:, :], AF.Ln, bias=consts["onescol"][:, :])
    ctx.pop()
    ctx.pop()

    bmpp = ctx.sbuf("l0a_bmpp", [128, T])
    cmpp = ctx.sbuf("l0a_cmpp", [128, T])
    ctx.push()
    pspp = ctx.psum("l0a_pspp", [128, NT])
    with nc.Block(f"l0a_pp_{ctx.uid()}") as blk:
        p0 = ctx.val("SP")
        sv0 = ctx.val("SV")

        @blk.vector
        def _(vr):
            v = ctx.v(vr)
            for i in range(8):
                v.wait("SP", p0 + i + 1)
                dst = (bmpp if i < 4 else cmpp)
                tc = i % 4
                v.tensor_copy(dst[:, tc * NT:(tc + 1) * NT], pspp[:, :])

        @blk.tensor
        def _(tr):
            te = ctx.t(tr)
            for i in range(8):
                src_t = (bm_bf if i < 4 else cm_bf)
                tc = i % 4
                if i >= 1:
                    te.wait("SV", sv0 + i)
                te.matmul(pspp[:, :], r2[:, :], src_t[:, tc * NT:(tc + 1) * NT],
                          start=True, stop=True)
    ctx.pop()

    ctx.push()
    atile = ctx.sbuf("l0a_a", [128, T])
    btile = ctx.sbuf("l0a_b", [128, T])
    htile = ctx.sbuf("l0a_h", [128, T], BF16)
    ptile = ctx.sbuf("l0a_p", [128, T], BF16)
    psdr = ctx.psum("l0a_psdr", [128, NT])
    psdx = ctx.psum("l0a_psdx", [128, NT])
    psy = [ctx.psum("l0a_psy", [128, NT]) for _ in range(4)]
    for dchunk in range(DC):
        for g in range(16):
            G = dchunk * 16 + g
            with nc.Block(f"l0a_scan_{G}_{ctx.uid()}") as blk:
                p0 = ctx.val("SP")
                sv0 = ctx.val("SV")
                ss0 = ctx.val("SS")

                @blk.scalar
                def _(sr):
                    sc = ctx.s(sr)
                    for tc in range(TC4):
                        sc.wait("SP", p0 + 2 * tc + 1)
                        sc.activation(atile[:, tc * NT:(tc + 1) * NT], psdr[:, :],
                                      AF.Exp, scale=app[:, G:G + 1])

                @blk.vector
                def _(vr):
                    v = ctx.v(vr)
                    for tc in range(TC4):
                        v.wait("SP", p0 + 2 * tc + 2)
                        v.tensor_tensor(btile[:, tc * NT:(tc + 1) * NT], psdx[:, :],
                                        bmpp[:, tc * NT:(tc + 1) * NT], ALU.mult)
                    v.wait("SS", ss0 + 4)
                    v.tensor_tensor_scan(htile[:, :], atile[:, :], btile[:, :],
                                         0.0, ALU.mult, ALU.add)
                    v.tensor_tensor(ptile[:, :], htile[:, :], cmpp[:, :], ALU.mult)

                @blk.tensor
                def _(tr):
                    te = ctx.t(tr)
                    for tc in range(TC4):
                        if tc >= 1:
                            te.wait("SS", ss0 + tc)
                            te.wait("SV", sv0 + tc)
                        te.matmul(psdr[:, :], rg[:, g * 128:(g + 1) * 128],
                                  delta_bf[:, dchunk * T + tc * NT: dchunk * T + tc * NT + NT],
                                  start=True, stop=True)
                        te.matmul(psdx[:, :], rg[:, g * 128:(g + 1) * 128],
                                  dx_bf[:, dchunk * T + tc * NT: dchunk * T + tc * NT + NT],
                                  start=True, stop=True)
                    te.wait("SV", sv0 + 6)
                    for tc in range(TC4):
                        te.matmul(psy[tc][:, :], bd[:, g * 128:(g + 1) * 128],
                                  ptile[:, tc * NT:(tc + 1) * NT],
                                  start=(g == 0), stop=(g == 15))

        with nc.Block(f"l0a_y_{dchunk}_{ctx.uid()}") as blk:
            @blk.vector
            def _(vr):
                v = ctx.v(vr)
                for tc in range(TC4):
                    xs = x[:, dchunk * T + tc * NT: dchunk * T + tc * NT + NT]
                    v.tensor_tensor(xs, xs, psy[tc][:, :], ALU.add)
    ctx.pop()
    ctx.pop()


def emit_l2_attn(ctx, x, weights, consts):
    nc = ctx.nc
    DIC = 16
    PW = T + 3
    ctx.push()
    zs_dram = nc.dram_tensor(f"l2a_zsd_{ctx.uid()}", [DIC * 128, T], BF16,
                             kind="Internal")
    xsp_dram = nc.dram_tensor(f"l2a_xspd_{ctx.uid()}", [DIC * 128, PW], BF16,
                              kind="Internal")
    xs_dram = nc.dram_tensor(f"l2a_xsd_{ctx.uid()}", [DIC * 128, T], BF16,
                             kind="Internal")
    y1_dram = nc.dram_tensor(f"l2a_y1d_{ctx.uid()}", [DIC * 128, T], BF16,
                             kind="Internal")

    convw = load_raw(ctx, weights["L2_attn_convwpp"].ap(), 128, 64, "l2a_cw")
    convb = load_raw(ctx, weights["L2_attn_convbc"].ap(), 128, 16, "l2a_cb")
    bdt = load_raw(ctx, weights["L2_attn_bdtc"].ap(), 128, 16, "l2a_bdt")
    dcol = load_raw(ctx, weights["L2_attn_dcol"].ap(), 128, 16, "l2a_dcol")
    identb = consts["identb"]
    bs_bf = ctx.sbuf("l2a_bs", [16, T], BF16)

    ctx.push()
    xh = ctx.sbuf("l2a_xh", [128, DC * T], BF16)
    emit_ln(ctx, x, xh, consts, name="l2a_ln")

    ctx.push()
    zs = ctx.sbuf("l2a_zsil", [128, DIC * T], BF16)
    ctx.push()
    wz = load_weight_bf16(ctx, weights["L2_attn_winz"].ap(), 1024, 2048, "l2a_wz", nstg=1)
    emit_mm(ctx,
            lambda kc, mc: wz[:, kc * 2048 + mc * 128: kc * 2048 + (mc + 1) * 128],
            lambda kc, tc: xh[:, kc * T + tc * NT: kc * T + tc * NT + NT],
            KC=8, MC=DIC, TCn=TC4,
            epilogue=lambda eng, ps, mc, tc: eng.activation(
                zs[:, mc * T + tc * NT: mc * T + tc * NT + NT], ps, AF.Silu),
            name="l2a_zmm", epi_engine="scalar")
    ctx.pop()
    emit_spill(ctx, zs, zs_dram.ap(), DIC, T, "l2a_zs")
    ctx.pop()

    ctx.push()
    wxi = load_weight_bf16(ctx, weights["L2_attn_winx"].ap(), 1024, 2048, "l2a_wxi", nstg=1)
    xpb = ctx.sbuf("l2a_xpb", [128, PW], BF16)
    for mc in range(DIC):
        with nc.Block(f"l2a_xpad_{mc}_{ctx.uid()}") as blk:
            @blk.vector
            def _(vr):
                ctx.v(vr).memset(xpb[:, 0:3], 0.0)
        emit_mm(ctx,
                lambda kc, _mc, mc=mc: wxi[:, kc * 2048 + mc * 128: kc * 2048 + (mc + 1) * 128],
                lambda kc, tc: xh[:, kc * T + tc * NT: kc * T + tc * NT + NT],
                KC=8, MC=1, TCn=TC4,
                epilogue=lambda eng, ps, _mc, tc: eng.tensor_copy(
                    xpb[:, 3 + tc * NT: 3 + tc * NT + NT], ps),
                name=f"l2a_xmm{mc}")
        emit_spill(ctx, xpb, xsp_dram.ap()[mc * 128:(mc + 1) * 128, :], 1, PW,
                   f"l2a_xsp{mc}")
    ctx.pop()
    ctx.pop()

    dt_dram = nc.dram_tensor(f"l2a_dtd_{ctx.uid()}", [DIC * 128, T], BF16,
                             kind="Internal")
    ctx.push()
    xs = ctx.sbuf("l2a_xs", [128, DIC * T], BF16)
    xpb2 = ctx.sbuf("l2a_xpb2", [128, PW], BF16)
    diag4 = ctx.sbuf("l2a_diag4", [128, 4 * 128], BF16)
    psc = ctx.psum("l2a_psc", [128, NT])
    for mc in range(DIC):
        emit_restore(ctx, xpb2, xsp_dram.ap()[mc * 128:(mc + 1) * 128, :], 1, PW,
                     f"l2a_xrs{mc}")
        with nc.Block(f"l2a_conv_{mc}_{ctx.uid()}") as blk:
            p0 = ctx.val("SP")
            sv0 = ctx.val("SV")
            ss0 = ctx.val("SS")

            @blk.vector
            def _(vr):
                v = ctx.v(vr)
                for k in range(4):
                    v.tensor_scalar(diag4[:, k * 128:(k + 1) * 128], identb[:, :],
                                    convw[:, mc * 4 + k: mc * 4 + k + 1],
                                    None, ALU.mult)

            @blk.scalar
            def _(sr):
                sc = ctx.s(sr)
                for tc in range(TC4):
                    sc.wait("SP", p0 + 4 * (tc + 1))
                    sc.activation(xs[:, mc * T + tc * NT: mc * T + tc * NT + NT],
                                  psc[:, :], AF.Silu, bias=convb[:, mc:mc + 1])

            @blk.tensor
            def _(tr):
                te = ctx.t(tr)
                te.wait("SV", sv0 + 4)
                for tc in range(TC4):
                    if tc >= 1:
                        te.wait("SS", ss0 + tc)
                    for k in range(4):
                        te.matmul(psc[:, :], diag4[:, k * 128:(k + 1) * 128],
                                  xpb2[:, k + tc * NT: k + tc * NT + NT],
                                  start=(k == 0), stop=(k == 3))

    for mq in range(4):
        ctx.push()
        wq = load_weight_bf16(ctx, weights[f"L2_attn_wdtq{mq}"].ap(), 2048, 512,
                              f"l2a_wdt{mq}")
        dtq = ctx.sbuf(f"l2a_dtq{mq}", [128, 4 * T], BF16)
        emit_mm(ctx,
                lambda kc, mc, mq=mq: wq[:, kc * 512 + mc * 128: kc * 512 + (mc + 1) * 128],
                lambda kc, tc: xs[:, kc * T + tc * NT: kc * T + tc * NT + NT],
                KC=DIC, MC=4, TCn=TC4,
                epilogue=lambda eng, ps, mc, tc, mq=mq: eng.tensor_scalar(
                    dtq[:, mc * T + tc * NT: mc * T + tc * NT + NT],
                    ps, bdt[:, mq * 4 + mc: mq * 4 + mc + 1], None, ALU.add),
                name=f"l2a_dtm{mq}")
        spt = ctx.sbuf(f"l2a_spt{mq}", [128, NT])
        spe = ctx.sbuf(f"l2a_spe{mq}", [128, NT])
        spl = ctx.sbuf(f"l2a_spl{mq}", [128, NT])
        spr = ctx.sbuf(f"l2a_spr{mq}", [128, NT])
        for i in range(4 * TC4):
            mc, tc = divmod(i, TC4)
            sl = slice(mc * T + tc * NT, mc * T + tc * NT + NT)
            with nc.Block(f"l2a_sp_{mq}_{i}_{ctx.uid()}") as blk:
                sv0 = ctx.val("SV")
                ss0 = ctx.val("SS")

                @blk.vector
                def _(vr):
                    v = ctx.v(vr)
                    v.tensor_scalar(spr[:, :], dtq[:, sl], 0.0, None, ALU.max)
                    v.scalar_tensor_tensor(spt[:, :], spr[:, :], 2.0, dtq[:, sl],
                                           ALU.mult, ALU.subtract)
                    v.wait("SS", ss0 + 2)
                    v.tensor_tensor(dtq[:, sl], spr[:, :], spl[:, :], ALU.add)

                @blk.scalar
                def _(sr):
                    sc = ctx.s(sr)
                    sc.wait("SV", sv0 + 2)
                    sc.activation(spe[:, :], spt[:, :], AF.Exp, scale=-1.0)
                    sc.activation(spl[:, :], spe[:, :], AF.Ln,
                                  bias=consts["onescol"][:, :])
        emit_spill(ctx, dtq, dt_dram.ap()[mq * 512:(mq + 1) * 512, :], 4, T,
                   f"l2a_dtsp{mq}")
        ctx.pop()

    ctx.push()
    wx = load_weight_bf16(ctx, weights["L2_attn_wx"].ap(), 2048, 16, "l2a_wxs")
    emit_mm(ctx, lambda kc, mc: wx[:, kc * 16:(kc + 1) * 16],
            lambda kc, tc: xs[:, kc * T + tc * NT: kc * T + tc * NT + NT],
            KC=DIC, MC=1, TCn=TC4,
            epilogue=lambda eng, ps, mc, tc: eng.tensor_copy(
                bs_bf[:, tc * NT:(tc + 1) * NT], ps),
            name="l2a_bsm", Mp=16)
    ctx.pop()
    emit_spill(ctx, xs, xs_dram.ap(), DIC, T, "l2a_xs")
    ctx.pop()

    ctx.push()
    rg = load_raw(ctx, weights["L0_rg"].ap(), 128, 16 * 128, "l2a_rg", BF16)
    bd = load_raw(ctx, weights["L0_bd"].ap(), 128, 16 * 128, "l2a_bd", BF16)
    r2 = load_raw(ctx, weights["L0_r2"].ap(), 16, 128, "l2a_r2", BF16)
    alog = load_raw(ctx, weights["L2_attn_alogpp"].ap(), 128, 256, "l2a_al")
    app = ctx.sbuf("l2a_app", [128, 256])
    with nc.Block(f"l2a_app_{ctx.uid()}") as blk:
        @blk.scalar
        def _(sr):
            ctx.s(sr).activation(app[:, :], alog[:, :], AF.Exp)
        @blk.vector
        def _(vr):
            v = ctx.v(vr)
            v.wait("SS", ctx.val("SS"))
            v.tensor_scalar(app[:, :], app[:, :], -1.0, None, ALU.mult)

    bspp = ctx.sbuf("l2a_bspp", [128, T])
    pspp = ctx.psum("l2a_pspp", [128, NT])
    with nc.Block(f"l2a_pp_{ctx.uid()}") as blk:
        p0 = ctx.val("SP")
        sv0 = ctx.val("SV")

        @blk.vector
        def _(vr):
            v = ctx.v(vr)
            for i in range(TC4):
                v.wait("SP", p0 + i + 1)
                v.tensor_copy(bspp[:, i * NT:(i + 1) * NT], pspp[:, :])

        @blk.tensor
        def _(tr):
            te = ctx.t(tr)
            for i in range(TC4):
                if i >= 1:
                    te.wait("SV", sv0 + i)
                te.matmul(pspp[:, :], r2[:, :], bs_bf[:, i * NT:(i + 1) * NT],
                          start=True, stop=True)

    atile = ctx.sbuf("l2a_a", [128, T])
    htile = ctx.sbuf("l2a_h", [128, T], BF16)
    y1c = ctx.sbuf("l2a_y1c", [128, T], BF16)
    dtc = ctx.sbuf("l2a_dtc", [128, T], BF16)
    psdr = ctx.psum("l2a_psdr", [128, NT])
    psy = [ctx.psum("l2a_psy", [128, NT]) for _ in range(4)]
    for dchunk in range(DIC):
        emit_restore(ctx, dtc, dt_dram.ap()[dchunk * 128:(dchunk + 1) * 128, :],
                     1, T, f"l2a_dtr{dchunk}")
        for g in range(16):
            G = dchunk * 16 + g
            with nc.Block(f"l2a_scan_{G}_{ctx.uid()}") as blk:
                p0 = ctx.val("SP")
                sv0 = ctx.val("SV")
                ss0 = ctx.val("SS")

                @blk.scalar
                def _(sr):
                    sc = ctx.s(sr)
                    for tc in range(TC4):
                        sc.wait("SP", p0 + tc + 1)
                        sc.activation(atile[:, tc * NT:(tc + 1) * NT], psdr[:, :],
                                      AF.Exp, scale=app[:, G:G + 1])

                @blk.vector
                def _(vr):
                    v = ctx.v(vr)
                    v.wait("SS", ss0 + 4)
                    v.tensor_tensor_scan(htile[:, :], atile[:, :], bspp[:, :],
                                         0.0, ALU.mult, ALU.add)

                @blk.tensor
                def _(tr):
                    te = ctx.t(tr)
                    for tc in range(TC4):
                        if tc >= 1:
                            te.wait("SS", ss0 + tc)
                        te.matmul(psdr[:, :], rg[:, g * 128:(g + 1) * 128],
                                  dtc[:, tc * NT: tc * NT + NT],
                                  start=True, stop=True)
                    te.wait("SV", sv0 + 1)
                    for tc in range(TC4):
                        te.matmul(psy[tc][:, :], bd[:, g * 128:(g + 1) * 128],
                                  htile[:, tc * NT:(tc + 1) * NT],
                                  start=(g == 0), stop=(g == 15))

        with nc.Block(f"l2a_y_{dchunk}_{ctx.uid()}") as blk:
            @blk.vector
            def _(vr):
                v = ctx.v(vr)
                for tc in range(TC4):
                    v.tensor_copy(y1c[:, tc * NT:(tc + 1) * NT], psy[tc][:, :])
        emit_spill(ctx, y1c, y1_dram.ap()[dchunk * 128:(dchunk + 1) * 128, :], 1, T,
                   f"l2a_y1{dchunk}")
    ctx.pop()

    ctx.push()
    wout = load_weight_bf16(ctx, weights["L2_attn_wout"].ap(), 2048, 1024, "l2a_wo")
    y2 = ctx.sbuf("l2a_y2", [128, DIC * T], BF16)
    y1b = ctx.sbuf("l2a_y1b", [128, T], BF16)
    xsb = ctx.sbuf("l2a_xsb", [128, T], BF16)
    zsb = ctx.sbuf("l2a_zsb", [128, T], BF16)
    for c in range(DIC):
        emit_restore(ctx, y1b, y1_dram.ap()[c * 128:(c + 1) * 128, :], 1, T,
                     f"l2a_ry1{c}")
        emit_restore(ctx, xsb, xs_dram.ap()[c * 128:(c + 1) * 128, :], 1, T,
                     f"l2a_rxs{c}")
        emit_restore(ctx, zsb, zs_dram.ap()[c * 128:(c + 1) * 128, :], 1, T,
                     f"l2a_rzs{c}")
        with nc.Block(f"l2a_y2_{c}_{ctx.uid()}") as blk:
            @blk.vector
            def _(vr):
                v = ctx.v(vr)
                v.scalar_tensor_tensor(y2[:, c * T:(c + 1) * T], xsb[:, :],
                                       dcol[:, c:c + 1], y1b[:, :],
                                       ALU.mult, ALU.add)
                v.tensor_tensor(y2[:, c * T:(c + 1) * T], y2[:, c * T:(c + 1) * T],
                                zsb[:, :], ALU.mult)

    emit_mm(ctx,
            lambda kc, mc: wout[:, kc * 1024 + mc * 128: kc * 1024 + (mc + 1) * 128],
            lambda kc, tc: y2[:, kc * T + tc * NT: kc * T + tc * NT + NT],
            KC=DIC, MC=DC, TCn=TC4,
            epilogue=lambda eng, ps, mc, tc: eng.tensor_tensor(
                x[:, mc * T + tc * NT: mc * T + tc * NT + NT],
                x[:, mc * T + tc * NT: mc * T + tc * NT + NT], ps, ALU.add),
            name="l2a_womm")
    ctx.pop()
    ctx.pop()


def _consts_host():
    c = {}
    c["onescol"] = np.ones((128, 1), np.float32)
    c["onesrow"] = np.ones((1, 128), np.float32)
    c["identf"] = np.eye(128, dtype=np.float32)
    c["identb"] = np.eye(128, dtype=np.float32).astype(bf16)
    c["onescolb"] = np.ones((128, 1), np.float32).astype(bf16)
    esel = np.zeros((8, 8 * 128), np.float32)
    for e in range(8):
        esel[e, e * 128:(e + 1) * 128] = 1.0
    c["esel"] = esel
    return c


def _prep_weights_host(inputs):
    w = {}
    f32 = np.float32

    def put(key, arr):
        w[key] = np.ascontiguousarray(np.asarray(arr, f32))

    put("L1_ffn_w1", inputs["L1_ffn_w1"])
    put("L1_ffn_w2", inputs["L1_ffn_w2"])
    put("L1_ffn_b1c", np.asarray(inputs["L1_ffn_b1"], f32).reshape(32, 128).T)
    put("L1_ffn_b2c", np.asarray(inputs["L1_ffn_b2"], f32).reshape(8, 128).T)
    for k in ["L2_ffn_Wq", "L2_ffn_Wk", "L2_ffn_Wv", "L2_ffn_Wo", "L2_ffn_Wff"]:
        put(k, inputs[k])
    put("L2_attn_winx", np.asarray(inputs["L2_attn_W_in"], f32)[:, :2048])
    put("L2_attn_winz", np.asarray(inputs["L2_attn_W_in"], f32)[:, 2048:])
    _cw = np.asarray(inputs["L2_attn_conv_w"], f32)[:, 0, :]
    put("L2_attn_convwpp", _cw.reshape(16, 128, 4).transpose(1, 0, 2).reshape(128, 64))
    put("L2_attn_convbc", np.asarray(inputs["L2_attn_conv_b"], f32).reshape(16, 128).T)
    _wdt = np.asarray(inputs["L2_attn_W_dt"], f32)
    for _q in range(4):
        put(f"L2_attn_wdtq{_q}", _wdt[:, _q * 512:(_q + 1) * 512])
    put("L2_attn_bdtc", np.asarray(inputs["L2_attn_b_dt"], f32).reshape(16, 128).T)
    put("L2_attn_wx", inputs["L2_attn_W_x"])
    put("L2_attn_alogpp", np.asarray(inputs["L2_attn_A_log"], f32).reshape(256, 128).T)
    put("L2_attn_dcol", np.asarray(inputs["L2_attn_D"], f32).reshape(16, 128).T)
    put("L2_attn_wout", inputs["L2_attn_W_out"])
    put("L0_attn_wdbcT", np.asarray(inputs["L0_attn_W_dbc"], f32).T)
    put("L0_attn_wupT", np.asarray(inputs["L0_attn_W_up"], f32).T)
    _bdbc = np.asarray(inputs["L0_attn_b_dbc"], f32).reshape(96, 1)
    put("L0_attn_bdlc", _bdbc[0:64])
    put("L0_attn_bbmc", _bdbc[64:80])
    put("L0_attn_bcmc", _bdbc[80:96])
    put("L0_attn_bupc", np.asarray(inputs["L0_attn_b_up"], f32).reshape(8, 128).T)
    put("L0_attn_alogpp", np.asarray(inputs["L0_attn_A_log"], f32).reshape(128, 128).T)
    rgc = np.zeros((128, 16 * 128), f32)
    bdc = np.zeros((128, 16 * 128), f32)
    for g in range(16):
        for p in range(128):
            rgc[g * 8 + p // 16, g * 128 + p] = 1.0
            bdc[p, g * 128 + g * 8 + p // 16] = 1.0
    w["L0_rg"] = rgc.astype(bf16)
    w["L0_bd"] = bdc.astype(bf16)
    r2c = np.zeros((16, 128), f32)
    for p in range(128):
        r2c[p % 16, p] = 1.0
    w["L0_r2"] = r2c.astype(bf16)
    put("L0_ffn_ctlR", np.asarray(inputs["L0_ffn_controller"], f32).reshape(1024, 64))
    put("L0_ffn_f1R", np.asarray(inputs["L0_ffn_f1"], f32).reshape(1024, 4096))
    put("L0_ffn_f2R", np.asarray(inputs["L0_ffn_f2"], f32).reshape(4096, 1024))
    put("L0_ffn_besf", np.asarray(inputs["L0_ffn_bias"], f32).reshape(32, 128).T)
    tie = np.linspace(0.0, 1e-6, 16).astype(f32)
    w["L0_ffn_tieT"] = np.ascontiguousarray(
        np.tile(tie[None, :], (64, T // 16)).astype(f32))
    rp = np.zeros((64, 32 * 128), f32)
    for ch in range(32):
        for p in range(128):
            rp[2 * ch + p // 64, ch * 128 + p] = 1.0
    w["L0_ffn_rp"] = rp.astype(bf16)
    put("L1_attn_wqkvT", np.asarray(inputs["L1_attn_w_qkv"], f32).T)
    put("L1_attn_wprojT", np.asarray(inputs["L1_attn_w_proj"], f32).T)
    put("L1_attn_bpc", np.asarray(inputs["L1_attn_b_proj"], f32).reshape(8, 128).T)
    w["L1_attn_ln4096"] = np.full((1, 1), np.log(4096.0), f32)
    put("L3_attn_wqkvT", np.asarray(inputs["L3_attn_w_qkv"], f32).T)
    put("L3_attn_bqc", np.asarray(inputs["L3_attn_b_qkv"], f32)[:2048].reshape(16, 128).T)
    put("L3_attn_bvrow", np.asarray(inputs["L3_attn_b_qkv"], f32)[2048:].reshape(1, 1024))
    put("L3_attn_woutT", np.asarray(inputs["L3_attn_w_out"], f32).T)
    put("L3_attn_boc", np.asarray(inputs["L3_attn_b_out"], f32).reshape(8, 128).T)
    masks = np.full((128, 3 * 384), -1e30, f32)
    for dlt in range(3):
        ii = np.arange(128)[:, None] + dlt * 128
        jj = np.arange(384)[None, :]
        masks[:, dlt * 384:(dlt + 1) * 384] = np.where(np.abs(ii - jj) <= 8, 0.0, -1e30)
    w["L3_attn_masks"] = masks
    put("L3_ffn_router", inputs["L3_ffn_router"])
    put("L3_ffn_wg", np.asarray(inputs["L3_ffn_experts_gate"], f32).transpose(1, 0, 2).reshape(1024, 4096))
    put("L3_ffn_wu", np.asarray(inputs["L3_ffn_experts_up"], f32).transpose(1, 0, 2).reshape(1024, 4096))
    put("L3_ffn_wd", np.asarray(inputs["L3_ffn_experts_down"], f32).reshape(4096, 1024))
    put("L3_ffn_shared_gate", inputs["L3_ffn_shared_gate"])
    put("L3_ffn_shared_up", inputs["L3_ffn_shared_up"])
    put("L3_ffn_shared_down", inputs["L3_ffn_shared_down"])
    return w


L1F_KEYS = ["L1_ffn_w1", "L1_ffn_w2", "L1_ffn_b1c", "L1_ffn_b2c"]
L2F_KEYS = ["L2_ffn_Wq", "L2_ffn_Wk", "L2_ffn_Wv", "L2_ffn_Wo", "L2_ffn_Wff"]
L2A_KEYS = ["L2_attn_winx", "L2_attn_winz", "L2_attn_convwpp", "L2_attn_convbc",
            "L2_attn_wdtq0", "L2_attn_wdtq1", "L2_attn_wdtq2", "L2_attn_wdtq3",
            "L2_attn_bdtc", "L2_attn_wx", "L2_attn_alogpp", "L2_attn_dcol",
            "L2_attn_wout", "L0_rg", "L0_bd", "L0_r2"]
L0A_KEYS = ["L0_attn_wdbcT", "L0_attn_wupT", "L0_attn_bdlc", "L0_attn_bbmc",
            "L0_attn_bcmc", "L0_attn_bupc", "L0_attn_alogpp",
            "L0_rg", "L0_bd", "L0_r2"]
L0F_KEYS = ["L0_ffn_ctlR", "L0_ffn_f1R", "L0_ffn_f2R", "L0_ffn_besf",
            "L0_ffn_tieT", "L0_ffn_rp"]
L1A_KEYS = ["L1_attn_wqkvT", "L1_attn_wprojT", "L1_attn_bpc", "L1_attn_ln4096"]
L3A_KEYS = ["L3_attn_wqkvT", "L3_attn_bqc", "L3_attn_bvrow", "L3_attn_woutT",
            "L3_attn_boc", "L3_attn_masks"]
L3F_KEYS = ["L3_ffn_router", "L3_ffn_wg", "L3_ffn_wu", "L3_ffn_wd",
            "L3_ffn_shared_gate", "L3_ffn_shared_up", "L3_ffn_shared_down"]

STAGE_CONSTS = {
    "L1F": ["onescol", "onesrow"],
    "L2F": ["onescol", "onesrow", "onescolb"],
    "L3A": ["onescol", "onesrow", "identb", "onescolb"],
    "L1A": ["onescol", "onesrow", "onescolb"],
    "L3F": ["onescol", "onesrow", "identf", "onescolb", "esel"],
    "L0A": ["onescol", "onesrow", "onescolb"],
    "L0F": ["onescol", "onesrow", "onescolb"],
    "L2A": ["onescol", "onesrow", "onescolb", "identb"],
}

STAGES = [
    ("L0A", emit_l0_attn, L0A_KEYS),
    ("L0F", emit_l0_ffn, L0F_KEYS),
    ("L1A", emit_l1_attn, L1A_KEYS),
    ("L1F", emit_l1_ffn, L1F_KEYS),
    ("L2A", emit_l2_attn, L2A_KEYS),
    ("L2F", emit_l2_ffn, L2F_KEYS),
    ("L3A", emit_l3_attn, L3A_KEYS),
    ("L3F", emit_l3_ffn, L3F_KEYS),
]

_NPDT = {np.float32: F32, np.int32: I32}


def build_stage_program(emit_fn, wkeys, w, consts, stage_name=None):
    if stage_name in STAGE_CONSTS:
        consts = {k: consts[k] for k in STAGE_CONSTS[stage_name]}
    nc = bass.Bass("TRN2", target_bir_lowering=False)
    ctx = Ctx(nc)
    xin = nc.dram_tensor("xin", [D, T], F32, kind="ExternalInput")
    xout = nc.dram_tensor("xout", [D, T], F32, kind="ExternalOutput")
    weights = {k: nc.dram_tensor(k, list(w[k].shape),
                                 _NPDT.get(w[k].dtype.type, BF16), kind="ExternalInput")
               for k in wkeys}
    cdram = {k: nc.dram_tensor("c_" + k, list(consts[k].shape),
                               _NPDT.get(consts[k].dtype.type, BF16), kind="ExternalInput")
             for k in consts}

    x = ctx.sbuf("xres", [128, DC * T], F32)
    emit_restore(ctx, x, xin.ap(), DC, T, "ldx")
    cs = {k: load_raw(ctx, cdram[k].ap(), consts[k].shape[0], consts[k].shape[1],
                      "c_" + k, _NPDT.get(consts[k].dtype.type, BF16)) for k in consts}

    emit_fn(ctx, x, weights, cs)

    emit_spill(ctx, x, xout.ap(), DC, T, "stx")
    ctx.close()
    mybir.codegen_inst_isa_subclasses(nc)
    return nc


FUSE = True


def build_fused_program(w, consts, allkeys, callkeys):
    nc = bass.Bass("TRN2", target_bir_lowering=False)
    ctx = Ctx(nc)
    xin = nc.dram_tensor("xin", [D, T], F32, kind="ExternalInput")
    xout = nc.dram_tensor("xout", [D, T], F32, kind="ExternalOutput")
    weights = {k: nc.dram_tensor(k, list(w[k].shape),
                                 _NPDT.get(w[k].dtype.type, BF16), kind="ExternalInput")
               for k in allkeys}
    cdram = {k: nc.dram_tensor("c_" + k, list(consts[k].shape),
                               _NPDT.get(consts[k].dtype.type, BF16),
                               kind="ExternalInput")
             for k in callkeys}

    x = ctx.sbuf("xres", [128, DC * T], F32)
    emit_restore(ctx, x, xin.ap(), DC, T, "ldx")
    common = [k for k in ("onescol", "onesrow", "onescolb") if k in callkeys]
    cs = {k: load_raw(ctx, cdram[k].ap(), consts[k].shape[0], consts[k].shape[1],
                      "c_" + k, _NPDT.get(consts[k].dtype.type, BF16))
          for k in common}

    for name, emit_fn, wkeys in STAGES:
        extra = [k for k in STAGE_CONSTS[name] if k not in common]
        ctx.push()
        cse = dict(cs)
        for k in extra:
            cse[k] = load_raw(ctx, cdram[k].ap(), consts[k].shape[0],
                              consts[k].shape[1], "c_" + k,
                              _NPDT.get(consts[k].dtype.type, BF16))
        emit_fn(ctx, x, weights, cse)
        ctx.pop()

    emit_spill(ctx, x, xout.ap(), DC, T, "stx")
    ctx.close()
    mybir.codegen_inst_isa_subclasses(nc)
    return nc


def kernel(**inputs):
    xfull = np.asarray(inputs["L0_attn_x"], np.float32)
    consts = _consts_host()
    w = _prep_weights_host(inputs)
    xs = [np.ascontiguousarray(xfull[b].T) for b in range(NCORES)]

    if FUSE:
        allkeys, callkeys = [], []
        for name, _, wkeys in STAGES:
            for k in wkeys:
                if k not in allkeys:
                    allkeys.append(k)
            for k in STAGE_CONSTS[name]:
                if k not in callkeys:
                    callkeys.append(k)
        nc = build_fused_program(w, consts, allkeys, callkeys)
        in_maps = []
        for b in range(NCORES):
            m = {"xin": xs[b]}
            for k in allkeys:
                m[k] = w[k]
            for k in callkeys:
                m["c_" + k] = consts[k]
            in_maps.append(m)
        res = run_bass_kernel_spmd(nc, in_maps, core_ids=list(range(NCORES)))
        xs = [np.asarray(res.results[b]["xout"]) for b in range(NCORES)]
        out = np.stack([x.T for x in xs]).astype(np.float32)
        return np.ascontiguousarray(out)

    for name, emit_fn, wkeys in STAGES:
        cuse = {k: consts[k] for k in STAGE_CONSTS.get(name, list(consts))}
        nc = build_stage_program(emit_fn, wkeys, w, consts, stage_name=name)
        in_maps = []
        for b in range(NCORES):
            m = {"xin": xs[b]}
            for k in wkeys:
                m[k] = w[k]
            for k in cuse:
                m["c_" + k] = cuse[k]
            in_maps.append(m)
        res = run_bass_kernel_spmd(nc, in_maps, core_ids=list(range(NCORES)))
        xs = [np.asarray(res.results[b]["xout"]) for b in range(NCORES)]

    out = np.stack([x.T for x in xs]).astype(np.float32)
    return np.ascontiguousarray(out)


# version 04
# speedup vs baseline: 1.1143x; 1.1143x over previous
import numpy as np
import ml_dtypes as mld
from contextlib import ExitStack

import concourse.bass as bass
import concourse.mybir as mybir
from concourse.bass_utils import run_bass_kernel_spmd
from concourse.dve_spec import Spec, Src0, Src1, Zero, maxx, minn, select, lower
from concourse.dve_spec import C0, C1, C2
from concourse.dve_ops import DveOp
from concourse.dve_table_gen import dve_ver_for

F32 = mybir.dt.float32
BF16 = mybir.dt.bfloat16
I32 = mybir.dt.int32
AF = mybir.ActivationFunctionType
ALU = mybir.AluOpType
AX = mybir.AxisListType
bf16 = mld.bfloat16

T = 2048
D = 1024
DC = D // 128
NT = 512
TC4 = T // NT
NCORES = 8

def _register_custom_ops():
    import concourse.dve_ops as dvo
    ver = dve_ver_for("TRN2")

    def mk(name, spec, reference):
        for o in dvo.OPS:
            if o.name == name:
                return o
        sha = lower(spec, ver=ver, perf_en=False, name=name).sha(ver)
        op = DveOp(name, spec, subdim=False, uops_sha={ver: sha}, reference=reference)
        dvo.OPS.append(op)
        row = dvo._CUSTOM_DVE_ROW_BASE + len(dvo.OPS) - 1
        assert row <= 31, f"custom DVE row overflow: {row}"
        dvo._SUB_OPCODE_FOR_NAME[name] = row
        dvo.CUSTOM_DVE_REFS[name] = reference
        dvo.CUSTOM_DVE_SPECS[name] = spec
        return op

    _v = minn(maxx(Src0, C0), C1)
    sgn = mk(
        "FXTRUNC_SGN",
        Spec(body=select(_v > Zero, Src1 - (Src1 > _v), Src1 + (Src1 < _v))),
        lambda in0, in1, s0, s1, imm2: np.where(
            np.clip(in0, s0, s1) > 0,
            in1 - (in1 > np.clip(in0, s0, s1)),
            in1 + (in1 < np.clip(in0, s0, s1)),
        ),
    )
    pos = mk(
        "FXTRUNC_POS",
        Spec(body=(Src1 - (Src1 > Src0 * C0)) * C2),
        lambda in0, in1, s0, s1, imm2: (in1 - (in1 > in0 * s0)) * imm2,
    )
    return sgn, pos

FXTRUNC_SGN, FXTRUNC_POS = _register_custom_ops()

class Eng:

    _PASS = {"wait_ge", "then_inc", "sem_inc", "nop", "drain", "engine_nop"}

    def __init__(self, ctx, raw, key, serialize=True):
        self._ctx = ctx
        self._raw = raw
        self._key = key
        self._ser = serialize

    def wait(self, key, val):
        if val > 0:
            self._raw.wait_ge(self._ctx.sem(key), val)

    def raw(self):
        return self._raw

    def dma(self, dst, src, key="dml"):
        self._raw.dma_start(dst, src).then_inc(self._ctx.sem(key), 16)
        return self._ctx.bump(key, 16)

    def matmul(self, out, lhsT, rhs, start=True, stop=True, **kw):
        ins = self._raw.matmul(out, lhsT, rhs, start=start, stop=stop, **kw)
        ins.then_inc(self._ctx.sem(self._key), 1)
        self._ctx.bump(self._key, 1)
        return ins

    def custom(self, op, **kw):
        return self._wrap(self._raw._custom_dve)(op, **kw)

    def _wrap(self, raw_fn):
        ctx = self._ctx
        key = self._key

        def call(*a, **k):
            sem = ctx.sem(key)
            if self._ser and ctx.cnt[key] > 0:
                self._raw.wait_ge(sem, ctx.cnt[key])
            ins = raw_fn(*a, **k)
            ins.then_inc(sem, 1)
            ctx.bump(key, 1)
            return ins

        return call

    def __getattr__(self, name):
        raw_fn = getattr(self._raw, name)
        if name in Eng._PASS or name.startswith("dma") or name.startswith("reg"):
            return raw_fn
        return self._wrap(raw_fn)


class Ctx:
    def __init__(self, nc):
        self.nc = nc
        self.scopes = [ExitStack()]
        self._sems = {}
        self.cnt = {}
        self._uid = 0

    def uid(self):
        self._uid += 1
        return self._uid

    def sem(self, name):
        if name not in self._sems:
            self._sems[name] = self.scopes[0].enter_context(self.nc.semaphore(name))
            self.cnt[name] = 0
        return self._sems[name]

    def bump(self, name, k=1):
        self.sem(name)
        self.cnt[name] += k
        return self.cnt[name]

    def val(self, name):
        self.sem(name)
        return self.cnt[name]

    def v(self, raw):
        return Eng(self, raw, "SV")

    def s(self, raw):
        return Eng(self, raw, "SS")

    def q(self, raw):
        return Eng(self, raw, "SQ")

    def t(self, raw):
        return Eng(self, raw, "SP", serialize=False)

    def sbuf(self, name, shape, dt=F32):
        return self.scopes[-1].enter_context(
            self.nc.sbuf_tensor(f"{name}_{self.uid()}", shape, dt))

    def psum(self, name, shape, dt=F32):
        return self.scopes[-1].enter_context(
            self.nc.psum_tensor(f"{name}_{self.uid()}", shape, dt))

    def push(self):
        self.scopes.append(ExitStack())

    def pop(self):
        self.scopes.pop().close()

    def close(self):
        while self.scopes:
            self.scopes.pop().close()


def load_weight_bf16(ctx, w_hbm, K, M, name="w", nstg=2):
    nc = ctx.nc
    KC = K // 128
    wb = ctx.sbuf(name, [128, KC * M], BF16)
    ctx.push()
    nstg = min(nstg, KC)
    stg = [ctx.sbuf(name + f"_stg{i}", [128, M], F32) for i in range(nstg)]
    with nc.Block(f"ldw_{name}_{ctx.uid()}") as blk:
        d0 = [ctx.val("dmlA"), ctx.val("dmlB")]
        sv0 = ctx.val("SV")

        @blk.vector
        def _(vr):
            v = ctx.v(vr)
            for kc in range(KC):
                par = kc % 2
                v.raw().wait_ge(ctx.sem("dmlA" if par == 0 else "dmlB"),
                                d0[par] + 16 * (kc // 2 + 1))
                v.tensor_copy(wb[:, kc * M:(kc + 1) * M], stg[kc % nstg][:, 0:M])

        @blk.sync
        def _(s):
            for kc in range(KC):
                par = kc % 2
                if kc >= nstg:
                    s.wait_ge(ctx.sem("SV"), sv0 + kc - nstg + 1)
                psem = ctx.sem("dmlA" if par == 0 else "dmlB")
                s.dma_start(stg[kc % nstg][:, 0:M],
                            w_hbm[kc * 128:(kc + 1) * 128, :]).then_inc(psem, 16)
                ctx.bump("dmlA" if par == 0 else "dmlB", 16)
    ctx.pop()
    return wb


def load_raw(ctx, hbm_ap, P, Fr, name="c", dt=F32):
    nc = ctx.nc
    tile = ctx.sbuf(name, [P, Fr], dt)
    with nc.Block(f"ldc_{name}_{ctx.uid()}") as blk:
        @blk.sync
        def _(s):
            s.dma_start(tile[:, :], hbm_ap).then_inc(ctx.sem("dml"), 16)
            s.wait_ge(ctx.sem("dml"), ctx.bump("dml", 16))
    return tile


def emit_spill(ctx, sb, dram_ap, nch, width, name="sp"):
    nc = ctx.nc
    with nc.Block(f"spill_{name}_{ctx.uid()}") as blk:
        @blk.sync
        def _(s):
            for c in range(nch):
                s.dma_start(dram_ap[c * 128:(c + 1) * 128, :],
                            sb[:, c * width:(c + 1) * width]).then_inc(ctx.sem("dml"), 16)
                ctx.bump("dml", 16)
            s.wait_ge(ctx.sem("dml"), ctx.val("dml"))


def emit_restore(ctx, sb, dram_ap, nch, width, name="rs"):
    nc = ctx.nc
    with nc.Block(f"rest_{name}_{ctx.uid()}") as blk:
        @blk.sync
        def _(s):
            for c in range(nch):
                s.dma_start(sb[:, c * width:(c + 1) * width],
                            dram_ap[c * 128:(c + 1) * 128, :]).then_inc(ctx.sem("dml"), 16)
                ctx.bump("dml", 16)
            s.wait_ge(ctx.sem("dml"), ctx.val("dml"))


def emit_mm(ctx, lhsT_fn, rhs_fn, KC, MC, TCn, epilogue, Nw=NT, name="mm",
            psum_banks=4, epi_engine="vector", Mp=128):
    nc = ctx.nc
    ekey = "SV" if epi_engine == "vector" else "SS"
    ctx.push()
    ps = [ctx.psum(f"ps_{name}", [Mp, Nw]) for _ in range(psum_banks)]

    with nc.Block(f"{name}_{ctx.uid()}") as blk:
        p0 = ctx.val("SP")
        epi_done = []

        sec = blk.vector if epi_engine == "vector" else blk.scalar

        @sec
        def _(er):
            eng = ctx.v(er) if epi_engine == "vector" else ctx.s(er)
            for i in range(MC * TCn):
                mc, tc = divmod(i, TCn)
                b = i % psum_banks
                eng.wait("SP", p0 + (i + 1) * KC)
                epilogue(eng, ps[b][:, :], mc, tc)
                epi_done.append(ctx.val(ekey))

        @blk.tensor
        def _(tr):
            te = ctx.t(tr)
            for i in range(MC * TCn):
                mc, tc = divmod(i, TCn)
                b = i % psum_banks
                if i >= psum_banks:
                    te.wait(ekey, epi_done[i - psum_banks])
                for kc in range(KC):
                    te.matmul(ps[b][:, :], lhsT_fn(kc, mc), rhs_fn(kc, tc),
                              start=(kc == 0), stop=(kc == KC - 1))
    ctx.pop()


def emit_ln_tc(ctx, get_x, xh_put, consts, tc, name="ln"):
    nc = ctx.nc
    onescol = consts["onescol"]
    onesrow = consts["onesrow"]
    ctx.push()
    mean_r = ctx.sbuf(f"{name}_mean", [1, NT])
    var_r = ctx.sbuf(f"{name}_var", [1, NT])
    xsq = ctx.sbuf(f"{name}_xsq", [128, NT])
    tmp = ctx.sbuf(f"{name}_tmp", [128, NT])
    tmp2 = ctx.sbuf(f"{name}_tmp2", [128, NT])
    rstd_b = ctx.sbuf(f"{name}_rstdb", [128, NT])
    ps_s = ctx.psum(f"{name}_ps_s", [1, NT])
    ps_q = ctx.psum(f"{name}_ps_q", [1, NT])
    ps_mb = ctx.psum(f"{name}_ps_mb", [128, NT])
    ps_vb = ctx.psum(f"{name}_ps_vb", [128, NT])

    with nc.Block(f"{name}_t{tc}_{ctx.uid()}") as blk:
        p0 = ctx.val("SP")
        ss0 = ctx.val("SS")
        sq_done = []
        stats_done = []

        @blk.vector
        def _(vr):
            v = ctx.v(vr)
            for c in range(DC):
                if c >= 1:
                    v.wait("SP", p0 + 2 * c)
                v.tensor_tensor(xsq[:, :], get_x(c), get_x(c), ALU.mult)
                sq_done.append(ctx.val("SV"))
            v.wait("SP", p0 + 2 * DC)
            v.tensor_scalar(mean_r[:, :], ps_s[:, :], 1.0 / D, None, ALU.mult)
            v.tensor_scalar(var_r[:, :], ps_q[:, :], 1.0 / D, 1e-5, ALU.mult, ALU.add)
            v.tensor_tensor(tmp[0:1, :], mean_r[:, :], mean_r[:, :], ALU.mult)
            v.tensor_tensor(var_r[:, :], var_r[:, :], tmp[0:1, :], ALU.subtract)
            stats_done.append(ctx.val("SV"))
            v.wait("SP", p0 + 2 * DC + 1)
            v.wait("SS", ss0 + 2)
            for c in range(DC):
                bf_ap, f32_ap = xh_put(c)
                v.tensor_tensor(tmp2[:, :], get_x(c), ps_mb[:, :], ALU.subtract)
                if f32_ap is not None:
                    v.tensor_tensor(f32_ap, tmp2[:, :], rstd_b[:, :], ALU.mult)
                    v.tensor_copy(bf_ap, f32_ap)
                else:
                    v.tensor_tensor(bf_ap, tmp2[:, :], rstd_b[:, :], ALU.mult)

        @blk.tensor
        def _(tr):
            te = ctx.t(tr)
            for c in range(DC):
                te.matmul(ps_s[:, :], onescol[:, :], get_x(c),
                          start=(c == 0), stop=(c == DC - 1))
                te.wait("SV", sq_done[c])
                te.matmul(ps_q[:, :], onescol[:, :], xsq[:, :],
                          start=(c == 0), stop=(c == DC - 1))
            te.wait("SV", stats_done[0])
            te.matmul(ps_mb[:, :], onesrow[:, :], mean_r[:, :], start=True, stop=True)
            te.matmul(ps_vb[:, :], onesrow[:, :], var_r[:, :], start=True, stop=True)

        @blk.scalar
        def _(sr):
            sc = ctx.s(sr)
            sc.wait("SP", p0 + 2 * DC + 2)
            sc.activation(tmp[:, :], ps_vb[:, :], AF.Ln)
            sc.activation(rstd_b[:, :], tmp[:, :], AF.Exp, scale=-0.5)
    ctx.pop()


def emit_ln(ctx, x, xh_bf, consts, xh_f32=None, name="ln"):
    for tc in range(TC4):
        emit_ln_tc(
            ctx,
            lambda c, tc=tc: x[:, c * T + tc * NT: c * T + tc * NT + NT],
            lambda c, tc=tc: (
                xh_bf[:, c * T + tc * NT: c * T + tc * NT + NT],
                None if xh_f32 is None else xh_f32[:, c * T + tc * NT: c * T + tc * NT + NT],
            ),
            consts, tc, name=name)


def emit_l2norm(ctx, zin, zout, consts, name="l2n", gelu_res_x=None):
    nc = ctx.nc
    onescol_b = consts["onescolb"]
    onesrow = consts["onesrow"]
    ctx.push()
    zsq = ctx.sbuf(f"{name}_zsq", [128, NT], BF16)
    rr = ctx.sbuf(f"{name}_rr", [1, NT])
    tmp = ctx.sbuf(f"{name}_tmp", [128, NT])
    rb_sb = ctx.sbuf(f"{name}_rb", [128, NT])
    tmpg = [ctx.sbuf(f"{name}_tg", [128, NT]) for _ in range(2)]
    tmpg2 = [ctx.sbuf(f"{name}_tg2", [128, NT]) for _ in range(2)]
    ps_q = ctx.psum(f"{name}_psq", [1, NT])
    ps_rb = ctx.psum(f"{name}_psrb", [128, NT])

    for tc in range(TC4):
        with nc.Block(f"{name}_t{tc}_{ctx.uid()}") as blk:
            p0 = ctx.val("SP")
            sq_done = []
            rr_done = []
            rb_done = []
            mul_done = []

            @blk.vector
            def _(vr):
                v = ctx.v(vr)
                for c in range(DC):
                    if c >= 1:
                        v.wait("SP", p0 + c)
                    zsl = zin[:, c * T + tc * NT: c * T + tc * NT + NT]
                    v.tensor_tensor(zsq[:, :], zsl, zsl, ALU.mult)
                    sq_done.append(ctx.val("SV"))
                v.wait("SP", p0 + DC)
                v.tensor_scalar(rr[:, :], ps_q[:, :], 1e-24, None, ALU.max)
                rr_done.append(ctx.val("SV"))

            @blk.tensor
            def _(tr):
                te = ctx.t(tr)
                for c in range(DC):
                    te.wait("SV", sq_done[c])
                    te.matmul(ps_q[:, :], onescol_b[:, :], zsq[:, :],
                              start=(c == 0), stop=(c == DC - 1))
                te.wait("SV", rr_done[0])
                te.matmul(ps_rb[:, :], onesrow[:, :], rr[:, :], start=True, stop=True)

            @blk.scalar
            def _(sr):
                sc = ctx.s(sr)
                sc.wait("SP", p0 + DC + 1)
                sc.activation(tmp[:, :], ps_rb[:, :], AF.Ln)
                sc.activation(rb_sb[:, :], tmp[:, :], AF.Exp, scale=-0.5)
                rb_done.append(ctx.val("SS"))

            if gelu_res_x is None:
                @blk.vector
                def _(vr):
                    v = ctx.v(vr)
                    v.wait("SS", rb_done[0])
                    for c in range(DC):
                        zsl = zin[:, c * T + tc * NT: c * T + tc * NT + NT]
                        v.tensor_tensor(zout[:, c * T + tc * NT: c * T + tc * NT + NT],
                                        zsl, rb_sb[:, :], ALU.mult)
            else:
                @blk.vector
                def _(vr):
                    v = ctx.v(vr)
                    v.wait("SS", rb_done[0])
                    for c in range(DC):
                        zsl = zin[:, c * T + tc * NT: c * T + tc * NT + NT]
                        if c >= 2:
                            v.wait("SS", rb_done[0] + c - 1)
                        v.tensor_tensor(tmpg[c % 2][:, :], zsl, rb_sb[:, :], ALU.mult)
                        mul_done.append(ctx.val("SV"))
                        v.wait("SS", rb_done[0] + c + 1)
                        xs = gelu_res_x[:, c * T + tc * NT: c * T + tc * NT + NT]
                        v.tensor_tensor(xs, xs, tmpg2[c % 2][:, :], ALU.add)

                @blk.scalar
                def _(sr):
                    sc = ctx.s(sr)
                    for c in range(DC):
                        sc.wait("SV", mul_done[c])
                        sc.activation(tmpg2[c % 2][:, :], tmpg[c % 2][:, :], AF.Gelu)
    ctx.pop()


def emit_l1_ffn(ctx, x, weights, consts):
    nc = ctx.nc
    ctx.push()
    w1 = load_weight_bf16(ctx, weights["L1_ffn_w1"].ap(), 1024, 4096, "l1f_w1")
    b1 = load_raw(ctx, weights["L1_ffn_b1c"].ap(), 128, 32, "l1f_b1")
    b2 = load_raw(ctx, weights["L1_ffn_b2c"].ap(), 128, 8, "l1f_b2")
    xh = ctx.sbuf("l1f_xh", [128, DC * NT], BF16)
    h = ctx.sbuf("l1f_h", [128, 32 * NT], BF16)

    for tc in range(TC4):
        emit_ln_tc(ctx,
                   lambda c: x[:, c * T + tc * NT: c * T + tc * NT + NT],
                   lambda c: (xh[:, c * NT:(c + 1) * NT], None),
                   consts, tc, name="l1f_ln")

        def epi1(eng, ps_ap, mc, _tc):
            eng.activation(h[:, mc * NT:(mc + 1) * NT], ps_ap,
                           AF.Gelu, bias=b1[:, mc:mc + 1])

        emit_mm(ctx,
                lambda kc, mc: w1[:, kc * 4096 + mc * 128: kc * 4096 + mc * 128 + 128],
                lambda kc, _tc: xh[:, kc * NT:(kc + 1) * NT],
                KC=8, MC=32, TCn=1, epilogue=epi1, name=f"l1f_m1_{tc}",
                epi_engine="scalar")

        emit_mm_kstream(ctx, weights["L1_ffn_w2"].ap(), 4096, 1024,
                        lambda kc: h[:, kc * NT:(kc + 1) * NT],
                        lambda eng, ps_ap, mc, tc=tc: eng.scalar_tensor_tensor(
                            x[:, mc * T + tc * NT: mc * T + tc * NT + NT],
                            ps_ap, b2[:, mc:mc + 1],
                            x[:, mc * T + tc * NT: mc * T + tc * NT + NT],
                            ALU.add, ALU.add),
                        name=f"l1f_m2_{tc}")
    ctx.pop()


def emit_mm_kstream(ctx, w_hbm, K, M, rhs_fn, epilogue, Nw=NT, name="mmk"):
    nc = ctx.nc
    KC = K // 128
    MC = M // 128
    assert MC <= 8
    ctx.push()
    ps = [ctx.psum(f"ps_{name}", [128, Nw]) for _ in range(MC)]
    stg = [ctx.sbuf(f"{name}_stg{i}", [128, M], F32) for i in range(2)]
    wt = [ctx.sbuf(f"{name}_wt{i}", [128, M], BF16) for i in range(2)]

    with nc.Block(f"{name}_{ctx.uid()}") as blk:
        d0 = [ctx.val("dmlA"), ctx.val("dmlB")]
        p0 = ctx.val("SP")
        cv_done = []

        @blk.vector
        def _(vr):
            v = ctx.v(vr)
            for kc in range(KC):
                par = kc % 2
                v.raw().wait_ge(ctx.sem("dmlA" if par == 0 else "dmlB"),
                                d0[par] + 16 * (kc // 2 + 1))
                if kc >= 2:
                    v.wait("SP", p0 + (kc - 1) * MC)
                v.tensor_copy(wt[kc % 2][:, :], stg[kc % 2][:, :])
                cv_done.append(ctx.val("SV"))
            for mc in range(MC):
                v.wait("SP", p0 + KC * MC)
                epilogue(v, ps[mc][:, :], mc)

        @blk.sync
        def _(s):
            for kc in range(KC):
                par = kc % 2
                if kc >= 2:
                    s.wait_ge(ctx.sem("SV"), cv_done[kc - 1])
                psem = ctx.sem("dmlA" if par == 0 else "dmlB")
                s.dma_start(stg[kc % 2][:, :],
                            w_hbm[kc * 128:(kc + 1) * 128, :]).then_inc(psem, 16)
                ctx.bump("dmlA" if par == 0 else "dmlB", 16)

        @blk.tensor
        def _(tr):
            te = ctx.t(tr)
            for kc in range(KC):
                te.wait("SV", cv_done[kc])
                for mc in range(MC):
                    te.matmul(ps[mc][:, :], wt[kc % 2][:, mc * 128:(mc + 1) * 128],
                              rhs_fn(kc), start=(kc == 0), stop=(kc == KC - 1))
    ctx.pop()


def emit_l3_ffn(ctx, x, weights, consts):
    nc = ctx.nc
    ctx.push()
    xh = ctx.sbuf("l3f_xh", [128, DC * T], BF16)
    emit_ln(ctx, x, xh, consts, name="l3f_ln")

    ctx.push()
    rt_fm = ctx.sbuf("l3f_rt", [8, T])
    coef_fm = ctx.sbuf("l3f_coef", [8, T])
    ctx.push()
    router = load_weight_bf16(ctx, weights["L3_ffn_router"].ap(), 1024, 8, "l3f_rtw")

    emit_mm(ctx, lambda kc, mc: router[:, kc * 8:(kc + 1) * 8],
            lambda kc, tc: xh[:, kc * T + tc * NT: kc * T + tc * NT + NT],
            KC=8, MC=1, TCn=TC4,
            epilogue=lambda eng, ps, mc, tc: eng.tensor_copy(
                rt_fm[:, tc * NT:(tc + 1) * NT], ps),
            name="l3f_rtmm", psum_banks=2, Mp=8)

    ident = consts["identf"]
    rt_tm = ctx.sbuf("l3f_rttm", [128, 16 * 8])
    co_tm = ctx.sbuf("l3f_cotm", [128, 16 * 8])
    tmp_a = ctx.sbuf("l3f_ta", [128, 16 * 8])
    tmp_b = ctx.sbuf("l3f_tb", [128, 16 * 8])
    s1 = ctx.sbuf("l3f_s1", [128, 16])
    s1b = ctx.sbuf("l3f_s1b", [128, 16])
    pst2 = [ctx.psum("l3f_pst", [128, 128]) for _ in range(2)]

    def b3(t):
        return bass.AP(t, 0, [[16, 128], [1, 16], [0, 8]])

    def r3(t):
        return t[:, :].rearrange("p (a b) -> p a b", b=8)

    with nc.Block(f"l3f_rtt_{ctx.uid()}") as blk:
        p0 = ctx.val("SP")
        cp_done = []

        @blk.vector
        def _(vr):
            v = ctx.v(vr)
            for i in range(16):
                v.wait("SP", p0 + i + 1)
                v.tensor_copy(rt_tm[:, i * 8:(i + 1) * 8], pst2[i % 2][:, 0:8])
                cp_done.append(ctx.val("SV"))

        @blk.tensor
        def _(tr):
            te = ctx.t(tr)
            for i in range(16):
                if i >= 2:
                    te.wait("SV", cp_done[i - 2])
                te.transpose(pst2[i % 2][:, 0:8], rt_fm[:, i * 128:(i + 1) * 128],
                             ident[0:8, 0:8])

    with nc.Block(f"l3f_top2_{ctx.uid()}") as blk:
        ss0 = ctx.val("SS")
        pre_done = []
        sum_done = []

        @blk.vector
        def _(vr):
            v = ctx.v(vr)
            v.reduce_max(s1[:, :], r3(rt_tm), AX.X, negate=True)
            v.tensor_tensor(r3(tmp_a), r3(rt_tm), b3(s1), ALU.add)
            pre_done.append(ctx.val("SV"))
            v.wait("SS", ss0 + 1)
            v.reduce_sum(s1[:, :], r3(tmp_b), AX.X)
            sum_done.append(ctx.val("SV"))
            v.wait("SS", ss0 + 3)
            v.tensor_tensor(r3(rt_tm), r3(tmp_b), b3(s1b), ALU.mult)
            v.reduce_max(s1[:, :], r3(rt_tm), AX.X)
            v.tensor_tensor(r3(tmp_a), r3(rt_tm), b3(s1), ALU.is_lt)
            v.tensor_tensor(tmp_b[:, :], rt_tm[:, :], tmp_a[:, :], ALU.mult)
            v.reduce_max(s1[:, :], r3(tmp_b), AX.X)
            v.tensor_tensor(r3(tmp_a), r3(rt_tm), b3(s1), ALU.is_ge)
            v.tensor_tensor(co_tm[:, :], rt_tm[:, :], tmp_a[:, :], ALU.mult)

        @blk.scalar
        def _(sr):
            sc = ctx.s(sr)
            sc.wait("SV", pre_done[0])
            sc.activation(tmp_b[:, :], tmp_a[:, :], AF.Exp)
            sc.wait("SV", sum_done[0])
            sc.activation(tmp_a[:, 0:16], s1[:, :], AF.Ln)
            sc.activation(s1b[:, :], tmp_a[:, 0:16], AF.Exp, scale=-1.0)

    with nc.Block(f"l3f_cot_{ctx.uid()}") as blk:
        p0 = ctx.val("SP")
        cp2 = []

        @blk.vector
        def _(vr):
            v = ctx.v(vr)
            for i in range(16):
                v.wait("SP", p0 + i + 1)
                v.tensor_copy(coef_fm[:, i * 128:(i + 1) * 128], pst2[0][0:8, 0:128])
                cp2.append(ctx.val("SV"))

        @blk.tensor
        def _(tr):
            te = ctx.t(tr)
            for i in range(16):
                if i >= 1:
                    te.wait("SV", cp2[i - 1])
                te.transpose(pst2[0][0:8, 0:128], co_tm[:, i * 8:(i + 1) * 8],
                             ident[:, :])
    ctx.pop()

    inner_dram4 = [nc.dram_tensor(f"l3f_innerd{tc}_{ctx.uid()}", [32 * 128, NT], BF16,
                                  kind="Internal") for tc in range(TC4)]
    for es in range(2):
        ctx.push()
        wg = load_weight_bf16(ctx, weights["L3_ffn_wg"].ap()[:, es * 2048:(es + 1) * 2048],
                              1024, 2048, f"l3f_wg{es}")
        wu = load_weight_bf16(ctx, weights["L3_ffn_wu"].ap()[:, es * 2048:(es + 1) * 2048],
                              1024, 2048, f"l3f_wu{es}")
        inner = ctx.sbuf("l3f_inner", [128, 16 * NT], BF16)
        gtile = [ctx.sbuf("l3f_gt", [128, NT], BF16) for _ in range(2)]

        for tc in range(TC4):
            ctx.push()
            psg = [ctx.psum("l3f_psg", [128, NT]) for _ in range(2)]
            psu = [ctx.psum("l3f_psu", [128, NT]) for _ in range(2)]
            with nc.Block(f"l3f_gu_{es}_{tc}_{ctx.uid()}") as blk:
                p0 = ctx.val("SP")
                ss0 = ctx.val("SS")
                sv0 = ctx.val("SV")
                silu_at = lambda m: ss0 + m + 1
                mul_at = lambda m: sv0 + m + 1

                @blk.scalar
                def _(sr):
                    sc = ctx.s(sr)
                    for mc in range(16):
                        sc.wait("SP", p0 + (2 * mc + 1) * 8)
                        if mc >= 2:
                            sc.wait("SV", mul_at(mc - 2))
                        sc.activation(gtile[mc % 2][:, :], psg[mc % 2][:, :], AF.Silu)

                @blk.vector
                def _(vr):
                    v = ctx.v(vr)
                    for mc in range(16):
                        v.wait("SP", p0 + (2 * mc + 2) * 8)
                        v.wait("SS", silu_at(mc))
                        v.tensor_tensor(inner[:, mc * NT:(mc + 1) * NT],
                                        psu[mc % 2][:, :], gtile[mc % 2][:, :],
                                        ALU.mult)

                @blk.tensor
                def _(tr):
                    te = ctx.t(tr)
                    for mc in range(16):
                        if mc >= 2:
                            te.wait("SS", silu_at(mc - 2))
                            te.wait("SV", mul_at(mc - 2))
                        for kc in range(8):
                            te.matmul(psg[mc % 2][:, :],
                                      wg[:, kc * 2048 + mc * 128: kc * 2048 + mc * 128 + 128],
                                      xh[:, kc * T + tc * NT: kc * T + tc * NT + NT],
                                      start=(kc == 0), stop=(kc == 7))
                        for kc in range(8):
                            te.matmul(psu[mc % 2][:, :],
                                      wu[:, kc * 2048 + mc * 128: kc * 2048 + mc * 128 + 128],
                                      xh[:, kc * T + tc * NT: kc * T + tc * NT + NT],
                                      start=(kc == 0), stop=(kc == 7))
            ctx.pop()

            ctx.push()
            psc = ctx.psum("l3f_psc", [128, NT])
            with nc.Block(f"l3f_coef_{es}_{tc}_{ctx.uid()}") as blk:
                p0 = ctx.val("SP")
                cmul = []

                @blk.vector
                def _(vr):
                    v = ctx.v(vr)
                    for el in range(4):
                        v.wait("SP", p0 + el + 1)
                        iv = inner[:, el * 4 * NT:(el + 1) * 4 * NT].rearrange(
                            "p (c n) -> p c n", n=NT)
                        pb = bass.AP(psc, 0, [[NT, 128], [0, 4], [1, NT]])
                        v.tensor_tensor(iv, iv, pb, ALU.mult)
                        cmul.append(ctx.val("SV"))

                @blk.tensor
                def _(tr):
                    te = ctx.t(tr)
                    for el in range(4):
                        e = es * 4 + el
                        if el >= 1:
                            te.wait("SV", cmul[el - 1])
                        te.matmul(psc[:, :], consts["esel"][:, e * 128:(e + 1) * 128],
                                  coef_fm[:, tc * NT:(tc + 1) * NT],
                                  start=True, stop=True)
            ctx.pop()
            emit_spill(ctx, inner, inner_dram4[tc].ap()[es * 2048:(es + 1) * 2048, :],
                       16, NT, f"l3f_in{es}_{tc}")
        ctx.pop()
    ctx.pop()

    ctx.push()
    wd = load_weight_bf16(ctx, weights["L3_ffn_wd"].ap(), 4096, 1024, "l3f_wd")
    inner = ctx.sbuf("l3f_inner2", [128, 32 * NT], BF16)
    for tc in range(TC4):
        emit_restore(ctx, inner, inner_dram4[tc].ap(), 32, NT, f"l3f_in{tc}")
        emit_mm(ctx,
                lambda kc, mc: wd[:, kc * 1024 + mc * 128: kc * 1024 + mc * 128 + 128],
                lambda kc, _tc: inner[:, kc * NT:(kc + 1) * NT],
                KC=32, MC=8, TCn=1,
                epilogue=lambda eng, ps, mc, _tc, tc=tc: eng.tensor_tensor(
                    x[:, mc * T + tc * NT: mc * T + tc * NT + NT],
                    x[:, mc * T + tc * NT: mc * T + tc * NT + NT], ps, ALU.add),
                name=f"l3f_down_{tc}")
    ctx.pop()

    ctx.push()
    wsg = load_weight_bf16(ctx, weights["L3_ffn_shared_gate"].ap(), 1024, 1024, "l3f_wsg")
    wsu = load_weight_bf16(ctx, weights["L3_ffn_shared_up"].ap(), 1024, 1024, "l3f_wsu")
    sboth = ctx.sbuf("l3f_sb", [128, DC * T], BF16)
    stile = [ctx.sbuf("l3f_st", [128, NT], BF16) for _ in range(2)]
    ctx.push()
    psg = [ctx.psum("l3f_psg2", [128, NT]) for _ in range(2)]
    psu = [ctx.psum("l3f_psu2", [128, NT]) for _ in range(2)]
    with nc.Block(f"l3f_sgu_{ctx.uid()}") as blk:
        p0 = ctx.val("SP")
        ss0 = ctx.val("SS")
        sv0 = ctx.val("SV")
        silu_at = lambda m: ss0 + m + 1
        mul_at = lambda m: sv0 + m + 1

        @blk.scalar
        def _(sr):
            sc = ctx.s(sr)
            for i in range(32):
                sc.wait("SP", p0 + (2 * i + 1) * 8)
                if i >= 2:
                    sc.wait("SV", mul_at(i - 2))
                sc.activation(stile[i % 2][:, :], psg[i % 2][:, :], AF.Silu)

        @blk.vector
        def _(vr):
            v = ctx.v(vr)
            for i in range(32):
                mc, tc = divmod(i, TC4)
                v.wait("SP", p0 + (2 * i + 2) * 8)
                v.wait("SS", silu_at(i))
                v.tensor_tensor(sboth[:, mc * T + tc * NT: mc * T + tc * NT + NT],
                                psu[i % 2][:, :], stile[i % 2][:, :], ALU.mult)

        @blk.tensor
        def _(tr):
            te = ctx.t(tr)
            for i in range(32):
                mc, tc = divmod(i, TC4)
                if i >= 2:
                    te.wait("SS", silu_at(i - 2))
                    te.wait("SV", mul_at(i - 2))
                for kc in range(8):
                    te.matmul(psg[i % 2][:, :],
                              wsg[:, kc * 1024 + mc * 128: kc * 1024 + mc * 128 + 128],
                              xh[:, kc * T + tc * NT: kc * T + tc * NT + NT],
                              start=(kc == 0), stop=(kc == 7))
                for kc in range(8):
                    te.matmul(psu[i % 2][:, :],
                              wsu[:, kc * 1024 + mc * 128: kc * 1024 + mc * 128 + 128],
                              xh[:, kc * T + tc * NT: kc * T + tc * NT + NT],
                              start=(kc == 0), stop=(kc == 7))
    ctx.pop()

    wsd = load_weight_bf16(ctx, weights["L3_ffn_shared_down"].ap(), 1024, 1024, "l3f_wsd")
    emit_mm(ctx,
            lambda kc, mc: wsd[:, kc * 1024 + mc * 128: kc * 1024 + mc * 128 + 128],
            lambda kc, tc: sboth[:, kc * T + tc * NT: kc * T + tc * NT + NT],
            KC=8, MC=8, TCn=TC4,
            epilogue=lambda eng, ps, mc, tc: eng.tensor_tensor(
                x[:, mc * T + tc * NT: mc * T + tc * NT + NT],
                x[:, mc * T + tc * NT: mc * T + tc * NT + NT], ps, ALU.add),
            name="l3f_sdown")
    ctx.pop()
    ctx.pop()


def emit_l2_ffn(ctx, x, weights, consts):
    nc = ctx.nc
    ctx.push()
    zA_dram = nc.dram_tensor(f"l2f_zad_{ctx.uid()}", [D, T], BF16, kind="Internal")

    ctx.push()
    q_fm = ctx.sbuf("l2f_q", [128, DC * T], BF16)
    k_fm = ctx.sbuf("l2f_k", [128, DC * T], BF16)
    v_tm = ctx.sbuf("l2f_v", [128, 16 * 1024], BF16)
    TH = T // 2
    for th in range(2):
        ctx.push()
        xh = ctx.sbuf("l2f_xh", [128, DC * TH], BF16)
        for tci in range(2):
            tc = th * 2 + tci
            emit_ln_tc(ctx,
                       lambda c: x[:, c * T + tc * NT: c * T + tc * NT + NT],
                       lambda c: (xh[:, c * TH + tci * NT: c * TH + tci * NT + NT], None),
                       consts, tc, name="l2f_ln")

        ctx.push()
        wq = load_weight_bf16(ctx, weights["L2_ffn_Wq"].ap(), 1024, 1024, "l2f_wq")
        emit_mm(ctx,
                lambda kc, mc: wq[:, kc * 1024 + mc * 128: kc * 1024 + mc * 128 + 128],
                lambda kc, tci: xh[:, kc * TH + tci * NT: kc * TH + tci * NT + NT],
                KC=8, MC=8, TCn=2,
                epilogue=lambda eng, ps, mc, tci, th=th: eng.tensor_copy(
                    q_fm[:, mc * T + (th * 2 + tci) * NT: mc * T + (th * 2 + tci) * NT + NT], ps),
                name=f"l2f_qm{th}")
        ctx.pop()

        ctx.push()
        wk = load_weight_bf16(ctx, weights["L2_ffn_Wk"].ap(), 1024, 1024, "l2f_wk")
        emit_mm(ctx,
                lambda kc, mc: wk[:, kc * 1024 + mc * 128: kc * 1024 + mc * 128 + 128],
                lambda kc, tci: xh[:, kc * TH + tci * NT: kc * TH + tci * NT + NT],
                KC=8, MC=8, TCn=2,
                epilogue=lambda eng, ps, mc, tci, th=th: eng.tensor_scalar(
                    k_fm[:, mc * T + (th * 2 + tci) * NT: mc * T + (th * 2 + tci) * NT + NT],
                    ps, 1.0 / 32.0, None, ALU.mult),
                name=f"l2f_km{th}")
        ctx.pop()

        ctx.push()
        wv = load_weight_bf16(ctx, weights["L2_ffn_Wv"].ap(), 1024, 1024, "l2f_wv")
        emit_mm(ctx,
                lambda kc, mc: xh[:, kc * TH + mc * 128: kc * TH + mc * 128 + 128],
                lambda kc, nh: wv[:, kc * 1024 + nh * NT: kc * 1024 + nh * NT + NT],
                KC=8, MC=8, TCn=2,
                epilogue=lambda eng, ps, mc, nh, th=th: eng.tensor_copy(
                    v_tm[:, (th * 8 + mc) * 1024 + nh * NT: (th * 8 + mc) * 1024 + nh * NT + NT],
                    ps),
                name=f"l2f_vm{th}")
        ctx.pop()
        ctx.pop()

    ctx.push()
    expS = ctx.sbuf("l2f_es", [128, T], BF16)
    negmx = ctx.sbuf("l2f_nm", [128, 1])
    sums4 = ctx.sbuf("l2f_s4", [128, 4])
    rcol = ctx.sbuf("l2f_rc", [128, 1])
    rcol2 = ctx.sbuf("l2f_rc2", [128, 1])
    vt_h = ctx.sbuf("l2f_vt", [128, 64], BF16)
    oev = ctx.sbuf("l2f_oev", [64, T], BF16)
    psS = ctx.psum("l2f_psS", [128, T])
    psO = ctx.psum("l2f_psO", [64, T])

    for h in range(16):
        hb = (h % 2) * 64
        hc = h // 2
        with nc.Block(f"l2f_att_{h}_{ctx.uid()}") as blk:
            p0 = ctx.val("SP")
            v0 = ctx.val("SV")
            s0 = ctx.val("SS")

            @blk.tensor
            def _(tr):
                te = ctx.t(tr)
                for kt in range(16):
                    if kt >= 1:
                        te.wait("SS", s0 + 6 * (kt - 1) + 4)
                    for qc in range(TC4):
                        te.matmul(psS[:, qc * NT:(qc + 1) * NT],
                                  k_fm[hb:hb + 64, hc * T + kt * 128: hc * T + kt * 128 + 128],
                                  q_fm[hb:hb + 64, hc * T + qc * NT: hc * T + qc * NT + NT],
                                  start=True, stop=True)
                    te.wait("SV", v0 + 3 * kt + 3)
                    for qc in range(TC4):
                        te.matmul(psO[:, qc * NT:(qc + 1) * NT], vt_h[:, :],
                                  expS[:, qc * NT:(qc + 1) * NT],
                                  start=(kt == 0), stop=(kt == 15))

            @blk.vector
            def _(vr):
                v = ctx.v(vr)
                for kt in range(16):
                    v.wait("SP", p0 + 8 * kt + 4)
                    v.reduce_max(negmx[:, :], psS[:, :], AX.X, negate=True)
                    v.wait("SS", s0 + 6 * kt + 4)
                    v.reduce_sum(rcol[:, :], sums4[:, :], AX.X)
                    v.wait("SS", s0 + 6 * kt + 6)
                    v.tensor_scalar(vt_h[:, :],
                                    v_tm[:, kt * 1024 + h * 64: kt * 1024 + h * 64 + 64],
                                    rcol[:, :], None, ALU.mult)

            @blk.scalar
            def _(sr):
                sc = ctx.s(sr)
                for kt in range(16):
                    sc.wait("SV", v0 + 3 * kt + 1)
                    for qc in range(TC4):
                        sc.activation(expS[:, qc * NT:(qc + 1) * NT],
                                      psS[:, qc * NT:(qc + 1) * NT], AF.Exp,
                                      bias=negmx[:, :],
                                      accum_out=sums4[:, qc:qc + 1])
                    sc.wait("SV", v0 + 3 * kt + 2)
                    sc.activation(rcol2[:, :], rcol[:, :], AF.Ln)
                    sc.activation(rcol[:, :], rcol2[:, :], AF.Exp, scale=-1.0)

        with nc.Block(f"l2f_oev_{h}_{ctx.uid()}") as blk:
            ev_done = []

            @blk.vector
            def _(vr):
                v = ctx.v(vr)
                v.tensor_copy(oev[:, :], psO[:, :])
                ev_done.append(ctx.val("SV"))

            @blk.sync
            def _(s):
                s.wait_ge(ctx.sem("SV"), ev_done[0])
                s.dma_start(zA_dram.ap()[h * 64:(h + 1) * 64, :],
                            oev[:, :]).then_inc(ctx.sem("dml"), 16)
                s.wait_ge(ctx.sem("dml"), ctx.bump("dml", 16))
    ctx.pop()
    ctx.pop()

    ctx.push()
    zA = ctx.sbuf("l2f_zA", [128, DC * T], BF16)
    zB = ctx.sbuf("l2f_zB", [128, DC * T], BF16)
    emit_restore(ctx, zA, zA_dram.ap(), DC, T, "l2f_za")
    ctx.push()
    wo = load_weight_bf16(ctx, weights["L2_ffn_Wo"].ap(), 1024, 1024, "l2f_wo")
    emit_mm(ctx,
            lambda kc, mc: wo[:, kc * 1024 + mc * 128: kc * 1024 + mc * 128 + 128],
            lambda kc, tc: zA[:, kc * T + tc * NT: kc * T + tc * NT + NT],
            KC=8, MC=8, TCn=TC4,
            epilogue=lambda eng, ps, mc, tc: eng.tensor_copy(
                zB[:, mc * T + tc * NT: mc * T + tc * NT + NT], ps),
            name="l2f_wo")
    ctx.pop()

    emit_l2norm(ctx, zB, zA, consts, name="l2f_n1")

    ctx.push()
    wf = load_weight_bf16(ctx, weights["L2_ffn_Wff"].ap(), 1024, 1024, "l2f_wf")
    emit_mm(ctx,
            lambda kc, mc: wf[:, kc * 1024 + mc * 128: kc * 1024 + mc * 128 + 128],
            lambda kc, tc: zA[:, kc * T + tc * NT: kc * T + tc * NT + NT],
            KC=8, MC=8, TCn=TC4,
            epilogue=lambda eng, ps, mc, tc: eng.tensor_copy(
                zB[:, mc * T + tc * NT: mc * T + tc * NT + NT], ps),
            name="l2f_wf")
    ctx.pop()

    emit_l2norm(ctx, zB, zA, consts, name="l2f_n2", gelu_res_x=x)
    ctx.pop()
    ctx.pop()


def emit_l3_attn(ctx, x, weights, consts):
    nc = ctx.nc
    ctx.push()
    ao_dram = nc.dram_tensor(f"l3a_aod_{ctx.uid()}", [D, T], BF16, kind="Internal")
    ctx.push()
    q_fm = ctx.sbuf("l3a_q", [128, DC * T], BF16)
    k_fm = ctx.sbuf("l3a_k", [128, DC * T], BF16)
    v_tm = ctx.sbuf("l3a_v", [128, 16 * 1024], BF16)
    bq = load_raw(ctx, weights["L3_attn_bqc"].ap(), 128, 16, "l3a_bq")
    vbias = ctx.sbuf("l3a_vb", [128, 1024])

    vbr = load_raw(ctx, weights["L3_attn_bvrow"].ap(), 1, 1024, "l3a_vbr")
    ctx.push()
    psb = ctx.psum("l3a_psb", [128, NT])
    with nc.Block(f"l3a_vb_{ctx.uid()}") as blk:
        p0 = ctx.val("SP")
        sv0 = ctx.val("SV")

        @blk.vector
        def _(vr):
            v = ctx.v(vr)
            for i in range(2):
                v.wait("SP", p0 + i + 1)
                v.tensor_copy(vbias[:, i * NT:(i + 1) * NT], psb[:, :])

        @blk.tensor
        def _(tr):
            te = ctx.t(tr)
            for i in range(2):
                if i >= 1:
                    te.wait("SV", sv0 + i)
                te.matmul(psb[:, :], consts["onesrow"][:, :],
                          vbr[0:1, i * NT:(i + 1) * NT], start=True, stop=True)
    ctx.pop()

    TH = T // 2
    for th in range(2):
        ctx.push()
        xh = ctx.sbuf("l3a_xh", [128, DC * TH], BF16)
        for tci in range(2):
            tc = th * 2 + tci
            emit_ln_tc(ctx,
                       lambda c: x[:, c * T + tc * NT: c * T + tc * NT + NT],
                       lambda c: (xh[:, c * TH + tci * NT: c * TH + tci * NT + NT], None),
                       consts, tc, name="l3a_ln")

        ctx.push()
        wq = load_weight_bf16(ctx, weights["L3_attn_wqkvT"].ap()[:, 0:1024],
                              1024, 1024, "l3a_wq", nstg=1)
        emit_mm(ctx,
                lambda kc, mc: wq[:, kc * 1024 + mc * 128: kc * 1024 + mc * 128 + 128],
                lambda kc, tci: xh[:, kc * TH + tci * NT: kc * TH + tci * NT + NT],
                KC=8, MC=8, TCn=2,
                epilogue=lambda eng, ps, mc, tci, th=th: eng.tensor_scalar(
                    q_fm[:, mc * T + (th * 2 + tci) * NT: mc * T + (th * 2 + tci) * NT + NT],
                    ps, bq[:, mc:mc + 1], 0.125, ALU.add, ALU.mult),
                name=f"l3a_qm{th}")
        ctx.pop()

        ctx.push()
        wk = load_weight_bf16(ctx, weights["L3_attn_wqkvT"].ap()[:, 1024:2048],
                              1024, 1024, "l3a_wk", nstg=1)
        emit_mm(ctx,
                lambda kc, mc: wk[:, kc * 1024 + mc * 128: kc * 1024 + mc * 128 + 128],
                lambda kc, tci: xh[:, kc * TH + tci * NT: kc * TH + tci * NT + NT],
                KC=8, MC=8, TCn=2,
                epilogue=lambda eng, ps, mc, tci, th=th: eng.tensor_scalar(
                    k_fm[:, mc * T + (th * 2 + tci) * NT: mc * T + (th * 2 + tci) * NT + NT],
                    ps, bq[:, 8 + mc:8 + mc + 1], None, ALU.add),
                name=f"l3a_km{th}")
        ctx.pop()

        ctx.push()
        wv = load_weight_bf16(ctx, weights["L3_attn_wqkvT"].ap()[:, 2048:3072],
                              1024, 1024, "l3a_wv", nstg=1)
        emit_mm(ctx,
                lambda kc, mc: xh[:, kc * TH + mc * 128: kc * TH + mc * 128 + 128],
                lambda kc, nh: wv[:, kc * 1024 + nh * NT: kc * 1024 + nh * NT + NT],
                KC=8, MC=8, TCn=2,
                epilogue=lambda eng, ps, mc, nh, th=th: eng.tensor_tensor(
                    v_tm[:, (th * 8 + mc) * 1024 + nh * NT: (th * 8 + mc) * 1024 + nh * NT + NT],
                    ps, vbias[:, nh * NT:(nh + 1) * NT], ALU.add),
                name=f"l3a_vm{th}")
        ctx.pop()
        ctx.pop()

    masks = load_raw(ctx, weights["L3_attn_masks"].ap(), 128, 3 * 384, "l3a_mk")
    ctx.push()
    s_sb = ctx.sbuf("l3a_ssb", [128, 384])
    p_bf = ctx.sbuf("l3a_pbf", [128, 384], BF16)
    pt_sb = ctx.sbuf("l3a_pt", [128, 384], BF16)
    negm = ctx.sbuf("l3a_ng", [128, 1])
    rsum = ctx.sbuf("l3a_rs", [128, 1])
    rrec = ctx.sbuf("l3a_rr", [128, 1])
    rrec2 = ctx.sbuf("l3a_rr2", [128, 1])
    oevt = ctx.sbuf("l3a_oevt", [64, 128], BF16)
    psS = [ctx.psum("l3a_psS", [128, 384]) for _ in range(2)]
    psT = [ctx.psum("l3a_psT", [128, 384], BF16) for _ in range(2)]
    psO = [ctx.psum("l3a_psO", [64, 128]) for _ in range(2)]

    for h in range(16):
        hb = (h % 2) * 64
        hc = h // 2
        for qt in range(16):
            kw = min(max(qt * 128 - 128, 0), T - 384)
            kc0 = kw // 128
            dlt = (qt * 128 - kw) // 128
            pb = qt % 2
            with nc.Block(f"l3a_att_{h}_{qt}_{ctx.uid()}") as blk:
                p0 = ctx.val("SP")
                sv0 = ctx.val("SV")
                ss0 = ctx.val("SS")

                @blk.vector
                def _(vr):
                    v = ctx.v(vr)
                    v.wait("SP", p0 + 1)
                    v.tensor_tensor(s_sb[:, :], psS[pb][:, :],
                                    masks[:, dlt * 384:(dlt + 1) * 384], ALU.add)
                    v.reduce_max(negm[:, :], s_sb[:, :], AX.X, negate=True)
                    v.wait("SS", ss0 + 3)
                    v.tensor_scalar(p_bf[:, :], p_bf[:, :], rrec[:, :], None, ALU.mult)
                    v.wait("SP", p0 + 4)
                    v.tensor_copy(pt_sb[:, :], psT[pb][:, :])
                    v.wait("SP", p0 + 7)
                    v.tensor_copy(oevt[:, :], psO[pb][:, :])

                @blk.scalar
                def _(sr):
                    sc = ctx.s(sr)
                    sc.wait("SV", sv0 + 2)
                    sc.activation(p_bf[:, :], s_sb[:, :], AF.Exp, bias=negm[:, :],
                                  accum_out=rsum[:, :])
                    sc.activation(rrec2[:, :], rsum[:, :], AF.Ln)
                    sc.activation(rrec[:, :], rrec2[:, :], AF.Exp, scale=-1.0)

                @blk.tensor
                def _(tr):
                    te = ctx.t(tr)
                    te.matmul(psS[pb][:, :],
                              q_fm[hb:hb + 64, hc * T + qt * 128: hc * T + qt * 128 + 128],
                              k_fm[hb:hb + 64, hc * T + kw: hc * T + kw + 384],
                              start=True, stop=True)
                    te.wait("SV", sv0 + 3)
                    for c in range(3):
                        te.transpose(psT[pb][:, c * 128:(c + 1) * 128],
                                     p_bf[:, c * 128:(c + 1) * 128],
                                     consts["identb"][:, :])
                    te.wait("SV", sv0 + 4)
                    for c in range(3):
                        te.matmul(psO[pb][:, :],
                                  v_tm[:, (kc0 + c) * 1024 + h * 64: (kc0 + c) * 1024 + h * 64 + 64],
                                  pt_sb[:, c * 128:(c + 1) * 128],
                                  start=(c == 0), stop=(c == 2))

                @blk.sync
                def _(s):
                    s.wait_ge(ctx.sem("SV"), sv0 + 5)
                    s.dma_start(
                        ao_dram.ap()[hc * 128 + hb: hc * 128 + hb + 64,
                                     qt * 128:(qt + 1) * 128],
                        oevt[:, :]).then_inc(ctx.sem("dml"), 16)
                    s.wait_ge(ctx.sem("dml"), ctx.bump("dml", 16))
    ctx.pop()
    ctx.pop()

    ctx.push()
    ao_fm = ctx.sbuf("l3a_ao", [128, DC * T], BF16)
    emit_restore(ctx, ao_fm, ao_dram.ap(), DC, T, "l3a_ao")
    wo = load_weight_bf16(ctx, weights["L3_attn_woutT"].ap(), 1024, 1024, "l3a_wo")
    bo = load_raw(ctx, weights["L3_attn_boc"].ap(), 128, 8, "l3a_bo")
    emit_mm(ctx,
            lambda kc, mc: wo[:, kc * 1024 + mc * 128: kc * 1024 + mc * 128 + 128],
            lambda kc, tc: ao_fm[:, kc * T + tc * NT: kc * T + tc * NT + NT],
            KC=8, MC=8, TCn=TC4,
            epilogue=lambda eng, ps, mc, tc: eng.scalar_tensor_tensor(
                x[:, mc * T + tc * NT: mc * T + tc * NT + NT],
                ps, bo[:, mc:mc + 1],
                x[:, mc * T + tc * NT: mc * T + tc * NT + NT], ALU.add, ALU.add),
            name="l3a_wo")
    ctx.pop()
    ctx.pop()


def _fxp_epi(eng, ps_ap, out_ap, final_scale, tmp, tmpi, tmp2):
    eng.tensor_scalar(tmp, ps_ap, 4096.0, None, ALU.mult)
    eng.tensor_scalar(tmpi, tmp, -32767.0, 32767.0, ALU.max, ALU.min)
    eng.custom(FXTRUNC_SGN, out=tmp2, in0=tmp, in1=tmpi, s0=-32767.0, s1=32767.0)
    eng.tensor_scalar(out_ap, tmp2, final_scale, None, ALU.mult)


def emit_l1_attn(ctx, x, weights, consts):
    nc = ctx.nc
    ctx.push()
    ao_dram = nc.dram_tensor(f"l1a_aod_{ctx.uid()}", [D, T], BF16, kind="Internal")
    ln4096 = load_raw(ctx, weights["L1_attn_ln4096"].ap(), 1, 1, "l1a_l4")
    ctx.push()
    q_fm = ctx.sbuf("l1a_q", [128, DC * T], BF16)
    k_fm = ctx.sbuf("l1a_k", [128, DC * T], BF16)
    v_tm = ctx.sbuf("l1a_v", [128, 16 * 1024], BF16)
    ftmp = ctx.sbuf("l1a_ft", [128, NT])
    ftmpi = ctx.sbuf("l1a_fti", [128, NT], I32)
    ftmp2 = ctx.sbuf("l1a_ft2", [128, NT])

    TH = T // 2
    for th in range(2):
        ctx.push()
        xh = ctx.sbuf("l1a_xh", [128, DC * TH], BF16)
        for tci in range(2):
            tc = th * 2 + tci
            emit_ln_tc(ctx,
                       lambda c: x[:, c * T + tc * NT: c * T + tc * NT + NT],
                       lambda c: (xh[:, c * TH + tci * NT: c * TH + tci * NT + NT], None),
                       consts, tc, name="l1a_ln")

        ctx.push()
        wq = load_weight_bf16(ctx, weights["L1_attn_wqkvT"].ap()[:, 0:1024],
                              1024, 1024, "l1a_wq", nstg=1)
        emit_mm(ctx,
                lambda kc, mc: wq[:, kc * 1024 + mc * 128: kc * 1024 + mc * 128 + 128],
                lambda kc, tci: xh[:, kc * TH + tci * NT: kc * TH + tci * NT + NT],
                KC=8, MC=8, TCn=2,
                epilogue=lambda eng, ps, mc, tci, th=th: _fxp_epi(
                    eng, ps,
                    q_fm[:, mc * T + (th * 2 + tci) * NT: mc * T + (th * 2 + tci) * NT + NT],
                    0.125, ftmp[:, :], ftmpi[:, :], ftmp2[:, :]),
                name=f"l1a_qm{th}", psum_banks=3)
        ctx.pop()

        ctx.push()
        wk = load_weight_bf16(ctx, weights["L1_attn_wqkvT"].ap()[:, 1024:2048],
                              1024, 1024, "l1a_wk", nstg=1)
        emit_mm(ctx,
                lambda kc, mc: wk[:, kc * 1024 + mc * 128: kc * 1024 + mc * 128 + 128],
                lambda kc, tci: xh[:, kc * TH + tci * NT: kc * TH + tci * NT + NT],
                KC=8, MC=8, TCn=2,
                epilogue=lambda eng, ps, mc, tci, th=th: _fxp_epi(
                    eng, ps,
                    k_fm[:, mc * T + (th * 2 + tci) * NT: mc * T + (th * 2 + tci) * NT + NT],
                    2.0 ** -12, ftmp[:, :], ftmpi[:, :], ftmp2[:, :]),
                name=f"l1a_km{th}", psum_banks=3)
        ctx.pop()

        ctx.push()
        wv = load_weight_bf16(ctx, weights["L1_attn_wqkvT"].ap()[:, 2048:3072],
                              1024, 1024, "l1a_wv", nstg=1)
        emit_mm(ctx,
                lambda kc, mc: xh[:, kc * TH + mc * 128: kc * TH + mc * 128 + 128],
                lambda kc, nh: wv[:, kc * 1024 + nh * NT: kc * 1024 + nh * NT + NT],
                KC=8, MC=8, TCn=2,
                epilogue=lambda eng, ps, mc, nh, th=th: _fxp_epi(
                    eng, ps,
                    v_tm[:, (th * 8 + mc) * 1024 + nh * NT: (th * 8 + mc) * 1024 + nh * NT + NT],
                    2.0 ** -12, ftmp[:, :], ftmpi[:, :], ftmp2[:, :]),
                name=f"l1a_vm{th}", psum_banks=3)
        ctx.pop()
        ctx.pop()

    ctx.push()
    expS = ctx.sbuf("l1a_es", [128, 16 * NT], BF16)
    fint = ctx.sbuf("l1a_fi", [128, NT])
    rint = ctx.sbuf("l1a_ri", [128, NT], I32)
    ntile = ctx.sbuf("l1a_nt", [128, NT])
    nint = ctx.sbuf("l1a_ni", [128, NT], I32)
    attn_bf = [ctx.sbuf("l1a_ab", [128, NT], BF16) for _ in range(2)]
    rb_sb = ctx.sbuf("l1a_rb", [128, NT])
    rrow = ctx.sbuf("l1a_rrow", [1, NT])
    rrow2 = ctx.sbuf("l1a_rrow2", [1, NT])
    oevt = ctx.sbuf("l1a_oevt", [64, NT], BF16)
    oft = ctx.sbuf("l1a_oft", [64, NT])
    ofti = ctx.sbuf("l1a_ofti", [64, NT], I32)
    oft2 = ctx.sbuf("l1a_oft2", [64, NT])
    psS = [ctx.psum("l1a_psS", [128, NT]) for _ in range(2)]
    psSum = ctx.psum("l1a_psSum", [1, NT])
    psRB = ctx.psum("l1a_psRB", [128, NT])
    psO = ctx.psum("l1a_psO", [64, NT])

    for h in range(16):
        hb = (h % 2) * 64
        hc = h // 2
        for qq in range(TC4):
            with nc.Block(f"l1a_att_{h}_{qq}_{ctx.uid()}") as blk:
                p0 = ctx.val("SP")
                sv0 = ctx.val("SV")
                ss0 = ctx.val("SS")

                @blk.vector
                def _(vr):
                    v = ctx.v(vr)
                    for kt in range(16):
                        v.wait("SP", p0 + 2 * kt + 1)
                        if kt >= 1:
                            v.wait("SS", ss0 + kt)
                        v.tensor_scalar(rint[:, :], psS[kt % 2][:, :],
                                        -32767.0, 32767.0, ALU.max, ALU.min)
                        v.custom(FXTRUNC_SGN, out=fint[:, :], in0=psS[kt % 2][:, :],
                                 in1=rint[:, :], s0=-32767.0, s1=32767.0)
                    v.wait("SP", p0 + 33)
                    v.tensor_copy(rb_sb[:, :], psRB[:, :])
                    for kt in range(16):
                        v.tensor_tensor(ntile[:, :], expS[:, kt * NT:(kt + 1) * NT],
                                        rb_sb[:, :], ALU.mult)
                        v.tensor_scalar(nint[:, :], ntile[:, :], 1.0, None, ALU.mult)
                        if kt >= 2:
                            v.wait("SP", p0 + 33 + kt - 1)
                        v.custom(FXTRUNC_POS, out=attn_bf[kt % 2][:, :], in0=ntile[:, :],
                                 in1=nint[:, :], s0=1.0, imm2=2.0 ** -12)

                @blk.scalar
                def _(sr):
                    sc = ctx.s(sr)
                    for kt in range(16):
                        sc.wait("SV", sv0 + 2 * kt + 2)
                        sc.activation(expS[:, kt * NT:(kt + 1) * NT], fint[:, :],
                                      AF.Exp, scale=2.0 ** -12)
                    sc.wait("SP", p0 + 32)
                    sc.activation(rrow2[:, :], psSum[:, :], AF.Ln)
                    sc.activation(rrow[:, :], rrow2[:, :], AF.Exp,
                                  scale=-1.0, bias=ln4096[0:1, 0:1])

                @blk.tensor
                def _(tr):
                    te = ctx.t(tr)
                    for kt in range(16):
                        if kt >= 2:
                            te.wait("SS", ss0 + kt - 1)
                        te.matmul(psS[kt % 2][:, :],
                                  k_fm[hb:hb + 64, hc * T + kt * 128: hc * T + kt * 128 + 128],
                                  q_fm[hb:hb + 64, hc * T + qq * NT: hc * T + qq * NT + NT],
                                  start=True, stop=True)
                        te.wait("SS", ss0 + kt + 1)
                        te.matmul(psSum[:, :], consts["onescolb"][:, :],
                                  expS[:, kt * NT:(kt + 1) * NT],
                                  start=(kt == 0), stop=(kt == 15))
                    te.wait("SS", ss0 + 18)
                    te.matmul(psRB[:, :], consts["onesrow"][:, :], rrow[:, :],
                              start=True, stop=True)
                    for kt in range(16):
                        te.wait("SV", sv0 + 33 + 3 * kt + 3)
                        te.matmul(psO[:, :],
                                  v_tm[:, kt * 1024 + h * 64: kt * 1024 + h * 64 + 64],
                                  attn_bf[kt % 2][:, :],
                                  start=(kt == 0), stop=(kt == 15))

            with nc.Block(f"l1a_oev_{h}_{qq}_{ctx.uid()}") as blk:
                done = []

                @blk.vector
                def _(vr):
                    v = ctx.v(vr)
                    _fxp_epi(v, psO[:, :], oevt[:, :], 2.0 ** -12,
                             oft[:, :], ofti[:, :], oft2[:, :])
                    done.append(ctx.val("SV"))

                @blk.sync
                def _(s):
                    s.wait_ge(ctx.sem("SV"), done[0])
                    s.dma_start(
                        ao_dram.ap()[hc * 128 + hb: hc * 128 + hb + 64,
                                     qq * NT:(qq + 1) * NT],
                        oevt[:, :]).then_inc(ctx.sem("dml"), 16)
                    s.wait_ge(ctx.sem("dml"), ctx.bump("dml", 16))
    ctx.pop()
    ctx.pop()

    ctx.push()
    ao_fm = ctx.sbuf("l1a_ao", [128, DC * T], BF16)
    emit_restore(ctx, ao_fm, ao_dram.ap(), DC, T, "l1a_ao")
    wp = load_weight_bf16(ctx, weights["L1_attn_wprojT"].ap(), 1024, 1024, "l1a_wp")
    bp = load_raw(ctx, weights["L1_attn_bpc"].ap(), 128, 8, "l1a_bp")
    ftmp = ctx.sbuf("l1a_pt", [128, NT])
    ftmpi = ctx.sbuf("l1a_pti", [128, NT], I32)
    ftmp2 = ctx.sbuf("l1a_pt2", [128, NT])

    def epi_p(eng, ps, mc, tc):
        eng.tensor_scalar(ftmp[:, :], ps, bp[:, mc:mc + 1], 4096.0, ALU.add, ALU.mult)
        eng.tensor_scalar(ftmpi[:, :], ftmp[:, :], -32767.0, 32767.0, ALU.max, ALU.min)
        eng.custom(FXTRUNC_SGN, out=ftmp2[:, :], in0=ftmp[:, :], in1=ftmpi[:, :],
                   s0=-32767.0, s1=32767.0)
        xs = x[:, mc * T + tc * NT: mc * T + tc * NT + NT]
        eng.scalar_tensor_tensor(xs, ftmp2[:, :], 2.0 ** -12, xs, ALU.mult, ALU.add)

    emit_mm(ctx,
            lambda kc, mc: wp[:, kc * 1024 + mc * 128: kc * 1024 + mc * 128 + 128],
            lambda kc, tc: ao_fm[:, kc * T + tc * NT: kc * T + tc * NT + NT],
            KC=8, MC=8, TCn=TC4, epilogue=epi_p, name="l1a_wp", psum_banks=3)
    ctx.pop()
    ctx.pop()


def emit_l0_ffn(ctx, x, weights, consts):
    nc = ctx.nc
    ctx.push()
    mask = ctx.sbuf("l0f_mk", [64, T], BF16)
    inner = ctx.sbuf("l0f_in", [128, 32 * 128], BF16)
    inner2 = ctx.sbuf("l0f_in2", [128, 32 * 128], BF16)
    rp = load_raw(ctx, weights["L0_ffn_rp"].ap(), 64, 32 * 128, "l0f_rp", BF16)
    besf = load_raw(ctx, weights["L0_ffn_besf"].ap(), 128, 32, "l0f_besf")

    ctx.push()
    xh = ctx.sbuf("l0f_xh", [128, DC * T], BF16)
    emit_ln(ctx, x, xh, consts, name="l0f_ln")

    ctx.push()
    logits = ctx.sbuf("l0f_lg", [64, T])
    gmax = ctx.sbuf("l0f_gm", [64, 128])
    ctx.push()
    ctl = load_weight_bf16(ctx, weights["L0_ffn_ctlR"].ap(), 1024, 64, "l0f_ctl")
    tieT = load_raw(ctx, weights["L0_ffn_tieT"].ap(), 64, T, "l0f_tie")
    emit_mm(ctx, lambda kc, mc: ctl[:, kc * 64:(kc + 1) * 64],
            lambda kc, tc: xh[:, kc * T + tc * NT: kc * T + tc * NT + NT],
            KC=8, MC=1, TCn=TC4,
            epilogue=lambda eng, ps, mc, tc: eng.tensor_tensor(
                logits[:, tc * NT:(tc + 1) * NT], ps,
                tieT[:, tc * NT:(tc + 1) * NT], ALU.add),
            name="l0f_lgmm", psum_banks=2, Mp=64)
    ctx.pop()

    with nc.Block(f"l0f_mask_{ctx.uid()}") as blk:
        @blk.vector
        def _(vr):
            v = ctx.v(vr)
            lg3 = logits[:, :].rearrange("p (a b) -> p a b", b=16)
            v.reduce_max(gmax[:, :], lg3, AX.X)
            gb = bass.AP(gmax, 0, [[128, 64], [1, 128], [0, 16]])
            v.tensor_tensor(mask[:, :].rearrange("p (a b) -> p a b", b=16),
                            lg3, gb, ALU.is_equal)
    ctx.pop()

    ctx.push()
    f1R = load_weight_bf16(ctx, weights["L0_ffn_f1R"].ap(), 1024, 4096, "l0f_f1",
                           nstg=1)
    mrep = ctx.sbuf("l0f_mr", [128, NT], BF16)
    ptile = ctx.sbuf("l0f_pt", [128, NT])
    psG = [ctx.psum("l0f_psG", [128, NT]) for _ in range(2)]
    psM = ctx.psum("l0f_psM", [128, NT])
    for tc in range(TC4):
        for ch in range(32):
            with nc.Block(f"l0f_g_{tc}_{ch}_{ctx.uid()}") as blk:
                p0 = ctx.val("SP")
                sv0 = ctx.val("SV")

                @blk.vector
                def _(vr):
                    v = ctx.v(vr)
                    v.wait("SP", p0 + 1)
                    v.tensor_copy(mrep[:, :], psM[:, :])
                    v.wait("SP", p0 + 9)
                    v.tensor_tensor(ptile[:, :], psG[ch % 2][:, :], mrep[:, :],
                                    ALU.mult)
                    with nc.allow_low_precision(reason="bf16 store of fp32 group-sum"):
                        v.reduce_sum(inner[:, ch * 128 + tc * 32: ch * 128 + tc * 32 + 32],
                                     ptile[:, :].rearrange("p (a b) -> p a b", b=16), AX.X)

                @blk.tensor
                def _(tr):
                    te = ctx.t(tr)
                    te.matmul(psM[:, :], rp[:, ch * 128:(ch + 1) * 128],
                              mask[:, tc * NT:(tc + 1) * NT], start=True, stop=True)
                    for kc in range(8):
                        te.matmul(psG[ch % 2][:, :],
                                  f1R[:, kc * 4096 + ch * 128: kc * 4096 + ch * 128 + 128],
                                  xh[:, kc * T + tc * NT: kc * T + tc * NT + NT],
                                  start=(kc == 0), stop=(kc == 7))

    with nc.Block(f"l0f_relu_{ctx.uid()}") as blk:
        @blk.scalar
        def _(sr):
            sc = ctx.s(sr)
            for ch in range(32):
                sc.activation(inner2[:, ch * 128:(ch + 1) * 128],
                              inner[:, ch * 128:(ch + 1) * 128], AF.Relu,
                              bias=besf[:, ch:ch + 1])
    ctx.pop()
    ctx.pop()

    ctx.push()
    f2R = load_weight_bf16(ctx, weights["L0_ffn_f2R"].ap(), 4096, 1024, "l0f_f2",
                           nstg=1)
    V = ctx.sbuf("l0f_V", [128, 32 * NT], BF16)
    mrep = ctx.sbuf("l0f_mr2", [128, NT], BF16)
    psM = ctx.psum("l0f_psM2", [128, NT])
    for tc in range(TC4):
        for ch in range(32):
            with nc.Block(f"l0f_v_{tc}_{ch}_{ctx.uid()}") as blk:
                p0 = ctx.val("SP")

                @blk.vector
                def _(vr):
                    v = ctx.v(vr)
                    v.wait("SP", p0 + 1)
                    v.tensor_copy(mrep[:, :], psM[:, :])
                    iexp = bass.AP(inner2, ch * 128 + tc * 32,
                                   [[32 * 128, 128], [1, 32], [0, 16]])
                    v.tensor_tensor(V[:, ch * NT:(ch + 1) * NT].rearrange(
                        "p (a b) -> p a b", b=16), iexp, mrep[:, :].rearrange(
                        "p (a b) -> p a b", b=16), ALU.mult)

                @blk.tensor
                def _(tr):
                    te = ctx.t(tr)
                    te.matmul(psM[:, :], rp[:, ch * 128:(ch + 1) * 128],
                              mask[:, tc * NT:(tc + 1) * NT], start=True, stop=True)

        emit_mm(ctx,
                lambda kc, mc: f2R[:, kc * 1024 + mc * 128: kc * 1024 + mc * 128 + 128],
                lambda kc, _tc: V[:, kc * NT:(kc + 1) * NT],
                KC=32, MC=8, TCn=1,
                epilogue=lambda eng, ps, mc, _tc, tc=tc: eng.tensor_tensor(
                    x[:, mc * T + tc * NT: mc * T + tc * NT + NT],
                    x[:, mc * T + tc * NT: mc * T + tc * NT + NT], ps, ALU.add),
                name=f"l0f_f2_{tc}")
    ctx.pop()
    ctx.pop()


def emit_l0_attn(ctx, x, weights, consts):
    nc = ctx.nc
    ctx.push()
    rg = load_raw(ctx, weights["L0_rg"].ap(), 128, 16 * 128, "l0a_rg", BF16)
    bd = load_raw(ctx, weights["L0_bd"].ap(), 128, 16 * 128, "l0a_bd", BF16)
    r2 = load_raw(ctx, weights["L0_r2"].ap(), 16, 128, "l0a_r2", BF16)
    alog = load_raw(ctx, weights["L0_attn_alogpp"].ap(), 128, 128, "l0a_al")
    app = ctx.sbuf("l0a_app", [128, 128])
    bdl = load_raw(ctx, weights["L0_attn_bdlc"].ap(), 64, 1, "l0a_bdl")
    bbm = load_raw(ctx, weights["L0_attn_bbmc"].ap(), 16, 1, "l0a_bbm")
    bcm = load_raw(ctx, weights["L0_attn_bcmc"].ap(), 16, 1, "l0a_bcm")
    bup = load_raw(ctx, weights["L0_attn_bupc"].ap(), 128, 8, "l0a_bup")

    with nc.Block(f"l0a_app_{ctx.uid()}") as blk:
        @blk.scalar
        def _(sr):
            sc = ctx.s(sr)
            sc.activation(app[:, :], alog[:, :], AF.Exp)
        @blk.vector
        def _(vr):
            v = ctx.v(vr)
            v.wait("SS", ctx.val("SS"))
            v.tensor_scalar(app[:, :], app[:, :], -1.0, None, ALU.mult)

    delta_bf = ctx.sbuf("l0a_delta", [128, DC * T], BF16)
    dx_bf = ctx.sbuf("l0a_dx", [128, DC * T], BF16)
    bm_bf = ctx.sbuf("l0a_bm", [16, T], BF16)
    cm_bf = ctx.sbuf("l0a_cm", [16, T], BF16)
    ctx.push()
    xh = ctx.sbuf("l0a_xh", [128, DC * T], BF16)
    emit_ln(ctx, x, xh, consts, name="l0a_ln")

    dlow_bf = ctx.sbuf("l0a_dlow", [64, T], BF16)
    praw = delta_bf
    ctx.push()
    wdbc = load_weight_bf16(ctx, weights["L0_attn_wdbcT"].ap(), 1024, 96, "l0a_wdbc")

    emit_mm(ctx, lambda kc, mc: wdbc[:, kc * 96: kc * 96 + 64],
            lambda kc, tc: xh[:, kc * T + tc * NT: kc * T + tc * NT + NT],
            KC=8, MC=1, TCn=TC4,
            epilogue=lambda eng, ps, mc, tc: eng.tensor_scalar(
                dlow_bf[:, tc * NT:(tc + 1) * NT], ps,
                bdl[:, 0:1], None, ALU.add),
            name="l0a_dbc1", Mp=64)

    emit_mm(ctx, lambda kc, mc: wdbc[:, kc * 96 + 64: kc * 96 + 80],
            lambda kc, tc: xh[:, kc * T + tc * NT: kc * T + tc * NT + NT],
            KC=8, MC=1, TCn=TC4,
            epilogue=lambda eng, ps, mc, tc: eng.tensor_scalar(
                bm_bf[:, tc * NT:(tc + 1) * NT], ps,
                bbm[:, 0:1], None, ALU.add),
            name="l0a_dbc2", Mp=16)

    emit_mm(ctx, lambda kc, mc: wdbc[:, kc * 96 + 80: kc * 96 + 96],
            lambda kc, tc: xh[:, kc * T + tc * NT: kc * T + tc * NT + NT],
            KC=8, MC=1, TCn=TC4,
            epilogue=lambda eng, ps, mc, tc: eng.tensor_scalar(
                cm_bf[:, tc * NT:(tc + 1) * NT], ps,
                bcm[:, 0:1], None, ALU.add),
            name="l0a_dbc3", Mp=16)
    ctx.pop()

    ctx.push()
    wup = ctx.sbuf("l0a_wup", [64, 1024], BF16)
    ctx.push()
    wup_f = load_raw(ctx, weights["L0_attn_wupT"].ap(), 64, 1024, "l0a_wupf")
    with nc.Block(f"l0a_wupc_{ctx.uid()}") as blk:
        @blk.vector
        def _(vr):
            ctx.v(vr).tensor_copy(wup[:, :], wup_f[:, :])
    ctx.pop()
    emit_mm(ctx, lambda kc, mc: wup[:, mc * 128:(mc + 1) * 128],
            lambda kc, tc: dlow_bf[:, tc * NT:(tc + 1) * NT],
            KC=1, MC=8, TCn=TC4,
            epilogue=lambda eng, ps, mc, tc: eng.tensor_scalar(
                praw[:, mc * T + tc * NT: mc * T + tc * NT + NT], ps,
                bup[:, mc:mc + 1], None, ALU.add),
            name="l0a_dup")
    ctx.pop()

    ctx.push()
    spt = ctx.sbuf("l0a_spt", [128, NT])
    spe = ctx.sbuf("l0a_spe", [128, NT])
    spl = ctx.sbuf("l0a_spl", [128, NT])
    spr = ctx.sbuf("l0a_spr", [128, NT])
    for i in range(DC * TC4):
        mc, tc = divmod(i, TC4)
        sl = slice(mc * T + tc * NT, mc * T + tc * NT + NT)
        with nc.Block(f"l0a_sp_{i}_{ctx.uid()}") as blk:
            sv0 = ctx.val("SV")
            ss0 = ctx.val("SS")

            @blk.vector
            def _(vr):
                v = ctx.v(vr)
                v.tensor_scalar(spr[:, :], praw[:, sl], 0.0, None, ALU.max)
                v.scalar_tensor_tensor(spt[:, :], spr[:, :], 2.0, praw[:, sl],
                                       ALU.mult, ALU.subtract)
                v.wait("SS", ss0 + 2)
                v.tensor_tensor(delta_bf[:, sl], spr[:, :], spl[:, :], ALU.add)
                v.tensor_tensor(dx_bf[:, sl], delta_bf[:, sl], xh[:, sl], ALU.mult)

            @blk.scalar
            def _(sr):
                sc = ctx.s(sr)
                sc.wait("SV", sv0 + 2)
                sc.activation(spe[:, :], spt[:, :], AF.Exp, scale=-1.0)
                sc.activation(spl[:, :], spe[:, :], AF.Ln, bias=consts["onescol"][:, :])
    ctx.pop()
    ctx.pop()

    bmpp = ctx.sbuf("l0a_bmpp", [128, T])
    cmpp = ctx.sbuf("l0a_cmpp", [128, T])
    ctx.push()
    pspp = ctx.psum("l0a_pspp", [128, NT])
    with nc.Block(f"l0a_pp_{ctx.uid()}") as blk:
        p0 = ctx.val("SP")
        sv0 = ctx.val("SV")

        @blk.vector
        def _(vr):
            v = ctx.v(vr)
            for i in range(8):
                v.wait("SP", p0 + i + 1)
                dst = (bmpp if i < 4 else cmpp)
                tc = i % 4
                v.tensor_copy(dst[:, tc * NT:(tc + 1) * NT], pspp[:, :])

        @blk.tensor
        def _(tr):
            te = ctx.t(tr)
            for i in range(8):
                src_t = (bm_bf if i < 4 else cm_bf)
                tc = i % 4
                if i >= 1:
                    te.wait("SV", sv0 + i)
                te.matmul(pspp[:, :], r2[:, :], src_t[:, tc * NT:(tc + 1) * NT],
                          start=True, stop=True)
    ctx.pop()

    ctx.push()
    atile2 = [ctx.sbuf("l0a_a0", [128, T]), ctx.sbuf("l0a_a1", [128, T])]
    btile2 = [ctx.sbuf("l0a_b0", [128, T]), ctx.sbuf("l0a_b1", [128, T])]
    htile = ctx.sbuf("l0a_h", [128, T], BF16)
    ptile2 = [ctx.sbuf("l0a_p0", [128, T], BF16), ctx.sbuf("l0a_p1", [128, T], BF16)]
    psdr2 = [ctx.psum(f"l0a_psdr{i}", [128, NT]) for i in range(2)]
    psdx2 = [ctx.psum(f"l0a_psdx{i}", [128, NT]) for i in range(2)]
    psy = [ctx.psum("l0a_psy", [128, NT]) for _ in range(4)]
    for dchunk in range(DC):
        with nc.Block(f"l0a_scan_{dchunk}_{ctx.uid()}") as blk:
            p0 = ctx.val("SP")
            sv0 = ctx.val("SV")
            ss0 = ctx.val("SS")

            def base(g):
                return 0 if g == 0 else 8 + 12 * (g - 1)

            def psdr_done(g, tc):
                return p0 + base(g) + 2 * tc + 1

            def psdx_done(g, tc):
                return p0 + base(g) + 2 * tc + 2

            def psy_done(g, tc):
                return p0 + (base(g + 1) + 8 + tc + 1 if g <= 14 else 188 + tc + 1)

            @blk.scalar
            def _(sr, dchunk=dchunk):
                sc = ctx.s(sr)
                for g in range(16):
                    G = dchunk * 16 + g
                    for tc in range(TC4):
                        sc.wait("SP", psdr_done(g, tc))
                        if g >= 2 and tc == 0:
                            sc.wait("SV", sv0 + 6 * g - 7)
                        sc.activation(atile2[g % 2][:, tc * NT:(tc + 1) * NT],
                                      psdr2[g % 2][:, :], AF.Exp,
                                      scale=app[:, G:G + 1])

            @blk.vector
            def _(vr):
                v = ctx.v(vr)
                for g in range(16):
                    for tc in range(TC4):
                        v.wait("SP", psdx_done(g, tc))
                        v.tensor_tensor(btile2[g % 2][:, tc * NT:(tc + 1) * NT],
                                        psdx2[g % 2][:, :],
                                        bmpp[:, tc * NT:(tc + 1) * NT], ALU.mult)
                    v.wait("SS", ss0 + 4 * g + 4)
                    v.tensor_tensor_scan(htile[:, :], atile2[g % 2][:, :],
                                         btile2[g % 2][:, :], 0.0, ALU.mult, ALU.add)
                    if g >= 2:
                        v.wait("SP", psy_done(g - 2, 3))
                    v.tensor_tensor(ptile2[g % 2][:, :], htile[:, :], cmpp[:, :],
                                    ALU.mult)

            @blk.tensor
            def _(tr):
                te = ctx.t(tr)

                def psdrx_mm(g):
                    for tc in range(TC4):
                        if tc >= 1:
                            te.wait("SS", ss0 + 4 * g + tc)
                        elif g >= 2:
                            te.wait("SS", ss0 + 4 * (g - 2) + 4)
                        te.matmul(psdr2[g % 2][:, :], rg[:, g * 128:(g + 1) * 128],
                                  delta_bf[:, dchunk * T + tc * NT: dchunk * T + tc * NT + NT],
                                  start=True, stop=True)
                        if tc >= 1:
                            te.wait("SV", sv0 + 6 * g + tc)
                        elif g >= 2:
                            te.wait("SV", sv0 + 6 * (g - 2) + 4)
                        te.matmul(psdx2[g % 2][:, :], rg[:, g * 128:(g + 1) * 128],
                                  dx_bf[:, dchunk * T + tc * NT: dchunk * T + tc * NT + NT],
                                  start=True, stop=True)

                def psy_mm(g):
                    te.wait("SV", sv0 + 6 * g + 6)
                    for tc in range(TC4):
                        te.matmul(psy[tc][:, :], bd[:, g * 128:(g + 1) * 128],
                                  ptile2[g % 2][:, tc * NT:(tc + 1) * NT],
                                  start=(g == 0), stop=(g == 15))

                psdrx_mm(0)
                for g in range(1, 16):
                    psdrx_mm(g)
                    psy_mm(g - 1)
                psy_mm(15)

        with nc.Block(f"l0a_y_{dchunk}_{ctx.uid()}") as blk:
            @blk.vector
            def _(vr):
                v = ctx.v(vr)
                for tc in range(TC4):
                    xs = x[:, dchunk * T + tc * NT: dchunk * T + tc * NT + NT]
                    v.tensor_tensor(xs, xs, psy[tc][:, :], ALU.add)
    ctx.pop()
    ctx.pop()


def emit_l2_attn(ctx, x, weights, consts):
    nc = ctx.nc
    DIC = 16
    PW = T + 3
    ctx.push()
    zs_dram = nc.dram_tensor(f"l2a_zsd_{ctx.uid()}", [DIC * 128, T], BF16,
                             kind="Internal")
    xsp_dram = nc.dram_tensor(f"l2a_xspd_{ctx.uid()}", [DIC * 128, PW], BF16,
                              kind="Internal")
    xs_dram = nc.dram_tensor(f"l2a_xsd_{ctx.uid()}", [DIC * 128, T], BF16,
                             kind="Internal")
    y1_dram = nc.dram_tensor(f"l2a_y1d_{ctx.uid()}", [DIC * 128, T], BF16,
                             kind="Internal")

    convw = load_raw(ctx, weights["L2_attn_convwpp"].ap(), 128, 64, "l2a_cw")
    convb = load_raw(ctx, weights["L2_attn_convbc"].ap(), 128, 16, "l2a_cb")
    bdt = load_raw(ctx, weights["L2_attn_bdtc"].ap(), 128, 16, "l2a_bdt")
    dcol = load_raw(ctx, weights["L2_attn_dcol"].ap(), 128, 16, "l2a_dcol")
    identb = consts["identb"]
    bs_bf = ctx.sbuf("l2a_bs", [16, T], BF16)

    ctx.push()
    xh = ctx.sbuf("l2a_xh", [128, DC * T], BF16)
    emit_ln(ctx, x, xh, consts, name="l2a_ln")

    ctx.push()
    zs = ctx.sbuf("l2a_zsil", [128, DIC * T], BF16)
    ctx.push()
    wz = load_weight_bf16(ctx, weights["L2_attn_winz"].ap(), 1024, 2048, "l2a_wz", nstg=1)
    emit_mm(ctx,
            lambda kc, mc: wz[:, kc * 2048 + mc * 128: kc * 2048 + (mc + 1) * 128],
            lambda kc, tc: xh[:, kc * T + tc * NT: kc * T + tc * NT + NT],
            KC=8, MC=DIC, TCn=TC4,
            epilogue=lambda eng, ps, mc, tc: eng.activation(
                zs[:, mc * T + tc * NT: mc * T + tc * NT + NT], ps, AF.Silu),
            name="l2a_zmm", epi_engine="scalar")
    ctx.pop()
    emit_spill(ctx, zs, zs_dram.ap(), DIC, T, "l2a_zs")
    ctx.pop()

    ctx.push()
    wxi = load_weight_bf16(ctx, weights["L2_attn_winx"].ap(), 1024, 2048, "l2a_wxi", nstg=1)
    xpb = ctx.sbuf("l2a_xpb", [128, PW], BF16)
    for mc in range(DIC):
        with nc.Block(f"l2a_xpad_{mc}_{ctx.uid()}") as blk:
            @blk.vector
            def _(vr):
                ctx.v(vr).memset(xpb[:, 0:3], 0.0)
        emit_mm(ctx,
                lambda kc, _mc, mc=mc: wxi[:, kc * 2048 + mc * 128: kc * 2048 + (mc + 1) * 128],
                lambda kc, tc: xh[:, kc * T + tc * NT: kc * T + tc * NT + NT],
                KC=8, MC=1, TCn=TC4,
                epilogue=lambda eng, ps, _mc, tc: eng.tensor_copy(
                    xpb[:, 3 + tc * NT: 3 + tc * NT + NT], ps),
                name=f"l2a_xmm{mc}")
        emit_spill(ctx, xpb, xsp_dram.ap()[mc * 128:(mc + 1) * 128, :], 1, PW,
                   f"l2a_xsp{mc}")
    ctx.pop()
    ctx.pop()

    dt_dram = nc.dram_tensor(f"l2a_dtd_{ctx.uid()}", [DIC * 128, T], BF16,
                             kind="Internal")
    ctx.push()
    xs = ctx.sbuf("l2a_xs", [128, DIC * T], BF16)
    xpb2 = ctx.sbuf("l2a_xpb2", [128, PW], BF16)
    diag4 = ctx.sbuf("l2a_diag4", [128, 4 * 128], BF16)
    psc = ctx.psum("l2a_psc", [128, NT])
    for mc in range(DIC):
        emit_restore(ctx, xpb2, xsp_dram.ap()[mc * 128:(mc + 1) * 128, :], 1, PW,
                     f"l2a_xrs{mc}")
        with nc.Block(f"l2a_conv_{mc}_{ctx.uid()}") as blk:
            p0 = ctx.val("SP")
            sv0 = ctx.val("SV")
            ss0 = ctx.val("SS")

            @blk.vector
            def _(vr):
                v = ctx.v(vr)
                for k in range(4):
                    v.tensor_scalar(diag4[:, k * 128:(k + 1) * 128], identb[:, :],
                                    convw[:, mc * 4 + k: mc * 4 + k + 1],
                                    None, ALU.mult)

            @blk.scalar
            def _(sr):
                sc = ctx.s(sr)
                for tc in range(TC4):
                    sc.wait("SP", p0 + 4 * (tc + 1))
                    sc.activation(xs[:, mc * T + tc * NT: mc * T + tc * NT + NT],
                                  psc[:, :], AF.Silu, bias=convb[:, mc:mc + 1])

            @blk.tensor
            def _(tr):
                te = ctx.t(tr)
                te.wait("SV", sv0 + 4)
                for tc in range(TC4):
                    if tc >= 1:
                        te.wait("SS", ss0 + tc)
                    for k in range(4):
                        te.matmul(psc[:, :], diag4[:, k * 128:(k + 1) * 128],
                                  xpb2[:, k + tc * NT: k + tc * NT + NT],
                                  start=(k == 0), stop=(k == 3))

    for mq in range(4):
        ctx.push()
        wq = load_weight_bf16(ctx, weights[f"L2_attn_wdtq{mq}"].ap(), 2048, 512,
                              f"l2a_wdt{mq}")
        dtq = ctx.sbuf(f"l2a_dtq{mq}", [128, 4 * T], BF16)
        emit_mm(ctx,
                lambda kc, mc, mq=mq: wq[:, kc * 512 + mc * 128: kc * 512 + (mc + 1) * 128],
                lambda kc, tc: xs[:, kc * T + tc * NT: kc * T + tc * NT + NT],
                KC=DIC, MC=4, TCn=TC4,
                epilogue=lambda eng, ps, mc, tc, mq=mq: eng.tensor_scalar(
                    dtq[:, mc * T + tc * NT: mc * T + tc * NT + NT],
                    ps, bdt[:, mq * 4 + mc: mq * 4 + mc + 1], None, ALU.add),
                name=f"l2a_dtm{mq}")
        spt = ctx.sbuf(f"l2a_spt{mq}", [128, NT])
        spe = ctx.sbuf(f"l2a_spe{mq}", [128, NT])
        spl = ctx.sbuf(f"l2a_spl{mq}", [128, NT])
        spr = ctx.sbuf(f"l2a_spr{mq}", [128, NT])
        for i in range(4 * TC4):
            mc, tc = divmod(i, TC4)
            sl = slice(mc * T + tc * NT, mc * T + tc * NT + NT)
            with nc.Block(f"l2a_sp_{mq}_{i}_{ctx.uid()}") as blk:
                sv0 = ctx.val("SV")
                ss0 = ctx.val("SS")

                @blk.vector
                def _(vr):
                    v = ctx.v(vr)
                    v.tensor_scalar(spr[:, :], dtq[:, sl], 0.0, None, ALU.max)
                    v.scalar_tensor_tensor(spt[:, :], spr[:, :], 2.0, dtq[:, sl],
                                           ALU.mult, ALU.subtract)
                    v.wait("SS", ss0 + 2)
                    v.tensor_tensor(dtq[:, sl], spr[:, :], spl[:, :], ALU.add)

                @blk.scalar
                def _(sr):
                    sc = ctx.s(sr)
                    sc.wait("SV", sv0 + 2)
                    sc.activation(spe[:, :], spt[:, :], AF.Exp, scale=-1.0)
                    sc.activation(spl[:, :], spe[:, :], AF.Ln,
                                  bias=consts["onescol"][:, :])
        emit_spill(ctx, dtq, dt_dram.ap()[mq * 512:(mq + 1) * 512, :], 4, T,
                   f"l2a_dtsp{mq}")
        ctx.pop()

    ctx.push()
    wx = load_weight_bf16(ctx, weights["L2_attn_wx"].ap(), 2048, 16, "l2a_wxs")
    emit_mm(ctx, lambda kc, mc: wx[:, kc * 16:(kc + 1) * 16],
            lambda kc, tc: xs[:, kc * T + tc * NT: kc * T + tc * NT + NT],
            KC=DIC, MC=1, TCn=TC4,
            epilogue=lambda eng, ps, mc, tc: eng.tensor_copy(
                bs_bf[:, tc * NT:(tc + 1) * NT], ps),
            name="l2a_bsm", Mp=16)
    ctx.pop()
    emit_spill(ctx, xs, xs_dram.ap(), DIC, T, "l2a_xs")
    ctx.pop()

    ctx.push()
    rg = load_raw(ctx, weights["L0_rg"].ap(), 128, 16 * 128, "l2a_rg", BF16)
    bd = load_raw(ctx, weights["L0_bd"].ap(), 128, 16 * 128, "l2a_bd", BF16)
    r2 = load_raw(ctx, weights["L0_r2"].ap(), 16, 128, "l2a_r2", BF16)
    alog = load_raw(ctx, weights["L2_attn_alogpp"].ap(), 128, 256, "l2a_al")
    app = ctx.sbuf("l2a_app", [128, 256])
    with nc.Block(f"l2a_app_{ctx.uid()}") as blk:
        @blk.scalar
        def _(sr):
            ctx.s(sr).activation(app[:, :], alog[:, :], AF.Exp)
        @blk.vector
        def _(vr):
            v = ctx.v(vr)
            v.wait("SS", ctx.val("SS"))
            v.tensor_scalar(app[:, :], app[:, :], -1.0, None, ALU.mult)

    bspp = ctx.sbuf("l2a_bspp", [128, T])
    ctx.push()
    pspp = ctx.psum("l2a_pspp", [128, NT])
    with nc.Block(f"l2a_pp_{ctx.uid()}") as blk:
        p0 = ctx.val("SP")
        sv0 = ctx.val("SV")

        @blk.vector
        def _(vr):
            v = ctx.v(vr)
            for i in range(TC4):
                v.wait("SP", p0 + i + 1)
                v.tensor_copy(bspp[:, i * NT:(i + 1) * NT], pspp[:, :])

        @blk.tensor
        def _(tr):
            te = ctx.t(tr)
            for i in range(TC4):
                if i >= 1:
                    te.wait("SV", sv0 + i)
                te.matmul(pspp[:, :], r2[:, :], bs_bf[:, i * NT:(i + 1) * NT],
                          start=True, stop=True)
    ctx.pop()

    atile2 = [ctx.sbuf("l2a_a0", [128, T]), ctx.sbuf("l2a_a1", [128, T])]
    htile2 = [ctx.sbuf("l2a_h0", [128, T], BF16), ctx.sbuf("l2a_h1", [128, T], BF16)]
    y1c = ctx.sbuf("l2a_y1c", [128, T], BF16)
    dtc = ctx.sbuf("l2a_dtc", [128, T], BF16)
    psdr4 = [ctx.psum(f"l2a_psdr{i}", [128, NT]) for i in range(4)]
    psy = [ctx.psum("l2a_psy", [128, NT]) for _ in range(4)]
    for dchunk in range(DIC):
        emit_restore(ctx, dtc, dt_dram.ap()[dchunk * 128:(dchunk + 1) * 128, :],
                     1, T, f"l2a_dtr{dchunk}")
        with nc.Block(f"l2a_scan_{dchunk}_{ctx.uid()}") as blk:
            p0 = ctx.val("SP")
            sv0 = ctx.val("SV")
            ss0 = ctx.val("SS")

            def psdr_done(g, tc):
                return p0 + (tc + 1 if g == 0 else 8 * g - 4 + tc + 1)

            def psy_done(g, tc):
                return p0 + (8 * (g + 1) + tc + 1 if g <= 14 else 124 + tc + 1)

            @blk.scalar
            def _(sr, dchunk=dchunk):
                sc = ctx.s(sr)
                for g in range(16):
                    G = dchunk * 16 + g
                    for tc in range(TC4):
                        sc.wait("SP", psdr_done(g, tc))
                        if g >= 2 and tc == 0:
                            sc.wait("SV", sv0 + g - 1)
                        sc.activation(atile2[g % 2][:, tc * NT:(tc + 1) * NT],
                                      psdr4[2 * (g % 2) + (tc % 2)][:, :],
                                      AF.Exp, scale=app[:, G:G + 1])

            @blk.vector
            def _(vr):
                v = ctx.v(vr)
                for g in range(16):
                    v.wait("SS", ss0 + 4 * g + 4)
                    if g >= 2:
                        v.wait("SP", psy_done(g - 2, 3))
                    v.tensor_tensor_scan(htile2[g % 2][:, :], atile2[g % 2][:, :],
                                         bspp[:, :], 0.0, ALU.mult, ALU.add)

            @blk.tensor
            def _(tr):
                te = ctx.t(tr)

                def psdr_mm(g):
                    for tc in range(TC4):
                        if tc >= 2:
                            te.wait("SS", ss0 + 4 * g + tc - 1)
                        elif g >= 2:
                            te.wait("SS", ss0 + 4 * (g - 2) + tc + 3)
                        te.matmul(psdr4[2 * (g % 2) + (tc % 2)][:, :],
                                  rg[:, g * 128:(g + 1) * 128],
                                  dtc[:, tc * NT: tc * NT + NT],
                                  start=True, stop=True)

                def psy_mm(g):
                    te.wait("SV", sv0 + g + 1)
                    for tc in range(TC4):
                        te.matmul(psy[tc][:, :], bd[:, g * 128:(g + 1) * 128],
                                  htile2[g % 2][:, tc * NT:(tc + 1) * NT],
                                  start=(g == 0), stop=(g == 15))

                psdr_mm(0)
                for g in range(1, 16):
                    psdr_mm(g)
                    psy_mm(g - 1)
                psy_mm(15)

        with nc.Block(f"l2a_y_{dchunk}_{ctx.uid()}") as blk:
            @blk.vector
            def _(vr):
                v = ctx.v(vr)
                for tc in range(TC4):
                    v.tensor_copy(y1c[:, tc * NT:(tc + 1) * NT], psy[tc][:, :])
        emit_spill(ctx, y1c, y1_dram.ap()[dchunk * 128:(dchunk + 1) * 128, :], 1, T,
                   f"l2a_y1{dchunk}")
    ctx.pop()

    ctx.push()
    wout = load_weight_bf16(ctx, weights["L2_attn_wout"].ap(), 2048, 1024, "l2a_wo")
    y2 = ctx.sbuf("l2a_y2", [128, DIC * T], BF16)
    y1b = ctx.sbuf("l2a_y1b", [128, T], BF16)
    xsb = ctx.sbuf("l2a_xsb", [128, T], BF16)
    zsb = ctx.sbuf("l2a_zsb", [128, T], BF16)
    for c in range(DIC):
        emit_restore(ctx, y1b, y1_dram.ap()[c * 128:(c + 1) * 128, :], 1, T,
                     f"l2a_ry1{c}")
        emit_restore(ctx, xsb, xs_dram.ap()[c * 128:(c + 1) * 128, :], 1, T,
                     f"l2a_rxs{c}")
        emit_restore(ctx, zsb, zs_dram.ap()[c * 128:(c + 1) * 128, :], 1, T,
                     f"l2a_rzs{c}")
        with nc.Block(f"l2a_y2_{c}_{ctx.uid()}") as blk:
            @blk.vector
            def _(vr):
                v = ctx.v(vr)
                v.scalar_tensor_tensor(y2[:, c * T:(c + 1) * T], xsb[:, :],
                                       dcol[:, c:c + 1], y1b[:, :],
                                       ALU.mult, ALU.add)
                v.tensor_tensor(y2[:, c * T:(c + 1) * T], y2[:, c * T:(c + 1) * T],
                                zsb[:, :], ALU.mult)

    emit_mm(ctx,
            lambda kc, mc: wout[:, kc * 1024 + mc * 128: kc * 1024 + (mc + 1) * 128],
            lambda kc, tc: y2[:, kc * T + tc * NT: kc * T + tc * NT + NT],
            KC=DIC, MC=DC, TCn=TC4,
            epilogue=lambda eng, ps, mc, tc: eng.tensor_tensor(
                x[:, mc * T + tc * NT: mc * T + tc * NT + NT],
                x[:, mc * T + tc * NT: mc * T + tc * NT + NT], ps, ALU.add),
            name="l2a_womm")
    ctx.pop()
    ctx.pop()


def _consts_host():
    c = {}
    c["onescol"] = np.ones((128, 1), np.float32)
    c["onesrow"] = np.ones((1, 128), np.float32)
    c["identf"] = np.eye(128, dtype=np.float32)
    c["identb"] = np.eye(128, dtype=np.float32).astype(bf16)
    c["onescolb"] = np.ones((128, 1), np.float32).astype(bf16)
    esel = np.zeros((8, 8 * 128), np.float32)
    for e in range(8):
        esel[e, e * 128:(e + 1) * 128] = 1.0
    c["esel"] = esel
    return c


def _prep_weights_host(inputs):
    w = {}
    f32 = np.float32

    def put(key, arr):
        w[key] = np.ascontiguousarray(np.asarray(arr, f32))

    put("L1_ffn_w1", inputs["L1_ffn_w1"])
    put("L1_ffn_w2", inputs["L1_ffn_w2"])
    put("L1_ffn_b1c", np.asarray(inputs["L1_ffn_b1"], f32).reshape(32, 128).T)
    put("L1_ffn_b2c", np.asarray(inputs["L1_ffn_b2"], f32).reshape(8, 128).T)
    for k in ["L2_ffn_Wq", "L2_ffn_Wk", "L2_ffn_Wv", "L2_ffn_Wo", "L2_ffn_Wff"]:
        put(k, inputs[k])
    put("L2_attn_winx", np.asarray(inputs["L2_attn_W_in"], f32)[:, :2048])
    put("L2_attn_winz", np.asarray(inputs["L2_attn_W_in"], f32)[:, 2048:])
    _cw = np.asarray(inputs["L2_attn_conv_w"], f32)[:, 0, :]
    put("L2_attn_convwpp", _cw.reshape(16, 128, 4).transpose(1, 0, 2).reshape(128, 64))
    put("L2_attn_convbc", np.asarray(inputs["L2_attn_conv_b"], f32).reshape(16, 128).T)
    _wdt = np.asarray(inputs["L2_attn_W_dt"], f32)
    for _q in range(4):
        put(f"L2_attn_wdtq{_q}", _wdt[:, _q * 512:(_q + 1) * 512])
    put("L2_attn_bdtc", np.asarray(inputs["L2_attn_b_dt"], f32).reshape(16, 128).T)
    put("L2_attn_wx", inputs["L2_attn_W_x"])
    put("L2_attn_alogpp", np.asarray(inputs["L2_attn_A_log"], f32).reshape(256, 128).T)
    put("L2_attn_dcol", np.asarray(inputs["L2_attn_D"], f32).reshape(16, 128).T)
    put("L2_attn_wout", inputs["L2_attn_W_out"])
    put("L0_attn_wdbcT", np.asarray(inputs["L0_attn_W_dbc"], f32).T)
    put("L0_attn_wupT", np.asarray(inputs["L0_attn_W_up"], f32).T)
    _bdbc = np.asarray(inputs["L0_attn_b_dbc"], f32).reshape(96, 1)
    put("L0_attn_bdlc", _bdbc[0:64])
    put("L0_attn_bbmc", _bdbc[64:80])
    put("L0_attn_bcmc", _bdbc[80:96])
    put("L0_attn_bupc", np.asarray(inputs["L0_attn_b_up"], f32).reshape(8, 128).T)
    put("L0_attn_alogpp", np.asarray(inputs["L0_attn_A_log"], f32).reshape(128, 128).T)
    rgc = np.zeros((128, 16 * 128), f32)
    bdc = np.zeros((128, 16 * 128), f32)
    for g in range(16):
        for p in range(128):
            rgc[g * 8 + p // 16, g * 128 + p] = 1.0
            bdc[p, g * 128 + g * 8 + p // 16] = 1.0
    w["L0_rg"] = rgc.astype(bf16)
    w["L0_bd"] = bdc.astype(bf16)
    r2c = np.zeros((16, 128), f32)
    for p in range(128):
        r2c[p % 16, p] = 1.0
    w["L0_r2"] = r2c.astype(bf16)
    put("L0_ffn_ctlR", np.asarray(inputs["L0_ffn_controller"], f32).reshape(1024, 64))
    put("L0_ffn_f1R", np.asarray(inputs["L0_ffn_f1"], f32).reshape(1024, 4096))
    put("L0_ffn_f2R", np.asarray(inputs["L0_ffn_f2"], f32).reshape(4096, 1024))
    put("L0_ffn_besf", np.asarray(inputs["L0_ffn_bias"], f32).reshape(32, 128).T)
    tie = np.linspace(0.0, 1e-6, 16).astype(f32)
    w["L0_ffn_tieT"] = np.ascontiguousarray(
        np.tile(tie[None, :], (64, T // 16)).astype(f32))
    rp = np.zeros((64, 32 * 128), f32)
    for ch in range(32):
        for p in range(128):
            rp[2 * ch + p // 64, ch * 128 + p] = 1.0
    w["L0_ffn_rp"] = rp.astype(bf16)
    put("L1_attn_wqkvT", np.asarray(inputs["L1_attn_w_qkv"], f32).T)
    put("L1_attn_wprojT", np.asarray(inputs["L1_attn_w_proj"], f32).T)
    put("L1_attn_bpc", np.asarray(inputs["L1_attn_b_proj"], f32).reshape(8, 128).T)
    w["L1_attn_ln4096"] = np.full((1, 1), np.log(4096.0), f32)
    put("L3_attn_wqkvT", np.asarray(inputs["L3_attn_w_qkv"], f32).T)
    put("L3_attn_bqc", np.asarray(inputs["L3_attn_b_qkv"], f32)[:2048].reshape(16, 128).T)
    put("L3_attn_bvrow", np.asarray(inputs["L3_attn_b_qkv"], f32)[2048:].reshape(1, 1024))
    put("L3_attn_woutT", np.asarray(inputs["L3_attn_w_out"], f32).T)
    put("L3_attn_boc", np.asarray(inputs["L3_attn_b_out"], f32).reshape(8, 128).T)
    masks = np.full((128, 3 * 384), -1e30, f32)
    for dlt in range(3):
        ii = np.arange(128)[:, None] + dlt * 128
        jj = np.arange(384)[None, :]
        masks[:, dlt * 384:(dlt + 1) * 384] = np.where(np.abs(ii - jj) <= 8, 0.0, -1e30)
    w["L3_attn_masks"] = masks
    put("L3_ffn_router", inputs["L3_ffn_router"])
    put("L3_ffn_wg", np.asarray(inputs["L3_ffn_experts_gate"], f32).transpose(1, 0, 2).reshape(1024, 4096))
    put("L3_ffn_wu", np.asarray(inputs["L3_ffn_experts_up"], f32).transpose(1, 0, 2).reshape(1024, 4096))
    put("L3_ffn_wd", np.asarray(inputs["L3_ffn_experts_down"], f32).reshape(4096, 1024))
    put("L3_ffn_shared_gate", inputs["L3_ffn_shared_gate"])
    put("L3_ffn_shared_up", inputs["L3_ffn_shared_up"])
    put("L3_ffn_shared_down", inputs["L3_ffn_shared_down"])
    return w


L1F_KEYS = ["L1_ffn_w1", "L1_ffn_w2", "L1_ffn_b1c", "L1_ffn_b2c"]
L2F_KEYS = ["L2_ffn_Wq", "L2_ffn_Wk", "L2_ffn_Wv", "L2_ffn_Wo", "L2_ffn_Wff"]
L2A_KEYS = ["L2_attn_winx", "L2_attn_winz", "L2_attn_convwpp", "L2_attn_convbc",
            "L2_attn_wdtq0", "L2_attn_wdtq1", "L2_attn_wdtq2", "L2_attn_wdtq3",
            "L2_attn_bdtc", "L2_attn_wx", "L2_attn_alogpp", "L2_attn_dcol",
            "L2_attn_wout", "L0_rg", "L0_bd", "L0_r2"]
L0A_KEYS = ["L0_attn_wdbcT", "L0_attn_wupT", "L0_attn_bdlc", "L0_attn_bbmc",
            "L0_attn_bcmc", "L0_attn_bupc", "L0_attn_alogpp",
            "L0_rg", "L0_bd", "L0_r2"]
L0F_KEYS = ["L0_ffn_ctlR", "L0_ffn_f1R", "L0_ffn_f2R", "L0_ffn_besf",
            "L0_ffn_tieT", "L0_ffn_rp"]
L1A_KEYS = ["L1_attn_wqkvT", "L1_attn_wprojT", "L1_attn_bpc", "L1_attn_ln4096"]
L3A_KEYS = ["L3_attn_wqkvT", "L3_attn_bqc", "L3_attn_bvrow", "L3_attn_woutT",
            "L3_attn_boc", "L3_attn_masks"]
L3F_KEYS = ["L3_ffn_router", "L3_ffn_wg", "L3_ffn_wu", "L3_ffn_wd",
            "L3_ffn_shared_gate", "L3_ffn_shared_up", "L3_ffn_shared_down"]

STAGE_CONSTS = {
    "L1F": ["onescol", "onesrow"],
    "L2F": ["onescol", "onesrow", "onescolb"],
    "L3A": ["onescol", "onesrow", "identb", "onescolb"],
    "L1A": ["onescol", "onesrow", "onescolb"],
    "L3F": ["onescol", "onesrow", "identf", "onescolb", "esel"],
    "L0A": ["onescol", "onesrow", "onescolb"],
    "L0F": ["onescol", "onesrow", "onescolb"],
    "L2A": ["onescol", "onesrow", "onescolb", "identb"],
}

STAGES = [
    ("L0A", emit_l0_attn, L0A_KEYS),
    ("L0F", emit_l0_ffn, L0F_KEYS),
    ("L1A", emit_l1_attn, L1A_KEYS),
    ("L1F", emit_l1_ffn, L1F_KEYS),
    ("L2A", emit_l2_attn, L2A_KEYS),
    ("L2F", emit_l2_ffn, L2F_KEYS),
    ("L3A", emit_l3_attn, L3A_KEYS),
    ("L3F", emit_l3_ffn, L3F_KEYS),
]

_NPDT = {np.float32: F32, np.int32: I32}


def build_stage_program(emit_fn, wkeys, w, consts, stage_name=None):
    if stage_name in STAGE_CONSTS:
        consts = {k: consts[k] for k in STAGE_CONSTS[stage_name]}
    nc = bass.Bass("TRN2", target_bir_lowering=False)
    ctx = Ctx(nc)
    xin = nc.dram_tensor("xin", [D, T], F32, kind="ExternalInput")
    xout = nc.dram_tensor("xout", [D, T], F32, kind="ExternalOutput")
    weights = {k: nc.dram_tensor(k, list(w[k].shape),
                                 _NPDT.get(w[k].dtype.type, BF16), kind="ExternalInput")
               for k in wkeys}
    cdram = {k: nc.dram_tensor("c_" + k, list(consts[k].shape),
                               _NPDT.get(consts[k].dtype.type, BF16), kind="ExternalInput")
             for k in consts}

    x = ctx.sbuf("xres", [128, DC * T], F32)
    emit_restore(ctx, x, xin.ap(), DC, T, "ldx")
    cs = {k: load_raw(ctx, cdram[k].ap(), consts[k].shape[0], consts[k].shape[1],
                      "c_" + k, _NPDT.get(consts[k].dtype.type, BF16)) for k in consts}

    emit_fn(ctx, x, weights, cs)

    emit_spill(ctx, x, xout.ap(), DC, T, "stx")
    ctx.close()
    mybir.codegen_inst_isa_subclasses(nc)
    return nc


FUSE = True


def build_fused_program(w, consts, allkeys, callkeys):
    nc = bass.Bass("TRN2", target_bir_lowering=False)
    ctx = Ctx(nc)
    xin = nc.dram_tensor("xin", [D, T], F32, kind="ExternalInput")
    xout = nc.dram_tensor("xout", [D, T], F32, kind="ExternalOutput")
    weights = {k: nc.dram_tensor(k, list(w[k].shape),
                                 _NPDT.get(w[k].dtype.type, BF16), kind="ExternalInput")
               for k in allkeys}
    cdram = {k: nc.dram_tensor("c_" + k, list(consts[k].shape),
                               _NPDT.get(consts[k].dtype.type, BF16),
                               kind="ExternalInput")
             for k in callkeys}

    x = ctx.sbuf("xres", [128, DC * T], F32)
    emit_restore(ctx, x, xin.ap(), DC, T, "ldx")
    common = [k for k in ("onescol", "onesrow", "onescolb") if k in callkeys]
    cs = {k: load_raw(ctx, cdram[k].ap(), consts[k].shape[0], consts[k].shape[1],
                      "c_" + k, _NPDT.get(consts[k].dtype.type, BF16))
          for k in common}

    for name, emit_fn, wkeys in STAGES:
        extra = [k for k in STAGE_CONSTS[name] if k not in common]
        ctx.push()
        cse = dict(cs)
        for k in extra:
            cse[k] = load_raw(ctx, cdram[k].ap(), consts[k].shape[0],
                              consts[k].shape[1], "c_" + k,
                              _NPDT.get(consts[k].dtype.type, BF16))
        emit_fn(ctx, x, weights, cse)
        ctx.pop()

    emit_spill(ctx, x, xout.ap(), DC, T, "stx")
    ctx.close()
    mybir.codegen_inst_isa_subclasses(nc)
    return nc


def kernel(**inputs):
    xfull = np.asarray(inputs["L0_attn_x"], np.float32)
    consts = _consts_host()
    w = _prep_weights_host(inputs)
    xs = [np.ascontiguousarray(xfull[b].T) for b in range(NCORES)]

    if FUSE:
        allkeys, callkeys = [], []
        for name, _, wkeys in STAGES:
            for k in wkeys:
                if k not in allkeys:
                    allkeys.append(k)
            for k in STAGE_CONSTS[name]:
                if k not in callkeys:
                    callkeys.append(k)
        nc = build_fused_program(w, consts, allkeys, callkeys)
        in_maps = []
        for b in range(NCORES):
            m = {"xin": xs[b]}
            for k in allkeys:
                m[k] = w[k]
            for k in callkeys:
                m["c_" + k] = consts[k]
            in_maps.append(m)
        res = run_bass_kernel_spmd(nc, in_maps, core_ids=list(range(NCORES)))
        xs = [np.asarray(res.results[b]["xout"]) for b in range(NCORES)]
        out = np.stack([x.T for x in xs]).astype(np.float32)
        return np.ascontiguousarray(out)

    for name, emit_fn, wkeys in STAGES:
        cuse = {k: consts[k] for k in STAGE_CONSTS.get(name, list(consts))}
        nc = build_stage_program(emit_fn, wkeys, w, consts, stage_name=name)
        in_maps = []
        for b in range(NCORES):
            m = {"xin": xs[b]}
            for k in wkeys:
                m[k] = w[k]
            for k in cuse:
                m["c_" + k] = cuse[k]
            in_maps.append(m)
        res = run_bass_kernel_spmd(nc, in_maps, core_ids=list(range(NCORES)))
        xs = [np.asarray(res.results[b]["xout"]) for b in range(NCORES)]

    out = np.stack([x.T for x in xs]).astype(np.float32)
    return np.ascontiguousarray(out)


# version 05
# speedup vs baseline: 1.2318x; 1.1054x over previous
import numpy as np
import ml_dtypes as mld
from contextlib import ExitStack

import concourse.bass as bass
import concourse.mybir as mybir
from concourse.bass_utils import run_bass_kernel_spmd
from concourse.dve_spec import Spec, Src0, Src1, Zero, maxx, minn, select, lower
from concourse.dve_spec import C0, C1, C2
from concourse.dve_ops import DveOp
from concourse.dve_table_gen import dve_ver_for

F32 = mybir.dt.float32
BF16 = mybir.dt.bfloat16
I32 = mybir.dt.int32
AF = mybir.ActivationFunctionType
ALU = mybir.AluOpType
AX = mybir.AxisListType
bf16 = mld.bfloat16

T = 2048
D = 1024
DC = D // 128
NT = 512
TC4 = T // NT
NCORES = 8

def _register_custom_ops():
    import concourse.dve_ops as dvo
    ver = dve_ver_for("TRN2")

    def mk(name, spec, reference):
        for o in dvo.OPS:
            if o.name == name:
                return o
        sha = lower(spec, ver=ver, perf_en=False, name=name).sha(ver)
        op = DveOp(name, spec, subdim=False, uops_sha={ver: sha}, reference=reference)
        dvo.OPS.append(op)
        row = dvo._CUSTOM_DVE_ROW_BASE + len(dvo.OPS) - 1
        assert row <= 31, f"custom DVE row overflow: {row}"
        dvo._SUB_OPCODE_FOR_NAME[name] = row
        dvo.CUSTOM_DVE_REFS[name] = reference
        dvo.CUSTOM_DVE_SPECS[name] = spec
        return op

    _v = minn(maxx(Src0, C0), C1)
    sgn = mk(
        "FXTRUNC_SGN",
        Spec(body=select(_v > Zero, Src1 - (Src1 > _v), Src1 + (Src1 < _v))),
        lambda in0, in1, s0, s1, imm2: np.where(
            np.clip(in0, s0, s1) > 0,
            in1 - (in1 > np.clip(in0, s0, s1)),
            in1 + (in1 < np.clip(in0, s0, s1)),
        ),
    )
    pos = mk(
        "FXTRUNC_POS",
        Spec(body=(Src1 - (Src1 > Src0 * C0)) * C2),
        lambda in0, in1, s0, s1, imm2: (in1 - (in1 > in0 * s0)) * imm2,
    )
    return sgn, pos

FXTRUNC_SGN, FXTRUNC_POS = _register_custom_ops()

class Eng:

    _PASS = {"wait_ge", "then_inc", "sem_inc", "nop", "drain", "engine_nop"}

    def __init__(self, ctx, raw, key, serialize=True):
        self._ctx = ctx
        self._raw = raw
        self._key = key
        self._ser = serialize

    def wait(self, key, val):
        if val > 0:
            self._raw.wait_ge(self._ctx.sem(key), val)

    def raw(self):
        return self._raw

    def dma(self, dst, src, key="dml"):
        self._raw.dma_start(dst, src).then_inc(self._ctx.sem(key), 16)
        return self._ctx.bump(key, 16)

    def matmul(self, out, lhsT, rhs, start=True, stop=True, **kw):
        ins = self._raw.matmul(out, lhsT, rhs, start=start, stop=stop, **kw)
        ins.then_inc(self._ctx.sem(self._key), 1)
        self._ctx.bump(self._key, 1)
        return ins

    def custom(self, op, **kw):
        return self._wrap(self._raw._custom_dve)(op, **kw)

    def _wrap(self, raw_fn):
        ctx = self._ctx
        key = self._key

        def call(*a, **k):
            sem = ctx.sem(key)
            if self._ser and ctx.cnt[key] > 0:
                self._raw.wait_ge(sem, ctx.cnt[key])
            ins = raw_fn(*a, **k)
            ins.then_inc(sem, 1)
            ctx.bump(key, 1)
            return ins

        return call

    def __getattr__(self, name):
        raw_fn = getattr(self._raw, name)
        if name in Eng._PASS or name.startswith("dma") or name.startswith("reg"):
            return raw_fn
        return self._wrap(raw_fn)


class Ctx:
    def __init__(self, nc):
        self.nc = nc
        self.scopes = [ExitStack()]
        self._sems = {}
        self.cnt = {}
        self._uid = 0

    def uid(self):
        self._uid += 1
        return self._uid

    def sem(self, name):
        if name not in self._sems:
            self._sems[name] = self.scopes[0].enter_context(self.nc.semaphore(name))
            self.cnt[name] = 0
        return self._sems[name]

    def bump(self, name, k=1):
        self.sem(name)
        self.cnt[name] += k
        return self.cnt[name]

    def val(self, name):
        self.sem(name)
        return self.cnt[name]

    def v(self, raw):
        return Eng(self, raw, "SV")

    def s(self, raw):
        return Eng(self, raw, "SS")

    def q(self, raw):
        return Eng(self, raw, "SQ")

    def t(self, raw):
        return Eng(self, raw, "SP", serialize=False)

    def sbuf(self, name, shape, dt=F32):
        return self.scopes[-1].enter_context(
            self.nc.sbuf_tensor(f"{name}_{self.uid()}", shape, dt))

    def psum(self, name, shape, dt=F32):
        return self.scopes[-1].enter_context(
            self.nc.psum_tensor(f"{name}_{self.uid()}", shape, dt))

    def push(self):
        self.scopes.append(ExitStack())

    def pop(self):
        self.scopes.pop().close()

    def close(self):
        while self.scopes:
            self.scopes.pop().close()


def load_weight_bf16(ctx, w_hbm, K, M, name="w", nstg=2):
    nc = ctx.nc
    KC = K // 128
    wb = ctx.sbuf(name, [128, KC * M], BF16)
    ctx.push()
    nstg = min(nstg, KC)
    stg = [ctx.sbuf(name + f"_stg{i}", [128, M], F32) for i in range(nstg)]
    with nc.Block(f"ldw_{name}_{ctx.uid()}") as blk:
        d0 = [ctx.val("dmlA"), ctx.val("dmlB")]
        sv0 = ctx.val("SV")

        @blk.vector
        def _(vr):
            v = ctx.v(vr)
            for kc in range(KC):
                par = kc % 2
                v.raw().wait_ge(ctx.sem("dmlA" if par == 0 else "dmlB"),
                                d0[par] + 16 * (kc // 2 + 1))
                v.tensor_copy(wb[:, kc * M:(kc + 1) * M], stg[kc % nstg][:, 0:M])

        @blk.sync
        def _(s):
            for kc in range(KC):
                par = kc % 2
                if kc >= nstg:
                    s.wait_ge(ctx.sem("SV"), sv0 + kc - nstg + 1)
                psem = ctx.sem("dmlA" if par == 0 else "dmlB")
                s.dma_start(stg[kc % nstg][:, 0:M],
                            w_hbm[kc * 128:(kc + 1) * 128, :]).then_inc(psem, 16)
                ctx.bump("dmlA" if par == 0 else "dmlB", 16)
    ctx.pop()
    return wb


def load_raw(ctx, hbm_ap, P, Fr, name="c", dt=F32):
    nc = ctx.nc
    tile = ctx.sbuf(name, [P, Fr], dt)
    with nc.Block(f"ldc_{name}_{ctx.uid()}") as blk:
        @blk.sync
        def _(s):
            s.dma_start(tile[:, :], hbm_ap).then_inc(ctx.sem("dml"), 16)
            s.wait_ge(ctx.sem("dml"), ctx.bump("dml", 16))
    return tile


def emit_spill(ctx, sb, dram_ap, nch, width, name="sp"):
    nc = ctx.nc
    with nc.Block(f"spill_{name}_{ctx.uid()}") as blk:
        @blk.sync
        def _(s):
            for c in range(nch):
                s.dma_start(dram_ap[c * 128:(c + 1) * 128, :],
                            sb[:, c * width:(c + 1) * width]).then_inc(ctx.sem("dml"), 16)
                ctx.bump("dml", 16)
            s.wait_ge(ctx.sem("dml"), ctx.val("dml"))


def emit_restore(ctx, sb, dram_ap, nch, width, name="rs"):
    nc = ctx.nc
    with nc.Block(f"rest_{name}_{ctx.uid()}") as blk:
        @blk.sync
        def _(s):
            for c in range(nch):
                s.dma_start(sb[:, c * width:(c + 1) * width],
                            dram_ap[c * 128:(c + 1) * 128, :]).then_inc(ctx.sem("dml"), 16)
                ctx.bump("dml", 16)
            s.wait_ge(ctx.sem("dml"), ctx.val("dml"))


def emit_mm(ctx, lhsT_fn, rhs_fn, KC, MC, TCn, epilogue, Nw=NT, name="mm",
            psum_banks=4, epi_engine="vector", Mp=128):
    nc = ctx.nc
    ekey = "SV" if epi_engine == "vector" else "SS"
    ctx.push()
    ps = [ctx.psum(f"ps_{name}", [Mp, Nw]) for _ in range(psum_banks)]

    with nc.Block(f"{name}_{ctx.uid()}") as blk:
        p0 = ctx.val("SP")
        epi_done = []

        sec = blk.vector if epi_engine == "vector" else blk.scalar

        @sec
        def _(er):
            eng = ctx.v(er) if epi_engine == "vector" else ctx.s(er)
            for i in range(MC * TCn):
                mc, tc = divmod(i, TCn)
                b = i % psum_banks
                eng.wait("SP", p0 + (i + 1) * KC)
                epilogue(eng, ps[b][:, :], mc, tc)
                epi_done.append(ctx.val(ekey))

        @blk.tensor
        def _(tr):
            te = ctx.t(tr)
            for i in range(MC * TCn):
                mc, tc = divmod(i, TCn)
                b = i % psum_banks
                if i >= psum_banks:
                    te.wait(ekey, epi_done[i - psum_banks])
                for kc in range(KC):
                    te.matmul(ps[b][:, :], lhsT_fn(kc, mc), rhs_fn(kc, tc),
                              start=(kc == 0), stop=(kc == KC - 1))
    ctx.pop()


def emit_ln_tc(ctx, get_x, xh_put, consts, tc, name="ln"):
    nc = ctx.nc
    onescol = consts["onescol"]
    onesrow = consts["onesrow"]
    ctx.push()
    mean_r = ctx.sbuf(f"{name}_mean", [1, NT])
    var_r = ctx.sbuf(f"{name}_var", [1, NT])
    xsq = ctx.sbuf(f"{name}_xsq", [128, NT])
    tmp = ctx.sbuf(f"{name}_tmp", [128, NT])
    tmp2 = ctx.sbuf(f"{name}_tmp2", [128, NT])
    rstd_b = ctx.sbuf(f"{name}_rstdb", [128, NT])
    ps_s = ctx.psum(f"{name}_ps_s", [1, NT])
    ps_q = ctx.psum(f"{name}_ps_q", [1, NT])
    ps_mb = ctx.psum(f"{name}_ps_mb", [128, NT])
    ps_vb = ctx.psum(f"{name}_ps_vb", [128, NT])

    with nc.Block(f"{name}_t{tc}_{ctx.uid()}") as blk:
        p0 = ctx.val("SP")
        ss0 = ctx.val("SS")
        sq_done = []
        stats_done = []

        @blk.vector
        def _(vr):
            v = ctx.v(vr)
            for c in range(DC):
                if c >= 1:
                    v.wait("SP", p0 + 2 * c)
                v.tensor_tensor(xsq[:, :], get_x(c), get_x(c), ALU.mult)
                sq_done.append(ctx.val("SV"))
            v.wait("SP", p0 + 2 * DC)
            v.tensor_scalar(mean_r[:, :], ps_s[:, :], 1.0 / D, None, ALU.mult)
            v.tensor_scalar(var_r[:, :], ps_q[:, :], 1.0 / D, 1e-5, ALU.mult, ALU.add)
            v.tensor_tensor(tmp[0:1, :], mean_r[:, :], mean_r[:, :], ALU.mult)
            v.tensor_tensor(var_r[:, :], var_r[:, :], tmp[0:1, :], ALU.subtract)
            stats_done.append(ctx.val("SV"))
            v.wait("SP", p0 + 2 * DC + 1)
            v.wait("SS", ss0 + 2)
            for c in range(DC):
                bf_ap, f32_ap = xh_put(c)
                v.tensor_tensor(tmp2[:, :], get_x(c), ps_mb[:, :], ALU.subtract)
                if f32_ap is not None:
                    v.tensor_tensor(f32_ap, tmp2[:, :], rstd_b[:, :], ALU.mult)
                    v.tensor_copy(bf_ap, f32_ap)
                else:
                    v.tensor_tensor(bf_ap, tmp2[:, :], rstd_b[:, :], ALU.mult)

        @blk.tensor
        def _(tr):
            te = ctx.t(tr)
            for c in range(DC):
                te.matmul(ps_s[:, :], onescol[:, :], get_x(c),
                          start=(c == 0), stop=(c == DC - 1))
                te.wait("SV", sq_done[c])
                te.matmul(ps_q[:, :], onescol[:, :], xsq[:, :],
                          start=(c == 0), stop=(c == DC - 1))
            te.wait("SV", stats_done[0])
            te.matmul(ps_mb[:, :], onesrow[:, :], mean_r[:, :], start=True, stop=True)
            te.matmul(ps_vb[:, :], onesrow[:, :], var_r[:, :], start=True, stop=True)

        @blk.scalar
        def _(sr):
            sc = ctx.s(sr)
            sc.wait("SP", p0 + 2 * DC + 2)
            sc.activation(tmp[:, :], ps_vb[:, :], AF.Ln)
            sc.activation(rstd_b[:, :], tmp[:, :], AF.Exp, scale=-0.5)
    ctx.pop()


def emit_ln(ctx, x, xh_bf, consts, xh_f32=None, name="ln"):
    for tc in range(TC4):
        emit_ln_tc(
            ctx,
            lambda c, tc=tc: x[:, c * T + tc * NT: c * T + tc * NT + NT],
            lambda c, tc=tc: (
                xh_bf[:, c * T + tc * NT: c * T + tc * NT + NT],
                None if xh_f32 is None else xh_f32[:, c * T + tc * NT: c * T + tc * NT + NT],
            ),
            consts, tc, name=name)


def emit_l2norm(ctx, zin, zout, consts, name="l2n", gelu_res_x=None):
    nc = ctx.nc
    onescol_b = consts["onescolb"]
    onesrow = consts["onesrow"]
    ctx.push()
    zsq = ctx.sbuf(f"{name}_zsq", [128, NT], BF16)
    rr = ctx.sbuf(f"{name}_rr", [1, NT])
    tmp = ctx.sbuf(f"{name}_tmp", [128, NT])
    rb_sb = ctx.sbuf(f"{name}_rb", [128, NT])
    tmpg = [ctx.sbuf(f"{name}_tg", [128, NT]) for _ in range(2)]
    tmpg2 = [ctx.sbuf(f"{name}_tg2", [128, NT]) for _ in range(2)]
    ps_q = ctx.psum(f"{name}_psq", [1, NT])
    ps_rb = ctx.psum(f"{name}_psrb", [128, NT])

    for tc in range(TC4):
        with nc.Block(f"{name}_t{tc}_{ctx.uid()}") as blk:
            p0 = ctx.val("SP")
            sq_done = []
            rr_done = []
            rb_done = []
            mul_done = []

            @blk.vector
            def _(vr):
                v = ctx.v(vr)
                for c in range(DC):
                    if c >= 1:
                        v.wait("SP", p0 + c)
                    zsl = zin[:, c * T + tc * NT: c * T + tc * NT + NT]
                    v.tensor_tensor(zsq[:, :], zsl, zsl, ALU.mult)
                    sq_done.append(ctx.val("SV"))
                v.wait("SP", p0 + DC)
                v.tensor_scalar(rr[:, :], ps_q[:, :], 1e-24, None, ALU.max)
                rr_done.append(ctx.val("SV"))

            @blk.tensor
            def _(tr):
                te = ctx.t(tr)
                for c in range(DC):
                    te.wait("SV", sq_done[c])
                    te.matmul(ps_q[:, :], onescol_b[:, :], zsq[:, :],
                              start=(c == 0), stop=(c == DC - 1))
                te.wait("SV", rr_done[0])
                te.matmul(ps_rb[:, :], onesrow[:, :], rr[:, :], start=True, stop=True)

            @blk.scalar
            def _(sr):
                sc = ctx.s(sr)
                sc.wait("SP", p0 + DC + 1)
                sc.activation(tmp[:, :], ps_rb[:, :], AF.Ln)
                sc.activation(rb_sb[:, :], tmp[:, :], AF.Exp, scale=-0.5)
                rb_done.append(ctx.val("SS"))

            if gelu_res_x is None:
                @blk.vector
                def _(vr):
                    v = ctx.v(vr)
                    v.wait("SS", rb_done[0])
                    for c in range(DC):
                        zsl = zin[:, c * T + tc * NT: c * T + tc * NT + NT]
                        v.tensor_tensor(zout[:, c * T + tc * NT: c * T + tc * NT + NT],
                                        zsl, rb_sb[:, :], ALU.mult)
            else:
                @blk.vector
                def _(vr):
                    v = ctx.v(vr)
                    v.wait("SS", rb_done[0])
                    for c in range(DC):
                        zsl = zin[:, c * T + tc * NT: c * T + tc * NT + NT]
                        if c >= 2:
                            v.wait("SS", rb_done[0] + c - 1)
                        v.tensor_tensor(tmpg[c % 2][:, :], zsl, rb_sb[:, :], ALU.mult)
                        mul_done.append(ctx.val("SV"))
                        v.wait("SS", rb_done[0] + c + 1)
                        xs = gelu_res_x[:, c * T + tc * NT: c * T + tc * NT + NT]
                        v.tensor_tensor(xs, xs, tmpg2[c % 2][:, :], ALU.add)

                @blk.scalar
                def _(sr):
                    sc = ctx.s(sr)
                    for c in range(DC):
                        sc.wait("SV", mul_done[c])
                        sc.activation(tmpg2[c % 2][:, :], tmpg[c % 2][:, :], AF.Gelu)
    ctx.pop()


def emit_l1_ffn(ctx, x, weights, consts):
    nc = ctx.nc
    ctx.push()
    w1 = load_weight_bf16(ctx, weights["L1_ffn_w1"].ap(), 1024, 4096, "l1f_w1")
    b1 = load_raw(ctx, weights["L1_ffn_b1c"].ap(), 128, 32, "l1f_b1")
    b2 = load_raw(ctx, weights["L1_ffn_b2c"].ap(), 128, 8, "l1f_b2")
    xh = ctx.sbuf("l1f_xh", [128, DC * NT], BF16)
    h = ctx.sbuf("l1f_h", [128, 32 * NT], BF16)

    for tc in range(TC4):
        emit_ln_tc(ctx,
                   lambda c: x[:, c * T + tc * NT: c * T + tc * NT + NT],
                   lambda c: (xh[:, c * NT:(c + 1) * NT], None),
                   consts, tc, name="l1f_ln")

        def epi1(eng, ps_ap, mc, _tc):
            eng.activation(h[:, mc * NT:(mc + 1) * NT], ps_ap,
                           AF.Gelu, bias=b1[:, mc:mc + 1])

        emit_mm(ctx,
                lambda kc, mc: w1[:, kc * 4096 + mc * 128: kc * 4096 + mc * 128 + 128],
                lambda kc, _tc: xh[:, kc * NT:(kc + 1) * NT],
                KC=8, MC=32, TCn=1, epilogue=epi1, name=f"l1f_m1_{tc}",
                epi_engine="scalar")

        emit_mm_kstream(ctx, weights["L1_ffn_w2"].ap(), 4096, 1024,
                        lambda kc: h[:, kc * NT:(kc + 1) * NT],
                        lambda eng, ps_ap, mc, tc=tc: eng.scalar_tensor_tensor(
                            x[:, mc * T + tc * NT: mc * T + tc * NT + NT],
                            ps_ap, b2[:, mc:mc + 1],
                            x[:, mc * T + tc * NT: mc * T + tc * NT + NT],
                            ALU.add, ALU.add),
                        name=f"l1f_m2_{tc}")
    ctx.pop()


def emit_mm_kstream(ctx, w_hbm, K, M, rhs_fn, epilogue, Nw=NT, name="mmk"):
    nc = ctx.nc
    KC = K // 128
    MC = M // 128
    assert MC <= 8
    ctx.push()
    ps = [ctx.psum(f"ps_{name}", [128, Nw]) for _ in range(MC)]
    stg = [ctx.sbuf(f"{name}_stg{i}", [128, M], F32) for i in range(2)]
    wt = [ctx.sbuf(f"{name}_wt{i}", [128, M], BF16) for i in range(2)]

    with nc.Block(f"{name}_{ctx.uid()}") as blk:
        d0 = [ctx.val("dmlA"), ctx.val("dmlB")]
        p0 = ctx.val("SP")
        cv_done = []

        @blk.vector
        def _(vr):
            v = ctx.v(vr)
            for kc in range(KC):
                par = kc % 2
                v.raw().wait_ge(ctx.sem("dmlA" if par == 0 else "dmlB"),
                                d0[par] + 16 * (kc // 2 + 1))
                if kc >= 2:
                    v.wait("SP", p0 + (kc - 1) * MC)
                v.tensor_copy(wt[kc % 2][:, :], stg[kc % 2][:, :])
                cv_done.append(ctx.val("SV"))
            for mc in range(MC):
                v.wait("SP", p0 + KC * MC)
                epilogue(v, ps[mc][:, :], mc)

        @blk.sync
        def _(s):
            for kc in range(KC):
                par = kc % 2
                if kc >= 2:
                    s.wait_ge(ctx.sem("SV"), cv_done[kc - 1])
                psem = ctx.sem("dmlA" if par == 0 else "dmlB")
                s.dma_start(stg[kc % 2][:, :],
                            w_hbm[kc * 128:(kc + 1) * 128, :]).then_inc(psem, 16)
                ctx.bump("dmlA" if par == 0 else "dmlB", 16)

        @blk.tensor
        def _(tr):
            te = ctx.t(tr)
            for kc in range(KC):
                te.wait("SV", cv_done[kc])
                for mc in range(MC):
                    te.matmul(ps[mc][:, :], wt[kc % 2][:, mc * 128:(mc + 1) * 128],
                              rhs_fn(kc), start=(kc == 0), stop=(kc == KC - 1))
    ctx.pop()


def emit_l3_ffn(ctx, x, weights, consts):
    nc = ctx.nc
    ctx.push()
    xh = ctx.sbuf("l3f_xh", [128, DC * T], BF16)
    emit_ln(ctx, x, xh, consts, name="l3f_ln")

    ctx.push()
    rt_fm = ctx.sbuf("l3f_rt", [8, T])
    coef_fm = ctx.sbuf("l3f_coef", [8, T])
    ctx.push()
    router = load_weight_bf16(ctx, weights["L3_ffn_router"].ap(), 1024, 8, "l3f_rtw")

    emit_mm(ctx, lambda kc, mc: router[:, kc * 8:(kc + 1) * 8],
            lambda kc, tc: xh[:, kc * T + tc * NT: kc * T + tc * NT + NT],
            KC=8, MC=1, TCn=TC4,
            epilogue=lambda eng, ps, mc, tc: eng.tensor_copy(
                rt_fm[:, tc * NT:(tc + 1) * NT], ps),
            name="l3f_rtmm", psum_banks=2, Mp=8)

    ident = consts["identf"]
    rt_tm = ctx.sbuf("l3f_rttm", [128, 16 * 8])
    co_tm = ctx.sbuf("l3f_cotm", [128, 16 * 8])
    tmp_a = ctx.sbuf("l3f_ta", [128, 16 * 8])
    tmp_b = ctx.sbuf("l3f_tb", [128, 16 * 8])
    s1 = ctx.sbuf("l3f_s1", [128, 16])
    s1b = ctx.sbuf("l3f_s1b", [128, 16])
    pst2 = [ctx.psum("l3f_pst", [128, 128]) for _ in range(2)]

    def b3(t):
        return bass.AP(t, 0, [[16, 128], [1, 16], [0, 8]])

    def r3(t):
        return t[:, :].rearrange("p (a b) -> p a b", b=8)

    with nc.Block(f"l3f_rtt_{ctx.uid()}") as blk:
        p0 = ctx.val("SP")
        cp_done = []

        @blk.vector
        def _(vr):
            v = ctx.v(vr)
            for i in range(16):
                v.wait("SP", p0 + i + 1)
                v.tensor_copy(rt_tm[:, i * 8:(i + 1) * 8], pst2[i % 2][:, 0:8])
                cp_done.append(ctx.val("SV"))

        @blk.tensor
        def _(tr):
            te = ctx.t(tr)
            for i in range(16):
                if i >= 2:
                    te.wait("SV", cp_done[i - 2])
                te.transpose(pst2[i % 2][:, 0:8], rt_fm[:, i * 128:(i + 1) * 128],
                             ident[0:8, 0:8])

    with nc.Block(f"l3f_top2_{ctx.uid()}") as blk:
        ss0 = ctx.val("SS")
        pre_done = []
        sum_done = []

        @blk.vector
        def _(vr):
            v = ctx.v(vr)
            v.reduce_max(s1[:, :], r3(rt_tm), AX.X, negate=True)
            v.tensor_tensor(r3(tmp_a), r3(rt_tm), b3(s1), ALU.add)
            pre_done.append(ctx.val("SV"))
            v.wait("SS", ss0 + 1)
            v.reduce_sum(s1[:, :], r3(tmp_b), AX.X)
            sum_done.append(ctx.val("SV"))
            v.wait("SS", ss0 + 3)
            v.tensor_tensor(r3(rt_tm), r3(tmp_b), b3(s1b), ALU.mult)
            v.reduce_max(s1[:, :], r3(rt_tm), AX.X)
            v.tensor_tensor(r3(tmp_a), r3(rt_tm), b3(s1), ALU.is_lt)
            v.tensor_tensor(tmp_b[:, :], rt_tm[:, :], tmp_a[:, :], ALU.mult)
            v.reduce_max(s1[:, :], r3(tmp_b), AX.X)
            v.tensor_tensor(r3(tmp_a), r3(rt_tm), b3(s1), ALU.is_ge)
            v.tensor_tensor(co_tm[:, :], rt_tm[:, :], tmp_a[:, :], ALU.mult)

        @blk.scalar
        def _(sr):
            sc = ctx.s(sr)
            sc.wait("SV", pre_done[0])
            sc.activation(tmp_b[:, :], tmp_a[:, :], AF.Exp)
            sc.wait("SV", sum_done[0])
            sc.activation(tmp_a[:, 0:16], s1[:, :], AF.Ln)
            sc.activation(s1b[:, :], tmp_a[:, 0:16], AF.Exp, scale=-1.0)

    with nc.Block(f"l3f_cot_{ctx.uid()}") as blk:
        p0 = ctx.val("SP")
        cp2 = []

        @blk.vector
        def _(vr):
            v = ctx.v(vr)
            for i in range(16):
                v.wait("SP", p0 + i + 1)
                v.tensor_copy(coef_fm[:, i * 128:(i + 1) * 128], pst2[0][0:8, 0:128])
                cp2.append(ctx.val("SV"))

        @blk.tensor
        def _(tr):
            te = ctx.t(tr)
            for i in range(16):
                if i >= 1:
                    te.wait("SV", cp2[i - 1])
                te.transpose(pst2[0][0:8, 0:128], co_tm[:, i * 8:(i + 1) * 8],
                             ident[:, :])
    ctx.pop()

    inner_dram4 = [nc.dram_tensor(f"l3f_innerd{tc}_{ctx.uid()}", [32 * 128, NT], BF16,
                                  kind="Internal") for tc in range(TC4)]
    for es in range(2):
        ctx.push()
        wg = load_weight_bf16(ctx, weights["L3_ffn_wg"].ap()[:, es * 2048:(es + 1) * 2048],
                              1024, 2048, f"l3f_wg{es}")
        wu = load_weight_bf16(ctx, weights["L3_ffn_wu"].ap()[:, es * 2048:(es + 1) * 2048],
                              1024, 2048, f"l3f_wu{es}")
        inner = ctx.sbuf("l3f_inner", [128, 16 * NT], BF16)
        gtile = [ctx.sbuf("l3f_gt", [128, NT], BF16) for _ in range(2)]

        for tc in range(TC4):
            ctx.push()
            psg = [ctx.psum("l3f_psg", [128, NT]) for _ in range(2)]
            psu = [ctx.psum("l3f_psu", [128, NT]) for _ in range(2)]
            with nc.Block(f"l3f_gu_{es}_{tc}_{ctx.uid()}") as blk:
                p0 = ctx.val("SP")
                ss0 = ctx.val("SS")
                sv0 = ctx.val("SV")
                silu_at = lambda m: ss0 + m + 1
                mul_at = lambda m: sv0 + m + 1

                @blk.scalar
                def _(sr):
                    sc = ctx.s(sr)
                    for mc in range(16):
                        sc.wait("SP", p0 + (2 * mc + 1) * 8)
                        if mc >= 2:
                            sc.wait("SV", mul_at(mc - 2))
                        sc.activation(gtile[mc % 2][:, :], psg[mc % 2][:, :], AF.Silu)

                @blk.vector
                def _(vr):
                    v = ctx.v(vr)
                    for mc in range(16):
                        v.wait("SP", p0 + (2 * mc + 2) * 8)
                        v.wait("SS", silu_at(mc))
                        v.tensor_tensor(inner[:, mc * NT:(mc + 1) * NT],
                                        psu[mc % 2][:, :], gtile[mc % 2][:, :],
                                        ALU.mult)

                @blk.tensor
                def _(tr):
                    te = ctx.t(tr)
                    for mc in range(16):
                        if mc >= 2:
                            te.wait("SS", silu_at(mc - 2))
                            te.wait("SV", mul_at(mc - 2))
                        for kc in range(8):
                            te.matmul(psg[mc % 2][:, :],
                                      wg[:, kc * 2048 + mc * 128: kc * 2048 + mc * 128 + 128],
                                      xh[:, kc * T + tc * NT: kc * T + tc * NT + NT],
                                      start=(kc == 0), stop=(kc == 7))
                        for kc in range(8):
                            te.matmul(psu[mc % 2][:, :],
                                      wu[:, kc * 2048 + mc * 128: kc * 2048 + mc * 128 + 128],
                                      xh[:, kc * T + tc * NT: kc * T + tc * NT + NT],
                                      start=(kc == 0), stop=(kc == 7))
            ctx.pop()

            ctx.push()
            psc = ctx.psum("l3f_psc", [128, NT])
            with nc.Block(f"l3f_coef_{es}_{tc}_{ctx.uid()}") as blk:
                p0 = ctx.val("SP")
                cmul = []

                @blk.vector
                def _(vr):
                    v = ctx.v(vr)
                    for el in range(4):
                        v.wait("SP", p0 + el + 1)
                        iv = inner[:, el * 4 * NT:(el + 1) * 4 * NT].rearrange(
                            "p (c n) -> p c n", n=NT)
                        pb = bass.AP(psc, 0, [[NT, 128], [0, 4], [1, NT]])
                        v.tensor_tensor(iv, iv, pb, ALU.mult)
                        cmul.append(ctx.val("SV"))

                @blk.tensor
                def _(tr):
                    te = ctx.t(tr)
                    for el in range(4):
                        e = es * 4 + el
                        if el >= 1:
                            te.wait("SV", cmul[el - 1])
                        te.matmul(psc[:, :], consts["esel"][:, e * 128:(e + 1) * 128],
                                  coef_fm[:, tc * NT:(tc + 1) * NT],
                                  start=True, stop=True)
            ctx.pop()
            emit_spill(ctx, inner, inner_dram4[tc].ap()[es * 2048:(es + 1) * 2048, :],
                       16, NT, f"l3f_in{es}_{tc}")
        ctx.pop()
    ctx.pop()

    ctx.push()
    wd = load_weight_bf16(ctx, weights["L3_ffn_wd"].ap(), 4096, 1024, "l3f_wd")
    inner = ctx.sbuf("l3f_inner2", [128, 32 * NT], BF16)
    for tc in range(TC4):
        emit_restore(ctx, inner, inner_dram4[tc].ap(), 32, NT, f"l3f_in{tc}")
        emit_mm(ctx,
                lambda kc, mc: wd[:, kc * 1024 + mc * 128: kc * 1024 + mc * 128 + 128],
                lambda kc, _tc: inner[:, kc * NT:(kc + 1) * NT],
                KC=32, MC=8, TCn=1,
                epilogue=lambda eng, ps, mc, _tc, tc=tc: eng.tensor_tensor(
                    x[:, mc * T + tc * NT: mc * T + tc * NT + NT],
                    x[:, mc * T + tc * NT: mc * T + tc * NT + NT], ps, ALU.add),
                name=f"l3f_down_{tc}")
    ctx.pop()

    ctx.push()
    wsg = load_weight_bf16(ctx, weights["L3_ffn_shared_gate"].ap(), 1024, 1024, "l3f_wsg")
    wsu = load_weight_bf16(ctx, weights["L3_ffn_shared_up"].ap(), 1024, 1024, "l3f_wsu")
    sboth = ctx.sbuf("l3f_sb", [128, DC * T], BF16)
    stile = [ctx.sbuf("l3f_st", [128, NT], BF16) for _ in range(2)]
    ctx.push()
    psg = [ctx.psum("l3f_psg2", [128, NT]) for _ in range(2)]
    psu = [ctx.psum("l3f_psu2", [128, NT]) for _ in range(2)]
    with nc.Block(f"l3f_sgu_{ctx.uid()}") as blk:
        p0 = ctx.val("SP")
        ss0 = ctx.val("SS")
        sv0 = ctx.val("SV")
        silu_at = lambda m: ss0 + m + 1
        mul_at = lambda m: sv0 + m + 1

        @blk.scalar
        def _(sr):
            sc = ctx.s(sr)
            for i in range(32):
                sc.wait("SP", p0 + (2 * i + 1) * 8)
                if i >= 2:
                    sc.wait("SV", mul_at(i - 2))
                sc.activation(stile[i % 2][:, :], psg[i % 2][:, :], AF.Silu)

        @blk.vector
        def _(vr):
            v = ctx.v(vr)
            for i in range(32):
                mc, tc = divmod(i, TC4)
                v.wait("SP", p0 + (2 * i + 2) * 8)
                v.wait("SS", silu_at(i))
                v.tensor_tensor(sboth[:, mc * T + tc * NT: mc * T + tc * NT + NT],
                                psu[i % 2][:, :], stile[i % 2][:, :], ALU.mult)

        @blk.tensor
        def _(tr):
            te = ctx.t(tr)
            for i in range(32):
                mc, tc = divmod(i, TC4)
                if i >= 2:
                    te.wait("SS", silu_at(i - 2))
                    te.wait("SV", mul_at(i - 2))
                for kc in range(8):
                    te.matmul(psg[i % 2][:, :],
                              wsg[:, kc * 1024 + mc * 128: kc * 1024 + mc * 128 + 128],
                              xh[:, kc * T + tc * NT: kc * T + tc * NT + NT],
                              start=(kc == 0), stop=(kc == 7))
                for kc in range(8):
                    te.matmul(psu[i % 2][:, :],
                              wsu[:, kc * 1024 + mc * 128: kc * 1024 + mc * 128 + 128],
                              xh[:, kc * T + tc * NT: kc * T + tc * NT + NT],
                              start=(kc == 0), stop=(kc == 7))
    ctx.pop()

    wsd = load_weight_bf16(ctx, weights["L3_ffn_shared_down"].ap(), 1024, 1024, "l3f_wsd")
    emit_mm(ctx,
            lambda kc, mc: wsd[:, kc * 1024 + mc * 128: kc * 1024 + mc * 128 + 128],
            lambda kc, tc: sboth[:, kc * T + tc * NT: kc * T + tc * NT + NT],
            KC=8, MC=8, TCn=TC4,
            epilogue=lambda eng, ps, mc, tc: eng.tensor_tensor(
                x[:, mc * T + tc * NT: mc * T + tc * NT + NT],
                x[:, mc * T + tc * NT: mc * T + tc * NT + NT], ps, ALU.add),
            name="l3f_sdown")
    ctx.pop()
    ctx.pop()


def emit_l2_ffn(ctx, x, weights, consts):
    nc = ctx.nc
    ctx.push()
    zA_dram = nc.dram_tensor(f"l2f_zad_{ctx.uid()}", [D, T], BF16, kind="Internal")

    ctx.push()
    q_fm = ctx.sbuf("l2f_q", [128, DC * T], BF16)
    k_fm = ctx.sbuf("l2f_k", [128, DC * T], BF16)
    v_tm = ctx.sbuf("l2f_v", [128, 16 * 1024], BF16)
    TH = T // 2
    for th in range(2):
        ctx.push()
        xh = ctx.sbuf("l2f_xh", [128, DC * TH], BF16)
        for tci in range(2):
            tc = th * 2 + tci
            emit_ln_tc(ctx,
                       lambda c: x[:, c * T + tc * NT: c * T + tc * NT + NT],
                       lambda c: (xh[:, c * TH + tci * NT: c * TH + tci * NT + NT], None),
                       consts, tc, name="l2f_ln")

        ctx.push()
        wq = load_weight_bf16(ctx, weights["L2_ffn_Wq"].ap(), 1024, 1024, "l2f_wq")
        emit_mm(ctx,
                lambda kc, mc: wq[:, kc * 1024 + mc * 128: kc * 1024 + mc * 128 + 128],
                lambda kc, tci: xh[:, kc * TH + tci * NT: kc * TH + tci * NT + NT],
                KC=8, MC=8, TCn=2,
                epilogue=lambda eng, ps, mc, tci, th=th: eng.tensor_copy(
                    q_fm[:, mc * T + (th * 2 + tci) * NT: mc * T + (th * 2 + tci) * NT + NT], ps),
                name=f"l2f_qm{th}")
        ctx.pop()

        ctx.push()
        wk = load_weight_bf16(ctx, weights["L2_ffn_Wk"].ap(), 1024, 1024, "l2f_wk")
        emit_mm(ctx,
                lambda kc, mc: wk[:, kc * 1024 + mc * 128: kc * 1024 + mc * 128 + 128],
                lambda kc, tci: xh[:, kc * TH + tci * NT: kc * TH + tci * NT + NT],
                KC=8, MC=8, TCn=2,
                epilogue=lambda eng, ps, mc, tci, th=th: eng.tensor_scalar(
                    k_fm[:, mc * T + (th * 2 + tci) * NT: mc * T + (th * 2 + tci) * NT + NT],
                    ps, 1.0 / 32.0, None, ALU.mult),
                name=f"l2f_km{th}")
        ctx.pop()

        ctx.push()
        wv = load_weight_bf16(ctx, weights["L2_ffn_Wv"].ap(), 1024, 1024, "l2f_wv")
        emit_mm(ctx,
                lambda kc, mc: xh[:, kc * TH + mc * 128: kc * TH + mc * 128 + 128],
                lambda kc, nh: wv[:, kc * 1024 + nh * NT: kc * 1024 + nh * NT + NT],
                KC=8, MC=8, TCn=2,
                epilogue=lambda eng, ps, mc, nh, th=th: eng.tensor_copy(
                    v_tm[:, (th * 8 + mc) * 1024 + nh * NT: (th * 8 + mc) * 1024 + nh * NT + NT],
                    ps),
                name=f"l2f_vm{th}")
        ctx.pop()
        ctx.pop()

    ctx.push()
    expS = ctx.sbuf("l2f_es", [128, T], BF16)
    negmx = ctx.sbuf("l2f_nm", [128, 1])
    sums4 = ctx.sbuf("l2f_s4", [128, 4])
    rcol = ctx.sbuf("l2f_rc", [128, 1])
    rcol2 = ctx.sbuf("l2f_rc2", [128, 1])
    vt_h = ctx.sbuf("l2f_vt", [128, 64], BF16)
    oev = ctx.sbuf("l2f_oev", [64, T], BF16)
    psS = ctx.psum("l2f_psS", [128, T])
    psO = ctx.psum("l2f_psO", [64, T])

    for h in range(16):
        hb = (h % 2) * 64
        hc = h // 2
        with nc.Block(f"l2f_att_{h}_{ctx.uid()}") as blk:
            p0 = ctx.val("SP")
            v0 = ctx.val("SV")
            s0 = ctx.val("SS")

            @blk.tensor
            def _(tr):
                te = ctx.t(tr)
                for kt in range(16):
                    if kt >= 1:
                        te.wait("SS", s0 + 6 * (kt - 1) + 4)
                    for qc in range(TC4):
                        te.matmul(psS[:, qc * NT:(qc + 1) * NT],
                                  k_fm[hb:hb + 64, hc * T + kt * 128: hc * T + kt * 128 + 128],
                                  q_fm[hb:hb + 64, hc * T + qc * NT: hc * T + qc * NT + NT],
                                  start=True, stop=True)
                    te.wait("SV", v0 + 3 * kt + 3)
                    for qc in range(TC4):
                        te.matmul(psO[:, qc * NT:(qc + 1) * NT], vt_h[:, :],
                                  expS[:, qc * NT:(qc + 1) * NT],
                                  start=(kt == 0), stop=(kt == 15))

            @blk.vector
            def _(vr):
                v = ctx.v(vr)
                for kt in range(16):
                    v.wait("SP", p0 + 8 * kt + 4)
                    v.reduce_max(negmx[:, :], psS[:, :], AX.X, negate=True)
                    v.wait("SS", s0 + 6 * kt + 4)
                    v.reduce_sum(rcol[:, :], sums4[:, :], AX.X)
                    v.wait("SS", s0 + 6 * kt + 6)
                    v.tensor_scalar(vt_h[:, :],
                                    v_tm[:, kt * 1024 + h * 64: kt * 1024 + h * 64 + 64],
                                    rcol[:, :], None, ALU.mult)

            @blk.scalar
            def _(sr):
                sc = ctx.s(sr)
                for kt in range(16):
                    sc.wait("SV", v0 + 3 * kt + 1)
                    for qc in range(TC4):
                        sc.activation(expS[:, qc * NT:(qc + 1) * NT],
                                      psS[:, qc * NT:(qc + 1) * NT], AF.Exp,
                                      bias=negmx[:, :],
                                      accum_out=sums4[:, qc:qc + 1])
                    sc.wait("SV", v0 + 3 * kt + 2)
                    sc.activation(rcol2[:, :], rcol[:, :], AF.Ln)
                    sc.activation(rcol[:, :], rcol2[:, :], AF.Exp, scale=-1.0)

        with nc.Block(f"l2f_oev_{h}_{ctx.uid()}") as blk:
            ev_done = []

            @blk.vector
            def _(vr):
                v = ctx.v(vr)
                v.tensor_copy(oev[:, :], psO[:, :])
                ev_done.append(ctx.val("SV"))

            @blk.sync
            def _(s):
                s.wait_ge(ctx.sem("SV"), ev_done[0])
                s.dma_start(zA_dram.ap()[h * 64:(h + 1) * 64, :],
                            oev[:, :]).then_inc(ctx.sem("dml"), 16)
                s.wait_ge(ctx.sem("dml"), ctx.bump("dml", 16))
    ctx.pop()
    ctx.pop()

    ctx.push()
    zA = ctx.sbuf("l2f_zA", [128, DC * T], BF16)
    zB = ctx.sbuf("l2f_zB", [128, DC * T], BF16)
    emit_restore(ctx, zA, zA_dram.ap(), DC, T, "l2f_za")
    ctx.push()
    wo = load_weight_bf16(ctx, weights["L2_ffn_Wo"].ap(), 1024, 1024, "l2f_wo")
    emit_mm(ctx,
            lambda kc, mc: wo[:, kc * 1024 + mc * 128: kc * 1024 + mc * 128 + 128],
            lambda kc, tc: zA[:, kc * T + tc * NT: kc * T + tc * NT + NT],
            KC=8, MC=8, TCn=TC4,
            epilogue=lambda eng, ps, mc, tc: eng.tensor_copy(
                zB[:, mc * T + tc * NT: mc * T + tc * NT + NT], ps),
            name="l2f_wo")
    ctx.pop()

    emit_l2norm(ctx, zB, zA, consts, name="l2f_n1")

    ctx.push()
    wf = load_weight_bf16(ctx, weights["L2_ffn_Wff"].ap(), 1024, 1024, "l2f_wf")
    emit_mm(ctx,
            lambda kc, mc: wf[:, kc * 1024 + mc * 128: kc * 1024 + mc * 128 + 128],
            lambda kc, tc: zA[:, kc * T + tc * NT: kc * T + tc * NT + NT],
            KC=8, MC=8, TCn=TC4,
            epilogue=lambda eng, ps, mc, tc: eng.tensor_copy(
                zB[:, mc * T + tc * NT: mc * T + tc * NT + NT], ps),
            name="l2f_wf")
    ctx.pop()

    emit_l2norm(ctx, zB, zA, consts, name="l2f_n2", gelu_res_x=x)
    ctx.pop()
    ctx.pop()


def emit_l3_attn(ctx, x, weights, consts):
    nc = ctx.nc
    ctx.push()
    ao_dram = nc.dram_tensor(f"l3a_aod_{ctx.uid()}", [D, T], BF16, kind="Internal")
    ctx.push()
    q_fm = ctx.sbuf("l3a_q", [128, DC * T], BF16)
    k_fm = ctx.sbuf("l3a_k", [128, DC * T], BF16)
    v_tm = ctx.sbuf("l3a_v", [128, 16 * 1024], BF16)
    bq = load_raw(ctx, weights["L3_attn_bqc"].ap(), 128, 16, "l3a_bq")
    vbias = ctx.sbuf("l3a_vb", [128, 1024])

    vbr = load_raw(ctx, weights["L3_attn_bvrow"].ap(), 1, 1024, "l3a_vbr")
    ctx.push()
    psb = ctx.psum("l3a_psb", [128, NT])
    with nc.Block(f"l3a_vb_{ctx.uid()}") as blk:
        p0 = ctx.val("SP")
        sv0 = ctx.val("SV")

        @blk.vector
        def _(vr):
            v = ctx.v(vr)
            for i in range(2):
                v.wait("SP", p0 + i + 1)
                v.tensor_copy(vbias[:, i * NT:(i + 1) * NT], psb[:, :])

        @blk.tensor
        def _(tr):
            te = ctx.t(tr)
            for i in range(2):
                if i >= 1:
                    te.wait("SV", sv0 + i)
                te.matmul(psb[:, :], consts["onesrow"][:, :],
                          vbr[0:1, i * NT:(i + 1) * NT], start=True, stop=True)
    ctx.pop()

    TH = T // 2
    for th in range(2):
        ctx.push()
        xh = ctx.sbuf("l3a_xh", [128, DC * TH], BF16)
        for tci in range(2):
            tc = th * 2 + tci
            emit_ln_tc(ctx,
                       lambda c: x[:, c * T + tc * NT: c * T + tc * NT + NT],
                       lambda c: (xh[:, c * TH + tci * NT: c * TH + tci * NT + NT], None),
                       consts, tc, name="l3a_ln")

        ctx.push()
        wq = load_weight_bf16(ctx, weights["L3_attn_wqkvT"].ap()[:, 0:1024],
                              1024, 1024, "l3a_wq", nstg=1)
        emit_mm(ctx,
                lambda kc, mc: wq[:, kc * 1024 + mc * 128: kc * 1024 + mc * 128 + 128],
                lambda kc, tci: xh[:, kc * TH + tci * NT: kc * TH + tci * NT + NT],
                KC=8, MC=8, TCn=2,
                epilogue=lambda eng, ps, mc, tci, th=th: eng.tensor_scalar(
                    q_fm[:, mc * T + (th * 2 + tci) * NT: mc * T + (th * 2 + tci) * NT + NT],
                    ps, bq[:, mc:mc + 1], 0.125, ALU.add, ALU.mult),
                name=f"l3a_qm{th}")
        ctx.pop()

        ctx.push()
        wk = load_weight_bf16(ctx, weights["L3_attn_wqkvT"].ap()[:, 1024:2048],
                              1024, 1024, "l3a_wk", nstg=1)
        emit_mm(ctx,
                lambda kc, mc: wk[:, kc * 1024 + mc * 128: kc * 1024 + mc * 128 + 128],
                lambda kc, tci: xh[:, kc * TH + tci * NT: kc * TH + tci * NT + NT],
                KC=8, MC=8, TCn=2,
                epilogue=lambda eng, ps, mc, tci, th=th: eng.tensor_scalar(
                    k_fm[:, mc * T + (th * 2 + tci) * NT: mc * T + (th * 2 + tci) * NT + NT],
                    ps, bq[:, 8 + mc:8 + mc + 1], None, ALU.add),
                name=f"l3a_km{th}")
        ctx.pop()

        ctx.push()
        wv = load_weight_bf16(ctx, weights["L3_attn_wqkvT"].ap()[:, 2048:3072],
                              1024, 1024, "l3a_wv", nstg=1)
        emit_mm(ctx,
                lambda kc, mc: xh[:, kc * TH + mc * 128: kc * TH + mc * 128 + 128],
                lambda kc, nh: wv[:, kc * 1024 + nh * NT: kc * 1024 + nh * NT + NT],
                KC=8, MC=8, TCn=2,
                epilogue=lambda eng, ps, mc, nh, th=th: eng.tensor_tensor(
                    v_tm[:, (th * 8 + mc) * 1024 + nh * NT: (th * 8 + mc) * 1024 + nh * NT + NT],
                    ps, vbias[:, nh * NT:(nh + 1) * NT], ALU.add),
                name=f"l3a_vm{th}")
        ctx.pop()
        ctx.pop()

    masks = load_raw(ctx, weights["L3_attn_masks"].ap(), 128, 3 * 384, "l3a_mk")
    ctx.push()
    s_sb = ctx.sbuf("l3a_ssb", [128, 384])
    p_bf = ctx.sbuf("l3a_pbf", [128, 384], BF16)
    pt_sb = ctx.sbuf("l3a_pt", [128, 384], BF16)
    negm = ctx.sbuf("l3a_ng", [128, 1])
    rsum = ctx.sbuf("l3a_rs", [128, 1])
    rrec = ctx.sbuf("l3a_rr", [128, 1])
    rrec2 = ctx.sbuf("l3a_rr2", [128, 1])
    oevt = ctx.sbuf("l3a_oevt", [64, 128], BF16)
    psS = [ctx.psum("l3a_psS", [128, 384]) for _ in range(2)]
    psT = [ctx.psum("l3a_psT", [128, 384], BF16) for _ in range(2)]
    psO = [ctx.psum("l3a_psO", [64, 128]) for _ in range(2)]

    for h in range(16):
        hb = (h % 2) * 64
        hc = h // 2
        for qt in range(16):
            kw = min(max(qt * 128 - 128, 0), T - 384)
            kc0 = kw // 128
            dlt = (qt * 128 - kw) // 128
            pb = qt % 2
            with nc.Block(f"l3a_att_{h}_{qt}_{ctx.uid()}") as blk:
                p0 = ctx.val("SP")
                sv0 = ctx.val("SV")
                ss0 = ctx.val("SS")

                @blk.vector
                def _(vr):
                    v = ctx.v(vr)
                    v.wait("SP", p0 + 1)
                    v.tensor_tensor(s_sb[:, :], psS[pb][:, :],
                                    masks[:, dlt * 384:(dlt + 1) * 384], ALU.add)
                    v.reduce_max(negm[:, :], s_sb[:, :], AX.X, negate=True)
                    v.wait("SS", ss0 + 3)
                    v.tensor_scalar(p_bf[:, :], p_bf[:, :], rrec[:, :], None, ALU.mult)
                    v.wait("SP", p0 + 4)
                    v.tensor_copy(pt_sb[:, :], psT[pb][:, :])
                    v.wait("SP", p0 + 7)
                    v.tensor_copy(oevt[:, :], psO[pb][:, :])

                @blk.scalar
                def _(sr):
                    sc = ctx.s(sr)
                    sc.wait("SV", sv0 + 2)
                    sc.activation(p_bf[:, :], s_sb[:, :], AF.Exp, bias=negm[:, :],
                                  accum_out=rsum[:, :])
                    sc.activation(rrec2[:, :], rsum[:, :], AF.Ln)
                    sc.activation(rrec[:, :], rrec2[:, :], AF.Exp, scale=-1.0)

                @blk.tensor
                def _(tr):
                    te = ctx.t(tr)
                    te.matmul(psS[pb][:, :],
                              q_fm[hb:hb + 64, hc * T + qt * 128: hc * T + qt * 128 + 128],
                              k_fm[hb:hb + 64, hc * T + kw: hc * T + kw + 384],
                              start=True, stop=True)
                    te.wait("SV", sv0 + 3)
                    for c in range(3):
                        te.transpose(psT[pb][:, c * 128:(c + 1) * 128],
                                     p_bf[:, c * 128:(c + 1) * 128],
                                     consts["identb"][:, :])
                    te.wait("SV", sv0 + 4)
                    for c in range(3):
                        te.matmul(psO[pb][:, :],
                                  v_tm[:, (kc0 + c) * 1024 + h * 64: (kc0 + c) * 1024 + h * 64 + 64],
                                  pt_sb[:, c * 128:(c + 1) * 128],
                                  start=(c == 0), stop=(c == 2))

                @blk.sync
                def _(s):
                    s.wait_ge(ctx.sem("SV"), sv0 + 5)
                    s.dma_start(
                        ao_dram.ap()[hc * 128 + hb: hc * 128 + hb + 64,
                                     qt * 128:(qt + 1) * 128],
                        oevt[:, :]).then_inc(ctx.sem("dml"), 16)
                    s.wait_ge(ctx.sem("dml"), ctx.bump("dml", 16))
    ctx.pop()
    ctx.pop()

    ctx.push()
    ao_fm = ctx.sbuf("l3a_ao", [128, DC * T], BF16)
    emit_restore(ctx, ao_fm, ao_dram.ap(), DC, T, "l3a_ao")
    wo = load_weight_bf16(ctx, weights["L3_attn_woutT"].ap(), 1024, 1024, "l3a_wo")
    bo = load_raw(ctx, weights["L3_attn_boc"].ap(), 128, 8, "l3a_bo")
    emit_mm(ctx,
            lambda kc, mc: wo[:, kc * 1024 + mc * 128: kc * 1024 + mc * 128 + 128],
            lambda kc, tc: ao_fm[:, kc * T + tc * NT: kc * T + tc * NT + NT],
            KC=8, MC=8, TCn=TC4,
            epilogue=lambda eng, ps, mc, tc: eng.scalar_tensor_tensor(
                x[:, mc * T + tc * NT: mc * T + tc * NT + NT],
                ps, bo[:, mc:mc + 1],
                x[:, mc * T + tc * NT: mc * T + tc * NT + NT], ALU.add, ALU.add),
            name="l3a_wo")
    ctx.pop()
    ctx.pop()


def _fxp_epi(eng, ps_ap, out_ap, final_scale, tmp, tmpi, tmp2):
    eng.tensor_scalar(tmp, ps_ap, 4096.0, None, ALU.mult)
    eng.tensor_scalar(tmpi, tmp, -32767.0, 32767.0, ALU.max, ALU.min)
    eng.custom(FXTRUNC_SGN, out=tmp2, in0=tmp, in1=tmpi, s0=-32767.0, s1=32767.0)
    eng.tensor_scalar(out_ap, tmp2, final_scale, None, ALU.mult)


def emit_l1_attn(ctx, x, weights, consts):
    nc = ctx.nc
    ctx.push()
    ao_dram = nc.dram_tensor(f"l1a_aod_{ctx.uid()}", [D, T], BF16, kind="Internal")
    ln4096 = load_raw(ctx, weights["L1_attn_ln4096"].ap(), 1, 1, "l1a_l4")
    ctx.push()
    q_fm = ctx.sbuf("l1a_q", [128, DC * T], BF16)
    k_fm = ctx.sbuf("l1a_k", [128, DC * T], BF16)
    v_tm = ctx.sbuf("l1a_v", [128, 16 * 1024], BF16)
    ftmp = ctx.sbuf("l1a_ft", [128, NT])
    ftmpi = ctx.sbuf("l1a_fti", [128, NT], I32)
    ftmp2 = ctx.sbuf("l1a_ft2", [128, NT])

    TH = T // 2
    for th in range(2):
        ctx.push()
        xh = ctx.sbuf("l1a_xh", [128, DC * TH], BF16)
        for tci in range(2):
            tc = th * 2 + tci
            emit_ln_tc(ctx,
                       lambda c: x[:, c * T + tc * NT: c * T + tc * NT + NT],
                       lambda c: (xh[:, c * TH + tci * NT: c * TH + tci * NT + NT], None),
                       consts, tc, name="l1a_ln")

        ctx.push()
        wq = load_weight_bf16(ctx, weights["L1_attn_wqkvT"].ap()[:, 0:1024],
                              1024, 1024, "l1a_wq", nstg=1)
        emit_mm(ctx,
                lambda kc, mc: wq[:, kc * 1024 + mc * 128: kc * 1024 + mc * 128 + 128],
                lambda kc, tci: xh[:, kc * TH + tci * NT: kc * TH + tci * NT + NT],
                KC=8, MC=8, TCn=2,
                epilogue=lambda eng, ps, mc, tci, th=th: _fxp_epi(
                    eng, ps,
                    q_fm[:, mc * T + (th * 2 + tci) * NT: mc * T + (th * 2 + tci) * NT + NT],
                    0.125, ftmp[:, :], ftmpi[:, :], ftmp2[:, :]),
                name=f"l1a_qm{th}", psum_banks=3)
        ctx.pop()

        ctx.push()
        wk = load_weight_bf16(ctx, weights["L1_attn_wqkvT"].ap()[:, 1024:2048],
                              1024, 1024, "l1a_wk", nstg=1)
        emit_mm(ctx,
                lambda kc, mc: wk[:, kc * 1024 + mc * 128: kc * 1024 + mc * 128 + 128],
                lambda kc, tci: xh[:, kc * TH + tci * NT: kc * TH + tci * NT + NT],
                KC=8, MC=8, TCn=2,
                epilogue=lambda eng, ps, mc, tci, th=th: _fxp_epi(
                    eng, ps,
                    k_fm[:, mc * T + (th * 2 + tci) * NT: mc * T + (th * 2 + tci) * NT + NT],
                    2.0 ** -12, ftmp[:, :], ftmpi[:, :], ftmp2[:, :]),
                name=f"l1a_km{th}", psum_banks=3)
        ctx.pop()

        ctx.push()
        wv = load_weight_bf16(ctx, weights["L1_attn_wqkvT"].ap()[:, 2048:3072],
                              1024, 1024, "l1a_wv", nstg=1)
        emit_mm(ctx,
                lambda kc, mc: xh[:, kc * TH + mc * 128: kc * TH + mc * 128 + 128],
                lambda kc, nh: wv[:, kc * 1024 + nh * NT: kc * 1024 + nh * NT + NT],
                KC=8, MC=8, TCn=2,
                epilogue=lambda eng, ps, mc, nh, th=th: _fxp_epi(
                    eng, ps,
                    v_tm[:, (th * 8 + mc) * 1024 + nh * NT: (th * 8 + mc) * 1024 + nh * NT + NT],
                    2.0 ** -12, ftmp[:, :], ftmpi[:, :], ftmp2[:, :]),
                name=f"l1a_vm{th}", psum_banks=3)
        ctx.pop()
        ctx.pop()

    ctx.push()
    expS = [ctx.sbuf(f"l1a_es{s}", [128, 16 * NT], BF16) for s in range(2)]
    fint = [[ctx.sbuf(f"l1a_fi{s}{b}", [128, NT]) for b in range(2)]
            for s in range(2)]
    rint = ftmpi
    ntile = ftmp
    nint = ftmpi
    rb_sb = ftmp2
    oevt = expS[0][0:64, 0:NT]
    rrow = [fint[s][0][0:1, :] for s in range(2)]
    rrow2 = [fint[s][0][32:33, :] for s in range(2)]
    oft = ftmp[0:64, :]
    ofti = ftmpi[0:64, :]
    oft2 = ftmp2[0:64, :]
    psS = [ctx.psum("l1a_psS", [128, NT]) for _ in range(2)]
    psSum = [ctx.psum(f"l1a_psSum{s}", [1, NT]) for s in range(2)]
    psRB = [ctx.psum(f"l1a_psRB{s}", [128, NT]) for s in range(2)]
    psO = [ctx.psum(f"l1a_psO{s}", [64, NT]) for s in range(2)]

    for h in range(16):
        hb = (h % 2) * 64
        hc = h // 2
        for qp in range(2):
            with nc.Block(f"l1a_att_{h}_{qp}_{ctx.uid()}") as blk:
                p0 = ctx.val("SP")
                sv0 = ctx.val("SV")
                ss0 = ctx.val("SS")
                d0 = ctx.val("dml")

                @blk.vector
                def _(vr):
                    v = ctx.v(vr)
                    for s in range(2):
                        for kt in range(16):
                            v.wait("SP", p0 + 32 * s + kt + 1)
                            if kt >= 2:
                                v.wait("SS", ss0 + 18 * s + kt - 1)
                            v.tensor_scalar(rint[:, :], psS[kt % 2][:, :],
                                            -32767.0, 32767.0, ALU.max, ALU.min)
                            v.custom(FXTRUNC_SGN, out=fint[s][kt % 2][:, :],
                                     in0=psS[kt % 2][:, :], in1=rint[:, :],
                                     s0=-32767.0, s1=32767.0)
                    for s in range(2):
                        v.wait("SP", p0 + 65 + 17 * s)
                        v.tensor_copy(rb_sb[:, :], psRB[s][:, :])
                        for kt in range(16):
                            v.tensor_tensor(ntile[:, :],
                                            expS[s][:, kt * NT:(kt + 1) * NT],
                                            rb_sb[:, :], ALU.mult)
                            v.tensor_scalar(nint[:, :], ntile[:, :], 1.0,
                                            None, ALU.mult)
                            v.custom(FXTRUNC_POS,
                                     out=expS[s][:, kt * NT:(kt + 1) * NT],
                                     in0=ntile[:, :], in1=nint[:, :],
                                     s0=1.0, imm2=2.0 ** -12)
                    for s in range(2):
                        v.wait("SP", p0 + 98)
                        if s == 1:
                            v.raw().wait_ge(ctx.sem("dml"), d0 + 16)
                        _fxp_epi(v, psO[s][:, :], oevt, 2.0 ** -12,
                                 oft, ofti, oft2)

                @blk.scalar
                def _(sr):
                    sc = ctx.s(sr)
                    for s in range(2):
                        for kt in range(16):
                            sc.wait("SV", sv0 + 32 * s + 2 * kt + 2)
                            sc.activation(expS[s][:, kt * NT:(kt + 1) * NT],
                                          fint[s][kt % 2][:, :], AF.Exp,
                                          scale=2.0 ** -12)
                        sc.wait("SP", p0 + 32 * (s + 1))
                        sc.activation(rrow2[s], psSum[s][:, :], AF.Ln)
                        sc.activation(rrow[s], rrow2[s], AF.Exp,
                                      scale=-1.0, bias=ln4096[0:1, 0:1])

                @blk.tensor
                def _(tr, qp=qp):
                    te = ctx.t(tr)
                    for s in range(2):
                        qq = qp * 2 + s
                        for kt in range(16):
                            if s == 1 and kt < 2:
                                te.wait("SV", sv0 + 2 * (14 + kt) + 2)
                            elif kt >= 2:
                                te.wait("SV", sv0 + 32 * s + 2 * kt - 2)
                            te.matmul(psS[kt % 2][:, :],
                                      k_fm[hb:hb + 64, hc * T + kt * 128: hc * T + kt * 128 + 128],
                                      q_fm[hb:hb + 64, hc * T + qq * NT: hc * T + qq * NT + NT],
                                      start=True, stop=True)
                        for kt in range(16):
                            te.wait("SS", ss0 + 18 * s + kt + 1)
                            te.matmul(psSum[s][:, :], consts["onescolb"][:, :],
                                      expS[s][:, kt * NT:(kt + 1) * NT],
                                      start=(kt == 0), stop=(kt == 15))
                    for s in range(2):
                        te.wait("SS", ss0 + 18 * (s + 1))
                        te.matmul(psRB[s][:, :], consts["onesrow"][:, :],
                                  rrow[s], start=True, stop=True)
                        for kt in range(16):
                            te.wait("SV", sv0 + 65 + 49 * s + 3 * kt + 3)
                            te.matmul(psO[s][:, :],
                                      v_tm[:, kt * 1024 + h * 64: kt * 1024 + h * 64 + 64],
                                      expS[s][:, kt * NT:(kt + 1) * NT],
                                      start=(kt == 0), stop=(kt == 15))

                @blk.sync
                def _(s_, qp=qp):
                    for s in range(2):
                        qq = qp * 2 + s
                        s_.wait_ge(ctx.sem("SV"), sv0 + 166 + 4 * s)
                        s_.dma_start(
                            ao_dram.ap()[hc * 128 + hb: hc * 128 + hb + 64,
                                         qq * NT:(qq + 1) * NT],
                            oevt).then_inc(ctx.sem("dml"), 16)
                        ctx.bump("dml", 16)
                    s_.wait_ge(ctx.sem("dml"), ctx.val("dml"))
    ctx.pop()
    ctx.pop()

    ctx.push()
    ao_fm = ctx.sbuf("l1a_ao", [128, DC * T], BF16)
    emit_restore(ctx, ao_fm, ao_dram.ap(), DC, T, "l1a_ao")
    wp = load_weight_bf16(ctx, weights["L1_attn_wprojT"].ap(), 1024, 1024, "l1a_wp")
    bp = load_raw(ctx, weights["L1_attn_bpc"].ap(), 128, 8, "l1a_bp")
    ftmp = ctx.sbuf("l1a_pt", [128, NT])
    ftmpi = ctx.sbuf("l1a_pti", [128, NT], I32)
    ftmp2 = ctx.sbuf("l1a_pt2", [128, NT])

    def epi_p(eng, ps, mc, tc):
        eng.tensor_scalar(ftmp[:, :], ps, bp[:, mc:mc + 1], 4096.0, ALU.add, ALU.mult)
        eng.tensor_scalar(ftmpi[:, :], ftmp[:, :], -32767.0, 32767.0, ALU.max, ALU.min)
        eng.custom(FXTRUNC_SGN, out=ftmp2[:, :], in0=ftmp[:, :], in1=ftmpi[:, :],
                   s0=-32767.0, s1=32767.0)
        xs = x[:, mc * T + tc * NT: mc * T + tc * NT + NT]
        eng.scalar_tensor_tensor(xs, ftmp2[:, :], 2.0 ** -12, xs, ALU.mult, ALU.add)

    emit_mm(ctx,
            lambda kc, mc: wp[:, kc * 1024 + mc * 128: kc * 1024 + mc * 128 + 128],
            lambda kc, tc: ao_fm[:, kc * T + tc * NT: kc * T + tc * NT + NT],
            KC=8, MC=8, TCn=TC4, epilogue=epi_p, name="l1a_wp", psum_banks=3)
    ctx.pop()
    ctx.pop()


def emit_l0_ffn(ctx, x, weights, consts):
    nc = ctx.nc
    ctx.push()
    mask = ctx.sbuf("l0f_mk", [64, T], BF16)
    inner = ctx.sbuf("l0f_in", [128, 32 * 128], BF16)
    inner2 = ctx.sbuf("l0f_in2", [128, 32 * 128], BF16)
    rp = load_raw(ctx, weights["L0_ffn_rp"].ap(), 64, 32 * 128, "l0f_rp", BF16)
    besf = load_raw(ctx, weights["L0_ffn_besf"].ap(), 128, 32, "l0f_besf")

    ctx.push()
    xh = ctx.sbuf("l0f_xh", [128, DC * T], BF16)
    emit_ln(ctx, x, xh, consts, name="l0f_ln")

    ctx.push()
    logits = ctx.sbuf("l0f_lg", [64, T])
    gmax = ctx.sbuf("l0f_gm", [64, 128])
    ctx.push()
    ctl = load_weight_bf16(ctx, weights["L0_ffn_ctlR"].ap(), 1024, 64, "l0f_ctl")
    tieT = load_raw(ctx, weights["L0_ffn_tieT"].ap(), 64, T, "l0f_tie")
    emit_mm(ctx, lambda kc, mc: ctl[:, kc * 64:(kc + 1) * 64],
            lambda kc, tc: xh[:, kc * T + tc * NT: kc * T + tc * NT + NT],
            KC=8, MC=1, TCn=TC4,
            epilogue=lambda eng, ps, mc, tc: eng.tensor_tensor(
                logits[:, tc * NT:(tc + 1) * NT], ps,
                tieT[:, tc * NT:(tc + 1) * NT], ALU.add),
            name="l0f_lgmm", psum_banks=2, Mp=64)
    ctx.pop()

    with nc.Block(f"l0f_mask_{ctx.uid()}") as blk:
        @blk.vector
        def _(vr):
            v = ctx.v(vr)
            lg3 = logits[:, :].rearrange("p (a b) -> p a b", b=16)
            v.reduce_max(gmax[:, :], lg3, AX.X)
            gb = bass.AP(gmax, 0, [[128, 64], [1, 128], [0, 16]])
            v.tensor_tensor(mask[:, :].rearrange("p (a b) -> p a b", b=16),
                            lg3, gb, ALU.is_equal)
    ctx.pop()

    ctx.push()
    f1R = load_weight_bf16(ctx, weights["L0_ffn_f1R"].ap(), 1024, 4096, "l0f_f1",
                           nstg=1)
    mrep = ctx.sbuf("l0f_mr", [128, NT], BF16)
    ptile = ctx.sbuf("l0f_pt", [128, NT])
    psG = [ctx.psum("l0f_psG", [128, NT]) for _ in range(2)]
    psM = ctx.psum("l0f_psM", [128, NT])
    for tc in range(TC4):
        for ch in range(32):
            with nc.Block(f"l0f_g_{tc}_{ch}_{ctx.uid()}") as blk:
                p0 = ctx.val("SP")
                sv0 = ctx.val("SV")

                @blk.vector
                def _(vr):
                    v = ctx.v(vr)
                    v.wait("SP", p0 + 1)
                    v.tensor_copy(mrep[:, :], psM[:, :])
                    v.wait("SP", p0 + 9)
                    v.tensor_tensor(ptile[:, :], psG[ch % 2][:, :], mrep[:, :],
                                    ALU.mult)
                    with nc.allow_low_precision(reason="bf16 store of fp32 group-sum"):
                        v.reduce_sum(inner[:, ch * 128 + tc * 32: ch * 128 + tc * 32 + 32],
                                     ptile[:, :].rearrange("p (a b) -> p a b", b=16), AX.X)

                @blk.tensor
                def _(tr):
                    te = ctx.t(tr)
                    te.matmul(psM[:, :], rp[:, ch * 128:(ch + 1) * 128],
                              mask[:, tc * NT:(tc + 1) * NT], start=True, stop=True)
                    for kc in range(8):
                        te.matmul(psG[ch % 2][:, :],
                                  f1R[:, kc * 4096 + ch * 128: kc * 4096 + ch * 128 + 128],
                                  xh[:, kc * T + tc * NT: kc * T + tc * NT + NT],
                                  start=(kc == 0), stop=(kc == 7))

    with nc.Block(f"l0f_relu_{ctx.uid()}") as blk:
        @blk.scalar
        def _(sr):
            sc = ctx.s(sr)
            for ch in range(32):
                sc.activation(inner2[:, ch * 128:(ch + 1) * 128],
                              inner[:, ch * 128:(ch + 1) * 128], AF.Relu,
                              bias=besf[:, ch:ch + 1])
    ctx.pop()
    ctx.pop()

    ctx.push()
    f2R = load_weight_bf16(ctx, weights["L0_ffn_f2R"].ap(), 4096, 1024, "l0f_f2",
                           nstg=1)
    V = ctx.sbuf("l0f_V", [128, 32 * NT], BF16)
    mrep = ctx.sbuf("l0f_mr2", [128, NT], BF16)
    psM = ctx.psum("l0f_psM2", [128, NT])
    for tc in range(TC4):
        for ch in range(32):
            with nc.Block(f"l0f_v_{tc}_{ch}_{ctx.uid()}") as blk:
                p0 = ctx.val("SP")

                @blk.vector
                def _(vr):
                    v = ctx.v(vr)
                    v.wait("SP", p0 + 1)
                    v.tensor_copy(mrep[:, :], psM[:, :])
                    iexp = bass.AP(inner2, ch * 128 + tc * 32,
                                   [[32 * 128, 128], [1, 32], [0, 16]])
                    v.tensor_tensor(V[:, ch * NT:(ch + 1) * NT].rearrange(
                        "p (a b) -> p a b", b=16), iexp, mrep[:, :].rearrange(
                        "p (a b) -> p a b", b=16), ALU.mult)

                @blk.tensor
                def _(tr):
                    te = ctx.t(tr)
                    te.matmul(psM[:, :], rp[:, ch * 128:(ch + 1) * 128],
                              mask[:, tc * NT:(tc + 1) * NT], start=True, stop=True)

        emit_mm(ctx,
                lambda kc, mc: f2R[:, kc * 1024 + mc * 128: kc * 1024 + mc * 128 + 128],
                lambda kc, _tc: V[:, kc * NT:(kc + 1) * NT],
                KC=32, MC=8, TCn=1,
                epilogue=lambda eng, ps, mc, _tc, tc=tc: eng.tensor_tensor(
                    x[:, mc * T + tc * NT: mc * T + tc * NT + NT],
                    x[:, mc * T + tc * NT: mc * T + tc * NT + NT], ps, ALU.add),
                name=f"l0f_f2_{tc}")
    ctx.pop()
    ctx.pop()


def emit_l0_attn(ctx, x, weights, consts):
    nc = ctx.nc
    ctx.push()
    rg = load_raw(ctx, weights["L0_rg"].ap(), 128, 16 * 128, "l0a_rg", BF16)
    bd = load_raw(ctx, weights["L0_bd"].ap(), 128, 16 * 128, "l0a_bd", BF16)
    r2 = load_raw(ctx, weights["L0_r2"].ap(), 16, 128, "l0a_r2", BF16)
    alog = load_raw(ctx, weights["L0_attn_alogpp"].ap(), 128, 128, "l0a_al")
    app = ctx.sbuf("l0a_app", [128, 128])
    bdl = load_raw(ctx, weights["L0_attn_bdlc"].ap(), 64, 1, "l0a_bdl")
    bbm = load_raw(ctx, weights["L0_attn_bbmc"].ap(), 16, 1, "l0a_bbm")
    bcm = load_raw(ctx, weights["L0_attn_bcmc"].ap(), 16, 1, "l0a_bcm")
    bup = load_raw(ctx, weights["L0_attn_bupc"].ap(), 128, 8, "l0a_bup")

    with nc.Block(f"l0a_app_{ctx.uid()}") as blk:
        @blk.scalar
        def _(sr):
            sc = ctx.s(sr)
            sc.activation(app[:, :], alog[:, :], AF.Exp)
        @blk.vector
        def _(vr):
            v = ctx.v(vr)
            v.wait("SS", ctx.val("SS"))
            v.tensor_scalar(app[:, :], app[:, :], -1.0, None, ALU.mult)

    delta_bf = ctx.sbuf("l0a_delta", [128, DC * T], BF16)
    dx_bf = ctx.sbuf("l0a_dx", [128, DC * T], BF16)
    bm_bf = ctx.sbuf("l0a_bm", [16, T], BF16)
    cm_bf = ctx.sbuf("l0a_cm", [16, T], BF16)
    ctx.push()
    xh = ctx.sbuf("l0a_xh", [128, DC * T], BF16)
    emit_ln(ctx, x, xh, consts, name="l0a_ln")

    dlow_bf = ctx.sbuf("l0a_dlow", [64, T], BF16)
    praw = delta_bf
    ctx.push()
    wdbc = load_weight_bf16(ctx, weights["L0_attn_wdbcT"].ap(), 1024, 96, "l0a_wdbc")

    emit_mm(ctx, lambda kc, mc: wdbc[:, kc * 96: kc * 96 + 64],
            lambda kc, tc: xh[:, kc * T + tc * NT: kc * T + tc * NT + NT],
            KC=8, MC=1, TCn=TC4,
            epilogue=lambda eng, ps, mc, tc: eng.tensor_scalar(
                dlow_bf[:, tc * NT:(tc + 1) * NT], ps,
                bdl[:, 0:1], None, ALU.add),
            name="l0a_dbc1", Mp=64)

    emit_mm(ctx, lambda kc, mc: wdbc[:, kc * 96 + 64: kc * 96 + 80],
            lambda kc, tc: xh[:, kc * T + tc * NT: kc * T + tc * NT + NT],
            KC=8, MC=1, TCn=TC4,
            epilogue=lambda eng, ps, mc, tc: eng.tensor_scalar(
                bm_bf[:, tc * NT:(tc + 1) * NT], ps,
                bbm[:, 0:1], None, ALU.add),
            name="l0a_dbc2", Mp=16)

    emit_mm(ctx, lambda kc, mc: wdbc[:, kc * 96 + 80: kc * 96 + 96],
            lambda kc, tc: xh[:, kc * T + tc * NT: kc * T + tc * NT + NT],
            KC=8, MC=1, TCn=TC4,
            epilogue=lambda eng, ps, mc, tc: eng.tensor_scalar(
                cm_bf[:, tc * NT:(tc + 1) * NT], ps,
                bcm[:, 0:1], None, ALU.add),
            name="l0a_dbc3", Mp=16)
    ctx.pop()

    ctx.push()
    wup = ctx.sbuf("l0a_wup", [64, 1024], BF16)
    ctx.push()
    wup_f = load_raw(ctx, weights["L0_attn_wupT"].ap(), 64, 1024, "l0a_wupf")
    with nc.Block(f"l0a_wupc_{ctx.uid()}") as blk:
        @blk.vector
        def _(vr):
            ctx.v(vr).tensor_copy(wup[:, :], wup_f[:, :])
    ctx.pop()
    emit_mm(ctx, lambda kc, mc: wup[:, mc * 128:(mc + 1) * 128],
            lambda kc, tc: dlow_bf[:, tc * NT:(tc + 1) * NT],
            KC=1, MC=8, TCn=TC4,
            epilogue=lambda eng, ps, mc, tc: eng.tensor_scalar(
                praw[:, mc * T + tc * NT: mc * T + tc * NT + NT], ps,
                bup[:, mc:mc + 1], None, ALU.add),
            name="l0a_dup")
    ctx.pop()

    ctx.push()
    spt = ctx.sbuf("l0a_spt", [128, NT])
    spe = ctx.sbuf("l0a_spe", [128, NT])
    spl = ctx.sbuf("l0a_spl", [128, NT])
    spr = ctx.sbuf("l0a_spr", [128, NT])
    for i in range(DC * TC4):
        mc, tc = divmod(i, TC4)
        sl = slice(mc * T + tc * NT, mc * T + tc * NT + NT)
        with nc.Block(f"l0a_sp_{i}_{ctx.uid()}") as blk:
            sv0 = ctx.val("SV")
            ss0 = ctx.val("SS")

            @blk.vector
            def _(vr):
                v = ctx.v(vr)
                v.tensor_scalar(spr[:, :], praw[:, sl], 0.0, None, ALU.max)
                v.scalar_tensor_tensor(spt[:, :], spr[:, :], 2.0, praw[:, sl],
                                       ALU.mult, ALU.subtract)
                v.wait("SS", ss0 + 2)
                v.tensor_tensor(delta_bf[:, sl], spr[:, :], spl[:, :], ALU.add)
                v.tensor_tensor(dx_bf[:, sl], delta_bf[:, sl], xh[:, sl], ALU.mult)

            @blk.scalar
            def _(sr):
                sc = ctx.s(sr)
                sc.wait("SV", sv0 + 2)
                sc.activation(spe[:, :], spt[:, :], AF.Exp, scale=-1.0)
                sc.activation(spl[:, :], spe[:, :], AF.Ln, bias=consts["onescol"][:, :])
    ctx.pop()
    ctx.pop()

    bmpp = ctx.sbuf("l0a_bmpp", [128, T])
    cmpp = ctx.sbuf("l0a_cmpp", [128, T])
    ctx.push()
    pspp = ctx.psum("l0a_pspp", [128, NT])
    with nc.Block(f"l0a_pp_{ctx.uid()}") as blk:
        p0 = ctx.val("SP")
        sv0 = ctx.val("SV")

        @blk.vector
        def _(vr):
            v = ctx.v(vr)
            for i in range(8):
                v.wait("SP", p0 + i + 1)
                dst = (bmpp if i < 4 else cmpp)
                tc = i % 4
                v.tensor_copy(dst[:, tc * NT:(tc + 1) * NT], pspp[:, :])

        @blk.tensor
        def _(tr):
            te = ctx.t(tr)
            for i in range(8):
                src_t = (bm_bf if i < 4 else cm_bf)
                tc = i % 4
                if i >= 1:
                    te.wait("SV", sv0 + i)
                te.matmul(pspp[:, :], r2[:, :], src_t[:, tc * NT:(tc + 1) * NT],
                          start=True, stop=True)
    ctx.pop()

    ctx.push()
    atile2 = [ctx.sbuf("l0a_a0", [128, T]), ctx.sbuf("l0a_a1", [128, T])]
    btile2 = [ctx.sbuf("l0a_b0", [128, T]), ctx.sbuf("l0a_b1", [128, T])]
    htile = ctx.sbuf("l0a_h", [128, T], BF16)
    ptile2 = [ctx.sbuf("l0a_p0", [128, T], BF16), ctx.sbuf("l0a_p1", [128, T], BF16)]
    psdr2 = [ctx.psum(f"l0a_psdr{i}", [128, NT]) for i in range(2)]
    psdx2 = [ctx.psum(f"l0a_psdx{i}", [128, NT]) for i in range(2)]
    psy = [ctx.psum("l0a_psy", [128, NT]) for _ in range(4)]
    for dchunk in range(DC):
        with nc.Block(f"l0a_scan_{dchunk}_{ctx.uid()}") as blk:
            p0 = ctx.val("SP")
            sv0 = ctx.val("SV")
            ss0 = ctx.val("SS")

            def base(g):
                return 0 if g == 0 else 8 + 12 * (g - 1)

            def psdr_done(g, tc):
                return p0 + base(g) + 2 * tc + 1

            def psdx_done(g, tc):
                return p0 + base(g) + 2 * tc + 2

            def psy_done(g, tc):
                return p0 + (base(g + 1) + 8 + tc + 1 if g <= 14 else 188 + tc + 1)

            @blk.scalar
            def _(sr, dchunk=dchunk):
                sc = ctx.s(sr)
                for g in range(16):
                    G = dchunk * 16 + g
                    for tc in range(TC4):
                        sc.wait("SP", psdr_done(g, tc))
                        if g >= 2 and tc == 0:
                            sc.wait("SV", sv0 + 6 * g - 7)
                        sc.activation(atile2[g % 2][:, tc * NT:(tc + 1) * NT],
                                      psdr2[g % 2][:, :], AF.Exp,
                                      scale=app[:, G:G + 1])

            @blk.vector
            def _(vr):
                v = ctx.v(vr)
                for g in range(16):
                    for tc in range(TC4):
                        v.wait("SP", psdx_done(g, tc))
                        v.tensor_tensor(btile2[g % 2][:, tc * NT:(tc + 1) * NT],
                                        psdx2[g % 2][:, :],
                                        bmpp[:, tc * NT:(tc + 1) * NT], ALU.mult)
                    v.wait("SS", ss0 + 4 * g + 4)
                    v.tensor_tensor_scan(htile[:, :], atile2[g % 2][:, :],
                                         btile2[g % 2][:, :], 0.0, ALU.mult, ALU.add)
                    if g >= 2:
                        v.wait("SP", psy_done(g - 2, 3))
                    v.tensor_tensor(ptile2[g % 2][:, :], htile[:, :], cmpp[:, :],
                                    ALU.mult)

            @blk.tensor
            def _(tr):
                te = ctx.t(tr)

                def psdrx_mm(g):
                    for tc in range(TC4):
                        if tc >= 1:
                            te.wait("SS", ss0 + 4 * g + tc)
                        elif g >= 2:
                            te.wait("SS", ss0 + 4 * (g - 2) + 4)
                        te.matmul(psdr2[g % 2][:, :], rg[:, g * 128:(g + 1) * 128],
                                  delta_bf[:, dchunk * T + tc * NT: dchunk * T + tc * NT + NT],
                                  start=True, stop=True)
                        if tc >= 1:
                            te.wait("SV", sv0 + 6 * g + tc)
                        elif g >= 2:
                            te.wait("SV", sv0 + 6 * (g - 2) + 4)
                        te.matmul(psdx2[g % 2][:, :], rg[:, g * 128:(g + 1) * 128],
                                  dx_bf[:, dchunk * T + tc * NT: dchunk * T + tc * NT + NT],
                                  start=True, stop=True)

                def psy_mm(g):
                    te.wait("SV", sv0 + 6 * g + 6)
                    for tc in range(TC4):
                        te.matmul(psy[tc][:, :], bd[:, g * 128:(g + 1) * 128],
                                  ptile2[g % 2][:, tc * NT:(tc + 1) * NT],
                                  start=(g == 0), stop=(g == 15))

                psdrx_mm(0)
                for g in range(1, 16):
                    psdrx_mm(g)
                    psy_mm(g - 1)
                psy_mm(15)

        with nc.Block(f"l0a_y_{dchunk}_{ctx.uid()}") as blk:
            @blk.vector
            def _(vr):
                v = ctx.v(vr)
                for tc in range(TC4):
                    xs = x[:, dchunk * T + tc * NT: dchunk * T + tc * NT + NT]
                    v.tensor_tensor(xs, xs, psy[tc][:, :], ALU.add)
    ctx.pop()
    ctx.pop()


def emit_l2_attn(ctx, x, weights, consts):
    nc = ctx.nc
    DIC = 16
    PW = T + 3
    ctx.push()
    zs_dram = nc.dram_tensor(f"l2a_zsd_{ctx.uid()}", [DIC * 128, T], BF16,
                             kind="Internal")
    xsp_dram = nc.dram_tensor(f"l2a_xspd_{ctx.uid()}", [DIC * 128, PW], BF16,
                              kind="Internal")
    xs_dram = nc.dram_tensor(f"l2a_xsd_{ctx.uid()}", [DIC * 128, T], BF16,
                             kind="Internal")
    y1_dram = nc.dram_tensor(f"l2a_y1d_{ctx.uid()}", [DIC * 128, T], BF16,
                             kind="Internal")

    convw = load_raw(ctx, weights["L2_attn_convwpp"].ap(), 128, 64, "l2a_cw")
    convb = load_raw(ctx, weights["L2_attn_convbc"].ap(), 128, 16, "l2a_cb")
    bdt = load_raw(ctx, weights["L2_attn_bdtc"].ap(), 128, 16, "l2a_bdt")
    dcol = load_raw(ctx, weights["L2_attn_dcol"].ap(), 128, 16, "l2a_dcol")
    identb = consts["identb"]
    bs_bf = ctx.sbuf("l2a_bs", [16, T], BF16)

    ctx.push()
    xh = ctx.sbuf("l2a_xh", [128, DC * T], BF16)
    emit_ln(ctx, x, xh, consts, name="l2a_ln")

    ctx.push()
    zs = ctx.sbuf("l2a_zsil", [128, DIC * T], BF16)
    ctx.push()
    wz = load_weight_bf16(ctx, weights["L2_attn_winz"].ap(), 1024, 2048, "l2a_wz", nstg=1)
    emit_mm(ctx,
            lambda kc, mc: wz[:, kc * 2048 + mc * 128: kc * 2048 + (mc + 1) * 128],
            lambda kc, tc: xh[:, kc * T + tc * NT: kc * T + tc * NT + NT],
            KC=8, MC=DIC, TCn=TC4,
            epilogue=lambda eng, ps, mc, tc: eng.activation(
                zs[:, mc * T + tc * NT: mc * T + tc * NT + NT], ps, AF.Silu),
            name="l2a_zmm", epi_engine="scalar")
    ctx.pop()
    emit_spill(ctx, zs, zs_dram.ap(), DIC, T, "l2a_zs")
    ctx.pop()

    ctx.push()
    wxi = load_weight_bf16(ctx, weights["L2_attn_winx"].ap(), 1024, 2048, "l2a_wxi", nstg=1)
    xpb = ctx.sbuf("l2a_xpb", [128, PW], BF16)
    for mc in range(DIC):
        with nc.Block(f"l2a_xpad_{mc}_{ctx.uid()}") as blk:
            @blk.vector
            def _(vr):
                ctx.v(vr).memset(xpb[:, 0:3], 0.0)
        emit_mm(ctx,
                lambda kc, _mc, mc=mc: wxi[:, kc * 2048 + mc * 128: kc * 2048 + (mc + 1) * 128],
                lambda kc, tc: xh[:, kc * T + tc * NT: kc * T + tc * NT + NT],
                KC=8, MC=1, TCn=TC4,
                epilogue=lambda eng, ps, _mc, tc: eng.tensor_copy(
                    xpb[:, 3 + tc * NT: 3 + tc * NT + NT], ps),
                name=f"l2a_xmm{mc}")
        emit_spill(ctx, xpb, xsp_dram.ap()[mc * 128:(mc + 1) * 128, :], 1, PW,
                   f"l2a_xsp{mc}")
    ctx.pop()
    ctx.pop()

    dt_dram = nc.dram_tensor(f"l2a_dtd_{ctx.uid()}", [DIC * 128, T], BF16,
                             kind="Internal")
    ctx.push()
    xs = ctx.sbuf("l2a_xs", [128, DIC * T], BF16)
    xpb2 = ctx.sbuf("l2a_xpb2", [128, PW], BF16)
    diag4 = ctx.sbuf("l2a_diag4", [128, 4 * 128], BF16)
    psc = ctx.psum("l2a_psc", [128, NT])
    for mc in range(DIC):
        emit_restore(ctx, xpb2, xsp_dram.ap()[mc * 128:(mc + 1) * 128, :], 1, PW,
                     f"l2a_xrs{mc}")
        with nc.Block(f"l2a_conv_{mc}_{ctx.uid()}") as blk:
            p0 = ctx.val("SP")
            sv0 = ctx.val("SV")
            ss0 = ctx.val("SS")

            @blk.vector
            def _(vr):
                v = ctx.v(vr)
                for k in range(4):
                    v.tensor_scalar(diag4[:, k * 128:(k + 1) * 128], identb[:, :],
                                    convw[:, mc * 4 + k: mc * 4 + k + 1],
                                    None, ALU.mult)

            @blk.scalar
            def _(sr):
                sc = ctx.s(sr)
                for tc in range(TC4):
                    sc.wait("SP", p0 + 4 * (tc + 1))
                    sc.activation(xs[:, mc * T + tc * NT: mc * T + tc * NT + NT],
                                  psc[:, :], AF.Silu, bias=convb[:, mc:mc + 1])

            @blk.tensor
            def _(tr):
                te = ctx.t(tr)
                te.wait("SV", sv0 + 4)
                for tc in range(TC4):
                    if tc >= 1:
                        te.wait("SS", ss0 + tc)
                    for k in range(4):
                        te.matmul(psc[:, :], diag4[:, k * 128:(k + 1) * 128],
                                  xpb2[:, k + tc * NT: k + tc * NT + NT],
                                  start=(k == 0), stop=(k == 3))

    for mq in range(4):
        ctx.push()
        wq = load_weight_bf16(ctx, weights[f"L2_attn_wdtq{mq}"].ap(), 2048, 512,
                              f"l2a_wdt{mq}")
        dtq = ctx.sbuf(f"l2a_dtq{mq}", [128, 4 * T], BF16)
        emit_mm(ctx,
                lambda kc, mc, mq=mq: wq[:, kc * 512 + mc * 128: kc * 512 + (mc + 1) * 128],
                lambda kc, tc: xs[:, kc * T + tc * NT: kc * T + tc * NT + NT],
                KC=DIC, MC=4, TCn=TC4,
                epilogue=lambda eng, ps, mc, tc, mq=mq: eng.tensor_scalar(
                    dtq[:, mc * T + tc * NT: mc * T + tc * NT + NT],
                    ps, bdt[:, mq * 4 + mc: mq * 4 + mc + 1], None, ALU.add),
                name=f"l2a_dtm{mq}")
        spt = ctx.sbuf(f"l2a_spt{mq}", [128, NT])
        spe = ctx.sbuf(f"l2a_spe{mq}", [128, NT])
        spl = ctx.sbuf(f"l2a_spl{mq}", [128, NT])
        spr = ctx.sbuf(f"l2a_spr{mq}", [128, NT])
        for i in range(4 * TC4):
            mc, tc = divmod(i, TC4)
            sl = slice(mc * T + tc * NT, mc * T + tc * NT + NT)
            with nc.Block(f"l2a_sp_{mq}_{i}_{ctx.uid()}") as blk:
                sv0 = ctx.val("SV")
                ss0 = ctx.val("SS")

                @blk.vector
                def _(vr):
                    v = ctx.v(vr)
                    v.tensor_scalar(spr[:, :], dtq[:, sl], 0.0, None, ALU.max)
                    v.scalar_tensor_tensor(spt[:, :], spr[:, :], 2.0, dtq[:, sl],
                                           ALU.mult, ALU.subtract)
                    v.wait("SS", ss0 + 2)
                    v.tensor_tensor(dtq[:, sl], spr[:, :], spl[:, :], ALU.add)

                @blk.scalar
                def _(sr):
                    sc = ctx.s(sr)
                    sc.wait("SV", sv0 + 2)
                    sc.activation(spe[:, :], spt[:, :], AF.Exp, scale=-1.0)
                    sc.activation(spl[:, :], spe[:, :], AF.Ln,
                                  bias=consts["onescol"][:, :])
        emit_spill(ctx, dtq, dt_dram.ap()[mq * 512:(mq + 1) * 512, :], 4, T,
                   f"l2a_dtsp{mq}")
        ctx.pop()

    ctx.push()
    wx = load_weight_bf16(ctx, weights["L2_attn_wx"].ap(), 2048, 16, "l2a_wxs")
    emit_mm(ctx, lambda kc, mc: wx[:, kc * 16:(kc + 1) * 16],
            lambda kc, tc: xs[:, kc * T + tc * NT: kc * T + tc * NT + NT],
            KC=DIC, MC=1, TCn=TC4,
            epilogue=lambda eng, ps, mc, tc: eng.tensor_copy(
                bs_bf[:, tc * NT:(tc + 1) * NT], ps),
            name="l2a_bsm", Mp=16)
    ctx.pop()
    emit_spill(ctx, xs, xs_dram.ap(), DIC, T, "l2a_xs")
    ctx.pop()

    ctx.push()
    rg = load_raw(ctx, weights["L0_rg"].ap(), 128, 16 * 128, "l2a_rg", BF16)
    bd = load_raw(ctx, weights["L0_bd"].ap(), 128, 16 * 128, "l2a_bd", BF16)
    r2 = load_raw(ctx, weights["L0_r2"].ap(), 16, 128, "l2a_r2", BF16)
    alog = load_raw(ctx, weights["L2_attn_alogpp"].ap(), 128, 256, "l2a_al")
    app = ctx.sbuf("l2a_app", [128, 256])
    with nc.Block(f"l2a_app_{ctx.uid()}") as blk:
        @blk.scalar
        def _(sr):
            ctx.s(sr).activation(app[:, :], alog[:, :], AF.Exp)
        @blk.vector
        def _(vr):
            v = ctx.v(vr)
            v.wait("SS", ctx.val("SS"))
            v.tensor_scalar(app[:, :], app[:, :], -1.0, None, ALU.mult)

    bspp = ctx.sbuf("l2a_bspp", [128, T])
    ctx.push()
    pspp = ctx.psum("l2a_pspp", [128, NT])
    with nc.Block(f"l2a_pp_{ctx.uid()}") as blk:
        p0 = ctx.val("SP")
        sv0 = ctx.val("SV")

        @blk.vector
        def _(vr):
            v = ctx.v(vr)
            for i in range(TC4):
                v.wait("SP", p0 + i + 1)
                v.tensor_copy(bspp[:, i * NT:(i + 1) * NT], pspp[:, :])

        @blk.tensor
        def _(tr):
            te = ctx.t(tr)
            for i in range(TC4):
                if i >= 1:
                    te.wait("SV", sv0 + i)
                te.matmul(pspp[:, :], r2[:, :], bs_bf[:, i * NT:(i + 1) * NT],
                          start=True, stop=True)
    ctx.pop()

    atile2 = [ctx.sbuf("l2a_a0", [128, T]), ctx.sbuf("l2a_a1", [128, T])]
    htile2 = [ctx.sbuf("l2a_h0", [128, T], BF16), ctx.sbuf("l2a_h1", [128, T], BF16)]
    y1c = ctx.sbuf("l2a_y1c", [128, T], BF16)
    dtc = ctx.sbuf("l2a_dtc", [128, T], BF16)
    psdr4 = [ctx.psum(f"l2a_psdr{i}", [128, NT]) for i in range(4)]
    psy = [ctx.psum("l2a_psy", [128, NT]) for _ in range(4)]
    for dchunk in range(DIC):
        emit_restore(ctx, dtc, dt_dram.ap()[dchunk * 128:(dchunk + 1) * 128, :],
                     1, T, f"l2a_dtr{dchunk}")
        with nc.Block(f"l2a_scan_{dchunk}_{ctx.uid()}") as blk:
            p0 = ctx.val("SP")
            sv0 = ctx.val("SV")
            ss0 = ctx.val("SS")

            def psdr_done(g, tc):
                return p0 + (tc + 1 if g == 0 else 8 * g - 4 + tc + 1)

            def psy_done(g, tc):
                return p0 + (8 * (g + 1) + tc + 1 if g <= 14 else 124 + tc + 1)

            @blk.scalar
            def _(sr, dchunk=dchunk):
                sc = ctx.s(sr)
                for g in range(16):
                    G = dchunk * 16 + g
                    for tc in range(TC4):
                        sc.wait("SP", psdr_done(g, tc))
                        if g >= 2 and tc == 0:
                            sc.wait("SV", sv0 + g - 1)
                        sc.activation(atile2[g % 2][:, tc * NT:(tc + 1) * NT],
                                      psdr4[2 * (g % 2) + (tc % 2)][:, :],
                                      AF.Exp, scale=app[:, G:G + 1])

            @blk.vector
            def _(vr):
                v = ctx.v(vr)
                for g in range(16):
                    v.wait("SS", ss0 + 4 * g + 4)
                    if g >= 2:
                        v.wait("SP", psy_done(g - 2, 3))
                    v.tensor_tensor_scan(htile2[g % 2][:, :], atile2[g % 2][:, :],
                                         bspp[:, :], 0.0, ALU.mult, ALU.add)

            @blk.tensor
            def _(tr):
                te = ctx.t(tr)

                def psdr_mm(g):
                    for tc in range(TC4):
                        if tc >= 2:
                            te.wait("SS", ss0 + 4 * g + tc - 1)
                        elif g >= 2:
                            te.wait("SS", ss0 + 4 * (g - 2) + tc + 3)
                        te.matmul(psdr4[2 * (g % 2) + (tc % 2)][:, :],
                                  rg[:, g * 128:(g + 1) * 128],
                                  dtc[:, tc * NT: tc * NT + NT],
                                  start=True, stop=True)

                def psy_mm(g):
                    te.wait("SV", sv0 + g + 1)
                    for tc in range(TC4):
                        te.matmul(psy[tc][:, :], bd[:, g * 128:(g + 1) * 128],
                                  htile2[g % 2][:, tc * NT:(tc + 1) * NT],
                                  start=(g == 0), stop=(g == 15))

                psdr_mm(0)
                for g in range(1, 16):
                    psdr_mm(g)
                    psy_mm(g - 1)
                psy_mm(15)

        with nc.Block(f"l2a_y_{dchunk}_{ctx.uid()}") as blk:
            @blk.vector
            def _(vr):
                v = ctx.v(vr)
                for tc in range(TC4):
                    v.tensor_copy(y1c[:, tc * NT:(tc + 1) * NT], psy[tc][:, :])
        emit_spill(ctx, y1c, y1_dram.ap()[dchunk * 128:(dchunk + 1) * 128, :], 1, T,
                   f"l2a_y1{dchunk}")
    ctx.pop()

    ctx.push()
    wout = load_weight_bf16(ctx, weights["L2_attn_wout"].ap(), 2048, 1024, "l2a_wo")
    y2 = ctx.sbuf("l2a_y2", [128, DIC * T], BF16)
    y1b = ctx.sbuf("l2a_y1b", [128, T], BF16)
    xsb = ctx.sbuf("l2a_xsb", [128, T], BF16)
    zsb = ctx.sbuf("l2a_zsb", [128, T], BF16)
    for c in range(DIC):
        emit_restore(ctx, y1b, y1_dram.ap()[c * 128:(c + 1) * 128, :], 1, T,
                     f"l2a_ry1{c}")
        emit_restore(ctx, xsb, xs_dram.ap()[c * 128:(c + 1) * 128, :], 1, T,
                     f"l2a_rxs{c}")
        emit_restore(ctx, zsb, zs_dram.ap()[c * 128:(c + 1) * 128, :], 1, T,
                     f"l2a_rzs{c}")
        with nc.Block(f"l2a_y2_{c}_{ctx.uid()}") as blk:
            @blk.vector
            def _(vr):
                v = ctx.v(vr)
                v.scalar_tensor_tensor(y2[:, c * T:(c + 1) * T], xsb[:, :],
                                       dcol[:, c:c + 1], y1b[:, :],
                                       ALU.mult, ALU.add)
                v.tensor_tensor(y2[:, c * T:(c + 1) * T], y2[:, c * T:(c + 1) * T],
                                zsb[:, :], ALU.mult)

    emit_mm(ctx,
            lambda kc, mc: wout[:, kc * 1024 + mc * 128: kc * 1024 + (mc + 1) * 128],
            lambda kc, tc: y2[:, kc * T + tc * NT: kc * T + tc * NT + NT],
            KC=DIC, MC=DC, TCn=TC4,
            epilogue=lambda eng, ps, mc, tc: eng.tensor_tensor(
                x[:, mc * T + tc * NT: mc * T + tc * NT + NT],
                x[:, mc * T + tc * NT: mc * T + tc * NT + NT], ps, ALU.add),
            name="l2a_womm")
    ctx.pop()
    ctx.pop()


def _consts_host():
    c = {}
    c["onescol"] = np.ones((128, 1), np.float32)
    c["onesrow"] = np.ones((1, 128), np.float32)
    c["identf"] = np.eye(128, dtype=np.float32)
    c["identb"] = np.eye(128, dtype=np.float32).astype(bf16)
    c["onescolb"] = np.ones((128, 1), np.float32).astype(bf16)
    esel = np.zeros((8, 8 * 128), np.float32)
    for e in range(8):
        esel[e, e * 128:(e + 1) * 128] = 1.0
    c["esel"] = esel
    return c


def _prep_weights_host(inputs):
    w = {}
    f32 = np.float32

    def put(key, arr):
        w[key] = np.ascontiguousarray(np.asarray(arr, f32))

    put("L1_ffn_w1", inputs["L1_ffn_w1"])
    put("L1_ffn_w2", inputs["L1_ffn_w2"])
    put("L1_ffn_b1c", np.asarray(inputs["L1_ffn_b1"], f32).reshape(32, 128).T)
    put("L1_ffn_b2c", np.asarray(inputs["L1_ffn_b2"], f32).reshape(8, 128).T)
    for k in ["L2_ffn_Wq", "L2_ffn_Wk", "L2_ffn_Wv", "L2_ffn_Wo", "L2_ffn_Wff"]:
        put(k, inputs[k])
    put("L2_attn_winx", np.asarray(inputs["L2_attn_W_in"], f32)[:, :2048])
    put("L2_attn_winz", np.asarray(inputs["L2_attn_W_in"], f32)[:, 2048:])
    _cw = np.asarray(inputs["L2_attn_conv_w"], f32)[:, 0, :]
    put("L2_attn_convwpp", _cw.reshape(16, 128, 4).transpose(1, 0, 2).reshape(128, 64))
    put("L2_attn_convbc", np.asarray(inputs["L2_attn_conv_b"], f32).reshape(16, 128).T)
    _wdt = np.asarray(inputs["L2_attn_W_dt"], f32)
    for _q in range(4):
        put(f"L2_attn_wdtq{_q}", _wdt[:, _q * 512:(_q + 1) * 512])
    put("L2_attn_bdtc", np.asarray(inputs["L2_attn_b_dt"], f32).reshape(16, 128).T)
    put("L2_attn_wx", inputs["L2_attn_W_x"])
    put("L2_attn_alogpp", np.asarray(inputs["L2_attn_A_log"], f32).reshape(256, 128).T)
    put("L2_attn_dcol", np.asarray(inputs["L2_attn_D"], f32).reshape(16, 128).T)
    put("L2_attn_wout", inputs["L2_attn_W_out"])
    put("L0_attn_wdbcT", np.asarray(inputs["L0_attn_W_dbc"], f32).T)
    put("L0_attn_wupT", np.asarray(inputs["L0_attn_W_up"], f32).T)
    _bdbc = np.asarray(inputs["L0_attn_b_dbc"], f32).reshape(96, 1)
    put("L0_attn_bdlc", _bdbc[0:64])
    put("L0_attn_bbmc", _bdbc[64:80])
    put("L0_attn_bcmc", _bdbc[80:96])
    put("L0_attn_bupc", np.asarray(inputs["L0_attn_b_up"], f32).reshape(8, 128).T)
    put("L0_attn_alogpp", np.asarray(inputs["L0_attn_A_log"], f32).reshape(128, 128).T)
    rgc = np.zeros((128, 16 * 128), f32)
    bdc = np.zeros((128, 16 * 128), f32)
    for g in range(16):
        for p in range(128):
            rgc[g * 8 + p // 16, g * 128 + p] = 1.0
            bdc[p, g * 128 + g * 8 + p // 16] = 1.0
    w["L0_rg"] = rgc.astype(bf16)
    w["L0_bd"] = bdc.astype(bf16)
    r2c = np.zeros((16, 128), f32)
    for p in range(128):
        r2c[p % 16, p] = 1.0
    w["L0_r2"] = r2c.astype(bf16)
    put("L0_ffn_ctlR", np.asarray(inputs["L0_ffn_controller"], f32).reshape(1024, 64))
    put("L0_ffn_f1R", np.asarray(inputs["L0_ffn_f1"], f32).reshape(1024, 4096))
    put("L0_ffn_f2R", np.asarray(inputs["L0_ffn_f2"], f32).reshape(4096, 1024))
    put("L0_ffn_besf", np.asarray(inputs["L0_ffn_bias"], f32).reshape(32, 128).T)
    tie = np.linspace(0.0, 1e-6, 16).astype(f32)
    w["L0_ffn_tieT"] = np.ascontiguousarray(
        np.tile(tie[None, :], (64, T // 16)).astype(f32))
    rp = np.zeros((64, 32 * 128), f32)
    for ch in range(32):
        for p in range(128):
            rp[2 * ch + p // 64, ch * 128 + p] = 1.0
    w["L0_ffn_rp"] = rp.astype(bf16)
    put("L1_attn_wqkvT", np.asarray(inputs["L1_attn_w_qkv"], f32).T)
    put("L1_attn_wprojT", np.asarray(inputs["L1_attn_w_proj"], f32).T)
    put("L1_attn_bpc", np.asarray(inputs["L1_attn_b_proj"], f32).reshape(8, 128).T)
    w["L1_attn_ln4096"] = np.full((1, 1), np.log(4096.0), f32)
    put("L3_attn_wqkvT", np.asarray(inputs["L3_attn_w_qkv"], f32).T)
    put("L3_attn_bqc", np.asarray(inputs["L3_attn_b_qkv"], f32)[:2048].reshape(16, 128).T)
    put("L3_attn_bvrow", np.asarray(inputs["L3_attn_b_qkv"], f32)[2048:].reshape(1, 1024))
    put("L3_attn_woutT", np.asarray(inputs["L3_attn_w_out"], f32).T)
    put("L3_attn_boc", np.asarray(inputs["L3_attn_b_out"], f32).reshape(8, 128).T)
    masks = np.full((128, 3 * 384), -1e30, f32)
    for dlt in range(3):
        ii = np.arange(128)[:, None] + dlt * 128
        jj = np.arange(384)[None, :]
        masks[:, dlt * 384:(dlt + 1) * 384] = np.where(np.abs(ii - jj) <= 8, 0.0, -1e30)
    w["L3_attn_masks"] = masks
    put("L3_ffn_router", inputs["L3_ffn_router"])
    put("L3_ffn_wg", np.asarray(inputs["L3_ffn_experts_gate"], f32).transpose(1, 0, 2).reshape(1024, 4096))
    put("L3_ffn_wu", np.asarray(inputs["L3_ffn_experts_up"], f32).transpose(1, 0, 2).reshape(1024, 4096))
    put("L3_ffn_wd", np.asarray(inputs["L3_ffn_experts_down"], f32).reshape(4096, 1024))
    put("L3_ffn_shared_gate", inputs["L3_ffn_shared_gate"])
    put("L3_ffn_shared_up", inputs["L3_ffn_shared_up"])
    put("L3_ffn_shared_down", inputs["L3_ffn_shared_down"])
    return w


L1F_KEYS = ["L1_ffn_w1", "L1_ffn_w2", "L1_ffn_b1c", "L1_ffn_b2c"]
L2F_KEYS = ["L2_ffn_Wq", "L2_ffn_Wk", "L2_ffn_Wv", "L2_ffn_Wo", "L2_ffn_Wff"]
L2A_KEYS = ["L2_attn_winx", "L2_attn_winz", "L2_attn_convwpp", "L2_attn_convbc",
            "L2_attn_wdtq0", "L2_attn_wdtq1", "L2_attn_wdtq2", "L2_attn_wdtq3",
            "L2_attn_bdtc", "L2_attn_wx", "L2_attn_alogpp", "L2_attn_dcol",
            "L2_attn_wout", "L0_rg", "L0_bd", "L0_r2"]
L0A_KEYS = ["L0_attn_wdbcT", "L0_attn_wupT", "L0_attn_bdlc", "L0_attn_bbmc",
            "L0_attn_bcmc", "L0_attn_bupc", "L0_attn_alogpp",
            "L0_rg", "L0_bd", "L0_r2"]
L0F_KEYS = ["L0_ffn_ctlR", "L0_ffn_f1R", "L0_ffn_f2R", "L0_ffn_besf",
            "L0_ffn_tieT", "L0_ffn_rp"]
L1A_KEYS = ["L1_attn_wqkvT", "L1_attn_wprojT", "L1_attn_bpc", "L1_attn_ln4096"]
L3A_KEYS = ["L3_attn_wqkvT", "L3_attn_bqc", "L3_attn_bvrow", "L3_attn_woutT",
            "L3_attn_boc", "L3_attn_masks"]
L3F_KEYS = ["L3_ffn_router", "L3_ffn_wg", "L3_ffn_wu", "L3_ffn_wd",
            "L3_ffn_shared_gate", "L3_ffn_shared_up", "L3_ffn_shared_down"]

STAGE_CONSTS = {
    "L1F": ["onescol", "onesrow"],
    "L2F": ["onescol", "onesrow", "onescolb"],
    "L3A": ["onescol", "onesrow", "identb", "onescolb"],
    "L1A": ["onescol", "onesrow", "onescolb"],
    "L3F": ["onescol", "onesrow", "identf", "onescolb", "esel"],
    "L0A": ["onescol", "onesrow", "onescolb"],
    "L0F": ["onescol", "onesrow", "onescolb"],
    "L2A": ["onescol", "onesrow", "onescolb", "identb"],
}

STAGES = [
    ("L0A", emit_l0_attn, L0A_KEYS),
    ("L0F", emit_l0_ffn, L0F_KEYS),
    ("L1A", emit_l1_attn, L1A_KEYS),
    ("L1F", emit_l1_ffn, L1F_KEYS),
    ("L2A", emit_l2_attn, L2A_KEYS),
    ("L2F", emit_l2_ffn, L2F_KEYS),
    ("L3A", emit_l3_attn, L3A_KEYS),
    ("L3F", emit_l3_ffn, L3F_KEYS),
]

_NPDT = {np.float32: F32, np.int32: I32}


def build_stage_program(emit_fn, wkeys, w, consts, stage_name=None):
    if stage_name in STAGE_CONSTS:
        consts = {k: consts[k] for k in STAGE_CONSTS[stage_name]}
    nc = bass.Bass("TRN2", target_bir_lowering=False)
    ctx = Ctx(nc)
    xin = nc.dram_tensor("xin", [D, T], F32, kind="ExternalInput")
    xout = nc.dram_tensor("xout", [D, T], F32, kind="ExternalOutput")
    weights = {k: nc.dram_tensor(k, list(w[k].shape),
                                 _NPDT.get(w[k].dtype.type, BF16), kind="ExternalInput")
               for k in wkeys}
    cdram = {k: nc.dram_tensor("c_" + k, list(consts[k].shape),
                               _NPDT.get(consts[k].dtype.type, BF16), kind="ExternalInput")
             for k in consts}

    x = ctx.sbuf("xres", [128, DC * T], F32)
    emit_restore(ctx, x, xin.ap(), DC, T, "ldx")
    cs = {k: load_raw(ctx, cdram[k].ap(), consts[k].shape[0], consts[k].shape[1],
                      "c_" + k, _NPDT.get(consts[k].dtype.type, BF16)) for k in consts}

    emit_fn(ctx, x, weights, cs)

    emit_spill(ctx, x, xout.ap(), DC, T, "stx")
    ctx.close()
    mybir.codegen_inst_isa_subclasses(nc)
    return nc


FUSE = True


def build_fused_program(w, consts, allkeys, callkeys):
    nc = bass.Bass("TRN2", target_bir_lowering=False)
    ctx = Ctx(nc)
    xin = nc.dram_tensor("xin", [D, T], F32, kind="ExternalInput")
    xout = nc.dram_tensor("xout", [D, T], F32, kind="ExternalOutput")
    weights = {k: nc.dram_tensor(k, list(w[k].shape),
                                 _NPDT.get(w[k].dtype.type, BF16), kind="ExternalInput")
               for k in allkeys}
    cdram = {k: nc.dram_tensor("c_" + k, list(consts[k].shape),
                               _NPDT.get(consts[k].dtype.type, BF16),
                               kind="ExternalInput")
             for k in callkeys}

    x = ctx.sbuf("xres", [128, DC * T], F32)
    emit_restore(ctx, x, xin.ap(), DC, T, "ldx")
    common = [k for k in ("onescol", "onesrow", "onescolb") if k in callkeys]
    cs = {k: load_raw(ctx, cdram[k].ap(), consts[k].shape[0], consts[k].shape[1],
                      "c_" + k, _NPDT.get(consts[k].dtype.type, BF16))
          for k in common}

    for name, emit_fn, wkeys in STAGES:
        extra = [k for k in STAGE_CONSTS[name] if k not in common]
        ctx.push()
        cse = dict(cs)
        for k in extra:
            cse[k] = load_raw(ctx, cdram[k].ap(), consts[k].shape[0],
                              consts[k].shape[1], "c_" + k,
                              _NPDT.get(consts[k].dtype.type, BF16))
        emit_fn(ctx, x, weights, cse)
        ctx.pop()

    emit_spill(ctx, x, xout.ap(), DC, T, "stx")
    ctx.close()
    mybir.codegen_inst_isa_subclasses(nc)
    return nc


def kernel(**inputs):
    xfull = np.asarray(inputs["L0_attn_x"], np.float32)
    consts = _consts_host()
    w = _prep_weights_host(inputs)
    xs = [np.ascontiguousarray(xfull[b].T) for b in range(NCORES)]

    if FUSE:
        allkeys, callkeys = [], []
        for name, _, wkeys in STAGES:
            for k in wkeys:
                if k not in allkeys:
                    allkeys.append(k)
            for k in STAGE_CONSTS[name]:
                if k not in callkeys:
                    callkeys.append(k)
        nc = build_fused_program(w, consts, allkeys, callkeys)
        in_maps = []
        for b in range(NCORES):
            m = {"xin": xs[b]}
            for k in allkeys:
                m[k] = w[k]
            for k in callkeys:
                m["c_" + k] = consts[k]
            in_maps.append(m)
        res = run_bass_kernel_spmd(nc, in_maps, core_ids=list(range(NCORES)))
        xs = [np.asarray(res.results[b]["xout"]) for b in range(NCORES)]
        out = np.stack([x.T for x in xs]).astype(np.float32)
        return np.ascontiguousarray(out)

    for name, emit_fn, wkeys in STAGES:
        cuse = {k: consts[k] for k in STAGE_CONSTS.get(name, list(consts))}
        nc = build_stage_program(emit_fn, wkeys, w, consts, stage_name=name)
        in_maps = []
        for b in range(NCORES):
            m = {"xin": xs[b]}
            for k in wkeys:
                m[k] = w[k]
            for k in cuse:
                m["c_" + k] = cuse[k]
            in_maps.append(m)
        res = run_bass_kernel_spmd(nc, in_maps, core_ids=list(range(NCORES)))
        xs = [np.asarray(res.results[b]["xout"]) for b in range(NCORES)]

    out = np.stack([x.T for x in xs]).astype(np.float32)
    return np.ascontiguousarray(out)


# version 06 (final)
# speedup vs baseline: 1.2333x; 1.0013x over previous
import numpy as np
import ml_dtypes as mld
from contextlib import ExitStack

import concourse.bass as bass
import concourse.mybir as mybir
from concourse.bass_utils import run_bass_kernel_spmd
from concourse.dve_spec import Spec, Src0, Src1, Zero, maxx, minn, select, lower
from concourse.dve_spec import C0, C1, C2
from concourse.dve_ops import DveOp
from concourse.dve_table_gen import dve_ver_for

F32 = mybir.dt.float32
BF16 = mybir.dt.bfloat16
I32 = mybir.dt.int32
AF = mybir.ActivationFunctionType
ALU = mybir.AluOpType
AX = mybir.AxisListType
bf16 = mld.bfloat16

T = 2048
D = 1024
DC = D // 128
NT = 512
TC4 = T // NT
NCORES = 8

def _register_custom_ops():
    import concourse.dve_ops as dvo
    ver = dve_ver_for("TRN2")

    def mk(name, spec, reference):
        for o in dvo.OPS:
            if o.name == name:
                return o
        sha = lower(spec, ver=ver, perf_en=False, name=name).sha(ver)
        op = DveOp(name, spec, subdim=False, uops_sha={ver: sha}, reference=reference)
        dvo.OPS.append(op)
        row = dvo._CUSTOM_DVE_ROW_BASE + len(dvo.OPS) - 1
        assert row <= 31, f"custom DVE row overflow: {row}"
        dvo._SUB_OPCODE_FOR_NAME[name] = row
        dvo.CUSTOM_DVE_REFS[name] = reference
        dvo.CUSTOM_DVE_SPECS[name] = spec
        return op

    _v = minn(maxx(Src0, C0), C1)
    sgn = mk(
        "FXTRUNC_SGN",
        Spec(body=select(_v > Zero, Src1 - (Src1 > _v), Src1 + (Src1 < _v))),
        lambda in0, in1, s0, s1, imm2: np.where(
            np.clip(in0, s0, s1) > 0,
            in1 - (in1 > np.clip(in0, s0, s1)),
            in1 + (in1 < np.clip(in0, s0, s1)),
        ),
    )
    pos = mk(
        "FXTRUNC_POS",
        Spec(body=(Src1 - (Src1 > Src0 * C0)) * C2),
        lambda in0, in1, s0, s1, imm2: (in1 - (in1 > in0 * s0)) * imm2,
    )
    return sgn, pos

FXTRUNC_SGN, FXTRUNC_POS = _register_custom_ops()

class Eng:

    _PASS = {"wait_ge", "then_inc", "sem_inc", "nop", "drain", "engine_nop"}

    def __init__(self, ctx, raw, key, serialize=True):
        self._ctx = ctx
        self._raw = raw
        self._key = key
        self._ser = serialize

    def wait(self, key, val):
        if val > 0:
            self._raw.wait_ge(self._ctx.sem(key), val)

    def raw(self):
        return self._raw

    def dma(self, dst, src, key="dml"):
        self._raw.dma_start(dst, src).then_inc(self._ctx.sem(key), 16)
        return self._ctx.bump(key, 16)

    def matmul(self, out, lhsT, rhs, start=True, stop=True, **kw):
        ins = self._raw.matmul(out, lhsT, rhs, start=start, stop=stop, **kw)
        ins.then_inc(self._ctx.sem(self._key), 1)
        self._ctx.bump(self._key, 1)
        return ins

    def custom(self, op, **kw):
        return self._wrap(self._raw._custom_dve)(op, **kw)

    def _wrap(self, raw_fn):
        ctx = self._ctx
        key = self._key

        def call(*a, **k):
            sem = ctx.sem(key)
            if self._ser and ctx.cnt[key] > 0:
                self._raw.wait_ge(sem, ctx.cnt[key])
            ins = raw_fn(*a, **k)
            ins.then_inc(sem, 1)
            ctx.bump(key, 1)
            return ins

        return call

    def __getattr__(self, name):
        raw_fn = getattr(self._raw, name)
        if name in Eng._PASS or name.startswith("dma") or name.startswith("reg"):
            return raw_fn
        return self._wrap(raw_fn)


class Ctx:
    def __init__(self, nc):
        self.nc = nc
        self.scopes = [ExitStack()]
        self._sems = {}
        self.cnt = {}
        self._uid = 0

    def uid(self):
        self._uid += 1
        return self._uid

    def sem(self, name):
        if name not in self._sems:
            self._sems[name] = self.scopes[0].enter_context(self.nc.semaphore(name))
            self.cnt[name] = 0
        return self._sems[name]

    def bump(self, name, k=1):
        self.sem(name)
        self.cnt[name] += k
        return self.cnt[name]

    def val(self, name):
        self.sem(name)
        return self.cnt[name]

    def v(self, raw):
        return Eng(self, raw, "SV")

    def s(self, raw):
        return Eng(self, raw, "SS")

    def q(self, raw):
        return Eng(self, raw, "SQ")

    def t(self, raw):
        return Eng(self, raw, "SP", serialize=False)

    def sbuf(self, name, shape, dt=F32):
        return self.scopes[-1].enter_context(
            self.nc.sbuf_tensor(f"{name}_{self.uid()}", shape, dt))

    def psum(self, name, shape, dt=F32):
        return self.scopes[-1].enter_context(
            self.nc.psum_tensor(f"{name}_{self.uid()}", shape, dt))

    def push(self):
        self.scopes.append(ExitStack())

    def pop(self):
        self.scopes.pop().close()

    def close(self):
        while self.scopes:
            self.scopes.pop().close()


def load_weight_bf16(ctx, w_hbm, K, M, name="w", nstg=2):
    nc = ctx.nc
    KC = K // 128
    wb = ctx.sbuf(name, [128, KC * M], BF16)
    ctx.push()
    nstg = min(nstg, KC)
    stg = [ctx.sbuf(name + f"_stg{i}", [128, M], F32) for i in range(nstg)]
    with nc.Block(f"ldw_{name}_{ctx.uid()}") as blk:
        d0 = [ctx.val("dmlA"), ctx.val("dmlB")]
        sv0 = ctx.val("SV")

        @blk.vector
        def _(vr):
            v = ctx.v(vr)
            for kc in range(KC):
                par = kc % 2
                v.raw().wait_ge(ctx.sem("dmlA" if par == 0 else "dmlB"),
                                d0[par] + 16 * (kc // 2 + 1))
                v.tensor_copy(wb[:, kc * M:(kc + 1) * M], stg[kc % nstg][:, 0:M])

        @blk.sync
        def _(s):
            for kc in range(KC):
                par = kc % 2
                if kc >= nstg:
                    s.wait_ge(ctx.sem("SV"), sv0 + kc - nstg + 1)
                psem = ctx.sem("dmlA" if par == 0 else "dmlB")
                s.dma_start(stg[kc % nstg][:, 0:M],
                            w_hbm[kc * 128:(kc + 1) * 128, :]).then_inc(psem, 16)
                ctx.bump("dmlA" if par == 0 else "dmlB", 16)
    ctx.pop()
    return wb


def load_raw(ctx, hbm_ap, P, Fr, name="c", dt=F32):
    nc = ctx.nc
    tile = ctx.sbuf(name, [P, Fr], dt)
    with nc.Block(f"ldc_{name}_{ctx.uid()}") as blk:
        @blk.sync
        def _(s):
            s.dma_start(tile[:, :], hbm_ap).then_inc(ctx.sem("dml"), 16)
            s.wait_ge(ctx.sem("dml"), ctx.bump("dml", 16))
    return tile


def emit_spill(ctx, sb, dram_ap, nch, width, name="sp"):
    nc = ctx.nc
    with nc.Block(f"spill_{name}_{ctx.uid()}") as blk:
        @blk.sync
        def _(s):
            for c in range(nch):
                s.dma_start(dram_ap[c * 128:(c + 1) * 128, :],
                            sb[:, c * width:(c + 1) * width]).then_inc(ctx.sem("dml"), 16)
                ctx.bump("dml", 16)
            s.wait_ge(ctx.sem("dml"), ctx.val("dml"))


def emit_restore(ctx, sb, dram_ap, nch, width, name="rs"):
    nc = ctx.nc
    with nc.Block(f"rest_{name}_{ctx.uid()}") as blk:
        @blk.sync
        def _(s):
            for c in range(nch):
                s.dma_start(sb[:, c * width:(c + 1) * width],
                            dram_ap[c * 128:(c + 1) * 128, :]).then_inc(ctx.sem("dml"), 16)
                ctx.bump("dml", 16)
            s.wait_ge(ctx.sem("dml"), ctx.val("dml"))


def emit_mm(ctx, lhsT_fn, rhs_fn, KC, MC, TCn, epilogue, Nw=NT, name="mm",
            psum_banks=4, epi_engine="vector", Mp=128):
    nc = ctx.nc
    ekey = "SV" if epi_engine == "vector" else "SS"
    ctx.push()
    ps = [ctx.psum(f"ps_{name}", [Mp, Nw]) for _ in range(psum_banks)]

    with nc.Block(f"{name}_{ctx.uid()}") as blk:
        p0 = ctx.val("SP")
        epi_done = []

        sec = blk.vector if epi_engine == "vector" else blk.scalar

        @sec
        def _(er):
            eng = ctx.v(er) if epi_engine == "vector" else ctx.s(er)
            for i in range(MC * TCn):
                mc, tc = divmod(i, TCn)
                b = i % psum_banks
                eng.wait("SP", p0 + (i + 1) * KC)
                epilogue(eng, ps[b][:, :], mc, tc)
                epi_done.append(ctx.val(ekey))

        @blk.tensor
        def _(tr):
            te = ctx.t(tr)
            for i in range(MC * TCn):
                mc, tc = divmod(i, TCn)
                b = i % psum_banks
                if i >= psum_banks:
                    te.wait(ekey, epi_done[i - psum_banks])
                for kc in range(KC):
                    te.matmul(ps[b][:, :], lhsT_fn(kc, mc), rhs_fn(kc, tc),
                              start=(kc == 0), stop=(kc == KC - 1))
    ctx.pop()


def emit_ln_tc(ctx, get_x, xh_put, consts, tc, name="ln"):
    nc = ctx.nc
    onescol = consts["onescol"]
    onesrow = consts["onesrow"]
    ctx.push()
    mean_r = ctx.sbuf(f"{name}_mean", [1, NT])
    var_r = ctx.sbuf(f"{name}_var", [1, NT])
    xsq = ctx.sbuf(f"{name}_xsq", [128, NT])
    tmp = ctx.sbuf(f"{name}_tmp", [128, NT])
    tmp2 = ctx.sbuf(f"{name}_tmp2", [128, NT])
    rstd_b = ctx.sbuf(f"{name}_rstdb", [128, NT])
    ps_s = ctx.psum(f"{name}_ps_s", [1, NT])
    ps_q = ctx.psum(f"{name}_ps_q", [1, NT])
    ps_mb = ctx.psum(f"{name}_ps_mb", [128, NT])
    ps_vb = ctx.psum(f"{name}_ps_vb", [128, NT])

    with nc.Block(f"{name}_t{tc}_{ctx.uid()}") as blk:
        p0 = ctx.val("SP")
        ss0 = ctx.val("SS")
        sq_done = []
        stats_done = []

        @blk.vector
        def _(vr):
            v = ctx.v(vr)
            for c in range(DC):
                if c >= 1:
                    v.wait("SP", p0 + 2 * c)
                v.tensor_tensor(xsq[:, :], get_x(c), get_x(c), ALU.mult)
                sq_done.append(ctx.val("SV"))
            v.wait("SP", p0 + 2 * DC)
            v.tensor_scalar(mean_r[:, :], ps_s[:, :], 1.0 / D, None, ALU.mult)
            v.tensor_scalar(var_r[:, :], ps_q[:, :], 1.0 / D, 1e-5, ALU.mult, ALU.add)
            v.tensor_tensor(tmp[0:1, :], mean_r[:, :], mean_r[:, :], ALU.mult)
            v.tensor_tensor(var_r[:, :], var_r[:, :], tmp[0:1, :], ALU.subtract)
            stats_done.append(ctx.val("SV"))
            v.wait("SP", p0 + 2 * DC + 1)
            v.wait("SS", ss0 + 2)
            for c in range(DC):
                bf_ap, f32_ap = xh_put(c)
                v.tensor_tensor(tmp2[:, :], get_x(c), ps_mb[:, :], ALU.subtract)
                if f32_ap is not None:
                    v.tensor_tensor(f32_ap, tmp2[:, :], rstd_b[:, :], ALU.mult)
                    v.tensor_copy(bf_ap, f32_ap)
                else:
                    v.tensor_tensor(bf_ap, tmp2[:, :], rstd_b[:, :], ALU.mult)

        @blk.tensor
        def _(tr):
            te = ctx.t(tr)
            for c in range(DC):
                te.matmul(ps_s[:, :], onescol[:, :], get_x(c),
                          start=(c == 0), stop=(c == DC - 1))
                te.wait("SV", sq_done[c])
                te.matmul(ps_q[:, :], onescol[:, :], xsq[:, :],
                          start=(c == 0), stop=(c == DC - 1))
            te.wait("SV", stats_done[0])
            te.matmul(ps_mb[:, :], onesrow[:, :], mean_r[:, :], start=True, stop=True)
            te.matmul(ps_vb[:, :], onesrow[:, :], var_r[:, :], start=True, stop=True)

        @blk.scalar
        def _(sr):
            sc = ctx.s(sr)
            sc.wait("SP", p0 + 2 * DC + 2)
            sc.activation(tmp[:, :], ps_vb[:, :], AF.Ln)
            sc.activation(rstd_b[:, :], tmp[:, :], AF.Exp, scale=-0.5)
    ctx.pop()


def emit_ln(ctx, x, xh_bf, consts, xh_f32=None, name="ln"):
    for tc in range(TC4):
        emit_ln_tc(
            ctx,
            lambda c, tc=tc: x[:, c * T + tc * NT: c * T + tc * NT + NT],
            lambda c, tc=tc: (
                xh_bf[:, c * T + tc * NT: c * T + tc * NT + NT],
                None if xh_f32 is None else xh_f32[:, c * T + tc * NT: c * T + tc * NT + NT],
            ),
            consts, tc, name=name)


def emit_l2norm(ctx, zin, zout, consts, name="l2n", gelu_res_x=None):
    nc = ctx.nc
    onescol_b = consts["onescolb"]
    onesrow = consts["onesrow"]
    ctx.push()
    zsq = ctx.sbuf(f"{name}_zsq", [128, NT], BF16)
    rr = ctx.sbuf(f"{name}_rr", [1, NT])
    tmp = ctx.sbuf(f"{name}_tmp", [128, NT])
    rb_sb = ctx.sbuf(f"{name}_rb", [128, NT])
    tmpg = [ctx.sbuf(f"{name}_tg", [128, NT]) for _ in range(2)]
    tmpg2 = [ctx.sbuf(f"{name}_tg2", [128, NT]) for _ in range(2)]
    ps_q = ctx.psum(f"{name}_psq", [1, NT])
    ps_rb = ctx.psum(f"{name}_psrb", [128, NT])

    for tc in range(TC4):
        with nc.Block(f"{name}_t{tc}_{ctx.uid()}") as blk:
            p0 = ctx.val("SP")
            sq_done = []
            rr_done = []
            rb_done = []
            mul_done = []

            @blk.vector
            def _(vr):
                v = ctx.v(vr)
                for c in range(DC):
                    if c >= 1:
                        v.wait("SP", p0 + c)
                    zsl = zin[:, c * T + tc * NT: c * T + tc * NT + NT]
                    v.tensor_tensor(zsq[:, :], zsl, zsl, ALU.mult)
                    sq_done.append(ctx.val("SV"))
                v.wait("SP", p0 + DC)
                v.tensor_scalar(rr[:, :], ps_q[:, :], 1e-24, None, ALU.max)
                rr_done.append(ctx.val("SV"))

            @blk.tensor
            def _(tr):
                te = ctx.t(tr)
                for c in range(DC):
                    te.wait("SV", sq_done[c])
                    te.matmul(ps_q[:, :], onescol_b[:, :], zsq[:, :],
                              start=(c == 0), stop=(c == DC - 1))
                te.wait("SV", rr_done[0])
                te.matmul(ps_rb[:, :], onesrow[:, :], rr[:, :], start=True, stop=True)

            @blk.scalar
            def _(sr):
                sc = ctx.s(sr)
                sc.wait("SP", p0 + DC + 1)
                sc.activation(tmp[:, :], ps_rb[:, :], AF.Ln)
                sc.activation(rb_sb[:, :], tmp[:, :], AF.Exp, scale=-0.5)
                rb_done.append(ctx.val("SS"))

            if gelu_res_x is None:
                @blk.vector
                def _(vr):
                    v = ctx.v(vr)
                    v.wait("SS", rb_done[0])
                    for c in range(DC):
                        zsl = zin[:, c * T + tc * NT: c * T + tc * NT + NT]
                        v.tensor_tensor(zout[:, c * T + tc * NT: c * T + tc * NT + NT],
                                        zsl, rb_sb[:, :], ALU.mult)
            else:
                @blk.vector
                def _(vr):
                    v = ctx.v(vr)
                    v.wait("SS", rb_done[0])
                    for c in range(DC):
                        zsl = zin[:, c * T + tc * NT: c * T + tc * NT + NT]
                        if c >= 2:
                            v.wait("SS", rb_done[0] + c - 1)
                        v.tensor_tensor(tmpg[c % 2][:, :], zsl, rb_sb[:, :], ALU.mult)
                        mul_done.append(ctx.val("SV"))
                        v.wait("SS", rb_done[0] + c + 1)
                        xs = gelu_res_x[:, c * T + tc * NT: c * T + tc * NT + NT]
                        v.tensor_tensor(xs, xs, tmpg2[c % 2][:, :], ALU.add)

                @blk.scalar
                def _(sr):
                    sc = ctx.s(sr)
                    for c in range(DC):
                        sc.wait("SV", mul_done[c])
                        sc.activation(tmpg2[c % 2][:, :], tmpg[c % 2][:, :], AF.Gelu)
    ctx.pop()


def emit_l1_ffn(ctx, x, weights, consts):
    nc = ctx.nc
    ctx.push()
    w1 = load_weight_bf16(ctx, weights["L1_ffn_w1"].ap(), 1024, 4096, "l1f_w1")
    b1 = load_raw(ctx, weights["L1_ffn_b1c"].ap(), 128, 32, "l1f_b1")
    b2 = load_raw(ctx, weights["L1_ffn_b2c"].ap(), 128, 8, "l1f_b2")
    xh = ctx.sbuf("l1f_xh", [128, DC * NT], BF16)
    h = ctx.sbuf("l1f_h", [128, 32 * NT], BF16)

    for tc in range(TC4):
        emit_ln_tc(ctx,
                   lambda c: x[:, c * T + tc * NT: c * T + tc * NT + NT],
                   lambda c: (xh[:, c * NT:(c + 1) * NT], None),
                   consts, tc, name="l1f_ln")

        def epi1(eng, ps_ap, mc, _tc):
            eng.activation(h[:, mc * NT:(mc + 1) * NT], ps_ap,
                           AF.Gelu, bias=b1[:, mc:mc + 1])

        emit_mm(ctx,
                lambda kc, mc: w1[:, kc * 4096 + mc * 128: kc * 4096 + mc * 128 + 128],
                lambda kc, _tc: xh[:, kc * NT:(kc + 1) * NT],
                KC=8, MC=32, TCn=1, epilogue=epi1, name=f"l1f_m1_{tc}",
                epi_engine="scalar")

        emit_mm_kstream(ctx, weights["L1_ffn_w2"].ap(), 4096, 1024,
                        lambda kc: h[:, kc * NT:(kc + 1) * NT],
                        lambda eng, ps_ap, mc, tc=tc: eng.scalar_tensor_tensor(
                            x[:, mc * T + tc * NT: mc * T + tc * NT + NT],
                            ps_ap, b2[:, mc:mc + 1],
                            x[:, mc * T + tc * NT: mc * T + tc * NT + NT],
                            ALU.add, ALU.add),
                        name=f"l1f_m2_{tc}")
    ctx.pop()


def emit_mm_kstream(ctx, w_hbm, K, M, rhs_fn, epilogue, Nw=NT, name="mmk"):
    nc = ctx.nc
    KC = K // 128
    MC = M // 128
    assert MC <= 8
    ctx.push()
    ps = [ctx.psum(f"ps_{name}", [128, Nw]) for _ in range(MC)]
    stg = [ctx.sbuf(f"{name}_stg{i}", [128, M], F32) for i in range(2)]
    wt = [ctx.sbuf(f"{name}_wt{i}", [128, M], BF16) for i in range(2)]

    with nc.Block(f"{name}_{ctx.uid()}") as blk:
        d0 = [ctx.val("dmlA"), ctx.val("dmlB")]
        p0 = ctx.val("SP")
        cv_done = []

        @blk.vector
        def _(vr):
            v = ctx.v(vr)
            for kc in range(KC):
                par = kc % 2
                v.raw().wait_ge(ctx.sem("dmlA" if par == 0 else "dmlB"),
                                d0[par] + 16 * (kc // 2 + 1))
                if kc >= 2:
                    v.wait("SP", p0 + (kc - 1) * MC)
                v.tensor_copy(wt[kc % 2][:, :], stg[kc % 2][:, :])
                cv_done.append(ctx.val("SV"))
            for mc in range(MC):
                v.wait("SP", p0 + KC * MC)
                epilogue(v, ps[mc][:, :], mc)

        @blk.sync
        def _(s):
            for kc in range(KC):
                par = kc % 2
                if kc >= 2:
                    s.wait_ge(ctx.sem("SV"), cv_done[kc - 1])
                psem = ctx.sem("dmlA" if par == 0 else "dmlB")
                s.dma_start(stg[kc % 2][:, :],
                            w_hbm[kc * 128:(kc + 1) * 128, :]).then_inc(psem, 16)
                ctx.bump("dmlA" if par == 0 else "dmlB", 16)

        @blk.tensor
        def _(tr):
            te = ctx.t(tr)
            for kc in range(KC):
                te.wait("SV", cv_done[kc])
                for mc in range(MC):
                    te.matmul(ps[mc][:, :], wt[kc % 2][:, mc * 128:(mc + 1) * 128],
                              rhs_fn(kc), start=(kc == 0), stop=(kc == KC - 1))
    ctx.pop()


def emit_l3_ffn(ctx, x, weights, consts):
    nc = ctx.nc
    ctx.push()
    xh = ctx.sbuf("l3f_xh", [128, DC * T], BF16)
    emit_ln(ctx, x, xh, consts, name="l3f_ln")

    ctx.push()
    rt_fm = ctx.sbuf("l3f_rt", [8, T])
    coef_fm = ctx.sbuf("l3f_coef", [8, T])
    ctx.push()
    router = load_weight_bf16(ctx, weights["L3_ffn_router"].ap(), 1024, 8, "l3f_rtw")

    emit_mm(ctx, lambda kc, mc: router[:, kc * 8:(kc + 1) * 8],
            lambda kc, tc: xh[:, kc * T + tc * NT: kc * T + tc * NT + NT],
            KC=8, MC=1, TCn=TC4,
            epilogue=lambda eng, ps, mc, tc: eng.tensor_copy(
                rt_fm[:, tc * NT:(tc + 1) * NT], ps),
            name="l3f_rtmm", psum_banks=2, Mp=8)

    ident = consts["identf"]
    rt_tm = ctx.sbuf("l3f_rttm", [128, 16 * 8])
    co_tm = ctx.sbuf("l3f_cotm", [128, 16 * 8])
    tmp_a = ctx.sbuf("l3f_ta", [128, 16 * 8])
    tmp_b = ctx.sbuf("l3f_tb", [128, 16 * 8])
    s1 = ctx.sbuf("l3f_s1", [128, 16])
    s1b = ctx.sbuf("l3f_s1b", [128, 16])
    pst2 = [ctx.psum("l3f_pst", [128, 128]) for _ in range(2)]

    def b3(t):
        return bass.AP(t, 0, [[16, 128], [1, 16], [0, 8]])

    def r3(t):
        return t[:, :].rearrange("p (a b) -> p a b", b=8)

    with nc.Block(f"l3f_rtt_{ctx.uid()}") as blk:
        p0 = ctx.val("SP")
        cp_done = []

        @blk.vector
        def _(vr):
            v = ctx.v(vr)
            for i in range(16):
                v.wait("SP", p0 + i + 1)
                v.tensor_copy(rt_tm[:, i * 8:(i + 1) * 8], pst2[i % 2][:, 0:8])
                cp_done.append(ctx.val("SV"))

        @blk.tensor
        def _(tr):
            te = ctx.t(tr)
            for i in range(16):
                if i >= 2:
                    te.wait("SV", cp_done[i - 2])
                te.transpose(pst2[i % 2][:, 0:8], rt_fm[:, i * 128:(i + 1) * 128],
                             ident[0:8, 0:8])

    with nc.Block(f"l3f_top2_{ctx.uid()}") as blk:
        ss0 = ctx.val("SS")
        pre_done = []
        sum_done = []

        @blk.vector
        def _(vr):
            v = ctx.v(vr)
            v.reduce_max(s1[:, :], r3(rt_tm), AX.X, negate=True)
            v.tensor_tensor(r3(tmp_a), r3(rt_tm), b3(s1), ALU.add)
            pre_done.append(ctx.val("SV"))
            v.wait("SS", ss0 + 1)
            v.reduce_sum(s1[:, :], r3(tmp_b), AX.X)
            sum_done.append(ctx.val("SV"))
            v.wait("SS", ss0 + 3)
            v.tensor_tensor(r3(rt_tm), r3(tmp_b), b3(s1b), ALU.mult)
            v.reduce_max(s1[:, :], r3(rt_tm), AX.X)
            v.tensor_tensor(r3(tmp_a), r3(rt_tm), b3(s1), ALU.is_lt)
            v.tensor_tensor(tmp_b[:, :], rt_tm[:, :], tmp_a[:, :], ALU.mult)
            v.reduce_max(s1[:, :], r3(tmp_b), AX.X)
            v.tensor_tensor(r3(tmp_a), r3(rt_tm), b3(s1), ALU.is_ge)
            v.tensor_tensor(co_tm[:, :], rt_tm[:, :], tmp_a[:, :], ALU.mult)

        @blk.scalar
        def _(sr):
            sc = ctx.s(sr)
            sc.wait("SV", pre_done[0])
            sc.activation(tmp_b[:, :], tmp_a[:, :], AF.Exp)
            sc.wait("SV", sum_done[0])
            sc.activation(tmp_a[:, 0:16], s1[:, :], AF.Ln)
            sc.activation(s1b[:, :], tmp_a[:, 0:16], AF.Exp, scale=-1.0)

    with nc.Block(f"l3f_cot_{ctx.uid()}") as blk:
        p0 = ctx.val("SP")
        cp2 = []

        @blk.vector
        def _(vr):
            v = ctx.v(vr)
            for i in range(16):
                v.wait("SP", p0 + i + 1)
                v.tensor_copy(coef_fm[:, i * 128:(i + 1) * 128], pst2[0][0:8, 0:128])
                cp2.append(ctx.val("SV"))

        @blk.tensor
        def _(tr):
            te = ctx.t(tr)
            for i in range(16):
                if i >= 1:
                    te.wait("SV", cp2[i - 1])
                te.transpose(pst2[0][0:8, 0:128], co_tm[:, i * 8:(i + 1) * 8],
                             ident[:, :])
    ctx.pop()

    inner_dram4 = [nc.dram_tensor(f"l3f_innerd{tc}_{ctx.uid()}", [32 * 128, NT], BF16,
                                  kind="Internal") for tc in range(TC4)]
    for es in range(2):
        ctx.push()
        wg = load_weight_bf16(ctx, weights["L3_ffn_wg"].ap()[:, es * 2048:(es + 1) * 2048],
                              1024, 2048, f"l3f_wg{es}")
        wu = load_weight_bf16(ctx, weights["L3_ffn_wu"].ap()[:, es * 2048:(es + 1) * 2048],
                              1024, 2048, f"l3f_wu{es}")
        inner = ctx.sbuf("l3f_inner", [128, 16 * NT], BF16)
        gtile = [ctx.sbuf("l3f_gt", [128, NT], BF16) for _ in range(2)]

        for tc in range(TC4):
            ctx.push()
            psg = [ctx.psum("l3f_psg", [128, NT]) for _ in range(2)]
            psu = [ctx.psum("l3f_psu", [128, NT]) for _ in range(2)]
            with nc.Block(f"l3f_gu_{es}_{tc}_{ctx.uid()}") as blk:
                p0 = ctx.val("SP")
                ss0 = ctx.val("SS")
                sv0 = ctx.val("SV")
                silu_at = lambda m: ss0 + m + 1
                mul_at = lambda m: sv0 + m + 1

                @blk.scalar
                def _(sr):
                    sc = ctx.s(sr)
                    for mc in range(16):
                        sc.wait("SP", p0 + (2 * mc + 1) * 8)
                        if mc >= 2:
                            sc.wait("SV", mul_at(mc - 2))
                        sc.activation(gtile[mc % 2][:, :], psg[mc % 2][:, :], AF.Silu)

                @blk.vector
                def _(vr):
                    v = ctx.v(vr)
                    for mc in range(16):
                        v.wait("SP", p0 + (2 * mc + 2) * 8)
                        v.wait("SS", silu_at(mc))
                        v.tensor_tensor(inner[:, mc * NT:(mc + 1) * NT],
                                        psu[mc % 2][:, :], gtile[mc % 2][:, :],
                                        ALU.mult)

                @blk.tensor
                def _(tr):
                    te = ctx.t(tr)
                    for mc in range(16):
                        if mc >= 2:
                            te.wait("SS", silu_at(mc - 2))
                            te.wait("SV", mul_at(mc - 2))
                        for kc in range(8):
                            te.matmul(psg[mc % 2][:, :],
                                      wg[:, kc * 2048 + mc * 128: kc * 2048 + mc * 128 + 128],
                                      xh[:, kc * T + tc * NT: kc * T + tc * NT + NT],
                                      start=(kc == 0), stop=(kc == 7))
                        for kc in range(8):
                            te.matmul(psu[mc % 2][:, :],
                                      wu[:, kc * 2048 + mc * 128: kc * 2048 + mc * 128 + 128],
                                      xh[:, kc * T + tc * NT: kc * T + tc * NT + NT],
                                      start=(kc == 0), stop=(kc == 7))
            ctx.pop()

            ctx.push()
            psc = ctx.psum("l3f_psc", [128, NT])
            with nc.Block(f"l3f_coef_{es}_{tc}_{ctx.uid()}") as blk:
                p0 = ctx.val("SP")
                cmul = []

                @blk.vector
                def _(vr):
                    v = ctx.v(vr)
                    for el in range(4):
                        v.wait("SP", p0 + el + 1)
                        iv = inner[:, el * 4 * NT:(el + 1) * 4 * NT].rearrange(
                            "p (c n) -> p c n", n=NT)
                        pb = bass.AP(psc, 0, [[NT, 128], [0, 4], [1, NT]])
                        v.tensor_tensor(iv, iv, pb, ALU.mult)
                        cmul.append(ctx.val("SV"))

                @blk.tensor
                def _(tr):
                    te = ctx.t(tr)
                    for el in range(4):
                        e = es * 4 + el
                        if el >= 1:
                            te.wait("SV", cmul[el - 1])
                        te.matmul(psc[:, :], consts["esel"][:, e * 128:(e + 1) * 128],
                                  coef_fm[:, tc * NT:(tc + 1) * NT],
                                  start=True, stop=True)
            ctx.pop()
            emit_spill(ctx, inner, inner_dram4[tc].ap()[es * 2048:(es + 1) * 2048, :],
                       16, NT, f"l3f_in{es}_{tc}")
        ctx.pop()
    ctx.pop()

    ctx.push()
    wd = load_weight_bf16(ctx, weights["L3_ffn_wd"].ap(), 4096, 1024, "l3f_wd")
    inner = ctx.sbuf("l3f_inner2", [128, 32 * NT], BF16)
    for tc in range(TC4):
        emit_restore(ctx, inner, inner_dram4[tc].ap(), 32, NT, f"l3f_in{tc}")
        emit_mm(ctx,
                lambda kc, mc: wd[:, kc * 1024 + mc * 128: kc * 1024 + mc * 128 + 128],
                lambda kc, _tc: inner[:, kc * NT:(kc + 1) * NT],
                KC=32, MC=8, TCn=1,
                epilogue=lambda eng, ps, mc, _tc, tc=tc: eng.tensor_tensor(
                    x[:, mc * T + tc * NT: mc * T + tc * NT + NT],
                    x[:, mc * T + tc * NT: mc * T + tc * NT + NT], ps, ALU.add),
                name=f"l3f_down_{tc}")
    ctx.pop()

    ctx.push()
    wsg = load_weight_bf16(ctx, weights["L3_ffn_shared_gate"].ap(), 1024, 1024, "l3f_wsg")
    wsu = load_weight_bf16(ctx, weights["L3_ffn_shared_up"].ap(), 1024, 1024, "l3f_wsu")
    sboth = ctx.sbuf("l3f_sb", [128, DC * T], BF16)
    stile = [ctx.sbuf("l3f_st", [128, NT], BF16) for _ in range(2)]
    ctx.push()
    psg = [ctx.psum("l3f_psg2", [128, NT]) for _ in range(2)]
    psu = [ctx.psum("l3f_psu2", [128, NT]) for _ in range(2)]
    with nc.Block(f"l3f_sgu_{ctx.uid()}") as blk:
        p0 = ctx.val("SP")
        ss0 = ctx.val("SS")
        sv0 = ctx.val("SV")
        silu_at = lambda m: ss0 + m + 1
        mul_at = lambda m: sv0 + m + 1

        @blk.scalar
        def _(sr):
            sc = ctx.s(sr)
            for i in range(32):
                sc.wait("SP", p0 + (2 * i + 1) * 8)
                if i >= 2:
                    sc.wait("SV", mul_at(i - 2))
                sc.activation(stile[i % 2][:, :], psg[i % 2][:, :], AF.Silu)

        @blk.vector
        def _(vr):
            v = ctx.v(vr)
            for i in range(32):
                mc, tc = divmod(i, TC4)
                v.wait("SP", p0 + (2 * i + 2) * 8)
                v.wait("SS", silu_at(i))
                v.tensor_tensor(sboth[:, mc * T + tc * NT: mc * T + tc * NT + NT],
                                psu[i % 2][:, :], stile[i % 2][:, :], ALU.mult)

        @blk.tensor
        def _(tr):
            te = ctx.t(tr)
            for i in range(32):
                mc, tc = divmod(i, TC4)
                if i >= 2:
                    te.wait("SS", silu_at(i - 2))
                    te.wait("SV", mul_at(i - 2))
                for kc in range(8):
                    te.matmul(psg[i % 2][:, :],
                              wsg[:, kc * 1024 + mc * 128: kc * 1024 + mc * 128 + 128],
                              xh[:, kc * T + tc * NT: kc * T + tc * NT + NT],
                              start=(kc == 0), stop=(kc == 7))
                for kc in range(8):
                    te.matmul(psu[i % 2][:, :],
                              wsu[:, kc * 1024 + mc * 128: kc * 1024 + mc * 128 + 128],
                              xh[:, kc * T + tc * NT: kc * T + tc * NT + NT],
                              start=(kc == 0), stop=(kc == 7))
    ctx.pop()

    wsd = load_weight_bf16(ctx, weights["L3_ffn_shared_down"].ap(), 1024, 1024, "l3f_wsd")
    emit_mm(ctx,
            lambda kc, mc: wsd[:, kc * 1024 + mc * 128: kc * 1024 + mc * 128 + 128],
            lambda kc, tc: sboth[:, kc * T + tc * NT: kc * T + tc * NT + NT],
            KC=8, MC=8, TCn=TC4,
            epilogue=lambda eng, ps, mc, tc: eng.tensor_tensor(
                x[:, mc * T + tc * NT: mc * T + tc * NT + NT],
                x[:, mc * T + tc * NT: mc * T + tc * NT + NT], ps, ALU.add),
            name="l3f_sdown")
    ctx.pop()
    ctx.pop()


def emit_l2_ffn(ctx, x, weights, consts):
    nc = ctx.nc
    ctx.push()
    zA_dram = nc.dram_tensor(f"l2f_zad_{ctx.uid()}", [D, T], BF16, kind="Internal")

    ctx.push()
    q_fm = ctx.sbuf("l2f_q", [128, DC * T], BF16)
    k_fm = ctx.sbuf("l2f_k", [128, DC * T], BF16)
    v_tm = ctx.sbuf("l2f_v", [128, 16 * 1024], BF16)
    TH = T // 2
    for th in range(2):
        ctx.push()
        xh = ctx.sbuf("l2f_xh", [128, DC * TH], BF16)
        for tci in range(2):
            tc = th * 2 + tci
            emit_ln_tc(ctx,
                       lambda c: x[:, c * T + tc * NT: c * T + tc * NT + NT],
                       lambda c: (xh[:, c * TH + tci * NT: c * TH + tci * NT + NT], None),
                       consts, tc, name="l2f_ln")

        ctx.push()
        wq = load_weight_bf16(ctx, weights["L2_ffn_Wq"].ap(), 1024, 1024, "l2f_wq")
        emit_mm(ctx,
                lambda kc, mc: wq[:, kc * 1024 + mc * 128: kc * 1024 + mc * 128 + 128],
                lambda kc, tci: xh[:, kc * TH + tci * NT: kc * TH + tci * NT + NT],
                KC=8, MC=8, TCn=2,
                epilogue=lambda eng, ps, mc, tci, th=th: eng.tensor_copy(
                    q_fm[:, mc * T + (th * 2 + tci) * NT: mc * T + (th * 2 + tci) * NT + NT], ps),
                name=f"l2f_qm{th}")
        ctx.pop()

        ctx.push()
        wk = load_weight_bf16(ctx, weights["L2_ffn_Wk"].ap(), 1024, 1024, "l2f_wk")
        emit_mm(ctx,
                lambda kc, mc: wk[:, kc * 1024 + mc * 128: kc * 1024 + mc * 128 + 128],
                lambda kc, tci: xh[:, kc * TH + tci * NT: kc * TH + tci * NT + NT],
                KC=8, MC=8, TCn=2,
                epilogue=lambda eng, ps, mc, tci, th=th: eng.tensor_scalar(
                    k_fm[:, mc * T + (th * 2 + tci) * NT: mc * T + (th * 2 + tci) * NT + NT],
                    ps, 1.0 / 32.0, None, ALU.mult),
                name=f"l2f_km{th}")
        ctx.pop()

        ctx.push()
        wv = load_weight_bf16(ctx, weights["L2_ffn_Wv"].ap(), 1024, 1024, "l2f_wv")
        emit_mm(ctx,
                lambda kc, mc: xh[:, kc * TH + mc * 128: kc * TH + mc * 128 + 128],
                lambda kc, nh: wv[:, kc * 1024 + nh * NT: kc * 1024 + nh * NT + NT],
                KC=8, MC=8, TCn=2,
                epilogue=lambda eng, ps, mc, nh, th=th: eng.tensor_copy(
                    v_tm[:, (th * 8 + mc) * 1024 + nh * NT: (th * 8 + mc) * 1024 + nh * NT + NT],
                    ps),
                name=f"l2f_vm{th}")
        ctx.pop()
        ctx.pop()

    ctx.push()
    expS = ctx.sbuf("l2f_es", [128, T], BF16)
    negmx = ctx.sbuf("l2f_nm", [128, 1])
    sums4 = ctx.sbuf("l2f_s4", [128, 4])
    rcol = ctx.sbuf("l2f_rc", [128, 1])
    rcol2 = ctx.sbuf("l2f_rc2", [128, 1])
    vt_h = ctx.sbuf("l2f_vt", [128, 64], BF16)
    oev = ctx.sbuf("l2f_oev", [64, T], BF16)
    psS = ctx.psum("l2f_psS", [128, T])
    psO = ctx.psum("l2f_psO", [64, T])

    for h in range(16):
        hb = (h % 2) * 64
        hc = h // 2
        with nc.Block(f"l2f_att_{h}_{ctx.uid()}") as blk:
            p0 = ctx.val("SP")
            v0 = ctx.val("SV")
            s0 = ctx.val("SS")

            @blk.tensor
            def _(tr):
                te = ctx.t(tr)
                for kt in range(16):
                    if kt >= 1:
                        te.wait("SS", s0 + 6 * (kt - 1) + 4)
                    for qc in range(TC4):
                        te.matmul(psS[:, qc * NT:(qc + 1) * NT],
                                  k_fm[hb:hb + 64, hc * T + kt * 128: hc * T + kt * 128 + 128],
                                  q_fm[hb:hb + 64, hc * T + qc * NT: hc * T + qc * NT + NT],
                                  start=True, stop=True)
                    te.wait("SV", v0 + 3 * kt + 3)
                    for qc in range(TC4):
                        te.matmul(psO[:, qc * NT:(qc + 1) * NT], vt_h[:, :],
                                  expS[:, qc * NT:(qc + 1) * NT],
                                  start=(kt == 0), stop=(kt == 15))

            @blk.vector
            def _(vr):
                v = ctx.v(vr)
                for kt in range(16):
                    v.wait("SP", p0 + 8 * kt + 4)
                    v.reduce_max(negmx[:, :], psS[:, :], AX.X, negate=True)
                    v.wait("SS", s0 + 6 * kt + 4)
                    v.reduce_sum(rcol[:, :], sums4[:, :], AX.X)
                    v.wait("SS", s0 + 6 * kt + 6)
                    v.tensor_scalar(vt_h[:, :],
                                    v_tm[:, kt * 1024 + h * 64: kt * 1024 + h * 64 + 64],
                                    rcol[:, :], None, ALU.mult)

            @blk.scalar
            def _(sr):
                sc = ctx.s(sr)
                for kt in range(16):
                    sc.wait("SV", v0 + 3 * kt + 1)
                    for qc in range(TC4):
                        sc.activation(expS[:, qc * NT:(qc + 1) * NT],
                                      psS[:, qc * NT:(qc + 1) * NT], AF.Exp,
                                      bias=negmx[:, :],
                                      accum_out=sums4[:, qc:qc + 1])
                    sc.wait("SV", v0 + 3 * kt + 2)
                    sc.activation(rcol2[:, :], rcol[:, :], AF.Ln)
                    sc.activation(rcol[:, :], rcol2[:, :], AF.Exp, scale=-1.0)

        with nc.Block(f"l2f_oev_{h}_{ctx.uid()}") as blk:
            ev_done = []

            @blk.vector
            def _(vr):
                v = ctx.v(vr)
                v.tensor_copy(oev[:, :], psO[:, :])
                ev_done.append(ctx.val("SV"))

            @blk.sync
            def _(s):
                s.wait_ge(ctx.sem("SV"), ev_done[0])
                s.dma_start(zA_dram.ap()[h * 64:(h + 1) * 64, :],
                            oev[:, :]).then_inc(ctx.sem("dml"), 16)
                s.wait_ge(ctx.sem("dml"), ctx.bump("dml", 16))
    ctx.pop()
    ctx.pop()

    ctx.push()
    zA = ctx.sbuf("l2f_zA", [128, DC * T], BF16)
    zB = ctx.sbuf("l2f_zB", [128, DC * T], BF16)
    emit_restore(ctx, zA, zA_dram.ap(), DC, T, "l2f_za")
    ctx.push()
    wo = load_weight_bf16(ctx, weights["L2_ffn_Wo"].ap(), 1024, 1024, "l2f_wo")
    emit_mm(ctx,
            lambda kc, mc: wo[:, kc * 1024 + mc * 128: kc * 1024 + mc * 128 + 128],
            lambda kc, tc: zA[:, kc * T + tc * NT: kc * T + tc * NT + NT],
            KC=8, MC=8, TCn=TC4,
            epilogue=lambda eng, ps, mc, tc: eng.tensor_copy(
                zB[:, mc * T + tc * NT: mc * T + tc * NT + NT], ps),
            name="l2f_wo")
    ctx.pop()

    emit_l2norm(ctx, zB, zA, consts, name="l2f_n1")

    ctx.push()
    wf = load_weight_bf16(ctx, weights["L2_ffn_Wff"].ap(), 1024, 1024, "l2f_wf")
    emit_mm(ctx,
            lambda kc, mc: wf[:, kc * 1024 + mc * 128: kc * 1024 + mc * 128 + 128],
            lambda kc, tc: zA[:, kc * T + tc * NT: kc * T + tc * NT + NT],
            KC=8, MC=8, TCn=TC4,
            epilogue=lambda eng, ps, mc, tc: eng.tensor_copy(
                zB[:, mc * T + tc * NT: mc * T + tc * NT + NT], ps),
            name="l2f_wf")
    ctx.pop()

    emit_l2norm(ctx, zB, zA, consts, name="l2f_n2", gelu_res_x=x)
    ctx.pop()
    ctx.pop()


def emit_l3_attn(ctx, x, weights, consts):
    nc = ctx.nc
    ctx.push()
    ao_dram = nc.dram_tensor(f"l3a_aod_{ctx.uid()}", [D, T], BF16, kind="Internal")
    ctx.push()
    q_fm = ctx.sbuf("l3a_q", [128, DC * T], BF16)
    k_fm = ctx.sbuf("l3a_k", [128, DC * T], BF16)
    v_tm = ctx.sbuf("l3a_v", [128, 16 * 1024], BF16)
    bq = load_raw(ctx, weights["L3_attn_bqc"].ap(), 128, 16, "l3a_bq")
    vbias = ctx.sbuf("l3a_vb", [128, 1024])

    vbr = load_raw(ctx, weights["L3_attn_bvrow"].ap(), 1, 1024, "l3a_vbr")
    ctx.push()
    psb = ctx.psum("l3a_psb", [128, NT])
    with nc.Block(f"l3a_vb_{ctx.uid()}") as blk:
        p0 = ctx.val("SP")
        sv0 = ctx.val("SV")

        @blk.vector
        def _(vr):
            v = ctx.v(vr)
            for i in range(2):
                v.wait("SP", p0 + i + 1)
                v.tensor_copy(vbias[:, i * NT:(i + 1) * NT], psb[:, :])

        @blk.tensor
        def _(tr):
            te = ctx.t(tr)
            for i in range(2):
                if i >= 1:
                    te.wait("SV", sv0 + i)
                te.matmul(psb[:, :], consts["onesrow"][:, :],
                          vbr[0:1, i * NT:(i + 1) * NT], start=True, stop=True)
    ctx.pop()

    TH = T // 2
    for th in range(2):
        ctx.push()
        xh = ctx.sbuf("l3a_xh", [128, DC * TH], BF16)
        for tci in range(2):
            tc = th * 2 + tci
            emit_ln_tc(ctx,
                       lambda c: x[:, c * T + tc * NT: c * T + tc * NT + NT],
                       lambda c: (xh[:, c * TH + tci * NT: c * TH + tci * NT + NT], None),
                       consts, tc, name="l3a_ln")

        ctx.push()
        wq = load_weight_bf16(ctx, weights["L3_attn_wqkvT"].ap()[:, 0:1024],
                              1024, 1024, "l3a_wq", nstg=1)
        emit_mm(ctx,
                lambda kc, mc: wq[:, kc * 1024 + mc * 128: kc * 1024 + mc * 128 + 128],
                lambda kc, tci: xh[:, kc * TH + tci * NT: kc * TH + tci * NT + NT],
                KC=8, MC=8, TCn=2,
                epilogue=lambda eng, ps, mc, tci, th=th: eng.tensor_scalar(
                    q_fm[:, mc * T + (th * 2 + tci) * NT: mc * T + (th * 2 + tci) * NT + NT],
                    ps, bq[:, mc:mc + 1], 0.125, ALU.add, ALU.mult),
                name=f"l3a_qm{th}")
        ctx.pop()

        ctx.push()
        wk = load_weight_bf16(ctx, weights["L3_attn_wqkvT"].ap()[:, 1024:2048],
                              1024, 1024, "l3a_wk", nstg=1)
        emit_mm(ctx,
                lambda kc, mc: wk[:, kc * 1024 + mc * 128: kc * 1024 + mc * 128 + 128],
                lambda kc, tci: xh[:, kc * TH + tci * NT: kc * TH + tci * NT + NT],
                KC=8, MC=8, TCn=2,
                epilogue=lambda eng, ps, mc, tci, th=th: eng.tensor_scalar(
                    k_fm[:, mc * T + (th * 2 + tci) * NT: mc * T + (th * 2 + tci) * NT + NT],
                    ps, bq[:, 8 + mc:8 + mc + 1], None, ALU.add),
                name=f"l3a_km{th}")
        ctx.pop()

        ctx.push()
        wv = load_weight_bf16(ctx, weights["L3_attn_wqkvT"].ap()[:, 2048:3072],
                              1024, 1024, "l3a_wv", nstg=1)
        emit_mm(ctx,
                lambda kc, mc: xh[:, kc * TH + mc * 128: kc * TH + mc * 128 + 128],
                lambda kc, nh: wv[:, kc * 1024 + nh * NT: kc * 1024 + nh * NT + NT],
                KC=8, MC=8, TCn=2,
                epilogue=lambda eng, ps, mc, nh, th=th: eng.tensor_tensor(
                    v_tm[:, (th * 8 + mc) * 1024 + nh * NT: (th * 8 + mc) * 1024 + nh * NT + NT],
                    ps, vbias[:, nh * NT:(nh + 1) * NT], ALU.add),
                name=f"l3a_vm{th}")
        ctx.pop()
        ctx.pop()

    masks = load_raw(ctx, weights["L3_attn_masks"].ap(), 128, 3 * 384, "l3a_mk")
    ctx.push()
    s_sb = ctx.sbuf("l3a_ssb", [128, 384])
    p_bf = ctx.sbuf("l3a_pbf", [128, 384], BF16)
    pt_sb = ctx.sbuf("l3a_pt", [128, 384], BF16)
    negm = ctx.sbuf("l3a_ng", [128, 1])
    rsum = ctx.sbuf("l3a_rs", [128, 1])
    rrec = ctx.sbuf("l3a_rr", [128, 1])
    rrec2 = ctx.sbuf("l3a_rr2", [128, 1])
    oevt = ctx.sbuf("l3a_oevt", [64, 128], BF16)
    psS = [ctx.psum("l3a_psS", [128, 384]) for _ in range(2)]
    psT = [ctx.psum("l3a_psT", [128, 384], BF16) for _ in range(2)]
    psO = [ctx.psum("l3a_psO", [64, 128]) for _ in range(2)]

    for h in range(16):
        hb = (h % 2) * 64
        hc = h // 2
        for qt in range(16):
            kw = min(max(qt * 128 - 128, 0), T - 384)
            kc0 = kw // 128
            dlt = (qt * 128 - kw) // 128
            pb = qt % 2
            with nc.Block(f"l3a_att_{h}_{qt}_{ctx.uid()}") as blk:
                p0 = ctx.val("SP")
                sv0 = ctx.val("SV")
                ss0 = ctx.val("SS")

                @blk.vector
                def _(vr):
                    v = ctx.v(vr)
                    v.wait("SP", p0 + 1)
                    v.tensor_tensor(s_sb[:, :], psS[pb][:, :],
                                    masks[:, dlt * 384:(dlt + 1) * 384], ALU.add)
                    v.reduce_max(negm[:, :], s_sb[:, :], AX.X, negate=True)
                    v.wait("SS", ss0 + 3)
                    v.tensor_scalar(p_bf[:, :], p_bf[:, :], rrec[:, :], None, ALU.mult)
                    v.wait("SP", p0 + 4)
                    v.tensor_copy(pt_sb[:, :], psT[pb][:, :])
                    v.wait("SP", p0 + 7)
                    v.tensor_copy(oevt[:, :], psO[pb][:, :])

                @blk.scalar
                def _(sr):
                    sc = ctx.s(sr)
                    sc.wait("SV", sv0 + 2)
                    sc.activation(p_bf[:, :], s_sb[:, :], AF.Exp, bias=negm[:, :],
                                  accum_out=rsum[:, :])
                    sc.activation(rrec2[:, :], rsum[:, :], AF.Ln)
                    sc.activation(rrec[:, :], rrec2[:, :], AF.Exp, scale=-1.0)

                @blk.tensor
                def _(tr):
                    te = ctx.t(tr)
                    te.matmul(psS[pb][:, :],
                              q_fm[hb:hb + 64, hc * T + qt * 128: hc * T + qt * 128 + 128],
                              k_fm[hb:hb + 64, hc * T + kw: hc * T + kw + 384],
                              start=True, stop=True)
                    te.wait("SV", sv0 + 3)
                    for c in range(3):
                        te.transpose(psT[pb][:, c * 128:(c + 1) * 128],
                                     p_bf[:, c * 128:(c + 1) * 128],
                                     consts["identb"][:, :])
                    te.wait("SV", sv0 + 4)
                    for c in range(3):
                        te.matmul(psO[pb][:, :],
                                  v_tm[:, (kc0 + c) * 1024 + h * 64: (kc0 + c) * 1024 + h * 64 + 64],
                                  pt_sb[:, c * 128:(c + 1) * 128],
                                  start=(c == 0), stop=(c == 2))

                @blk.sync
                def _(s):
                    s.wait_ge(ctx.sem("SV"), sv0 + 5)
                    s.dma_start(
                        ao_dram.ap()[hc * 128 + hb: hc * 128 + hb + 64,
                                     qt * 128:(qt + 1) * 128],
                        oevt[:, :]).then_inc(ctx.sem("dml"), 16)
                    s.wait_ge(ctx.sem("dml"), ctx.bump("dml", 16))
    ctx.pop()
    ctx.pop()

    ctx.push()
    ao_fm = ctx.sbuf("l3a_ao", [128, DC * T], BF16)
    emit_restore(ctx, ao_fm, ao_dram.ap(), DC, T, "l3a_ao")
    wo = load_weight_bf16(ctx, weights["L3_attn_woutT"].ap(), 1024, 1024, "l3a_wo")
    bo = load_raw(ctx, weights["L3_attn_boc"].ap(), 128, 8, "l3a_bo")
    emit_mm(ctx,
            lambda kc, mc: wo[:, kc * 1024 + mc * 128: kc * 1024 + mc * 128 + 128],
            lambda kc, tc: ao_fm[:, kc * T + tc * NT: kc * T + tc * NT + NT],
            KC=8, MC=8, TCn=TC4,
            epilogue=lambda eng, ps, mc, tc: eng.scalar_tensor_tensor(
                x[:, mc * T + tc * NT: mc * T + tc * NT + NT],
                ps, bo[:, mc:mc + 1],
                x[:, mc * T + tc * NT: mc * T + tc * NT + NT], ALU.add, ALU.add),
            name="l3a_wo")
    ctx.pop()
    ctx.pop()


def _fxp_epi(eng, ps_ap, out_ap, final_scale, tmp, tmpi, tmp2):
    eng.tensor_scalar(tmp, ps_ap, 4096.0, None, ALU.mult)
    eng.tensor_scalar(tmpi, tmp, -32767.0, 32767.0, ALU.max, ALU.min)
    eng.custom(FXTRUNC_SGN, out=tmp2, in0=tmp, in1=tmpi, s0=-32767.0, s1=32767.0)
    eng.tensor_scalar(out_ap, tmp2, final_scale, None, ALU.mult)


def emit_l1_attn(ctx, x, weights, consts):
    nc = ctx.nc
    ctx.push()
    ao_dram = nc.dram_tensor(f"l1a_aod_{ctx.uid()}", [D, T], BF16, kind="Internal")
    ln4096 = load_raw(ctx, weights["L1_attn_ln4096"].ap(), 1, 1, "l1a_l4")
    ctx.push()
    q_fm = ctx.sbuf("l1a_q", [128, DC * T], BF16)
    k_fm = ctx.sbuf("l1a_k", [128, DC * T], BF16)
    v_tm = ctx.sbuf("l1a_v", [128, 16 * 1024], BF16)
    ftmp = ctx.sbuf("l1a_ft", [128, NT])
    ftmpi = ctx.sbuf("l1a_fti", [128, NT], I32)
    ftmp2 = ctx.sbuf("l1a_ft2", [128, NT])

    TH = T // 2
    for th in range(2):
        ctx.push()
        xh = ctx.sbuf("l1a_xh", [128, DC * TH], BF16)
        for tci in range(2):
            tc = th * 2 + tci
            emit_ln_tc(ctx,
                       lambda c: x[:, c * T + tc * NT: c * T + tc * NT + NT],
                       lambda c: (xh[:, c * TH + tci * NT: c * TH + tci * NT + NT], None),
                       consts, tc, name="l1a_ln")

        ctx.push()
        wq = load_weight_bf16(ctx, weights["L1_attn_wqkvT"].ap()[:, 0:1024],
                              1024, 1024, "l1a_wq", nstg=1)
        emit_mm(ctx,
                lambda kc, mc: wq[:, kc * 1024 + mc * 128: kc * 1024 + mc * 128 + 128],
                lambda kc, tci: xh[:, kc * TH + tci * NT: kc * TH + tci * NT + NT],
                KC=8, MC=8, TCn=2,
                epilogue=lambda eng, ps, mc, tci, th=th: _fxp_epi(
                    eng, ps,
                    q_fm[:, mc * T + (th * 2 + tci) * NT: mc * T + (th * 2 + tci) * NT + NT],
                    0.125, ftmp[:, :], ftmpi[:, :], ftmp2[:, :]),
                name=f"l1a_qm{th}", psum_banks=4)
        ctx.pop()

        ctx.push()
        wk = load_weight_bf16(ctx, weights["L1_attn_wqkvT"].ap()[:, 1024:2048],
                              1024, 1024, "l1a_wk", nstg=1)
        emit_mm(ctx,
                lambda kc, mc: wk[:, kc * 1024 + mc * 128: kc * 1024 + mc * 128 + 128],
                lambda kc, tci: xh[:, kc * TH + tci * NT: kc * TH + tci * NT + NT],
                KC=8, MC=8, TCn=2,
                epilogue=lambda eng, ps, mc, tci, th=th: _fxp_epi(
                    eng, ps,
                    k_fm[:, mc * T + (th * 2 + tci) * NT: mc * T + (th * 2 + tci) * NT + NT],
                    2.0 ** -12, ftmp[:, :], ftmpi[:, :], ftmp2[:, :]),
                name=f"l1a_km{th}", psum_banks=4)
        ctx.pop()

        ctx.push()
        wv = load_weight_bf16(ctx, weights["L1_attn_wqkvT"].ap()[:, 2048:3072],
                              1024, 1024, "l1a_wv", nstg=1)
        emit_mm(ctx,
                lambda kc, mc: xh[:, kc * TH + mc * 128: kc * TH + mc * 128 + 128],
                lambda kc, nh: wv[:, kc * 1024 + nh * NT: kc * 1024 + nh * NT + NT],
                KC=8, MC=8, TCn=2,
                epilogue=lambda eng, ps, mc, nh, th=th: _fxp_epi(
                    eng, ps,
                    v_tm[:, (th * 8 + mc) * 1024 + nh * NT: (th * 8 + mc) * 1024 + nh * NT + NT],
                    2.0 ** -12, ftmp[:, :], ftmpi[:, :], ftmp2[:, :]),
                name=f"l1a_vm{th}", psum_banks=4)
        ctx.pop()
        ctx.pop()

    ctx.push()
    expS = [ctx.sbuf(f"l1a_es{s}", [128, 16 * NT], BF16) for s in range(2)]
    fint = [[ctx.sbuf(f"l1a_fi{s}{b}", [128, NT]) for b in range(2)]
            for s in range(2)]
    rint = ftmpi
    ntile = ftmp
    nint = ftmpi
    rb_sb = ftmp2
    oevt = expS[0][0:64, 0:NT]
    rrow = [fint[s][0][0:1, :] for s in range(2)]
    rrow2 = [fint[s][0][32:33, :] for s in range(2)]
    oft = ftmp[0:64, :]
    ofti = ftmpi[0:64, :]
    oft2 = ftmp2[0:64, :]
    psS = [ctx.psum("l1a_psS", [128, NT]) for _ in range(2)]
    psSum = [ctx.psum(f"l1a_psSum{s}", [1, NT]) for s in range(2)]
    psRB = [ctx.psum(f"l1a_psRB{s}", [128, NT]) for s in range(2)]
    psO = [ctx.psum(f"l1a_psO{s}", [64, NT]) for s in range(2)]

    for h in range(16):
        hb = (h % 2) * 64
        hc = h // 2
        for qp in range(2):
            with nc.Block(f"l1a_att_{h}_{qp}_{ctx.uid()}") as blk:
                p0 = ctx.val("SP")
                sv0 = ctx.val("SV")
                ss0 = ctx.val("SS")
                d0 = ctx.val("dml")

                @blk.vector
                def _(vr):
                    v = ctx.v(vr)
                    for s in range(2):
                        for kt in range(16):
                            v.wait("SP", p0 + 32 * s + kt + 1)
                            if kt >= 2:
                                v.wait("SS", ss0 + 18 * s + kt - 1)
                            v.tensor_scalar(rint[:, :], psS[kt % 2][:, :],
                                            -32767.0, 32767.0, ALU.max, ALU.min)
                            v.custom(FXTRUNC_SGN, out=fint[s][kt % 2][:, :],
                                     in0=psS[kt % 2][:, :], in1=rint[:, :],
                                     s0=-32767.0, s1=32767.0)
                    for s in range(2):
                        v.wait("SP", p0 + 65 + 17 * s)
                        v.tensor_copy(rb_sb[:, :], psRB[s][:, :])
                        for kt in range(16):
                            v.tensor_tensor(ntile[:, :],
                                            expS[s][:, kt * NT:(kt + 1) * NT],
                                            rb_sb[:, :], ALU.mult)
                            v.tensor_scalar(nint[:, :], ntile[:, :], 1.0,
                                            None, ALU.mult)
                            v.custom(FXTRUNC_POS,
                                     out=expS[s][:, kt * NT:(kt + 1) * NT],
                                     in0=ntile[:, :], in1=nint[:, :],
                                     s0=1.0, imm2=2.0 ** -12)
                    for s in range(2):
                        v.wait("SP", p0 + 98)
                        if s == 1:
                            v.raw().wait_ge(ctx.sem("dml"), d0 + 16)
                        _fxp_epi(v, psO[s][:, :], oevt, 2.0 ** -12,
                                 oft, ofti, oft2)

                @blk.scalar
                def _(sr):
                    sc = ctx.s(sr)
                    for s in range(2):
                        for kt in range(16):
                            sc.wait("SV", sv0 + 32 * s + 2 * kt + 2)
                            sc.activation(expS[s][:, kt * NT:(kt + 1) * NT],
                                          fint[s][kt % 2][:, :], AF.Exp,
                                          scale=2.0 ** -12)
                        sc.wait("SP", p0 + 32 * (s + 1))
                        sc.activation(rrow2[s], psSum[s][:, :], AF.Ln)
                        sc.activation(rrow[s], rrow2[s], AF.Exp,
                                      scale=-1.0, bias=ln4096[0:1, 0:1])

                @blk.tensor
                def _(tr, qp=qp):
                    te = ctx.t(tr)
                    for s in range(2):
                        qq = qp * 2 + s
                        for kt in range(16):
                            if s == 1 and kt < 2:
                                te.wait("SV", sv0 + 2 * (14 + kt) + 2)
                            elif kt >= 2:
                                te.wait("SV", sv0 + 32 * s + 2 * kt - 2)
                            te.matmul(psS[kt % 2][:, :],
                                      k_fm[hb:hb + 64, hc * T + kt * 128: hc * T + kt * 128 + 128],
                                      q_fm[hb:hb + 64, hc * T + qq * NT: hc * T + qq * NT + NT],
                                      start=True, stop=True)
                        for kt in range(16):
                            te.wait("SS", ss0 + 18 * s + kt + 1)
                            te.matmul(psSum[s][:, :], consts["onescolb"][:, :],
                                      expS[s][:, kt * NT:(kt + 1) * NT],
                                      start=(kt == 0), stop=(kt == 15))
                    for s in range(2):
                        te.wait("SS", ss0 + 18 * (s + 1))
                        te.matmul(psRB[s][:, :], consts["onesrow"][:, :],
                                  rrow[s], start=True, stop=True)
                        for kt in range(16):
                            te.wait("SV", sv0 + 65 + 49 * s + 3 * kt + 3)
                            te.matmul(psO[s][:, :],
                                      v_tm[:, kt * 1024 + h * 64: kt * 1024 + h * 64 + 64],
                                      expS[s][:, kt * NT:(kt + 1) * NT],
                                      start=(kt == 0), stop=(kt == 15))

                @blk.sync
                def _(s_, qp=qp):
                    for s in range(2):
                        qq = qp * 2 + s
                        s_.wait_ge(ctx.sem("SV"), sv0 + 166 + 4 * s)
                        s_.dma_start(
                            ao_dram.ap()[hc * 128 + hb: hc * 128 + hb + 64,
                                         qq * NT:(qq + 1) * NT],
                            oevt).then_inc(ctx.sem("dml"), 16)
                        ctx.bump("dml", 16)
                    s_.wait_ge(ctx.sem("dml"), ctx.val("dml"))
    ctx.pop()
    ctx.pop()

    ctx.push()
    ao_fm = ctx.sbuf("l1a_ao", [128, DC * T], BF16)
    emit_restore(ctx, ao_fm, ao_dram.ap(), DC, T, "l1a_ao")
    wp = load_weight_bf16(ctx, weights["L1_attn_wprojT"].ap(), 1024, 1024, "l1a_wp")
    bp = load_raw(ctx, weights["L1_attn_bpc"].ap(), 128, 8, "l1a_bp")
    ftmp = ctx.sbuf("l1a_pt", [128, NT])
    ftmpi = ctx.sbuf("l1a_pti", [128, NT], I32)
    ftmp2 = ctx.sbuf("l1a_pt2", [128, NT])

    def epi_p(eng, ps, mc, tc):
        eng.tensor_scalar(ftmp[:, :], ps, bp[:, mc:mc + 1], 4096.0, ALU.add, ALU.mult)
        eng.tensor_scalar(ftmpi[:, :], ftmp[:, :], -32767.0, 32767.0, ALU.max, ALU.min)
        eng.custom(FXTRUNC_SGN, out=ftmp2[:, :], in0=ftmp[:, :], in1=ftmpi[:, :],
                   s0=-32767.0, s1=32767.0)
        xs = x[:, mc * T + tc * NT: mc * T + tc * NT + NT]
        eng.scalar_tensor_tensor(xs, ftmp2[:, :], 2.0 ** -12, xs, ALU.mult, ALU.add)

    emit_mm(ctx,
            lambda kc, mc: wp[:, kc * 1024 + mc * 128: kc * 1024 + mc * 128 + 128],
            lambda kc, tc: ao_fm[:, kc * T + tc * NT: kc * T + tc * NT + NT],
            KC=8, MC=8, TCn=TC4, epilogue=epi_p, name="l1a_wp", psum_banks=3)
    ctx.pop()
    ctx.pop()


def emit_l0_ffn(ctx, x, weights, consts):
    nc = ctx.nc
    ctx.push()
    mask = ctx.sbuf("l0f_mk", [64, T], BF16)
    inner = ctx.sbuf("l0f_in", [128, 32 * 128], BF16)
    inner2 = ctx.sbuf("l0f_in2", [128, 32 * 128], BF16)
    rp = load_raw(ctx, weights["L0_ffn_rp"].ap(), 64, 32 * 128, "l0f_rp", BF16)
    besf = load_raw(ctx, weights["L0_ffn_besf"].ap(), 128, 32, "l0f_besf")

    ctx.push()
    xh = ctx.sbuf("l0f_xh", [128, DC * T], BF16)
    emit_ln(ctx, x, xh, consts, name="l0f_ln")

    ctx.push()
    logits = ctx.sbuf("l0f_lg", [64, T])
    gmax = ctx.sbuf("l0f_gm", [64, 128])
    ctx.push()
    ctl = load_weight_bf16(ctx, weights["L0_ffn_ctlR"].ap(), 1024, 64, "l0f_ctl")
    tieT = load_raw(ctx, weights["L0_ffn_tieT"].ap(), 64, T, "l0f_tie")
    emit_mm(ctx, lambda kc, mc: ctl[:, kc * 64:(kc + 1) * 64],
            lambda kc, tc: xh[:, kc * T + tc * NT: kc * T + tc * NT + NT],
            KC=8, MC=1, TCn=TC4,
            epilogue=lambda eng, ps, mc, tc: eng.tensor_tensor(
                logits[:, tc * NT:(tc + 1) * NT], ps,
                tieT[:, tc * NT:(tc + 1) * NT], ALU.add),
            name="l0f_lgmm", psum_banks=2, Mp=64)
    ctx.pop()

    with nc.Block(f"l0f_mask_{ctx.uid()}") as blk:
        @blk.vector
        def _(vr):
            v = ctx.v(vr)
            lg3 = logits[:, :].rearrange("p (a b) -> p a b", b=16)
            v.reduce_max(gmax[:, :], lg3, AX.X)
            gb = bass.AP(gmax, 0, [[128, 64], [1, 128], [0, 16]])
            v.tensor_tensor(mask[:, :].rearrange("p (a b) -> p a b", b=16),
                            lg3, gb, ALU.is_equal)
    ctx.pop()

    ctx.push()
    f1R = load_weight_bf16(ctx, weights["L0_ffn_f1R"].ap(), 1024, 4096, "l0f_f1",
                           nstg=1)
    mrep = ctx.sbuf("l0f_mr", [128, NT], BF16)
    ptile = ctx.sbuf("l0f_pt", [128, NT])
    psG = [ctx.psum("l0f_psG", [128, NT]) for _ in range(2)]
    psM = ctx.psum("l0f_psM", [128, NT])
    for tc in range(TC4):
        for ch in range(32):
            with nc.Block(f"l0f_g_{tc}_{ch}_{ctx.uid()}") as blk:
                p0 = ctx.val("SP")
                sv0 = ctx.val("SV")

                @blk.vector
                def _(vr):
                    v = ctx.v(vr)
                    v.wait("SP", p0 + 1)
                    v.tensor_copy(mrep[:, :], psM[:, :])
                    v.wait("SP", p0 + 9)
                    v.tensor_tensor(ptile[:, :], psG[ch % 2][:, :], mrep[:, :],
                                    ALU.mult)
                    with nc.allow_low_precision(reason="bf16 store of fp32 group-sum"):
                        v.reduce_sum(inner[:, ch * 128 + tc * 32: ch * 128 + tc * 32 + 32],
                                     ptile[:, :].rearrange("p (a b) -> p a b", b=16), AX.X)

                @blk.tensor
                def _(tr):
                    te = ctx.t(tr)
                    te.matmul(psM[:, :], rp[:, ch * 128:(ch + 1) * 128],
                              mask[:, tc * NT:(tc + 1) * NT], start=True, stop=True)
                    for kc in range(8):
                        te.matmul(psG[ch % 2][:, :],
                                  f1R[:, kc * 4096 + ch * 128: kc * 4096 + ch * 128 + 128],
                                  xh[:, kc * T + tc * NT: kc * T + tc * NT + NT],
                                  start=(kc == 0), stop=(kc == 7))

    with nc.Block(f"l0f_relu_{ctx.uid()}") as blk:
        @blk.scalar
        def _(sr):
            sc = ctx.s(sr)
            for ch in range(32):
                sc.activation(inner2[:, ch * 128:(ch + 1) * 128],
                              inner[:, ch * 128:(ch + 1) * 128], AF.Relu,
                              bias=besf[:, ch:ch + 1])
    ctx.pop()
    ctx.pop()

    ctx.push()
    f2R = load_weight_bf16(ctx, weights["L0_ffn_f2R"].ap(), 4096, 1024, "l0f_f2",
                           nstg=1)
    V = ctx.sbuf("l0f_V", [128, 32 * NT], BF16)
    mrep = ctx.sbuf("l0f_mr2", [128, NT], BF16)
    psM = ctx.psum("l0f_psM2", [128, NT])
    for tc in range(TC4):
        for ch in range(32):
            with nc.Block(f"l0f_v_{tc}_{ch}_{ctx.uid()}") as blk:
                p0 = ctx.val("SP")

                @blk.vector
                def _(vr):
                    v = ctx.v(vr)
                    v.wait("SP", p0 + 1)
                    v.tensor_copy(mrep[:, :], psM[:, :])
                    iexp = bass.AP(inner2, ch * 128 + tc * 32,
                                   [[32 * 128, 128], [1, 32], [0, 16]])
                    v.tensor_tensor(V[:, ch * NT:(ch + 1) * NT].rearrange(
                        "p (a b) -> p a b", b=16), iexp, mrep[:, :].rearrange(
                        "p (a b) -> p a b", b=16), ALU.mult)

                @blk.tensor
                def _(tr):
                    te = ctx.t(tr)
                    te.matmul(psM[:, :], rp[:, ch * 128:(ch + 1) * 128],
                              mask[:, tc * NT:(tc + 1) * NT], start=True, stop=True)

        emit_mm(ctx,
                lambda kc, mc: f2R[:, kc * 1024 + mc * 128: kc * 1024 + mc * 128 + 128],
                lambda kc, _tc: V[:, kc * NT:(kc + 1) * NT],
                KC=32, MC=8, TCn=1,
                epilogue=lambda eng, ps, mc, _tc, tc=tc: eng.tensor_tensor(
                    x[:, mc * T + tc * NT: mc * T + tc * NT + NT],
                    x[:, mc * T + tc * NT: mc * T + tc * NT + NT], ps, ALU.add),
                name=f"l0f_f2_{tc}")
    ctx.pop()
    ctx.pop()


def emit_l0_attn(ctx, x, weights, consts):
    nc = ctx.nc
    ctx.push()
    rg = load_raw(ctx, weights["L0_rg"].ap(), 128, 16 * 128, "l0a_rg", BF16)
    bd = load_raw(ctx, weights["L0_bd"].ap(), 128, 16 * 128, "l0a_bd", BF16)
    r2 = load_raw(ctx, weights["L0_r2"].ap(), 16, 128, "l0a_r2", BF16)
    alog = load_raw(ctx, weights["L0_attn_alogpp"].ap(), 128, 128, "l0a_al")
    app = ctx.sbuf("l0a_app", [128, 128])
    bdl = load_raw(ctx, weights["L0_attn_bdlc"].ap(), 64, 1, "l0a_bdl")
    bbm = load_raw(ctx, weights["L0_attn_bbmc"].ap(), 16, 1, "l0a_bbm")
    bcm = load_raw(ctx, weights["L0_attn_bcmc"].ap(), 16, 1, "l0a_bcm")
    bup = load_raw(ctx, weights["L0_attn_bupc"].ap(), 128, 8, "l0a_bup")

    with nc.Block(f"l0a_app_{ctx.uid()}") as blk:
        @blk.scalar
        def _(sr):
            sc = ctx.s(sr)
            sc.activation(app[:, :], alog[:, :], AF.Exp)
        @blk.vector
        def _(vr):
            v = ctx.v(vr)
            v.wait("SS", ctx.val("SS"))
            v.tensor_scalar(app[:, :], app[:, :], -1.0, None, ALU.mult)

    delta_bf = ctx.sbuf("l0a_delta", [128, DC * T], BF16)
    dx_bf = ctx.sbuf("l0a_dx", [128, DC * T], BF16)
    bm_bf = ctx.sbuf("l0a_bm", [16, T], BF16)
    cm_bf = ctx.sbuf("l0a_cm", [16, T], BF16)
    ctx.push()
    xh = ctx.sbuf("l0a_xh", [128, DC * T], BF16)
    emit_ln(ctx, x, xh, consts, name="l0a_ln")

    dlow_bf = ctx.sbuf("l0a_dlow", [64, T], BF16)
    praw = delta_bf
    ctx.push()
    wdbc = load_weight_bf16(ctx, weights["L0_attn_wdbcT"].ap(), 1024, 96, "l0a_wdbc")

    emit_mm(ctx, lambda kc, mc: wdbc[:, kc * 96: kc * 96 + 64],
            lambda kc, tc: xh[:, kc * T + tc * NT: kc * T + tc * NT + NT],
            KC=8, MC=1, TCn=TC4,
            epilogue=lambda eng, ps, mc, tc: eng.tensor_scalar(
                dlow_bf[:, tc * NT:(tc + 1) * NT], ps,
                bdl[:, 0:1], None, ALU.add),
            name="l0a_dbc1", Mp=64)

    emit_mm(ctx, lambda kc, mc: wdbc[:, kc * 96 + 64: kc * 96 + 80],
            lambda kc, tc: xh[:, kc * T + tc * NT: kc * T + tc * NT + NT],
            KC=8, MC=1, TCn=TC4,
            epilogue=lambda eng, ps, mc, tc: eng.tensor_scalar(
                bm_bf[:, tc * NT:(tc + 1) * NT], ps,
                bbm[:, 0:1], None, ALU.add),
            name="l0a_dbc2", Mp=16)

    emit_mm(ctx, lambda kc, mc: wdbc[:, kc * 96 + 80: kc * 96 + 96],
            lambda kc, tc: xh[:, kc * T + tc * NT: kc * T + tc * NT + NT],
            KC=8, MC=1, TCn=TC4,
            epilogue=lambda eng, ps, mc, tc: eng.tensor_scalar(
                cm_bf[:, tc * NT:(tc + 1) * NT], ps,
                bcm[:, 0:1], None, ALU.add),
            name="l0a_dbc3", Mp=16)
    ctx.pop()

    ctx.push()
    wup = ctx.sbuf("l0a_wup", [64, 1024], BF16)
    ctx.push()
    wup_f = load_raw(ctx, weights["L0_attn_wupT"].ap(), 64, 1024, "l0a_wupf")
    with nc.Block(f"l0a_wupc_{ctx.uid()}") as blk:
        @blk.vector
        def _(vr):
            ctx.v(vr).tensor_copy(wup[:, :], wup_f[:, :])
    ctx.pop()
    emit_mm(ctx, lambda kc, mc: wup[:, mc * 128:(mc + 1) * 128],
            lambda kc, tc: dlow_bf[:, tc * NT:(tc + 1) * NT],
            KC=1, MC=8, TCn=TC4,
            epilogue=lambda eng, ps, mc, tc: eng.tensor_scalar(
                praw[:, mc * T + tc * NT: mc * T + tc * NT + NT], ps,
                bup[:, mc:mc + 1], None, ALU.add),
            name="l0a_dup")
    ctx.pop()

    ctx.push()
    spt = ctx.sbuf("l0a_spt", [128, NT])
    spe = ctx.sbuf("l0a_spe", [128, NT])
    spl = ctx.sbuf("l0a_spl", [128, NT])
    spr = ctx.sbuf("l0a_spr", [128, NT])
    for i in range(DC * TC4):
        mc, tc = divmod(i, TC4)
        sl = slice(mc * T + tc * NT, mc * T + tc * NT + NT)
        with nc.Block(f"l0a_sp_{i}_{ctx.uid()}") as blk:
            sv0 = ctx.val("SV")
            ss0 = ctx.val("SS")

            @blk.vector
            def _(vr):
                v = ctx.v(vr)
                v.tensor_scalar(spr[:, :], praw[:, sl], 0.0, None, ALU.max)
                v.scalar_tensor_tensor(spt[:, :], spr[:, :], 2.0, praw[:, sl],
                                       ALU.mult, ALU.subtract)
                v.wait("SS", ss0 + 2)
                v.tensor_tensor(delta_bf[:, sl], spr[:, :], spl[:, :], ALU.add)
                v.tensor_tensor(dx_bf[:, sl], delta_bf[:, sl], xh[:, sl], ALU.mult)

            @blk.scalar
            def _(sr):
                sc = ctx.s(sr)
                sc.wait("SV", sv0 + 2)
                sc.activation(spe[:, :], spt[:, :], AF.Exp, scale=-1.0)
                sc.activation(spl[:, :], spe[:, :], AF.Ln, bias=consts["onescol"][:, :])
    ctx.pop()
    ctx.pop()

    bmpp = ctx.sbuf("l0a_bmpp", [128, T])
    cmpp = ctx.sbuf("l0a_cmpp", [128, T])
    ctx.push()
    pspp = ctx.psum("l0a_pspp", [128, NT])
    with nc.Block(f"l0a_pp_{ctx.uid()}") as blk:
        p0 = ctx.val("SP")
        sv0 = ctx.val("SV")

        @blk.vector
        def _(vr):
            v = ctx.v(vr)
            for i in range(8):
                v.wait("SP", p0 + i + 1)
                dst = (bmpp if i < 4 else cmpp)
                tc = i % 4
                v.tensor_copy(dst[:, tc * NT:(tc + 1) * NT], pspp[:, :])

        @blk.tensor
        def _(tr):
            te = ctx.t(tr)
            for i in range(8):
                src_t = (bm_bf if i < 4 else cm_bf)
                tc = i % 4
                if i >= 1:
                    te.wait("SV", sv0 + i)
                te.matmul(pspp[:, :], r2[:, :], src_t[:, tc * NT:(tc + 1) * NT],
                          start=True, stop=True)
    ctx.pop()

    ctx.push()
    atile2 = [ctx.sbuf("l0a_a0", [128, T]), ctx.sbuf("l0a_a1", [128, T])]
    btile2 = [ctx.sbuf("l0a_b0", [128, T]), ctx.sbuf("l0a_b1", [128, T])]
    htile = ctx.sbuf("l0a_h", [128, T], BF16)
    ptile2 = [ctx.sbuf("l0a_p0", [128, T], BF16), ctx.sbuf("l0a_p1", [128, T], BF16)]
    psdr2 = [ctx.psum(f"l0a_psdr{i}", [128, NT]) for i in range(2)]
    psdx2 = [ctx.psum(f"l0a_psdx{i}", [128, NT]) for i in range(2)]
    psy = [ctx.psum("l0a_psy", [128, NT]) for _ in range(4)]
    for dchunk in range(DC):
        with nc.Block(f"l0a_scan_{dchunk}_{ctx.uid()}") as blk:
            p0 = ctx.val("SP")
            sv0 = ctx.val("SV")
            ss0 = ctx.val("SS")

            def base(g):
                return 0 if g == 0 else 8 + 12 * (g - 1)

            def psdr_done(g, tc):
                return p0 + base(g) + 2 * tc + 1

            def psdx_done(g, tc):
                return p0 + base(g) + 2 * tc + 2

            def psy_done(g, tc):
                return p0 + (base(g + 1) + 8 + tc + 1 if g <= 14 else 188 + tc + 1)

            @blk.scalar
            def _(sr, dchunk=dchunk):
                sc = ctx.s(sr)
                for g in range(16):
                    G = dchunk * 16 + g
                    for tc in range(TC4):
                        sc.wait("SP", psdr_done(g, tc))
                        if g >= 2 and tc == 0:
                            sc.wait("SV", sv0 + 6 * g - 7)
                        sc.activation(atile2[g % 2][:, tc * NT:(tc + 1) * NT],
                                      psdr2[g % 2][:, :], AF.Exp,
                                      scale=app[:, G:G + 1])

            @blk.vector
            def _(vr):
                v = ctx.v(vr)
                for g in range(16):
                    for tc in range(TC4):
                        v.wait("SP", psdx_done(g, tc))
                        v.tensor_tensor(btile2[g % 2][:, tc * NT:(tc + 1) * NT],
                                        psdx2[g % 2][:, :],
                                        bmpp[:, tc * NT:(tc + 1) * NT], ALU.mult)
                    v.wait("SS", ss0 + 4 * g + 4)
                    v.tensor_tensor_scan(htile[:, :], atile2[g % 2][:, :],
                                         btile2[g % 2][:, :], 0.0, ALU.mult, ALU.add)
                    if g >= 2:
                        v.wait("SP", psy_done(g - 2, 3))
                    v.tensor_tensor(ptile2[g % 2][:, :], htile[:, :], cmpp[:, :],
                                    ALU.mult)

            @blk.tensor
            def _(tr):
                te = ctx.t(tr)

                def psdrx_mm(g):
                    for tc in range(TC4):
                        if tc >= 1:
                            te.wait("SS", ss0 + 4 * g + tc)
                        elif g >= 2:
                            te.wait("SS", ss0 + 4 * (g - 2) + 4)
                        te.matmul(psdr2[g % 2][:, :], rg[:, g * 128:(g + 1) * 128],
                                  delta_bf[:, dchunk * T + tc * NT: dchunk * T + tc * NT + NT],
                                  start=True, stop=True)
                        if tc >= 1:
                            te.wait("SV", sv0 + 6 * g + tc)
                        elif g >= 2:
                            te.wait("SV", sv0 + 6 * (g - 2) + 4)
                        te.matmul(psdx2[g % 2][:, :], rg[:, g * 128:(g + 1) * 128],
                                  dx_bf[:, dchunk * T + tc * NT: dchunk * T + tc * NT + NT],
                                  start=True, stop=True)

                def psy_mm(g):
                    te.wait("SV", sv0 + 6 * g + 6)
                    for tc in range(TC4):
                        te.matmul(psy[tc][:, :], bd[:, g * 128:(g + 1) * 128],
                                  ptile2[g % 2][:, tc * NT:(tc + 1) * NT],
                                  start=(g == 0), stop=(g == 15))

                psdrx_mm(0)
                for g in range(1, 16):
                    psdrx_mm(g)
                    psy_mm(g - 1)
                psy_mm(15)

        with nc.Block(f"l0a_y_{dchunk}_{ctx.uid()}") as blk:
            @blk.vector
            def _(vr):
                v = ctx.v(vr)
                for tc in range(TC4):
                    xs = x[:, dchunk * T + tc * NT: dchunk * T + tc * NT + NT]
                    v.tensor_tensor(xs, xs, psy[tc][:, :], ALU.add)
    ctx.pop()
    ctx.pop()


def emit_l2_attn(ctx, x, weights, consts):
    nc = ctx.nc
    DIC = 16
    PW = T + 3
    ctx.push()
    zs_dram = nc.dram_tensor(f"l2a_zsd_{ctx.uid()}", [DIC * 128, T], BF16,
                             kind="Internal")
    xsp_dram = nc.dram_tensor(f"l2a_xspd_{ctx.uid()}", [DIC * 128, PW], BF16,
                              kind="Internal")
    xs_dram = nc.dram_tensor(f"l2a_xsd_{ctx.uid()}", [DIC * 128, T], BF16,
                             kind="Internal")
    y1_dram = nc.dram_tensor(f"l2a_y1d_{ctx.uid()}", [DIC * 128, T], BF16,
                             kind="Internal")

    convw = load_raw(ctx, weights["L2_attn_convwpp"].ap(), 128, 64, "l2a_cw")
    convb = load_raw(ctx, weights["L2_attn_convbc"].ap(), 128, 16, "l2a_cb")
    bdt = load_raw(ctx, weights["L2_attn_bdtc"].ap(), 128, 16, "l2a_bdt")
    dcol = load_raw(ctx, weights["L2_attn_dcol"].ap(), 128, 16, "l2a_dcol")
    identb = consts["identb"]
    bs_bf = ctx.sbuf("l2a_bs", [16, T], BF16)

    ctx.push()
    xh = ctx.sbuf("l2a_xh", [128, DC * T], BF16)
    emit_ln(ctx, x, xh, consts, name="l2a_ln")

    ctx.push()
    zs = ctx.sbuf("l2a_zsil", [128, DIC * T], BF16)
    ctx.push()
    wz = load_weight_bf16(ctx, weights["L2_attn_winz"].ap(), 1024, 2048, "l2a_wz", nstg=1)
    emit_mm(ctx,
            lambda kc, mc: wz[:, kc * 2048 + mc * 128: kc * 2048 + (mc + 1) * 128],
            lambda kc, tc: xh[:, kc * T + tc * NT: kc * T + tc * NT + NT],
            KC=8, MC=DIC, TCn=TC4,
            epilogue=lambda eng, ps, mc, tc: eng.activation(
                zs[:, mc * T + tc * NT: mc * T + tc * NT + NT], ps, AF.Silu),
            name="l2a_zmm", epi_engine="scalar")
    ctx.pop()
    emit_spill(ctx, zs, zs_dram.ap(), DIC, T, "l2a_zs")
    ctx.pop()

    ctx.push()
    wxi = load_weight_bf16(ctx, weights["L2_attn_winx"].ap(), 1024, 2048, "l2a_wxi", nstg=1)
    xpb = ctx.sbuf("l2a_xpb", [128, PW], BF16)
    for mc in range(DIC):
        with nc.Block(f"l2a_xpad_{mc}_{ctx.uid()}") as blk:
            @blk.vector
            def _(vr):
                ctx.v(vr).memset(xpb[:, 0:3], 0.0)
        emit_mm(ctx,
                lambda kc, _mc, mc=mc: wxi[:, kc * 2048 + mc * 128: kc * 2048 + (mc + 1) * 128],
                lambda kc, tc: xh[:, kc * T + tc * NT: kc * T + tc * NT + NT],
                KC=8, MC=1, TCn=TC4,
                epilogue=lambda eng, ps, _mc, tc: eng.tensor_copy(
                    xpb[:, 3 + tc * NT: 3 + tc * NT + NT], ps),
                name=f"l2a_xmm{mc}")
        emit_spill(ctx, xpb, xsp_dram.ap()[mc * 128:(mc + 1) * 128, :], 1, PW,
                   f"l2a_xsp{mc}")
    ctx.pop()
    ctx.pop()

    dt_dram = nc.dram_tensor(f"l2a_dtd_{ctx.uid()}", [DIC * 128, T], BF16,
                             kind="Internal")
    ctx.push()
    xs = ctx.sbuf("l2a_xs", [128, DIC * T], BF16)
    xpb2 = ctx.sbuf("l2a_xpb2", [128, PW], BF16)
    diag4 = ctx.sbuf("l2a_diag4", [128, 4 * 128], BF16)
    psc = ctx.psum("l2a_psc", [128, NT])
    for mc in range(DIC):
        emit_restore(ctx, xpb2, xsp_dram.ap()[mc * 128:(mc + 1) * 128, :], 1, PW,
                     f"l2a_xrs{mc}")
        with nc.Block(f"l2a_conv_{mc}_{ctx.uid()}") as blk:
            p0 = ctx.val("SP")
            sv0 = ctx.val("SV")
            ss0 = ctx.val("SS")

            @blk.vector
            def _(vr):
                v = ctx.v(vr)
                for k in range(4):
                    v.tensor_scalar(diag4[:, k * 128:(k + 1) * 128], identb[:, :],
                                    convw[:, mc * 4 + k: mc * 4 + k + 1],
                                    None, ALU.mult)

            @blk.scalar
            def _(sr):
                sc = ctx.s(sr)
                for tc in range(TC4):
                    sc.wait("SP", p0 + 4 * (tc + 1))
                    sc.activation(xs[:, mc * T + tc * NT: mc * T + tc * NT + NT],
                                  psc[:, :], AF.Silu, bias=convb[:, mc:mc + 1])

            @blk.tensor
            def _(tr):
                te = ctx.t(tr)
                te.wait("SV", sv0 + 4)
                for tc in range(TC4):
                    if tc >= 1:
                        te.wait("SS", ss0 + tc)
                    for k in range(4):
                        te.matmul(psc[:, :], diag4[:, k * 128:(k + 1) * 128],
                                  xpb2[:, k + tc * NT: k + tc * NT + NT],
                                  start=(k == 0), stop=(k == 3))

    for mq in range(4):
        ctx.push()
        wq = load_weight_bf16(ctx, weights[f"L2_attn_wdtq{mq}"].ap(), 2048, 512,
                              f"l2a_wdt{mq}")
        dtq = ctx.sbuf(f"l2a_dtq{mq}", [128, 4 * T], BF16)
        emit_mm(ctx,
                lambda kc, mc, mq=mq: wq[:, kc * 512 + mc * 128: kc * 512 + (mc + 1) * 128],
                lambda kc, tc: xs[:, kc * T + tc * NT: kc * T + tc * NT + NT],
                KC=DIC, MC=4, TCn=TC4,
                epilogue=lambda eng, ps, mc, tc, mq=mq: eng.tensor_scalar(
                    dtq[:, mc * T + tc * NT: mc * T + tc * NT + NT],
                    ps, bdt[:, mq * 4 + mc: mq * 4 + mc + 1], None, ALU.add),
                name=f"l2a_dtm{mq}")
        spt = ctx.sbuf(f"l2a_spt{mq}", [128, NT])
        spe = ctx.sbuf(f"l2a_spe{mq}", [128, NT])
        spl = ctx.sbuf(f"l2a_spl{mq}", [128, NT])
        spr = ctx.sbuf(f"l2a_spr{mq}", [128, NT])
        for i in range(4 * TC4):
            mc, tc = divmod(i, TC4)
            sl = slice(mc * T + tc * NT, mc * T + tc * NT + NT)
            with nc.Block(f"l2a_sp_{mq}_{i}_{ctx.uid()}") as blk:
                sv0 = ctx.val("SV")
                ss0 = ctx.val("SS")

                @blk.vector
                def _(vr):
                    v = ctx.v(vr)
                    v.tensor_scalar(spr[:, :], dtq[:, sl], 0.0, None, ALU.max)
                    v.scalar_tensor_tensor(spt[:, :], spr[:, :], 2.0, dtq[:, sl],
                                           ALU.mult, ALU.subtract)
                    v.wait("SS", ss0 + 2)
                    v.tensor_tensor(dtq[:, sl], spr[:, :], spl[:, :], ALU.add)

                @blk.scalar
                def _(sr):
                    sc = ctx.s(sr)
                    sc.wait("SV", sv0 + 2)
                    sc.activation(spe[:, :], spt[:, :], AF.Exp, scale=-1.0)
                    sc.activation(spl[:, :], spe[:, :], AF.Ln,
                                  bias=consts["onescol"][:, :])
        emit_spill(ctx, dtq, dt_dram.ap()[mq * 512:(mq + 1) * 512, :], 4, T,
                   f"l2a_dtsp{mq}")
        ctx.pop()

    ctx.push()
    wx = load_weight_bf16(ctx, weights["L2_attn_wx"].ap(), 2048, 16, "l2a_wxs")
    emit_mm(ctx, lambda kc, mc: wx[:, kc * 16:(kc + 1) * 16],
            lambda kc, tc: xs[:, kc * T + tc * NT: kc * T + tc * NT + NT],
            KC=DIC, MC=1, TCn=TC4,
            epilogue=lambda eng, ps, mc, tc: eng.tensor_copy(
                bs_bf[:, tc * NT:(tc + 1) * NT], ps),
            name="l2a_bsm", Mp=16)
    ctx.pop()
    emit_spill(ctx, xs, xs_dram.ap(), DIC, T, "l2a_xs")
    ctx.pop()

    ctx.push()
    rg = load_raw(ctx, weights["L0_rg"].ap(), 128, 16 * 128, "l2a_rg", BF16)
    bd = load_raw(ctx, weights["L0_bd"].ap(), 128, 16 * 128, "l2a_bd", BF16)
    r2 = load_raw(ctx, weights["L0_r2"].ap(), 16, 128, "l2a_r2", BF16)
    alog = load_raw(ctx, weights["L2_attn_alogpp"].ap(), 128, 256, "l2a_al")
    app = ctx.sbuf("l2a_app", [128, 256])
    with nc.Block(f"l2a_app_{ctx.uid()}") as blk:
        @blk.scalar
        def _(sr):
            ctx.s(sr).activation(app[:, :], alog[:, :], AF.Exp)
        @blk.vector
        def _(vr):
            v = ctx.v(vr)
            v.wait("SS", ctx.val("SS"))
            v.tensor_scalar(app[:, :], app[:, :], -1.0, None, ALU.mult)

    bspp = ctx.sbuf("l2a_bspp", [128, T])
    ctx.push()
    pspp = ctx.psum("l2a_pspp", [128, NT])
    with nc.Block(f"l2a_pp_{ctx.uid()}") as blk:
        p0 = ctx.val("SP")
        sv0 = ctx.val("SV")

        @blk.vector
        def _(vr):
            v = ctx.v(vr)
            for i in range(TC4):
                v.wait("SP", p0 + i + 1)
                v.tensor_copy(bspp[:, i * NT:(i + 1) * NT], pspp[:, :])

        @blk.tensor
        def _(tr):
            te = ctx.t(tr)
            for i in range(TC4):
                if i >= 1:
                    te.wait("SV", sv0 + i)
                te.matmul(pspp[:, :], r2[:, :], bs_bf[:, i * NT:(i + 1) * NT],
                          start=True, stop=True)
    ctx.pop()

    atile2 = [ctx.sbuf("l2a_a0", [128, T]), ctx.sbuf("l2a_a1", [128, T])]
    htile2 = [ctx.sbuf("l2a_h0", [128, T], BF16), ctx.sbuf("l2a_h1", [128, T], BF16)]
    y1c = ctx.sbuf("l2a_y1c", [128, T], BF16)
    dtc = ctx.sbuf("l2a_dtc", [128, T], BF16)
    psdr4 = [ctx.psum(f"l2a_psdr{i}", [128, NT]) for i in range(4)]
    psy = [ctx.psum("l2a_psy", [128, NT]) for _ in range(4)]
    for dchunk in range(DIC):
        emit_restore(ctx, dtc, dt_dram.ap()[dchunk * 128:(dchunk + 1) * 128, :],
                     1, T, f"l2a_dtr{dchunk}")
        with nc.Block(f"l2a_scan_{dchunk}_{ctx.uid()}") as blk:
            p0 = ctx.val("SP")
            sv0 = ctx.val("SV")
            ss0 = ctx.val("SS")

            def psdr_done(g, tc):
                return p0 + (tc + 1 if g == 0 else 8 * g - 4 + tc + 1)

            def psy_done(g, tc):
                return p0 + (8 * (g + 1) + tc + 1 if g <= 14 else 124 + tc + 1)

            @blk.scalar
            def _(sr, dchunk=dchunk):
                sc = ctx.s(sr)
                for g in range(16):
                    G = dchunk * 16 + g
                    for tc in range(TC4):
                        sc.wait("SP", psdr_done(g, tc))
                        if g >= 2 and tc == 0:
                            sc.wait("SV", sv0 + g - 1)
                        sc.activation(atile2[g % 2][:, tc * NT:(tc + 1) * NT],
                                      psdr4[2 * (g % 2) + (tc % 2)][:, :],
                                      AF.Exp, scale=app[:, G:G + 1])

            @blk.vector
            def _(vr):
                v = ctx.v(vr)
                for g in range(16):
                    v.wait("SS", ss0 + 4 * g + 4)
                    if g >= 2:
                        v.wait("SP", psy_done(g - 2, 3))
                    v.tensor_tensor_scan(htile2[g % 2][:, :], atile2[g % 2][:, :],
                                         bspp[:, :], 0.0, ALU.mult, ALU.add)

            @blk.tensor
            def _(tr):
                te = ctx.t(tr)

                def psdr_mm(g):
                    for tc in range(TC4):
                        if tc >= 2:
                            te.wait("SS", ss0 + 4 * g + tc - 1)
                        elif g >= 2:
                            te.wait("SS", ss0 + 4 * (g - 2) + tc + 3)
                        te.matmul(psdr4[2 * (g % 2) + (tc % 2)][:, :],
                                  rg[:, g * 128:(g + 1) * 128],
                                  dtc[:, tc * NT: tc * NT + NT],
                                  start=True, stop=True)

                def psy_mm(g):
                    te.wait("SV", sv0 + g + 1)
                    for tc in range(TC4):
                        te.matmul(psy[tc][:, :], bd[:, g * 128:(g + 1) * 128],
                                  htile2[g % 2][:, tc * NT:(tc + 1) * NT],
                                  start=(g == 0), stop=(g == 15))

                psdr_mm(0)
                for g in range(1, 16):
                    psdr_mm(g)
                    psy_mm(g - 1)
                psy_mm(15)

        with nc.Block(f"l2a_y_{dchunk}_{ctx.uid()}") as blk:
            @blk.vector
            def _(vr):
                v = ctx.v(vr)
                for tc in range(TC4):
                    v.tensor_copy(y1c[:, tc * NT:(tc + 1) * NT], psy[tc][:, :])
        emit_spill(ctx, y1c, y1_dram.ap()[dchunk * 128:(dchunk + 1) * 128, :], 1, T,
                   f"l2a_y1{dchunk}")
    ctx.pop()

    ctx.push()
    wout = load_weight_bf16(ctx, weights["L2_attn_wout"].ap(), 2048, 1024, "l2a_wo")
    y2 = ctx.sbuf("l2a_y2", [128, DIC * T], BF16)
    y1b = ctx.sbuf("l2a_y1b", [128, T], BF16)
    xsb = ctx.sbuf("l2a_xsb", [128, T], BF16)
    zsb = ctx.sbuf("l2a_zsb", [128, T], BF16)
    for c in range(DIC):
        with nc.Block(f"l2a_rst_{c}_{ctx.uid()}") as blk:
            @blk.sync
            def _(s, c=c):
                for buf, dr in ((y1b, y1_dram), (xsb, xs_dram), (zsb, zs_dram)):
                    s.dma_start(buf[:, :],
                                dr.ap()[c * 128:(c + 1) * 128, :]).then_inc(
                                    ctx.sem("dml"), 16)
                    ctx.bump("dml", 16)
                s.wait_ge(ctx.sem("dml"), ctx.val("dml"))
        with nc.Block(f"l2a_y2_{c}_{ctx.uid()}") as blk:
            @blk.vector
            def _(vr):
                v = ctx.v(vr)
                v.scalar_tensor_tensor(y2[:, c * T:(c + 1) * T], xsb[:, :],
                                       dcol[:, c:c + 1], y1b[:, :],
                                       ALU.mult, ALU.add)
                v.tensor_tensor(y2[:, c * T:(c + 1) * T], y2[:, c * T:(c + 1) * T],
                                zsb[:, :], ALU.mult)

    emit_mm(ctx,
            lambda kc, mc: wout[:, kc * 1024 + mc * 128: kc * 1024 + (mc + 1) * 128],
            lambda kc, tc: y2[:, kc * T + tc * NT: kc * T + tc * NT + NT],
            KC=DIC, MC=DC, TCn=TC4,
            epilogue=lambda eng, ps, mc, tc: eng.tensor_tensor(
                x[:, mc * T + tc * NT: mc * T + tc * NT + NT],
                x[:, mc * T + tc * NT: mc * T + tc * NT + NT], ps, ALU.add),
            name="l2a_womm")
    ctx.pop()
    ctx.pop()


def _consts_host():
    c = {}
    c["onescol"] = np.ones((128, 1), np.float32)
    c["onesrow"] = np.ones((1, 128), np.float32)
    c["identf"] = np.eye(128, dtype=np.float32)
    c["identb"] = np.eye(128, dtype=np.float32).astype(bf16)
    c["onescolb"] = np.ones((128, 1), np.float32).astype(bf16)
    esel = np.zeros((8, 8 * 128), np.float32)
    for e in range(8):
        esel[e, e * 128:(e + 1) * 128] = 1.0
    c["esel"] = esel
    return c


def _prep_weights_host(inputs):
    w = {}
    f32 = np.float32

    def put(key, arr):
        w[key] = np.ascontiguousarray(np.asarray(arr, f32))

    put("L1_ffn_w1", inputs["L1_ffn_w1"])
    put("L1_ffn_w2", inputs["L1_ffn_w2"])
    put("L1_ffn_b1c", np.asarray(inputs["L1_ffn_b1"], f32).reshape(32, 128).T)
    put("L1_ffn_b2c", np.asarray(inputs["L1_ffn_b2"], f32).reshape(8, 128).T)
    for k in ["L2_ffn_Wq", "L2_ffn_Wk", "L2_ffn_Wv", "L2_ffn_Wo", "L2_ffn_Wff"]:
        put(k, inputs[k])
    put("L2_attn_winx", np.asarray(inputs["L2_attn_W_in"], f32)[:, :2048])
    put("L2_attn_winz", np.asarray(inputs["L2_attn_W_in"], f32)[:, 2048:])
    _cw = np.asarray(inputs["L2_attn_conv_w"], f32)[:, 0, :]
    put("L2_attn_convwpp", _cw.reshape(16, 128, 4).transpose(1, 0, 2).reshape(128, 64))
    put("L2_attn_convbc", np.asarray(inputs["L2_attn_conv_b"], f32).reshape(16, 128).T)
    _wdt = np.asarray(inputs["L2_attn_W_dt"], f32)
    for _q in range(4):
        put(f"L2_attn_wdtq{_q}", _wdt[:, _q * 512:(_q + 1) * 512])
    put("L2_attn_bdtc", np.asarray(inputs["L2_attn_b_dt"], f32).reshape(16, 128).T)
    put("L2_attn_wx", inputs["L2_attn_W_x"])
    put("L2_attn_alogpp", np.asarray(inputs["L2_attn_A_log"], f32).reshape(256, 128).T)
    put("L2_attn_dcol", np.asarray(inputs["L2_attn_D"], f32).reshape(16, 128).T)
    put("L2_attn_wout", inputs["L2_attn_W_out"])
    put("L0_attn_wdbcT", np.asarray(inputs["L0_attn_W_dbc"], f32).T)
    put("L0_attn_wupT", np.asarray(inputs["L0_attn_W_up"], f32).T)
    _bdbc = np.asarray(inputs["L0_attn_b_dbc"], f32).reshape(96, 1)
    put("L0_attn_bdlc", _bdbc[0:64])
    put("L0_attn_bbmc", _bdbc[64:80])
    put("L0_attn_bcmc", _bdbc[80:96])
    put("L0_attn_bupc", np.asarray(inputs["L0_attn_b_up"], f32).reshape(8, 128).T)
    put("L0_attn_alogpp", np.asarray(inputs["L0_attn_A_log"], f32).reshape(128, 128).T)
    rgc = np.zeros((128, 16 * 128), f32)
    bdc = np.zeros((128, 16 * 128), f32)
    for g in range(16):
        for p in range(128):
            rgc[g * 8 + p // 16, g * 128 + p] = 1.0
            bdc[p, g * 128 + g * 8 + p // 16] = 1.0
    w["L0_rg"] = rgc.astype(bf16)
    w["L0_bd"] = bdc.astype(bf16)
    r2c = np.zeros((16, 128), f32)
    for p in range(128):
        r2c[p % 16, p] = 1.0
    w["L0_r2"] = r2c.astype(bf16)
    put("L0_ffn_ctlR", np.asarray(inputs["L0_ffn_controller"], f32).reshape(1024, 64))
    put("L0_ffn_f1R", np.asarray(inputs["L0_ffn_f1"], f32).reshape(1024, 4096))
    put("L0_ffn_f2R", np.asarray(inputs["L0_ffn_f2"], f32).reshape(4096, 1024))
    put("L0_ffn_besf", np.asarray(inputs["L0_ffn_bias"], f32).reshape(32, 128).T)
    tie = np.linspace(0.0, 1e-6, 16).astype(f32)
    w["L0_ffn_tieT"] = np.ascontiguousarray(
        np.tile(tie[None, :], (64, T // 16)).astype(f32))
    rp = np.zeros((64, 32 * 128), f32)
    for ch in range(32):
        for p in range(128):
            rp[2 * ch + p // 64, ch * 128 + p] = 1.0
    w["L0_ffn_rp"] = rp.astype(bf16)
    put("L1_attn_wqkvT", np.asarray(inputs["L1_attn_w_qkv"], f32).T)
    put("L1_attn_wprojT", np.asarray(inputs["L1_attn_w_proj"], f32).T)
    put("L1_attn_bpc", np.asarray(inputs["L1_attn_b_proj"], f32).reshape(8, 128).T)
    w["L1_attn_ln4096"] = np.full((1, 1), np.log(4096.0), f32)
    put("L3_attn_wqkvT", np.asarray(inputs["L3_attn_w_qkv"], f32).T)
    put("L3_attn_bqc", np.asarray(inputs["L3_attn_b_qkv"], f32)[:2048].reshape(16, 128).T)
    put("L3_attn_bvrow", np.asarray(inputs["L3_attn_b_qkv"], f32)[2048:].reshape(1, 1024))
    put("L3_attn_woutT", np.asarray(inputs["L3_attn_w_out"], f32).T)
    put("L3_attn_boc", np.asarray(inputs["L3_attn_b_out"], f32).reshape(8, 128).T)
    masks = np.full((128, 3 * 384), -1e30, f32)
    for dlt in range(3):
        ii = np.arange(128)[:, None] + dlt * 128
        jj = np.arange(384)[None, :]
        masks[:, dlt * 384:(dlt + 1) * 384] = np.where(np.abs(ii - jj) <= 8, 0.0, -1e30)
    w["L3_attn_masks"] = masks
    put("L3_ffn_router", inputs["L3_ffn_router"])
    put("L3_ffn_wg", np.asarray(inputs["L3_ffn_experts_gate"], f32).transpose(1, 0, 2).reshape(1024, 4096))
    put("L3_ffn_wu", np.asarray(inputs["L3_ffn_experts_up"], f32).transpose(1, 0, 2).reshape(1024, 4096))
    put("L3_ffn_wd", np.asarray(inputs["L3_ffn_experts_down"], f32).reshape(4096, 1024))
    put("L3_ffn_shared_gate", inputs["L3_ffn_shared_gate"])
    put("L3_ffn_shared_up", inputs["L3_ffn_shared_up"])
    put("L3_ffn_shared_down", inputs["L3_ffn_shared_down"])
    return w


L1F_KEYS = ["L1_ffn_w1", "L1_ffn_w2", "L1_ffn_b1c", "L1_ffn_b2c"]
L2F_KEYS = ["L2_ffn_Wq", "L2_ffn_Wk", "L2_ffn_Wv", "L2_ffn_Wo", "L2_ffn_Wff"]
L2A_KEYS = ["L2_attn_winx", "L2_attn_winz", "L2_attn_convwpp", "L2_attn_convbc",
            "L2_attn_wdtq0", "L2_attn_wdtq1", "L2_attn_wdtq2", "L2_attn_wdtq3",
            "L2_attn_bdtc", "L2_attn_wx", "L2_attn_alogpp", "L2_attn_dcol",
            "L2_attn_wout", "L0_rg", "L0_bd", "L0_r2"]
L0A_KEYS = ["L0_attn_wdbcT", "L0_attn_wupT", "L0_attn_bdlc", "L0_attn_bbmc",
            "L0_attn_bcmc", "L0_attn_bupc", "L0_attn_alogpp",
            "L0_rg", "L0_bd", "L0_r2"]
L0F_KEYS = ["L0_ffn_ctlR", "L0_ffn_f1R", "L0_ffn_f2R", "L0_ffn_besf",
            "L0_ffn_tieT", "L0_ffn_rp"]
L1A_KEYS = ["L1_attn_wqkvT", "L1_attn_wprojT", "L1_attn_bpc", "L1_attn_ln4096"]
L3A_KEYS = ["L3_attn_wqkvT", "L3_attn_bqc", "L3_attn_bvrow", "L3_attn_woutT",
            "L3_attn_boc", "L3_attn_masks"]
L3F_KEYS = ["L3_ffn_router", "L3_ffn_wg", "L3_ffn_wu", "L3_ffn_wd",
            "L3_ffn_shared_gate", "L3_ffn_shared_up", "L3_ffn_shared_down"]

STAGE_CONSTS = {
    "L1F": ["onescol", "onesrow"],
    "L2F": ["onescol", "onesrow", "onescolb"],
    "L3A": ["onescol", "onesrow", "identb", "onescolb"],
    "L1A": ["onescol", "onesrow", "onescolb"],
    "L3F": ["onescol", "onesrow", "identf", "onescolb", "esel"],
    "L0A": ["onescol", "onesrow", "onescolb"],
    "L0F": ["onescol", "onesrow", "onescolb"],
    "L2A": ["onescol", "onesrow", "onescolb", "identb"],
}

STAGES = [
    ("L0A", emit_l0_attn, L0A_KEYS),
    ("L0F", emit_l0_ffn, L0F_KEYS),
    ("L1A", emit_l1_attn, L1A_KEYS),
    ("L1F", emit_l1_ffn, L1F_KEYS),
    ("L2A", emit_l2_attn, L2A_KEYS),
    ("L2F", emit_l2_ffn, L2F_KEYS),
    ("L3A", emit_l3_attn, L3A_KEYS),
    ("L3F", emit_l3_ffn, L3F_KEYS),
]

_NPDT = {np.float32: F32, np.int32: I32}


def build_stage_program(emit_fn, wkeys, w, consts, stage_name=None):
    if stage_name in STAGE_CONSTS:
        consts = {k: consts[k] for k in STAGE_CONSTS[stage_name]}
    nc = bass.Bass("TRN2", target_bir_lowering=False)
    ctx = Ctx(nc)
    xin = nc.dram_tensor("xin", [D, T], F32, kind="ExternalInput")
    xout = nc.dram_tensor("xout", [D, T], F32, kind="ExternalOutput")
    weights = {k: nc.dram_tensor(k, list(w[k].shape),
                                 _NPDT.get(w[k].dtype.type, BF16), kind="ExternalInput")
               for k in wkeys}
    cdram = {k: nc.dram_tensor("c_" + k, list(consts[k].shape),
                               _NPDT.get(consts[k].dtype.type, BF16), kind="ExternalInput")
             for k in consts}

    x = ctx.sbuf("xres", [128, DC * T], F32)
    emit_restore(ctx, x, xin.ap(), DC, T, "ldx")
    cs = {k: load_raw(ctx, cdram[k].ap(), consts[k].shape[0], consts[k].shape[1],
                      "c_" + k, _NPDT.get(consts[k].dtype.type, BF16)) for k in consts}

    emit_fn(ctx, x, weights, cs)

    emit_spill(ctx, x, xout.ap(), DC, T, "stx")
    ctx.close()
    mybir.codegen_inst_isa_subclasses(nc)
    return nc


FUSE = True


def build_fused_program(w, consts, allkeys, callkeys):
    nc = bass.Bass("TRN2", target_bir_lowering=False)
    ctx = Ctx(nc)
    xin = nc.dram_tensor("xin", [D, T], F32, kind="ExternalInput")
    xout = nc.dram_tensor("xout", [D, T], F32, kind="ExternalOutput")
    weights = {k: nc.dram_tensor(k, list(w[k].shape),
                                 _NPDT.get(w[k].dtype.type, BF16), kind="ExternalInput")
               for k in allkeys}
    cdram = {k: nc.dram_tensor("c_" + k, list(consts[k].shape),
                               _NPDT.get(consts[k].dtype.type, BF16),
                               kind="ExternalInput")
             for k in callkeys}

    x = ctx.sbuf("xres", [128, DC * T], F32)
    emit_restore(ctx, x, xin.ap(), DC, T, "ldx")
    common = [k for k in ("onescol", "onesrow", "onescolb") if k in callkeys]
    cs = {k: load_raw(ctx, cdram[k].ap(), consts[k].shape[0], consts[k].shape[1],
                      "c_" + k, _NPDT.get(consts[k].dtype.type, BF16))
          for k in common}

    for name, emit_fn, wkeys in STAGES:
        extra = [k for k in STAGE_CONSTS[name] if k not in common]
        ctx.push()
        cse = dict(cs)
        for k in extra:
            cse[k] = load_raw(ctx, cdram[k].ap(), consts[k].shape[0],
                              consts[k].shape[1], "c_" + k,
                              _NPDT.get(consts[k].dtype.type, BF16))
        emit_fn(ctx, x, weights, cse)
        ctx.pop()

    emit_spill(ctx, x, xout.ap(), DC, T, "stx")
    ctx.close()
    mybir.codegen_inst_isa_subclasses(nc)
    return nc


def kernel(**inputs):
    xfull = np.asarray(inputs["L0_attn_x"], np.float32)
    consts = _consts_host()
    w = _prep_weights_host(inputs)
    xs = [np.ascontiguousarray(xfull[b].T) for b in range(NCORES)]

    if FUSE:
        allkeys, callkeys = [], []
        for name, _, wkeys in STAGES:
            for k in wkeys:
                if k not in allkeys:
                    allkeys.append(k)
            for k in STAGE_CONSTS[name]:
                if k not in callkeys:
                    callkeys.append(k)
        nc = build_fused_program(w, consts, allkeys, callkeys)
        in_maps = []
        for b in range(NCORES):
            m = {"xin": xs[b]}
            for k in allkeys:
                m[k] = w[k]
            for k in callkeys:
                m["c_" + k] = consts[k]
            in_maps.append(m)
        res = run_bass_kernel_spmd(nc, in_maps, core_ids=list(range(NCORES)))
        xs = [np.asarray(res.results[b]["xout"]) for b in range(NCORES)]
        out = np.stack([x.T for x in xs]).astype(np.float32)
        return np.ascontiguousarray(out)

    for name, emit_fn, wkeys in STAGES:
        cuse = {k: consts[k] for k in STAGE_CONSTS.get(name, list(consts))}
        nc = build_stage_program(emit_fn, wkeys, w, consts, stage_name=name)
        in_maps = []
        for b in range(NCORES):
            m = {"xin": xs[b]}
            for k in wkeys:
                m[k] = w[k]
            for k in cuse:
                m["c_" + k] = cuse[k]
            in_maps.append(m)
        res = run_bass_kernel_spmd(nc, in_maps, core_ids=list(range(NCORES)))
        xs = [np.asarray(res.results[b]["xout"]) for b in range(NCORES)]

    out = np.stack([x.T for x in xs]).astype(np.float32)
    return np.ascontiguousarray(out)
